# version 01
# baseline (speedup 1.0000x reference)
import contextlib
import numpy as np
import ml_dtypes
import concourse.bass as bass
import concourse.mybir as mybir
from concourse.bass_utils import run_bass_kernel_spmd

F32 = mybir.dt.float32
BF16 = mybir.dt.bfloat16
I32 = mybir.dt.int32
AF = mybir.ActivationFunctionType
ALU = mybir.AluOpType
AX = mybir.AxisListType

NCORES = 8
BF = ml_dtypes.bfloat16

D = 2048
NJ = 16
SEQ = 2048
NSEQ = 2
H = 16
KVH = 4
HD = 128
LN_EPS = 1e-5
LA_EPS = 1e-6


class P:

    def __init__(self, name):
        self.nc = bass.Bass("TRN2", target_bir_lowering=False)
        self.es = contextlib.ExitStack()
        self.sems = {}
        self.C = {}

    def sem(self, name):
        self.sems[name] = self.es.enter_context(self.nc.semaphore(name))
        self.C[name] = 0

    def sb(self, name, shape, dt, es=None):
        return (es or self.es).enter_context(self.nc.sbuf_tensor(name, shape, dt))

    def ps(self, name, shape, dt=F32, es=None):
        return (es or self.es).enter_context(self.nc.psum_tensor(name, shape, dt))

    def dram(self, name, shape, dt, kind="Internal"):
        return self.nc.dram_tensor(name, shape, dt, kind=kind).ap()

    def inc(self, instr, name, k=1):
        instr.then_inc(self.sems[name], k)
        self.C[name] += k
        return self.C[name]

    def wait(self, eng, *pairs):
        first = True
        for nm, v in pairs:
            if v <= 0:
                continue
            if not first:
                eng.nop()
            eng.wait_ge(self.sems[nm], v)
            first = False

    def barrier(self, blk):
        if "bar" not in self.sems:
            self.sem("bar")
        @blk.sync
        def _(e):
            self.wait(e, *[(nm, self.C[nm]) for nm in self.sems if nm != "bar"])
            e.nop()
        tgt = self.C["bar"] + 5
        for engsec in (blk.sync, blk.tensor, blk.scalar, blk.vector, blk.gpsimd):
            @engsec
            def _(e):
                e.drain()
                e.nop().then_inc(self.sems["bar"], 1)
        self.C["bar"] = tgt
        for engsec in (blk.sync, blk.tensor, blk.scalar, blk.vector, blk.gpsimd):
            @engsec
            def _(e):
                e.wait_ge(self.sems["bar"], tgt)



def build_A(S=SEQ):
    p = P("A")
    nc = p.nc
    NT = S // 512
    xin = p.dram("xin", [NSEQ, NJ, 128, S], F32, kind="ExternalInput")
    wkv = p.dram("wkv", [2, NJ, 128, 512], F32, kind="ExternalInput")
    bkv = p.dram("bkv", [2, 512], F32, kind="ExternalInput")
    wq = p.dram("wq", [H, 128, NJ * 128], F32, kind="ExternalInput")
    bq = p.dram("bq", [1, H * 128], F32, kind="ExternalInput")
    wo = p.dram("wo", [NJ, 128, H * 128], F32, kind="ExternalInput")
    x2 = p.dram("x2", [NSEQ, NJ, 128, S], F32, kind="ExternalOutput")
    wqb_scr = p.dram("wqb_scr", [H, 128, NJ * 128], BF16)
    wob_scr = p.dram("wob_scr", [NJ, 128, H * 128], BF16)

    for s in ("lx", "lw", "pe", "ve", "ac", "ow", "sc",
              "w0", "w1", "s0", "s1", "q0", "q1", "o0", "o1", "x0", "x1"):
        p.sem(s)

    ones_r = p.sb("ones_r", [1, 128], F32)
    ones_c = p.sb("ones_c", [128, 1], BF16)
    bkv_row = p.sb("bkv_row", [1, 1024], F32)
    bkvB = p.sb("bkvB", [128, 1024], F32)
    bq_sb = p.sb("bq_sb", [128, H], F32)
    kv_f = p.sb("kv_f", [128, NSEQ * 4 * 132], F32)
    kv_b = p.sb("kv_b", [128, NSEQ * 4 * 132], BF16)

    pes_w = contextlib.ExitStack()
    wkv_sb = p.sb("wkv_sb", [128, 2 * NJ * 512], BF16, es=pes_w)
    with nc.Block() as blk:
        pes = contextlib.ExitStack()
        wtmp = [p.sb(f"wtmp{i}", [128, 2048], F32, es=pes) for i in range(2)]
        wqb_tmp = [p.sb(f"wqbt{i}", [128, 2048], BF16, es=pes) for i in range(2)]
        pb0 = p.ps("pb0", [128, 512], F32, es=pes)
        pb1 = p.ps("pb1", [128, 512], F32, es=pes)

        bq_row = p.sb("bq_row", [1, H * 128], F32, es=pes)

        @blk.sync
        def _(e):
            i = e.dma_start(out=bkv_row[0:1, 0:512], in_=bkv[0:1, :])
            p.inc(i, "lx", 16)
            i = e.dma_start(out=bkv_row[0:1, 512:1024], in_=bkv[1:2, :])
            p.inc(i, "lx", 16)
            i = e.dma_start(out=bq_row[:], in_=bq[:, :])
            p.inc(i, "lx", 16)

        @blk.vector
        def _(e):
            e.memset(ones_r[:], 1.0)
            i = e.memset(ones_c[:], 1.0)
            p.inc(i, "ve")

        @blk.tensor
        def _(e):
            p.wait(e, ("lx", 48), ("ve", 1))
            e.matmul(pb0[:], ones_r[:], bkv_row[0:1, 0:512], start=True, stop=True)
            i = e.matmul(pb1[:], ones_r[:], bkv_row[0:1, 512:1024], start=True, stop=True)
            p.inc(i, "pe")

        @blk.scalar
        def _(e):
            p.wait(e, ("pe", p.C["pe"]))
            e.copy(bkvB[:, 0:512], pb0[:])
            i = e.copy(bkvB[:, 512:1024], pb1[:])
            p.inc(i, "ac")

        @blk.tensor
        def _(e):
            p.wait(e, ("ac", p.C["ac"]))
            for h in range(H):
                i = e.matmul(pb0[:, h:h + 1], bq_row[0:1, h * 128:(h + 1) * 128],
                             ones_r[0:1, 0:1], start=True, stop=True,
                             skip_group_check=True)
            p.inc(i, "pe")

        @blk.scalar
        def _(e):
            p.wait(e, ("pe", p.C["pe"]))
            i = e.copy(bq_sb[:, 0:H], pb0[:, 0:H])
            p.inc(i, "ac")

        for t in range(2 * NJ):
            g, j = divmod(t, NJ)
            buf = wtmp[t % 2]
            ws = "w%d" % (t % 2)

            @blk.gpsimd
            def _(e):
                p.wait(e, ("ve", p.C["ve"] - 1))
                i = e.dma_start(out=buf[:, 0:512], in_=wkv[g, j])
                p.inc(i, ws, 16)

            @blk.vector
            def _(e):
                p.wait(e, (ws, p.C[ws]))
                i = e.tensor_copy(wkv_sb[:, (g * NJ + j) * 512:(g * NJ + j + 1) * 512], buf[:, 0:512])
                p.inc(i, "ve")

        for t in range(NJ):
            buf = wtmp[t % 2]
            bbuf = wqb_tmp[t % 2]
            ws = "w%d" % (t % 2)
            ss = "s%d" % (t % 2)

            @blk.gpsimd
            def _(e):
                p.wait(e, ("ve", p.C["ve"] - 1))
                i = e.dma_start(out=buf[:], in_=wo[t])
                p.inc(i, ws, 16)

            @blk.vector
            def _(e):
                p.wait(e, (ws, p.C[ws]), (ss, p.C[ss]))
                i = e.tensor_copy(bbuf[:], buf[:])
                p.inc(i, "ve")

            @blk.sync
            def _(e):
                p.wait(e, ("ve", p.C["ve"]))
                i = e.dma_start(out=wob_scr[t], in_=bbuf[:])
                p.inc(i, ss, 16)

        for t in range(H):
            buf = wtmp[t % 2]
            bbuf = wqb_tmp[t % 2]
            ws = "w%d" % (t % 2)
            ss = "s%d" % (t % 2)

            @blk.gpsimd
            def _(e):
                p.wait(e, ("ve", p.C["ve"] - 1))
                i = e.dma_start(out=buf[:], in_=wq[t])
                p.inc(i, ws, 16)

            @blk.vector
            def _(e):
                p.wait(e, (ws, p.C[ws]), (ss, p.C[ss]))
                i = e.tensor_copy(bbuf[:], buf[:])
                p.inc(i, "ve")

            @blk.sync
            def _(e):
                p.wait(e, ("ve", p.C["ve"]))
                i = e.dma_start(out=wqb_scr[t], in_=bbuf[:])
                p.inc(i, ss, 16)

        p.barrier(blk)
        pes.close()

        pes = contextlib.ExitStack()
        xt = p.sb("xtP1", [128, NJ * 512], F32, es=pes)
        xb1 = p.sb("xbP1", [128, NJ * 512], BF16, es=pes)
        t0 = p.sb("t0P1", [128, 512], F32, es=pes)
        tm = p.sb("tmP1", [128, 128], F32, es=pes)
        te = p.sb("teP1", [128, 128], F32, es=pes)
        tr = p.sb("trP1", [128, 128], F32, es=pes)
        skb = p.sb("skbP1", [128, 128], BF16, es=pes)
        vxb = p.sb("vxbP1", [128, 132], BF16, es=pes)
        pproj = [p.ps(f"pprojP1{i}", [128, 512], F32, es=pes) for i in range(2)]
        pkv = [p.ps(f"pkvP1{i}", [128, 512], F32, es=pes) for i in range(2)]

        @blk.vector
        def _(e):
            i = e.memset(vxb[:, 128:132], 1.0)
            p.inc(i, "ve")

        t0_ve = []
        skb2 = p.sb("skb2P1", [128, 128], BF16, es=pes)
        vxb2 = p.sb("vxb2P1", [128, 132], BF16, es=pes)

        @blk.vector
        def _(e):
            i = e.memset(vxb2[:, 128:132], 1.0)
            p.inc(i, "ve")

        skbs = [skb, skb2]
        vxbs = [vxb, vxb2]

        step_idx = 0
        kv_pe = {}
        skb_ve = {}
        for sq in range(NSEQ):
            for st in range(NT):
                @blk.sync
                def _(e):
                    if sq + st > 0:
                        p.wait(e, ("pe", p.C["pe"]))
                    for j in range(NJ):
                        i = e.dma_start(out=xt[:, j * 512:(j + 1) * 512],
                                        in_=xin[sq, j, :, st * 512:(st + 1) * 512])
                        p.inc(i, "lx", 16)

                @blk.vector
                def _(e):
                    p.wait(e, ("lx", p.C["lx"]))
                    for j in range(NJ):
                        i = e.tensor_copy(xb1[:, j * 512:(j + 1) * 512], xt[:, j * 512:(j + 1) * 512])
                    p.inc(i, "ve")
                cast_ve = p.C["ve"]

                for sub in range(4):
                    for g in range(2):
                        i_step = step_idx
                        pp = pproj[i_step % 2]

                        @blk.tensor
                        def _(e):
                            w2 = t0_ve[i_step - 2] if i_step >= 2 else 0
                            p.wait(e, ("ve", max(cast_ve, w2)))
                            for j in range(NJ):
                                i = e.matmul(pp[:],
                                             xb1[:, j * 512 + sub * 128: j * 512 + sub * 128 + 128],
                                             wkv_sb[:, (g * NJ + j) * 512:(g * NJ + j + 1) * 512],
                                             start=(j == 0), stop=(j == NJ - 1),
                                             skip_group_check=True)
                            p.inc(i, "pe")
                        proj_pe = p.C["pe"]

                        @blk.vector
                        def _(e):
                            p.wait(e, ("pe", proj_pe))
                            i = e.tensor_tensor(t0[:], pp[:], bkvB[:, g * 512:(g + 1) * 512], ALU.add)
                            p.inc(i, "ve")
                        t0_ve.append(p.C["ve"])

                        for kk in range(2):
                            kvh = g * 2 + kk
                            sk_b = skbs[kk]
                            vx_b = vxbs[kk]

                            @blk.vector
                            def _(e):
                                prev = kv_pe.get((i_step - 1, kk), 0)
                                p.wait(e, ("pe", prev))
                                e.tensor_scalar(tm[:], t0[:, kk * 256:kk * 256 + 128], 0.0, None, ALU.min)
                                i = e.tensor_relu(tr[:], t0[:, kk * 256:kk * 256 + 128])
                                p.inc(i, "ve")

                            @blk.scalar
                            def _(e):
                                p.wait(e, ("ve", p.C["ve"]))
                                i = e.activation(te[:], tm[:], AF.Exp)
                                p.inc(i, "ac")

                            @blk.vector
                            def _(e):
                                p.wait(e, ("ac", p.C["ac"]))
                                e.tensor_tensor(sk_b[:], te[:], tr[:], ALU.add)
                                i = e.tensor_copy(vx_b[:, 0:128], t0[:, kk * 256 + 128:kk * 256 + 256])
                                p.inc(i, "ve")
                            skb_ve[(i_step, kk)] = p.C["ve"]

                            @blk.tensor
                            def _(e):
                                p.wait(e, ("ve", skb_ve[(i_step, kk)]))
                                bank = pkv[kvh // 3]
                                off = (kvh % 3) * 132
                                i = e.matmul(bank[:, off:off + 132], sk_b[:], vx_b[:],
                                             start=(st == 0 and sub == 0 and kvh in (0, 3)),
                                             stop=(st == NT - 1 and sub == 3 and kvh in (2, 3)),
                                             skip_group_check=True)
                                p.inc(i, "pe")
                            kv_pe[(i_step, kk)] = p.C["pe"]

                        step_idx += 1

            @blk.vector
            def _(e):
                p.wait(e, ("pe", p.C["pe"]))
                base = sq * 4 * 132
                e.tensor_copy(kv_f[:, base:base + 396], pkv[0][:, 0:396])
                e.tensor_copy(kv_f[:, base + 396:base + 528], pkv[1][:, 0:132])
                e.tensor_copy(kv_b[:, base:base + 396], pkv[0][:, 0:396])
                i = e.tensor_copy(kv_b[:, base + 396:base + 528], pkv[1][:, 0:132])
                p.inc(i, "ve")

            @blk.tensor
            def _(e):
                p.wait(e, ("ve", p.C["ve"]))
                p.inc(e.nop(), "pe")

        p.barrier(blk)
        pes.close()
        pes_w.close()

        pes = contextlib.ExitStack()
        xt3 = p.sb("xtP3", [128, NJ * 512], F32, es=pes)
        xb3 = p.sb("xbP3", [128, NJ * 512], BF16, es=pes)
        wqh = [p.sb(f"wqhP3{i}", [128, 2048], BF16, es=pes) for i in range(2)]
        sqb = [p.sb(f"sqbP3{i}", [128, 512], BF16, es=pes) for i in range(2)]
        tq = p.sb("tqP3", [128, 512], F32, es=pes)
        tqm = p.sb("tqmP3", [128, 512], F32, es=pes)
        tqe = p.sb("tqeP3", [128, 512], F32, es=pes)
        tqr = p.sb("tqrP3", [128, 512], F32, es=pes)
        nrow = p.sb("nrowP3", [1, 512], F32, es=pes)
        recB = [p.sb(f"recBP3{i}", [128, 512], F32, es=pes) for i in range(2)]
        attnN = p.sb("attnNP3", [128, H * 512], BF16, es=pes)
        y1 = p.sb("y1P3", [128, NJ * 512], F32, es=pes)
        y1b = p.sb("y1bP3", [128, 512], BF16, es=pes)
        y1s = p.sb("y1sP3", [128, 512], BF16, es=pes)
        srow = p.sb("srowP3", [1, 1536], F32, es=pes)
        rstdB = p.sb("rstdBP3", [128, 512], F32, es=pes)
        mrB = p.sb("mrBP3", [128, 512], F32, es=pes)
        x2t = [p.sb(f"x2tP3{i}", [128, 512], F32, es=pes) for i in range(2)]
        woh = [p.sb(f"wohP3{i}", [128, 2048], BF16, es=pes) for i in range(2)]
        PS = [p.ps(f"PSP3{i}", [128, 512], F32, es=pes) for i in range(6)]

        for sq in range(NSEQ):
            for st in range(NT):
                pe_prev_tile = p.C["pe"]

                @blk.sync
                def _(e):
                    if sq + st > 0:
                        p.wait(e, ("ve", p.C["ve"]))
                    for j in range(NJ):
                        i = e.dma_start(out=xt3[:, j * 512:(j + 1) * 512],
                                        in_=xin[sq, j, :, st * 512:(st + 1) * 512])
                        p.inc(i, "lx", 16)
                    i = e.dma_start(out=wqh[0][:], in_=wqb_scr[0])
                    p.inc(i, "q0", 16)

                @blk.vector
                def _(e):
                    p.wait(e, ("lx", p.C["lx"]), ("pe", pe_prev_tile))
                    for j in range(NJ):
                        i = e.tensor_copy(xb3[:, j * 512:(j + 1) * 512], xt3[:, j * 512:(j + 1) * 512])
                    p.inc(i, "ve")
                xcast_ve = p.C["ve"]

                qproj_pe = {}
                evacq_ve = {}
                sqb_ve = {}

                def head_front(h):
                    pq = PS[h % 2]

                    @blk.sync
                    def _(e):
                        if h + 1 < H:
                            p.wait(e, ("pe", qproj_pe.get(h - 1, 0)))
                            i = e.dma_start(out=wqh[(h + 1) % 2][:], in_=wqb_scr[h + 1])
                            p.inc(i, "q%d" % ((h + 1) % 2), 16)

                    @blk.tensor
                    def _(e):
                        wv = max(xcast_ve, evacq_ve.get(h - 2, 0))
                        p.wait(e, ("q%d" % (h % 2), p.C["q%d" % (h % 2)]),
                               ("ve", wv))
                        for j in range(NJ):
                            i = e.matmul(pq[:], wqh[h % 2][:, j * 128:(j + 1) * 128],
                                         xb3[:, j * 512:(j + 1) * 512],
                                         start=(j == 0), stop=(j == NJ - 1),
                                         skip_group_check=True)
                        p.inc(i, "pe")
                    qproj_pe[h] = p.C["pe"]

                    @blk.vector
                    def _(e):
                        p.wait(e, ("pe", qproj_pe[h]))
                        e.tensor_scalar(tq[:], pq[:], bq_sb[:, h:h + 1], None, ALU.add)
                        e.tensor_scalar(tqm[:], tq[:], 0.0, None, ALU.min)
                        i = e.tensor_relu(tqr[:], tq[:])
                        p.inc(i, "ve")
                    evacq_ve[h] = p.C["ve"]

                    @blk.scalar
                    def _(e):
                        p.wait(e, ("ve", evacq_ve[h]))
                        i = e.activation(tqe[:], tqm[:], AF.Exp)
                        p.inc(i, "ac")

                    @blk.vector
                    def _(e):
                        p.wait(e, ("ac", p.C["ac"]))
                        i = e.tensor_tensor(sqb[h % 2][:], tqe[:], tqr[:], ALU.add)
                        p.inc(i, "ve")
                    sqb_ve[h] = p.C["ve"]

                def head_back(h):
                    pat = PS[2 + h % 2]
                    kvh = h // 4
                    kvoff = (sq * 4 + kvh) * 132

                    @blk.tensor
                    def _(e):
                        p.wait(e, ("ve", sqb_ve[h]))
                        e.matmul(PS[4][0:1, :], kv_b[:, kvoff + 128:kvoff + 129], sqb[h % 2][:],
                                 start=True, stop=True, skip_group_check=True)
                        i = e.matmul(pat[:], kv_b[:, kvoff:kvoff + 128], sqb[h % 2][:],
                                     start=True, stop=True, skip_group_check=True)
                        p.inc(i, "pe")
                    na_pe = p.C["pe"]

                    @blk.vector
                    def _(e):
                        p.wait(e, ("pe", na_pe))
                        e.tensor_scalar(nrow[:], PS[4][0:1, :], LA_EPS, None, ALU.add)
                        i = e.reciprocal(nrow[:], nrow[:])
                        p.inc(i, "ve")
                    nr_ve = p.C["ve"]

                    @blk.tensor
                    def _(e):
                        p.wait(e, ("ve", nr_ve))
                        i = e.matmul(PS[5][:], ones_r[:], nrow[:], start=True, stop=True,
                                     skip_group_check=True)
                        p.inc(i, "pe")
                    bc_pe = p.C["pe"]

                    @blk.scalar
                    def _(e):
                        p.wait(e, ("pe", bc_pe))
                        i = e.copy(recB[h % 2][:], PS[5][:])
                        p.inc(i, "ac")
                    rb_ac = p.C["ac"]

                    @blk.vector
                    def _(e):
                        p.wait(e, ("ac", rb_ac))
                        i = e.tensor_tensor(attnN[:, h * 512:(h + 1) * 512], pat[:], recB[h % 2][:], ALU.mult)
                        p.inc(i, "ve")

                for h in range(H):
                    head_front(h)
                    if h > 0:
                        head_back(h - 1)
                head_back(H - 1)

                attn_done_ve = p.C["ve"]
                y1_ve = {}
                sqr_ac = {}
                wo_lw = {}
                wo_pe = {}

                @blk.gpsimd
                def _(e):
                    p.wait(e, ("pe", pe_prev_tile))
                    for dmp in range(2):
                        i = e.dma_start(out=woh[dmp][:], in_=wob_scr[dmp])
                        wo_lw[dmp] = p.inc(i, "o%d" % dmp, 16)

                def wo_mm_tensor(dm):
                    po = PS[dm % 2]

                    @blk.tensor
                    def _(e):
                        p.wait(e, ("o%d" % (dm % 2), wo_lw[dm]))
                        e.nop()
                        if dm == 0:
                            p.wait(e, ("ve", attn_done_ve))
                        elif dm >= 2:
                            p.wait(e, ("ve", y1_ve[dm - 2]))
                        for hh in range(H):
                            i = e.matmul(po[:], woh[dm % 2][:, hh * 128:(hh + 1) * 128],
                                         attnN[:, hh * 512:(hh + 1) * 512],
                                         start=(hh == 0), stop=(hh == H - 1),
                                         skip_group_check=True)
                        p.inc(i, "pe")
                    wo_pe[dm] = p.C["pe"]

                    if dm + 2 < NJ:
                        @blk.gpsimd
                        def _(e):
                            p.wait(e, ("pe", wo_pe[dm]))
                            i = e.dma_start(out=woh[dm % 2][:], in_=wob_scr[dm + 2])
                            wo_lw[dm + 2] = p.inc(i, "o%d" % (dm % 2), 16)

                def stats_mm(dm):
                    @blk.tensor
                    def _(e):
                        p.wait(e, ("ac", sqr_ac[dm]))
                        e.matmul(PS[4][0:1, :], ones_c[:], y1b[:],
                                 start=(dm == 0), stop=(dm == NJ - 1), skip_group_check=True)
                        i = e.matmul(PS[5][0:1, :], ones_c[:], y1s[:],
                                     start=(dm == 0), stop=(dm == NJ - 1), skip_group_check=True)
                        p.inc(i, "pe")

                def wo_evac(dm):
                    po = PS[dm % 2]
                    pe_snap = p.C["pe"]

                    @blk.vector
                    def _(e):
                        p.wait(e, ("pe", pe_snap))
                        e.tensor_tensor(y1[:, dm * 512:(dm + 1) * 512], po[:],
                                        xt3[:, dm * 512:(dm + 1) * 512], ALU.add)
                        i = e.tensor_copy(y1b[:], y1[:, dm * 512:(dm + 1) * 512])
                        p.inc(i, "ve")
                    y1_ve[dm] = p.C["ve"]

                    @blk.scalar
                    def _(e):
                        p.wait(e, ("ve", y1_ve[dm]))
                        i = e.activation(y1s[:], y1[:, dm * 512:(dm + 1) * 512], AF.Square)
                        p.inc(i, "ac")
                    sqr_ac[dm] = p.C["ac"]

                for dm in range(NJ):
                    wo_mm_tensor(dm)
                    if dm > 0:
                        stats_mm(dm - 1)
                    wo_evac(dm)
                stats_mm(NJ - 1)

                @blk.vector
                def _(e):
                    p.wait(e, ("pe", p.C["pe"]))
                    e.tensor_scalar(srow[0:1, 0:512], PS[4][0:1, :], 1.0 / D, None, ALU.mult)
                    e.tensor_scalar(srow[0:1, 512:1024], PS[5][0:1, :], 1.0 / D, None, ALU.mult)
                    e.tensor_tensor(srow[0:1, 1024:1536], srow[0:1, 0:512], srow[0:1, 0:512], ALU.mult)
                    e.tensor_tensor(srow[0:1, 512:1024], srow[0:1, 512:1024], srow[0:1, 1024:1536], ALU.subtract)
                    i = e.tensor_scalar(srow[0:1, 512:1024], srow[0:1, 512:1024], LN_EPS, None, ALU.add)
                    p.inc(i, "ve")

                @blk.scalar
                def _(e):
                    p.wait(e, ("ve", p.C["ve"]))
                    i = e.activation(srow[0:1, 512:1024], srow[0:1, 512:1024], AF.Sqrt)
                    p.inc(i, "ac")

                @blk.vector
                def _(e):
                    p.wait(e, ("ac", p.C["ac"]))
                    e.reciprocal(srow[0:1, 512:1024], srow[0:1, 512:1024])
                    i = e.tensor_tensor(srow[0:1, 1024:1536], srow[0:1, 0:512], srow[0:1, 512:1024], ALU.mult)
                    p.inc(i, "ve")

                @blk.tensor
                def _(e):
                    p.wait(e, ("ve", p.C["ve"]))
                    e.matmul(PS[2][:], ones_r[:], srow[0:1, 512:1024], start=True, stop=True,
                             skip_group_check=True)
                    i = e.matmul(PS[3][:], ones_r[:], srow[0:1, 1024:1536], start=True, stop=True,
                                 skip_group_check=True)
                    p.inc(i, "pe")

                @blk.scalar
                def _(e):
                    p.wait(e, ("pe", p.C["pe"]))
                    e.copy(rstdB[:], PS[2][:])
                    i = e.copy(mrB[:], PS[3][:])
                    p.inc(i, "ac")
                ln_ac = p.C["ac"]

                for dm in range(NJ):
                    xb = x2t[dm % 2]
                    xs = "x%d" % (dm % 2)

                    @blk.vector
                    def _(e):
                        p.wait(e, ("ac", ln_ac), (xs, p.C[xs]))
                        e.tensor_tensor(xb[:], y1[:, dm * 512:(dm + 1) * 512], rstdB[:], ALU.mult)
                        i = e.tensor_tensor(xb[:], xb[:], mrB[:], ALU.subtract)
                        p.inc(i, "ve")

                    @blk.gpsimd
                    def _(e):
                        p.wait(e, ("ve", p.C["ve"]))
                        i = e.dma_start(out=x2[sq, dm, :, st * 512:(st + 1) * 512], in_=xb[:])
                        p.inc(i, xs, 16)

        @blk.gpsimd
        def _(e):
            p.wait(e, ("x0", p.C["x0"]))
            e.nop()
            e.wait_ge(p.sems["x1"], p.C["x1"])

    return p



def prep_A_weights(Wq, bqv, Wk, bk, Wv, bv, Wo):
    WkT = np.ascontiguousarray(Wk.T)
    WvT = np.ascontiguousarray(Wv.T)
    wkv = np.empty((2, NJ, 128, 512), np.float32)
    bkv = np.empty((2, 512), np.float32)
    for g in range(2):
        for kk in range(2):
            kvh = 2 * g + kk
            for j in range(NJ):
                wkv[g, j, :, kk * 256:kk * 256 + 128] = WkT[j * 128:(j + 1) * 128, kvh * 128:(kvh + 1) * 128]
                wkv[g, j, :, kk * 256 + 128:kk * 256 + 256] = WvT[j * 128:(j + 1) * 128, kvh * 128:(kvh + 1) * 128]
            bkv[g, kk * 256:kk * 256 + 128] = bk[kvh * 128:(kvh + 1) * 128]
            bkv[g, kk * 256 + 128:kk * 256 + 256] = bv[kvh * 128:(kvh + 1) * 128]
    wq = np.ascontiguousarray(
        Wq.reshape(H, 128, NJ, 128).transpose(0, 3, 2, 1).reshape(H, 128, NJ * 128))
    bq_r = np.ascontiguousarray(bqv.reshape(1, H * 128))
    wo = np.ascontiguousarray(
        Wo.reshape(NJ, 128, H, 128).transpose(0, 3, 2, 1).reshape(NJ, 128, H * 128))
    return dict(wkv=wkv, bkv=bkv, wq=wq, bq=bq_r, wo=wo)


def to_fm(x):
    n, S, d = x.shape
    return np.ascontiguousarray(x.transpose(0, 2, 1).reshape(n, NJ, 128, S))


def from_fm(x):
    n = x.shape[0]
    S = x.shape[3]
    return np.ascontiguousarray(x.reshape(n, NJ * 128, S).transpose(0, 2, 1))


def run_A(x_all, wa, S=SEQ, ncores=NCORES):
    p = build_A(S)
    in_maps = []
    for c in range(ncores):
        m = dict(wa)
        m["xin"] = to_fm(x_all[c * NSEQ:(c + 1) * NSEQ])
        in_maps.append(m)
    res = run_bass_kernel_spmd(p.nc, in_maps, core_ids=list(range(ncores)))
    out = np.concatenate([from_fm(r["x2"]) for r in res.results], axis=0)
    p.es.close()
    return out





NC0 = 64
RT2 = np.float32(1.0 / np.sqrt(8192.0))


def build_B(S=SEQ, dbg=False):
    p = P("B")
    nc = p.nc
    T = NSEQ * S
    NTS = T // 512
    x2 = p.dram("x2", [NSEQ, NJ, 128, S], F32, kind="ExternalInput")
    w1h = p.dram("w1h", [NC0, 128, NJ * 128], F32, kind="ExternalInput")
    w3h = p.dram("w3h", [NC0, 128, NJ * 128], F32, kind="ExternalInput")
    w2r = p.dram("w2r", [64, 128, D], F32, kind="ExternalInput")
    h64 = p.dram("h64", [64, 64], BF16, kind="ExternalInput")
    h128 = p.dram("h128", [128, 128], BF16, kind="ExternalInput")
    x3 = p.dram("x3", [NSEQ, NJ, 128, S], F32, kind="ExternalOutput")
    scrkind = "ExternalOutput" if dbg else "Internal"
    ascr = p.dram("ascr", [NC0, 128, T], BF16, kind=scrkind)
    w2b_scr = p.dram("w2b_scr", [NJ, 128, 64, 128], BF16, kind=scrkind)

    for s in ("lx", "lw", "pe", "ve", "ac", "ow", "sa",
              "w0", "w1", "s0", "s1", "a0", "a1", "d0", "d1", "x0", "x1"):
        p.sem(s)

    ones_r = p.sb("ones_r", [1, 128], F32)
    ones_c = p.sb("ones_c", [128, 1], BF16)
    h64_sb = p.sb("h64_sb", [64, 64], BF16)
    h128_sb = p.sb("h128_sb", [128, 128], BF16)

    with nc.Block() as blk:
        @blk.sync
        def _(e):
            i = e.dma_start(out=h64_sb[:], in_=h64[:, :])
            p.inc(i, "lx", 16)
            i = e.dma_start(out=h128_sb[:], in_=h128[:, :])
            p.inc(i, "lx", 16)

        @blk.vector
        def _(e):
            e.memset(ones_r[:], 1.0)
            i = e.memset(ones_c[:], 1.0)
            p.inc(i, "ve")

        pes = contextlib.ExitStack()
        wfb = [p.sb(f"wfbB{i}", [64, 512], F32, es=pes) for i in range(2)]
        wbb = [p.sb(f"wbbB{i}", [64, 512], BF16, es=pes) for i in range(2)]
        wob = [p.sb(f"wobB{i}", [64, 512], BF16, es=pes) for i in range(2)]
        psA = [p.ps(f"psAB{i}", [64, 512], F32, es=pes) for i in range(2)]

        cast_ve = {}
        mm_pe = {}
        ev_ve = {}
        wr_sa = {}
        for s in range(512):
            b, dsub = divmod(s, 4)
            par = s % 2

            @blk.gpsimd
            def _(e):
                if s >= 2:
                    p.wait(e, ("ve", cast_ve[s - 2]))
                i = e.dma_start(out=wfb[par][:], in_=w2r[:, b, dsub * 512:(dsub + 1) * 512])
                p.inc(i, "w%d" % par, 16)
            lw_s = p.C["w%d" % par]

            @blk.vector
            def _(e):
                p.wait(e, ("w%d" % par, lw_s))
                if s >= 2:
                    e.nop()
                    p.wait(e, ("pe", mm_pe[s - 2]))
                i = e.tensor_copy(wbb[par][:], wfb[par][:])
                p.inc(i, "ve")
            cast_ve[s] = p.C["ve"]

            @blk.tensor
            def _(e):
                if s == 0:
                    p.wait(e, ("lx", 32))
                    e.nop()
                p.wait(e, ("ve", cast_ve[s]))
                if s >= 2:
                    e.nop()
                    p.wait(e, ("ve", ev_ve[s - 2]))
                i = e.matmul(psA[par][:], h64_sb[:], wbb[par][:], start=True, stop=True,
                             skip_group_check=True)
                p.inc(i, "pe")
            mm_pe[s] = p.C["pe"]

            @blk.vector
            def _(e):
                p.wait(e, ("pe", mm_pe[s]))
                if s >= 2:
                    e.nop()
                    p.wait(e, ("s%d" % par, wr_sa[s - 2]))
                i = e.tensor_scalar(wob[par][:], psA[par][:], float(RT2), None, ALU.mult)
                p.inc(i, "ve")
            ev_ve[s] = p.C["ve"]

            @blk.sync
            def _(e):
                p.wait(e, ("ve", ev_ve[s]))
                dst = w2b_scr[dsub * 4:(dsub + 1) * 4, b, :, :].rearrange("m a d -> a m d")
                i = e.dma_start(out=dst, in_=wob[par][:].rearrange("a (m d) -> a m d", m=4))
                p.inc(i, "s%d" % par, 16)
            wr_sa[s] = p.C["s%d" % par]

        p.barrier(blk)
        pes.close()

        pes = contextlib.ExitStack()
        xb = p.sb("xbB", [128, NJ * T], BF16, es=pes)
        xstg = p.sb("xstgB", [128, T], F32, es=pes)
        wf = [p.sb(f"wfB{i}", [128, 2 * NJ * 128], F32, es=pes) for i in range(2)]
        wb = [p.sb(f"wbB{i}", [128, 2 * NJ * 128], BF16, es=pes) for i in range(2)]
        stmp = [p.sb(f"stmpB{i}", [128, 512], F32, es=pes) for i in range(2)]
        hsub = [p.sb(f"hsubB{i}", [128, 512], BF16, es=pes) for i in range(2)]
        asub = [p.sb(f"asubB{i}", [128, 512], BF16, es=pes) for i in range(2)]
        pG = [p.ps(f"pGB{i}", [128, 512], F32, es=pes) for i in range(2)]
        pU = [p.ps(f"pUB{i}", [128, 512], F32, es=pes) for i in range(2)]
        pA = [p.ps(f"pAB{i}", [128, 512], F32, es=pes) for i in range(2)]

        for j in range(NJ):
            @blk.sync
            def _(e):
                if j > 0:
                    p.wait(e, ("ve", p.C["ve"]))
                for sq in range(NSEQ):
                    i = e.dma_start(out=xstg[:, sq * S:(sq + 1) * S], in_=x2[sq, j])
                    p.inc(i, "lx", 16)

            @blk.vector
            def _(e):
                p.wait(e, ("lx", p.C["lx"]))
                i = e.tensor_copy(xb[:, j * T:(j + 1) * T], xstg[:])
                p.inc(i, "ve")
        xcast_ve = p.C["ve"]

        @blk.gpsimd
        def _(e):
            for c in range(2):
                i = e.dma_start(out=wf[c][:, 0:NJ * 128], in_=w1h[c])
                p.inc(i, "w%d" % c, 16)
                i = e.dma_start(out=wf[c][:, NJ * 128:], in_=w3h[c])
                p.inc(i, "w%d" % c, 16)

        wcast_ac = {}
        gu_pe = {}
        hmul_ve = {}
        amm_pe = {}
        aev_ve = {}
        awr_sa = {}
        silu_ac = {}

        for c in range(NC0):
            par = c % 2

            @blk.scalar
            def _(e):
                p.wait(e, ("w%d" % par, p.C["w%d" % par]))
                if c >= 2:
                    e.nop()
                    p.wait(e, ("pe", gu_pe[(c - 2, NTS - 1)]))
                e.copy(wb[par][:, 0:NJ * 128], wf[par][:, 0:NJ * 128])
                i = e.copy(wb[par][:, NJ * 128:], wf[par][:, NJ * 128:])
                p.inc(i, "ac")
            wcast_ac[c] = p.C["ac"]

            @blk.gpsimd
            def _(e):
                if c + 2 < NC0:
                    p.wait(e, ("ac", wcast_ac[c]))
                    i = e.dma_start(out=wf[par][:, 0:NJ * 128], in_=w1h[c + 2])
                    p.inc(i, "w%d" % par, 16)
                    i = e.dma_start(out=wf[par][:, NJ * 128:], in_=w3h[c + 2])
                    p.inc(i, "w%d" % par, 16)

            for ts in range(NTS):
                g_idx = c * NTS + ts
                q = g_idx % 2

                @blk.tensor
                def _(e):
                    wv = xcast_ve if g_idx == 0 else 0
                    if ts == 0:
                        p.wait(e, ("ac", wcast_ac[c]))
                        e.nop()
                    if g_idx >= 2:
                        p.wait(e, ("ve", hmul_ve[divmod(g_idx - 2, NTS)]))
                    elif wv:
                        p.wait(e, ("ve", wv))
                    for j in range(NJ):
                        e.matmul(pG[q][:], wb[par][:, j * 128:(j + 1) * 128],
                                 xb[:, j * T + ts * 512: j * T + ts * 512 + 512],
                                 start=(j == 0), stop=(j == NJ - 1), skip_group_check=True)
                    for j in range(NJ):
                        i = e.matmul(pU[q][:], wb[par][:, NJ * 128 + j * 128:NJ * 128 + (j + 1) * 128],
                                     xb[:, j * T + ts * 512: j * T + ts * 512 + 512],
                                     start=(j == 0), stop=(j == NJ - 1), skip_group_check=True)
                    p.inc(i, "pe")
                gu_pe[(c, ts)] = p.C["pe"]

                @blk.scalar
                def _(e):
                    p.wait(e, ("pe", gu_pe[(c, ts)]))
                    if g_idx >= 2:
                        e.nop()
                        p.wait(e, ("ve", hmul_ve[divmod(g_idx - 2, NTS)]))
                    i = e.activation(stmp[q][:], pG[q][:], AF.Silu)
                    p.inc(i, "ac")
                silu_ac[(c, ts)] = p.C["ac"]

                @blk.vector
                def _(e):
                    p.wait(e, ("ac", silu_ac[(c, ts)]))
                    if g_idx >= 2:
                        e.nop()
                        p.wait(e, ("pe", amm_pe[divmod(g_idx - 2, NTS)]))
                    i = e.tensor_tensor(hsub[q][:], stmp[q][:], pU[q][:], ALU.mult)
                    p.inc(i, "ve")
                hmul_ve[(c, ts)] = p.C["ve"]

                @blk.tensor
                def _(e):
                    p.wait(e, ("ve", hmul_ve[(c, ts)]))
                    if g_idx >= 2:
                        e.nop()
                        p.wait(e, ("ve", aev_ve[divmod(g_idx - 2, NTS)]))
                    i = e.matmul(pA[q][:], h128_sb[:], hsub[q][:], start=True, stop=True,
                                 skip_group_check=True)
                    p.inc(i, "pe")
                amm_pe[(c, ts)] = p.C["pe"]

                @blk.vector
                def _(e):
                    p.wait(e, ("pe", amm_pe[(c, ts)]))
                    if g_idx >= 2:
                        e.nop()
                        p.wait(e, ("a%d" % q, awr_sa[divmod(g_idx - 2, NTS)]))
                    i = e.tensor_copy(asub[q][:], pA[q][:])
                    p.inc(i, "ve")
                aev_ve[(c, ts)] = p.C["ve"]

                @blk.sync
                def _(e):
                    p.wait(e, ("ve", aev_ve[(c, ts)]))
                    i = e.dma_start(out=ascr[c, :, ts * 512:(ts + 1) * 512], in_=asub[q][:])
                    p.inc(i, "a%d" % q, 16)
                awr_sa[(c, ts)] = p.C["a%d" % q]

        p.barrier(blk)
        pes.close()

        pes = contextlib.ExitStack()
        acol = p.sb("acolB", [128, NC0 * 512], BF16, es=pes)
        w2bt = [p.sb(f"w2btB{i}", [128, 64 * 128], BF16, es=pes) for i in range(2)]
        res = [p.sb(f"resB{i}", [128, 512], F32, es=pes) for i in range(2)]
        y2 = p.sb("y2B", [128, NJ * 512], F32, es=pes)
        y2b = p.sb("y2bB", [128, 512], BF16, es=pes)
        y2s = p.sb("y2sB", [128, 512], BF16, es=pes)
        srow = p.sb("srowB", [1, 1536], F32, es=pes)
        rstdB = p.sb("rstdBB", [128, 512], F32, es=pes)
        mrB = p.sb("mrBB", [128, 512], F32, es=pes)
        x3t = [p.sb(f"x3tB{i}", [128, 512], F32, es=pes) for i in range(2)]
        PS = [p.ps(f"PSB{i}", [128, 512], F32, es=pes) for i in range(6)]

        NTT = T // 512
        for tt in range(NTT):
            sq, trs = divmod(tt * 512, S)
            pe_prev_tile = p.C["pe"]
            ve_prev_tile = p.C["ve"]

            @blk.sync
            def _(e):
                if tt > 0:
                    p.wait(e, ("pe", pe_prev_tile))
                for a in range(NC0):
                    i = e.dma_start(out=acol[:, a * 512:(a + 1) * 512],
                                    in_=ascr[a, :, tt * 512:(tt + 1) * 512])
                    p.inc(i, "lx", 16)
            acol_lx = p.C["lx"]

            w2b_lw = {}
            res_lx = {}

            @blk.gpsimd
            def _(e):
                if tt > 0:
                    p.wait(e, ("pe", pe_prev_tile))
                for dmp in range(2):
                    i = e.dma_start(out=w2bt[dmp][:], in_=w2b_scr[dmp])
                    i2 = e.dma_start(out=res[dmp][:], in_=x2[sq, dmp, :, trs:trs + 512])
                    i.then_inc(p.sems["d%d" % dmp], 16)
                    p.C["d%d" % dmp] += 16
                    w2b_lw[dmp] = p.inc(i2, "d%d" % dmp, 16)
                    res_lx[dmp] = w2b_lw[dmp]

            y2_ve = {}
            sqr_ac = {}
            dmm_pe = {}

            def down_mm(dm):
                po = PS[dm % 2]

                @blk.tensor
                def _(e):
                    p.wait(e, ("d%d" % (dm % 2), w2b_lw[dm]))
                    e.nop()
                    if dm == 0:
                        p.wait(e, ("lx", acol_lx))
                    elif dm >= 2:
                        p.wait(e, ("ve", y2_ve[dm - 2]))
                    for a in range(NC0):
                        i = e.matmul(po[:], w2bt[dm % 2][:, a * 128:(a + 1) * 128],
                                     acol[:, a * 512:(a + 1) * 512],
                                     start=(a == 0), stop=(a == NC0 - 1), skip_group_check=True)
                    p.inc(i, "pe")
                dmm_pe[dm] = p.C["pe"]

            def stats_mm(dm):
                @blk.tensor
                def _(e):
                    p.wait(e, ("ac", sqr_ac[dm]))
                    e.matmul(PS[4][0:1, :], ones_c[:], y2b[:],
                             start=(dm == 0), stop=(dm == NJ - 1), skip_group_check=True)
                    i = e.matmul(PS[5][0:1, :], ones_c[:], y2s[:],
                                 start=(dm == 0), stop=(dm == NJ - 1), skip_group_check=True)
                    p.inc(i, "pe")

            def down_evac(dm):
                po = PS[dm % 2]
                pe_snap = p.C["pe"]

                @blk.vector
                def _(e):
                    p.wait(e, ("d%d" % (dm % 2), res_lx[dm]))
                    e.nop()
                    p.wait(e, ("pe", pe_snap))
                    e.tensor_tensor(y2[:, dm * 512:(dm + 1) * 512], po[:], res[dm % 2][:], ALU.add)
                    i = e.tensor_copy(y2b[:], y2[:, dm * 512:(dm + 1) * 512])
                    p.inc(i, "ve")
                y2_ve[dm] = p.C["ve"]

                @blk.scalar
                def _(e):
                    p.wait(e, ("ve", y2_ve[dm]))
                    i = e.activation(y2s[:], y2[:, dm * 512:(dm + 1) * 512], AF.Square)
                    p.inc(i, "ac")
                sqr_ac[dm] = p.C["ac"]

            def prefetch_next(dm):
                @blk.gpsimd
                def _(e):
                    p.wait(e, ("pe", dmm_pe[dm]), ("ve", y2_ve[dm]))
                    i = e.dma_start(out=w2bt[dm % 2][:], in_=w2b_scr[dm + 2])
                    i2 = e.dma_start(out=res[dm % 2][:], in_=x2[sq, dm + 2, :, trs:trs + 512])
                    i.then_inc(p.sems["d%d" % (dm % 2)], 16)
                    p.C["d%d" % (dm % 2)] += 16
                    w2b_lw[dm + 2] = p.inc(i2, "d%d" % (dm % 2), 16)
                    res_lx[dm + 2] = w2b_lw[dm + 2]

            for dm in range(NJ):
                down_mm(dm)
                if dm > 0:
                    stats_mm(dm - 1)
                down_evac(dm)
                if dm + 2 < NJ:
                    prefetch_next(dm)
            stats_mm(NJ - 1)

            @blk.vector
            def _(e):
                p.wait(e, ("pe", p.C["pe"]))
                e.tensor_scalar(srow[0:1, 0:512], PS[4][0:1, :], 1.0 / D, None, ALU.mult)
                e.tensor_scalar(srow[0:1, 512:1024], PS[5][0:1, :], 1.0 / D, None, ALU.mult)
                e.tensor_tensor(srow[0:1, 1024:1536], srow[0:1, 0:512], srow[0:1, 0:512], ALU.mult)
                e.tensor_tensor(srow[0:1, 512:1024], srow[0:1, 512:1024], srow[0:1, 1024:1536], ALU.subtract)
                i = e.tensor_scalar(srow[0:1, 512:1024], srow[0:1, 512:1024], LN_EPS, None, ALU.add)
                p.inc(i, "ve")

            @blk.scalar
            def _(e):
                p.wait(e, ("ve", p.C["ve"]))
                i = e.activation(srow[0:1, 512:1024], srow[0:1, 512:1024], AF.Sqrt)
                p.inc(i, "ac")

            @blk.vector
            def _(e):
                p.wait(e, ("ac", p.C["ac"]))
                e.reciprocal(srow[0:1, 512:1024], srow[0:1, 512:1024])
                i = e.tensor_tensor(srow[0:1, 1024:1536], srow[0:1, 0:512], srow[0:1, 512:1024], ALU.mult)
                p.inc(i, "ve")

            @blk.tensor
            def _(e):
                p.wait(e, ("ve", p.C["ve"]))
                e.matmul(PS[2][:], ones_r[:], srow[0:1, 512:1024], start=True, stop=True,
                         skip_group_check=True)
                i = e.matmul(PS[3][:], ones_r[:], srow[0:1, 1024:1536], start=True, stop=True,
                             skip_group_check=True)
                p.inc(i, "pe")

            @blk.scalar
            def _(e):
                p.wait(e, ("pe", p.C["pe"]))
                e.copy(rstdB[:], PS[2][:])
                i = e.copy(mrB[:], PS[3][:])
                p.inc(i, "ac")
            ln_ac = p.C["ac"]

            for dm in range(NJ):
                xo = x3t[dm % 2]
                xs = "x%d" % (dm % 2)

                @blk.vector
                def _(e):
                    p.wait(e, ("ac", ln_ac), (xs, p.C[xs]))
                    e.tensor_tensor(xo[:], y2[:, dm * 512:(dm + 1) * 512], rstdB[:], ALU.mult)
                    i = e.tensor_tensor(xo[:], xo[:], mrB[:], ALU.subtract)
                    p.inc(i, "ve")

                @blk.gpsimd
                def _(e):
                    p.wait(e, ("ve", p.C["ve"]))
                    i = e.dma_start(out=x3[sq, dm, :, trs:trs + 512], in_=xo[:])
                    p.inc(i, xs, 16)

        @blk.gpsimd
        def _(e):
            p.wait(e, ("x0", p.C["x0"]))
            e.nop()
            e.wait_ge(p.sems["x1"], p.C["x1"])

    return p


def prep_B_weights(W1, W2, W3):
    w1h = np.ascontiguousarray(
        W1.reshape(NC0, 128, NJ, 128).transpose(0, 3, 2, 1).reshape(NC0, 128, NJ * 128)).astype(np.float32)
    w3h = np.ascontiguousarray(
        W3.reshape(NC0, 128, NJ, 128).transpose(0, 3, 2, 1).reshape(NC0, 128, NJ * 128)).astype(np.float32)
    w2r = np.ascontiguousarray(W2.reshape(D, 64, 128).transpose(1, 2, 0)).astype(np.float32)

    def had(k):
        Hm = np.array([[1.0]], np.float32)
        while Hm.shape[0] < k:
            Hm = np.block([[Hm, Hm], [Hm, -Hm]])
        return Hm
    h64 = had(64).astype(BF)
    h128 = had(128).astype(BF)
    return dict(w1h=w1h, w3h=w3h, w2r=w2r, h64=h64, h128=h128)





def build_C(S=SEQ, dbg=False):
    p = P("C")
    nc = p.nc
    NST = S // 512
    SC = float(1.0 / np.sqrt(128.0))
    x3 = p.dram("x3", [NSEQ, NJ, 128, S], F32, kind="ExternalInput")
    wqkv = p.dram("wqkv", [48, 128, NJ * 128], F32, kind="ExternalInput")
    bqkv = p.dram("bqkv", [1, 48 * 128], F32, kind="ExternalInput")
    wout = p.dram("wout", [NJ, 128, H * 128], F32, kind="ExternalInput")
    bout = p.dram("bout", [1, NJ * 128], F32, kind="ExternalInput")
    maskT = p.dram("maskT", [128, 128], F32, kind="ExternalInput")
    maskS = p.dram("maskS", [128, 128], F32, kind="ExternalInput")
    idb = p.dram("idb", [128, 128], BF16, kind="ExternalInput")
    idf = p.dram("idf", [128, 128], F32, kind="ExternalInput")
    x4 = p.dram("x4", [NSEQ, NJ, 128, S], F32, kind="ExternalOutput")
    wqkv_scr = p.dram("wqkv_scr", [48, 128, NJ * 128], BF16)
    wout_scr = p.dram("wout_scr", [NJ, 128, H * 128], BF16)
    sk = "ExternalOutput" if dbg else "Internal"
    qscr = p.dram("qscr", [NSEQ, H, 128, S], BF16, kind=sk)
    kscr = p.dram("kscr", [NSEQ, H, 128, S], BF16, kind=sk)
    vscr = p.dram("vscr", [NSEQ, H, S, 128], BF16, kind=sk)

    for s in ("lx", "pe", "ve", "ac", "w0", "w1", "s0", "s1",
              "k0", "k1", "o0", "o1", "x0", "x1", "q0", "q1"):
        p.sem(s)

    ones_r = p.sb("ones_r", [1, 128], F32)
    ones_c = p.sb("ones_c", [128, 1], BF16)
    maskT_sb = p.sb("maskT_sb", [128, 128], F32)
    maskS_sb = p.sb("maskS_sb", [128, 128], F32)
    idb_sb = p.sb("idb_sb", [128, 128], BF16)
    idf_sb = p.sb("idf_sb", [128, 128], F32)
    bqkv_sb = p.sb("bqkv_sb", [128, 48], F32)
    bout_sb = p.sb("bout_sb", [128, NJ], F32)

    with nc.Block() as blk:
        pes = contextlib.ExitStack()
        brow = p.sb("browC", [1, 48 * 128], F32, es=pes)
        borow = p.sb("borowC", [1, NJ * 128], F32, es=pes)
        pbias = p.ps("pbiasC", [128, 512], F32, es=pes)

        @blk.sync
        def _(e):
            for dst, srcap in ((brow[:], bqkv[:, :]), (borow[:], bout[:, :]),
                               (maskT_sb[:], maskT[:, :]), (maskS_sb[:], maskS[:, :]),
                               (idb_sb[:], idb[:, :]), (idf_sb[:], idf[:, :])):
                i = e.dma_start(out=dst, in_=srcap)
                p.inc(i, "lx", 16)

        @blk.vector
        def _(e):
            e.memset(ones_r[:], 1.0)
            i = e.memset(ones_c[:], 1.0)
            p.inc(i, "ve")

        @blk.tensor
        def _(e):
            p.wait(e, ("lx", p.C["lx"]), ("ve", p.C["ve"]))
            for t in range(48):
                i = e.matmul(pbias[:, t:t + 1], brow[0:1, t * 128:(t + 1) * 128],
                             ones_r[0:1, 0:1], start=True, stop=True, skip_group_check=True)
            p.inc(i, "pe")

        @blk.scalar
        def _(e):
            p.wait(e, ("pe", p.C["pe"]))
            i = e.copy(bqkv_sb[:, 0:48], pbias[:, 0:48])
            p.inc(i, "ac")

        @blk.tensor
        def _(e):
            p.wait(e, ("ac", p.C["ac"]))
            for t in range(NJ):
                i = e.matmul(pbias[:, t:t + 1], borow[0:1, t * 128:(t + 1) * 128],
                             ones_r[0:1, 0:1], start=True, stop=True, skip_group_check=True)
            p.inc(i, "pe")

        @blk.scalar
        def _(e):
            p.wait(e, ("pe", p.C["pe"]))
            i = e.copy(bout_sb[:, 0:NJ], pbias[:, 0:NJ])
            p.inc(i, "ac")

        wtmp = [p.sb(f"wtmpC{i}", [128, 2048], F32, es=pes) for i in range(2)]
        wbt = [p.sb(f"wbtC{i}", [128, 2048], BF16, es=pes) for i in range(2)]
        for t in range(48 + NJ):
            src_t = wqkv[t] if t < 48 else wout[t - 48]
            dst_t = wqkv_scr[t] if t < 48 else wout_scr[t - 48]
            buf = wtmp[t % 2]
            bbuf = wbt[t % 2]
            ws = "w%d" % (t % 2)
            ss = "s%d" % (t % 2)

            @blk.gpsimd
            def _(e):
                p.wait(e, ("ve", p.C["ve"] - 1))
                i = e.dma_start(out=buf[:], in_=src_t)
                p.inc(i, ws, 16)

            @blk.vector
            def _(e):
                p.wait(e, (ws, p.C[ws]), (ss, p.C[ss]))
                i = e.tensor_copy(bbuf[:], buf[:])
                p.inc(i, "ve")

            @blk.sync
            def _(e):
                p.wait(e, ("ve", p.C["ve"]))
                i = e.dma_start(out=dst_t, in_=bbuf[:])
                p.inc(i, ss, 16)

        p.barrier(blk)
        pes.close()

        pes = contextlib.ExitStack()
        xstg = p.sb("xstgC", [128, S], F32, es=pes)
        xb = p.sb("xbC", [128, NJ * S], BF16, es=pes)
        wqb = [p.sb(f"wqbC{i}", [128, 2048], BF16, es=pes) for i in range(2)]
        ev = [p.sb(f"evC{i}", [128, 512], BF16, es=pes) for i in range(2)]
        vtv = [p.sb(f"vtvC{i}", [128, 512], BF16, es=pes) for i in range(2)]
        PSq = [p.ps(f"PSqC{i}", [128, 512], F32, es=pes) for i in range(2)]
        PSt = [p.ps(f"PStC{i}", [128, 512], BF16, es=pes) for i in range(2)]

        for sq in range(NSEQ):
            for j in range(NJ):
                @blk.sync
                def _(e):
                    if j > 0 or sq > 0:
                        p.wait(e, ("ve", p.C["ve"]))
                    if sq > 0 and j == 0:
                        e.nop()
                        p.wait(e, ("pe", p.C["pe"]))
                    i = e.dma_start(out=xstg[:], in_=x3[sq, j])
                    p.inc(i, "lx", 16)

                @blk.vector
                def _(e):
                    p.wait(e, ("lx", p.C["lx"]))
                    i = e.tensor_copy(xb[:, j * S:(j + 1) * S], xstg[:])
                    p.inc(i, "ve")
            xcast_ve = p.C["ve"]

            @blk.gpsimd
            def _(e, xcast_ve=xcast_ve):
                p.wait(e, ("ve", xcast_ve))
                i = e.dma_start(out=wqb[0][:], in_=wqkv_scr[0])
                p.inc(i, "q0", 16)

            proj_pe = {}
            use_ve = {}
            for o in range(48):
                par = o % 2
                is_v = o >= 32

                @blk.gpsimd
                def _(e, o=o):
                    if o + 1 < 48:
                        p.wait(e, ("pe", proj_pe.get(o - 1, 0)))
                        i = e.dma_start(out=wqb[(o + 1) % 2][:], in_=wqkv_scr[o + 1])
                        p.inc(i, "q%d" % ((o + 1) % 2), 16)

                for st in range(NST):
                    fidx = o * NST + st
                    stq = fidx % 2

                    @blk.tensor
                    def _(e, o=o, st=st, stq=stq, par=par, fidx=fidx):
                        wv = use_ve.get(divmod(fidx - 2, NST), 0) if fidx >= 2 else 0
                        p.wait(e, ("q%d" % par, p.C["q%d" % par]),
                               ("ve", max(xcast_ve, wv)))
                        for j in range(NJ):
                            i = e.matmul(PSq[stq][:], wqb[par][:, j * 128:(j + 1) * 128],
                                         xb[:, j * S + st * 512: j * S + st * 512 + 512],
                                         start=(j == 0), stop=(j == NJ - 1), skip_group_check=True)
                        p.inc(i, "pe")
                    proj_pe[o] = p.C["pe"]
                    mm_pe = p.C["pe"]

                    @blk.vector
                    def _(e, o=o, st=st, stq=stq, mm_pe=mm_pe):
                        p.wait(e, ("pe", mm_pe), ("s%d" % stq, p.C["s%d" % stq]))
                        i = e.tensor_scalar(ev[stq][:], PSq[stq][:], bqkv_sb[:, o:o + 1], None, ALU.add)
                        p.inc(i, "ve")
                    bias_ve = p.C["ve"]

                    if not is_v:
                        use_ve[(o, st)] = bias_ve

                        @blk.sync
                        def _(e, o=o, st=st, stq=stq, bias_ve=bias_ve):
                            p.wait(e, ("ve", bias_ve))
                            dst = qscr if o < 16 else kscr
                            hh = o if o < 16 else o - 16
                            i = e.dma_start(out=dst[sq, hh, :, st * 512:(st + 1) * 512], in_=ev[stq][:])
                            p.inc(i, "s%d" % stq, 16)
                    else:
                        @blk.tensor
                        def _(e, st=st, stq=stq, bias_ve=bias_ve):
                            p.wait(e, ("ve", bias_ve))
                            for u in range(4):
                                i = e.transpose(PSt[stq][:, u * 128:(u + 1) * 128],
                                                ev[stq][:, u * 128:(u + 1) * 128], idb_sb[:])
                            p.inc(i, "pe")
                        tr_pe = p.C["pe"]

                        @blk.vector
                        def _(e, stq=stq, tr_pe=tr_pe):
                            p.wait(e, ("pe", tr_pe))
                            i = e.tensor_copy(vtv[stq][:], PSt[stq][:])
                            p.inc(i, "ve")
                        use_ve[(o, st)] = p.C["ve"]

                        @blk.sync
                        def _(e, o=o, st=st, stq=stq):
                            p.wait(e, ("ve", use_ve[(o, st)]))
                            hh = o - 32
                            dst = vscr[sq, hh, st * 512:(st + 1) * 512, :].rearrange(
                                "(u s) e -> s u e", u=4)
                            i = e.dma_start(out=dst, in_=vtv[stq][:].rearrange("s (u e) -> s u e", u=4))
                            p.inc(i, "s%d" % stq, 16)

        p.barrier(blk)
        pes.close()

        pes = contextlib.ExitStack()
        qt_b = [p.sb(f"qtC{i}", [128, 512], BF16, es=pes) for i in range(2)]
        kt_b = [p.sb(f"ktC{i}", [128, S], BF16, es=pes) for i in range(2)]
        vt_b = [p.sb(f"vtC{i}", [128, S], BF16, es=pes) for i in range(2)]
        ET = p.sb("ETC", [128, (S // 128) * 512], BF16, es=pes)
        tmpcol = p.sb("tmpcolC", [128, 1], F32, es=pes)
        etmp = [p.sb(f"etmpC{i}", [128, 512], F32, es=pes) for i in range(2)]
        nmcols = p.sb("nmcolsC", [128, 4], F32, es=pes)
        rrow = p.sb("rrowC", [1, 1024], F32, es=pes)
        negB = p.sb("negBC", [128, 512], F32, es=pes)
        recB = p.sb("recBC", [128, 512], F32, es=pes)
        rmax = p.sb("rmaxC", [128, 1], F32, es=pes)
        ostrip = p.sb("ostripC", [128, H * 512], BF16, es=pes)
        wob = [p.sb(f"wobC{i}", [128, 2048], BF16, es=pes) for i in range(2)]
        resb = [p.sb(f"resbC{i}", [128, 512], F32, es=pes) for i in range(2)]
        y1 = p.sb("y1C", [128, NJ * 512], F32, es=pes)
        y1b = p.sb("y1bC", [128, 512], BF16, es=pes)
        y1s = p.sb("y1sC", [128, 512], BF16, es=pes)
        srow = p.sb("srowC", [1, 1536], F32, es=pes)
        rstdB = p.sb("rstdBC", [128, 512], F32, es=pes)
        mrB = p.sb("mrBC", [128, 512], F32, es=pes)
        x4t = [p.sb(f"x4tC{i}", [128, 512], F32, es=pes) for i in range(2)]
        PS = [p.ps(f"PSC{i}", [128, 512], F32, es=pes) for i in range(6)]
        PTR = p.ps("PTRC", [128, 512], F32, es=pes)

        for sq in range(NSEQ):
            for qb in range(NST):
                nkt = (qb + 1) * 4
                kv_ve_prev = p.C["ve"]
                kv_done = {}

                @blk.gpsimd
                def _(e, sq=sq, qb=qb, nkt=nkt, kv_ve_prev=kv_ve_prev):
                    p.wait(e, ("pe", p.C["pe"]))
                    for hp in range(2):
                        ks = "k%d" % hp
                        i = e.dma_start(out=qt_b[hp][:], in_=qscr[sq, hp, :, qb * 512:(qb + 1) * 512])
                        p.inc(i, ks, 16)
                        i = e.dma_start(out=kt_b[hp][:, 0:nkt * 128], in_=kscr[sq, hp, :, 0:nkt * 128])
                        p.inc(i, ks, 16)
                        i = e.dma_start(out=vt_b[hp][:, 0:nkt * 128],
                                        in_=vscr[sq, hp, 0:nkt * 128, :].rearrange("(u s) e -> s u e", s=128))
                        kv_done[hp] = p.inc(i, ks, 16)

                wo_lw = {}

                @blk.sync
                def _(e, qb=qb):
                    p.wait(e, ("ve", kv_ve_prev))
                    for dmp in range(2):
                        i = e.dma_start(out=wob[dmp][:], in_=wout_scr[dmp])
                        wo_lw[dmp] = p.inc(i, "o%d" % dmp, 16)
                        i = e.dma_start(out=resb[dmp][:], in_=x3[sq, dmp, :, qb * 512:(qb + 1) * 512])
                        p.C["o%d" % dmp] += 16
                        i.then_inc(p.sems["o%d" % dmp], 16)
                        wo_lw[dmp] = p.C["o%d" % dmp]

                ostrip_ve = {}
                for h in range(H):
                    hp = h % 2

                    qb_start_ve = p.C["ve"]
                    blk_ve = []
                    for qt in range(4):
                        nkb = qb + 1
                        for kb in range(nkb):
                            qtkb = qt * nkb + kb
                            bank = PS[qtkb % 2]

                            @blk.tensor
                            def _(e, h=h, hp=hp, qt=qt, kb=kb, qtkb=qtkb, bank=bank):
                                if qtkb == 0:
                                    p.wait(e, ("k%d" % hp, kv_done[h]))
                                    e.nop()
                                    if h >= 2:
                                        p.wait(e, ("ve", ostrip_ve[h - 2]))
                                elif qtkb >= 2:
                                    p.wait(e, ("ve", blk_ve[qtkb - 2]))
                                i = e.matmul(bank[:], qt_b[hp][:, qt * 128:(qt + 1) * 128],
                                             kt_b[hp][:, kb * 512:(kb + 1) * 512],
                                             start=True, stop=True, skip_group_check=True)
                                p.inc(i, "pe")
                            blk_pe = p.C["pe"]

                            @blk.vector
                            def _(e, qt=qt, kb=kb, qb=qb, bank=bank, blk_pe=blk_pe):
                                p.wait(e, ("pe", blk_pe))
                                if kb == qb:
                                    e.tensor_tensor(bank[:, qt * 128:(qt + 1) * 128],
                                                    bank[:, qt * 128:(qt + 1) * 128],
                                                    maskS_sb[:], ALU.add)
                                    rng_hi = qt * 128 + 128
                                else:
                                    rng_hi = 512
                                if kb == 0:
                                    i = e.reduce_max(rmax[:], bank[:, 0:rng_hi], axis=AX.X)
                                else:
                                    e.reduce_max(tmpcol[:], bank[:, 0:rng_hi], axis=AX.X)
                                    i = e.tensor_tensor(rmax[:], rmax[:], tmpcol[:], ALU.max)
                                if kb == qb:
                                    i = e.tensor_scalar(nmcols[:, qt:qt + 1], rmax[:], -SC, None, ALU.mult)
                                p.inc(i, "ve")
                            blk_ve.append(p.C["ve"])
                    mxv_ve = p.C["ve"]

                    @blk.tensor
                    def _(e, mxv_ve=mxv_ve):
                        p.wait(e, ("ve", mxv_ve))
                        for qt in range(4):
                            i = e.transpose(PTR[0:1, qt * 128:(qt + 1) * 128],
                                            nmcols[:, qt:qt + 1], idf_sb[:])
                        p.inc(i, "pe")
                    trm_pe = p.C["pe"]

                    @blk.scalar
                    def _(e, trm_pe=trm_pe):
                        p.wait(e, ("pe", trm_pe))
                        i = e.copy(rrow[0:1, 0:512], PTR[0:1, 0:512])
                        p.inc(i, "ac")
                    row_ac = p.C["ac"]

                    @blk.tensor
                    def _(e, row_ac=row_ac):
                        p.wait(e, ("ac", row_ac))
                        i = e.matmul(PS[4][:], ones_r[:], rrow[0:1, 0:512], start=True, stop=True,
                                     skip_group_check=True)
                        p.inc(i, "pe")
                    nb_pe = p.C["pe"]

                    @blk.scalar
                    def _(e, nb_pe=nb_pe):
                        p.wait(e, ("pe", nb_pe))
                        i = e.copy(negB[:], PS[4][:])
                        p.inc(i, "ac")
                    negB_ac = p.C["ac"]

                    exp_ac = {}
                    for kt in range(nkt):
                        ktq = kt % 2
                        ktx = kt - qb * 4

                        @blk.tensor
                        def _(e, h=h, hp=hp, kt=kt, ktq=ktq, negB_ac=negB_ac):
                            if kt == 0:
                                p.wait(e, ("ac", negB_ac))
                            elif kt >= 2:
                                p.wait(e, ("ac", exp_ac[kt - 2]))
                            i = e.matmul(PS[2 + ktq][:], kt_b[hp][:, kt * 128:(kt + 1) * 128],
                                         qt_b[hp][:], start=True, stop=True, skip_group_check=True)
                            p.inc(i, "pe")
                        st_pe = p.C["pe"]

                        @blk.vector
                        def _(e, kt=kt, ktq=ktq, ktx=ktx, st_pe=st_pe):
                            et = etmp[ktq]
                            p.wait(e, ("pe", st_pe))
                            if kt >= 2:
                                e.nop()
                                p.wait(e, ("ac", exp_ac[kt - 2]))
                            e.tensor_scalar(et[:], PS[2 + ktq][:], SC, None, ALU.mult)
                            i = e.tensor_tensor(et[:], et[:], negB[:], ALU.add)
                            if ktx >= 0:
                                i = e.tensor_tensor(et[:, ktx * 128:(ktx + 1) * 128],
                                                    et[:, ktx * 128:(ktx + 1) * 128],
                                                    maskT_sb[:], ALU.add)
                                if ktx > 0:
                                    i = e.memset(et[:, 0:ktx * 128], -1e30)
                            p.inc(i, "ve")
                        sh_ve = p.C["ve"]

                        @blk.scalar
                        def _(e, kt=kt, ktq=ktq, sh_ve=sh_ve):
                            p.wait(e, ("ve", sh_ve))
                            i = e.activation(ET[:, kt * 512:(kt + 1) * 512], etmp[ktq][:], AF.Exp)
                            p.inc(i, "ac")
                        exp_ac[kt] = p.C["ac"]

                    @blk.tensor
                    def _(e, h=h, hp=hp, nkt=nkt):
                        p.wait(e, ("ac", exp_ac[nkt - 1]))
                        for kt in range(nkt):
                            i = e.matmul(PS[5][0:1, :], ones_c[:], ET[:, kt * 512:(kt + 1) * 512],
                                         start=(kt == 0), stop=(kt == nkt - 1), skip_group_check=True)
                        for kt in range(nkt):
                            i = e.matmul(PS[4][:], vt_b[hp][:, kt * 128:(kt + 1) * 128],
                                         ET[:, kt * 512:(kt + 1) * 512],
                                         start=(kt == 0), stop=(kt == nkt - 1), skip_group_check=True)
                        p.inc(i, "pe")
                    den_pe = p.C["pe"]

                    @blk.vector
                    def _(e, den_pe=den_pe):
                        p.wait(e, ("pe", den_pe))
                        e.tensor_scalar(rrow[0:1, 512:1024], PS[5][0:1, :], 1e-30, None, ALU.add)
                        i = e.reciprocal(rrow[0:1, 512:1024], rrow[0:1, 512:1024])
                        p.inc(i, "ve")
                    rec_ve = p.C["ve"]

                    @blk.tensor
                    def _(e, rec_ve=rec_ve):
                        p.wait(e, ("ve", rec_ve))
                        i = e.matmul(PTR[:, 0:512], ones_r[:], rrow[0:1, 512:1024],
                                     start=True, stop=True, skip_group_check=True)
                        p.inc(i, "pe")
                    rb_pe = p.C["pe"]

                    @blk.scalar
                    def _(e, rb_pe=rb_pe):
                        p.wait(e, ("pe", rb_pe))
                        i = e.copy(recB[:], PTR[:, 0:512])
                        p.inc(i, "ac")
                    rb_ac = p.C["ac"]

                    @blk.vector
                    def _(e, h=h, rb_ac=rb_ac):
                        p.wait(e, ("ac", rb_ac))
                        i = e.tensor_tensor(ostrip[:, h * 512:(h + 1) * 512], PS[4][:], recB[:], ALU.mult)
                        p.inc(i, "ve")
                    ostrip_ve[h] = p.C["ve"]

                    if h + 2 < H:
                        @blk.gpsimd
                        def _(e, sq=sq, qb=qb, h=h, hp=hp, nkt=nkt, den_pe=den_pe):
                            p.wait(e, ("pe", den_pe))
                            ks = "k%d" % hp
                            i = e.dma_start(out=qt_b[hp][:], in_=qscr[sq, h + 2, :, qb * 512:(qb + 1) * 512])
                            p.inc(i, ks, 16)
                            i = e.dma_start(out=kt_b[hp][:, 0:nkt * 128], in_=kscr[sq, h + 2, :, 0:nkt * 128])
                            p.inc(i, ks, 16)
                            i = e.dma_start(out=vt_b[hp][:, 0:nkt * 128],
                                            in_=vscr[sq, h + 2, 0:nkt * 128, :].rearrange("(u s) e -> s u e", s=128))
                            kv_done[h + 2] = p.inc(i, ks, 16)

                attn_done_ve = p.C["ve"]
                y1_ve = {}
                sqr_ac = {}
                wo_pe = {}

                def wo_mm_tensor(dm):
                    po = PS[dm % 2]

                    @blk.tensor
                    def _(e, dm=dm, po=po):
                        p.wait(e, ("o%d" % (dm % 2), wo_lw[dm]))
                        e.nop()
                        if dm == 0:
                            p.wait(e, ("ve", attn_done_ve))
                        elif dm >= 2:
                            p.wait(e, ("ve", y1_ve[dm - 2]))
                        for hh in range(H):
                            i = e.matmul(po[:], wob[dm % 2][:, hh * 128:(hh + 1) * 128],
                                         ostrip[:, hh * 512:(hh + 1) * 512],
                                         start=(hh == 0), stop=(hh == H - 1), skip_group_check=True)
                        p.inc(i, "pe")
                    wo_pe[dm] = p.C["pe"]

                def stats_mm(dm):
                    @blk.tensor
                    def _(e, dm=dm):
                        p.wait(e, ("ac", sqr_ac[dm]))
                        e.matmul(PS[4][0:1, :], ones_c[:], y1b[:],
                                 start=(dm == 0), stop=(dm == NJ - 1), skip_group_check=True)
                        i = e.matmul(PS[5][0:1, :], ones_c[:], y1s[:],
                                     start=(dm == 0), stop=(dm == NJ - 1), skip_group_check=True)
                        p.inc(i, "pe")

                def wo_evac(dm):
                    po = PS[dm % 2]
                    pe_snap = p.C["pe"]

                    @blk.vector
                    def _(e, dm=dm, po=po, pe_snap=pe_snap):
                        p.wait(e, ("pe", pe_snap))
                        e.tensor_scalar(y1[:, dm * 512:(dm + 1) * 512], po[:],
                                        bout_sb[:, dm:dm + 1], None, ALU.add)
                        e.tensor_tensor(y1[:, dm * 512:(dm + 1) * 512],
                                        y1[:, dm * 512:(dm + 1) * 512], resb[dm % 2][:], ALU.add)
                        i = e.tensor_copy(y1b[:], y1[:, dm * 512:(dm + 1) * 512])
                        p.inc(i, "ve")
                    y1_ve[dm] = p.C["ve"]

                    @blk.scalar
                    def _(e, dm=dm):
                        p.wait(e, ("ve", y1_ve[dm]))
                        i = e.activation(y1s[:], y1[:, dm * 512:(dm + 1) * 512], AF.Square)
                        p.inc(i, "ac")
                    sqr_ac[dm] = p.C["ac"]

                def prefetch_wo(dm):
                    @blk.sync
                    def _(e, dm=dm, sq=sq, qb=qb):
                        p.wait(e, ("pe", wo_pe[dm]), ("ve", y1_ve[dm]))
                        i = e.dma_start(out=wob[dm % 2][:], in_=wout_scr[dm + 2])
                        p.C["o%d" % (dm % 2)] += 16
                        i.then_inc(p.sems["o%d" % (dm % 2)], 16)
                        i = e.dma_start(out=resb[dm % 2][:], in_=x3[sq, dm + 2, :, qb * 512:(qb + 1) * 512])
                        wo_lw[dm + 2] = p.inc(i, "o%d" % (dm % 2), 16)

                for dm in range(NJ):
                    wo_mm_tensor(dm)
                    if dm > 0:
                        stats_mm(dm - 1)
                    wo_evac(dm)
                    if dm + 2 < NJ:
                        prefetch_wo(dm)
                stats_mm(NJ - 1)

                @blk.vector
                def _(e):
                    p.wait(e, ("pe", p.C["pe"]))
                    e.tensor_scalar(srow[0:1, 0:512], PS[4][0:1, :], 1.0 / D, None, ALU.mult)
                    e.tensor_scalar(srow[0:1, 512:1024], PS[5][0:1, :], 1.0 / D, None, ALU.mult)
                    e.tensor_tensor(srow[0:1, 1024:1536], srow[0:1, 0:512], srow[0:1, 0:512], ALU.mult)
                    e.tensor_tensor(srow[0:1, 512:1024], srow[0:1, 512:1024], srow[0:1, 1024:1536], ALU.subtract)
                    i = e.tensor_scalar(srow[0:1, 512:1024], srow[0:1, 512:1024], LN_EPS, None, ALU.add)
                    p.inc(i, "ve")

                @blk.scalar
                def _(e):
                    p.wait(e, ("ve", p.C["ve"]))
                    i = e.activation(srow[0:1, 512:1024], srow[0:1, 512:1024], AF.Sqrt)
                    p.inc(i, "ac")

                @blk.vector
                def _(e):
                    p.wait(e, ("ac", p.C["ac"]))
                    e.reciprocal(srow[0:1, 512:1024], srow[0:1, 512:1024])
                    i = e.tensor_tensor(srow[0:1, 1024:1536], srow[0:1, 0:512], srow[0:1, 512:1024], ALU.mult)
                    p.inc(i, "ve")

                @blk.tensor
                def _(e):
                    p.wait(e, ("ve", p.C["ve"]))
                    e.matmul(PS[2][:], ones_r[:], srow[0:1, 512:1024], start=True, stop=True,
                             skip_group_check=True)
                    i = e.matmul(PS[3][:], ones_r[:], srow[0:1, 1024:1536], start=True, stop=True,
                                 skip_group_check=True)
                    p.inc(i, "pe")

                @blk.scalar
                def _(e):
                    p.wait(e, ("pe", p.C["pe"]))
                    e.copy(rstdB[:], PS[2][:])
                    i = e.copy(mrB[:], PS[3][:])
                    p.inc(i, "ac")
                ln_ac = p.C["ac"]

                for dm in range(NJ):
                    xo = x4t[dm % 2]
                    xs = "x%d" % (dm % 2)

                    @blk.vector
                    def _(e, dm=dm, xo=xo, xs=xs, ln_ac=ln_ac):
                        p.wait(e, ("ac", ln_ac), (xs, p.C[xs]))
                        e.tensor_tensor(xo[:], y1[:, dm * 512:(dm + 1) * 512], rstdB[:], ALU.mult)
                        i = e.tensor_tensor(xo[:], xo[:], mrB[:], ALU.subtract)
                        p.inc(i, "ve")

                    @blk.gpsimd
                    def _(e, dm=dm, sq=sq, qb=qb, xo=xo, xs=xs):
                        p.wait(e, ("ve", p.C["ve"]))
                        i = e.dma_start(out=x4[sq, dm, :, qb * 512:(qb + 1) * 512], in_=xo[:])
                        p.inc(i, xs, 16)

        @blk.gpsimd
        def _(e):
            p.wait(e, ("x0", p.C["x0"]))
            e.nop()
            e.wait_ge(p.sems["x1"], p.C["x1"])

    return p


def prep_C_weights(W_qkv, b_qkv, W_out, b_out):
    wqkv = np.ascontiguousarray(
        W_qkv.reshape(48, 128, NJ, 128).transpose(0, 3, 2, 1).reshape(48, 128, NJ * 128))
    bq_r = np.ascontiguousarray(b_qkv.reshape(1, 48 * 128))
    wo_t = np.ascontiguousarray(
        W_out.reshape(NJ, 128, H, 128).transpose(0, 3, 2, 1).reshape(NJ, 128, H * 128))
    bo_r = np.ascontiguousarray(b_out.reshape(1, NJ * 128))
    kk, qq = np.meshgrid(np.arange(128), np.arange(128), indexing="ij")
    maskT = np.where(kk > qq, -1e30, 0.0).astype(np.float32)
    maskS = maskT.T.copy()
    idb = np.eye(128, dtype=BF)
    idf = np.eye(128, dtype=np.float32)
    return dict(wqkv=wqkv, bqkv=bq_r, wout=wo_t, bout=bo_r,
                maskT=maskT, maskS=maskS, idb=idb, idf=idf)





NC1 = 43


def build_D(S=SEQ):
    p = P("D")
    nc = p.nc
    T = NSEQ * S
    NTS = T // 512
    x4 = p.dram("x4", [NSEQ, NJ, 128, S], F32, kind="ExternalInput")
    gidx = p.dram("gidx", [NC1, 128, NJ * 128], I32, kind="ExternalInput")
    uidx = p.dram("uidx", [NC1, 128, NJ * 128], I32, kind="ExternalInput")
    didx = p.dram("didx", [NJ, 128, NC1 * 128], I32, kind="ExternalInput")
    lut = p.dram("lut", [1, 16], F32, kind="ExternalInput")
    yout = p.dram("yout", [NSEQ, NJ, 128, S], F32, kind="ExternalOutput")
    wg_scr = p.dram("wg_scr", [NC1, 128, NJ * 128], BF16)
    wu_scr = p.dram("wu_scr", [NC1, 128, NJ * 128], BF16)
    wd_scr = p.dram("wd_scr", [NJ, 128, NC1 * 128], BF16)
    hscr = p.dram("hscr", [NC1, 128, T], BF16)

    for s in ("lx", "lw", "pe", "ve", "ac", "po",
              "w0", "w1", "s0", "s1", "a0", "a1", "d0", "d1", "x0", "x1",
              "i0", "i1", "i2", "i3", "m0", "m1", "m2", "m3"):
        p.sem(s)

    ones_r = p.sb("ones_r", [1, 128], F32)
    ones_c = p.sb("ones_c", [128, 1], BF16)
    lut_row = p.sb("lut_row", [1, 16], F32)
    lutB = p.sb("lutB", [128, 16], F32)

    with nc.Block() as blk:
        pes = contextlib.ExitStack()
        pb = p.ps("pbD", [128, 512], F32, es=pes)

        @blk.sync
        def _(e):
            i = e.dma_start(out=lut_row[:], in_=lut[:, :])
            p.inc(i, "lx", 16)

        @blk.vector
        def _(e):
            e.memset(ones_r[:], 1.0)
            i = e.memset(ones_c[:], 1.0)
            p.inc(i, "ve")

        @blk.tensor
        def _(e):
            p.wait(e, ("lx", 16), ("ve", 1))
            i = e.matmul(pb[:, 0:16], ones_r[:], lut_row[:], start=True, stop=True,
                         skip_group_check=True)
            p.inc(i, "pe")

        @blk.scalar
        def _(e):
            p.wait(e, ("pe", p.C["pe"]))
            i = e.copy(lutB[:], pb[:, 0:16])
            p.inc(i, "ac")
        lut_ac = p.C["ac"]

        itile = {"v": [p.sb(f"itv{i}", [128, NC1 * 128], I32, es=pes) for i in range(2)],
                 "g": [p.sb(f"itg{i}", [128, NC1 * 128], I32, es=pes) for i in range(2)]}
        atile = {"v": [p.sb(f"atv{i}", [128, NC1 * 128], BF16, es=pes) for i in range(2)],
                 "g": [p.sb(f"atg{i}", [128, NC1 * 128], BF16, es=pes) for i in range(2)]}
        ttile = {"v": p.sb("ttv", [128, NC1 * 128], BF16, es=pes),
                 "g": p.sb("ttg", [128, NC1 * 128], BF16, es=pes)}

        jobs = []
        for c in range(NC1):
            jobs.append(("v", gidx[c], wg_scr[c], NJ * 128))
            jobs.append(("g", uidx[c], wu_scr[c], NJ * 128))
        for dm in range(NJ):
            jobs.append(("v" if dm % 2 == 0 else "g", didx[dm], wd_scr[dm], NC1 * 128))

        jcount = {"v": 0, "g": 0}
        done_cnt = {"v": {}, "g": {}}
        for (ek, iap, oap, ncols) in jobs:
            jn = jcount[ek]
            jcount[ek] += 1
            par = jn % 2
            ibuf = itile[ek][par]
            abuf = atile[ek][par]
            tbuf = ttile[ek]
            isem = ("i%d" % par) if ek == "v" else ("i%d" % (2 + par))
            msem = ("m%d" % par) if ek == "v" else ("m%d" % (2 + par))
            engsec = blk.vector if ek == "v" else blk.gpsimd
            prog = "ve" if ek == "v" else "po"
            dmaq = blk.sync if ek == "v" else blk.scalar

            @dmaq
            def _(e, jn=jn, iap=iap, ibuf=ibuf, isem=isem, ek=ek, ncols=ncols):
                if jn >= 2:
                    p.wait(e, (prog, done_cnt[ek][jn - 2]))
                i = e.dma_start(out=ibuf[:, 0:ncols], in_=iap)
                p.inc(i, isem, 16)

            @engsec
            def _(e, jn=jn, ibuf=ibuf, abuf=abuf, tbuf=tbuf, isem=isem, msem=msem,
                  ek=ek, ncols=ncols):
                p.wait(e, (isem, p.C[isem]), ("ac", lut_ac))
                if jn >= 2:
                    e.nop()
                    p.wait(e, (msem, p.C[msem]))
                e.tensor_scalar(abuf[:, 0:ncols], ibuf[:, 0:ncols], 0.0, lutB[:, 0:1],
                                ALU.is_equal, ALU.mult)
                for j in range(1, 16):
                    e.tensor_scalar(tbuf[:, 0:ncols], ibuf[:, 0:ncols], float(j),
                                    lutB[:, j:j + 1], ALU.is_equal, ALU.mult)
                    i = e.tensor_tensor(abuf[:, 0:ncols], abuf[:, 0:ncols], tbuf[:, 0:ncols],
                                        ALU.add)
                p.inc(i, prog)
            done_cnt[ek][jn] = p.C[prog]

            @dmaq
            def _(e, jn=jn, oap=oap, abuf=abuf, msem=msem, ek=ek, ncols=ncols):
                p.wait(e, (prog, done_cnt[ek][jn]))
                i = e.dma_start(out=oap, in_=abuf[:, 0:ncols])
                p.inc(i, msem, 16)

        p.barrier(blk)
        pes.close()

        pes = contextlib.ExitStack()
        xb = p.sb("xbD", [128, NJ * T], BF16, es=pes)
        xstg = p.sb("xstgD", [128, T], F32, es=pes)
        wb = [p.sb(f"wbD{i}", [128, 2 * NJ * 128], BF16, es=pes) for i in range(2)]
        stmp = [p.sb(f"stmpD{i}", [128, 512], F32, es=pes) for i in range(2)]
        hsub = [p.sb(f"hsubD{i}", [128, 512], BF16, es=pes) for i in range(2)]
        pG = [p.ps(f"pGD{i}", [128, 512], F32, es=pes) for i in range(2)]
        pU = [p.ps(f"pUD{i}", [128, 512], F32, es=pes) for i in range(2)]

        for j in range(NJ):
            @blk.sync
            def _(e, j=j):
                if j > 0:
                    p.wait(e, ("ve", p.C["ve"]))
                for sq in range(NSEQ):
                    i = e.dma_start(out=xstg[:, sq * S:(sq + 1) * S], in_=x4[sq, j])
                    p.inc(i, "lx", 16)

            @blk.vector
            def _(e, j=j):
                p.wait(e, ("lx", p.C["lx"]))
                i = e.tensor_copy(xb[:, j * T:(j + 1) * T], xstg[:])
                p.inc(i, "ve")
        xcast_ve = p.C["ve"]

        @blk.gpsimd
        def _(e):
            p.wait(e, ("ve", xcast_ve))
            for c in range(2):
                i = e.dma_start(out=wb[c][:, 0:NJ * 128], in_=wg_scr[c])
                p.inc(i, "w%d" % c, 16)
                i = e.dma_start(out=wb[c][:, NJ * 128:], in_=wu_scr[c])
                p.inc(i, "w%d" % c, 16)

        gu_pe = {}
        hmul_ve = {}
        hwr_sa = {}
        for c in range(NC1):
            par = c % 2

            for ts in range(NTS):
                g_idx = c * NTS + ts
                q = g_idx % 2

                @blk.tensor
                def _(e, c=c, ts=ts, par=par, q=q, g_idx=g_idx):
                    p.wait(e, ("w%d" % par, p.C["w%d" % par]))
                    e.nop()
                    if g_idx >= 2:
                        p.wait(e, ("ve", hmul_ve[divmod(g_idx - 2, NTS)]))
                    elif g_idx == 0:
                        p.wait(e, ("ve", xcast_ve))
                    for j in range(NJ):
                        e.matmul(pG[q][:], wb[par][:, j * 128:(j + 1) * 128],
                                 xb[:, j * T + ts * 512: j * T + ts * 512 + 512],
                                 start=(j == 0), stop=(j == NJ - 1), skip_group_check=True)
                    for j in range(NJ):
                        i = e.matmul(pU[q][:], wb[par][:, NJ * 128 + j * 128:NJ * 128 + (j + 1) * 128],
                                     xb[:, j * T + ts * 512: j * T + ts * 512 + 512],
                                     start=(j == 0), stop=(j == NJ - 1), skip_group_check=True)
                    p.inc(i, "pe")
                gu_pe[(c, ts)] = p.C["pe"]

                @blk.scalar
                def _(e, c=c, ts=ts, q=q, g_idx=g_idx):
                    p.wait(e, ("pe", gu_pe[(c, ts)]))
                    if g_idx >= 2:
                        e.nop()
                        p.wait(e, ("ve", hmul_ve[divmod(g_idx - 2, NTS)]))
                    i = e.activation(stmp[q][:], pG[q][:], AF.Silu)
                    p.inc(i, "ac")
                silu_ac = p.C["ac"]

                @blk.vector
                def _(e, c=c, ts=ts, q=q, g_idx=g_idx, silu_ac=silu_ac):
                    p.wait(e, ("ac", silu_ac))
                    if g_idx >= 2:
                        e.nop()
                        p.wait(e, ("a%d" % q, hwr_sa[divmod(g_idx - 2, NTS)]))
                    i = e.tensor_tensor(hsub[q][:], stmp[q][:], pU[q][:], ALU.mult)
                    p.inc(i, "ve")
                hmul_ve[(c, ts)] = p.C["ve"]

                @blk.sync
                def _(e, c=c, ts=ts, q=q):
                    p.wait(e, ("ve", hmul_ve[(c, ts)]))
                    i = e.dma_start(out=hscr[c, :, ts * 512:(ts + 1) * 512], in_=hsub[q][:])
                    p.inc(i, "a%d" % q, 16)
                hwr_sa[(c, ts)] = p.C["a%d" % q]

            @blk.gpsimd
            def _(e, c=c, par=par):
                if c + 2 < NC1:
                    p.wait(e, ("pe", gu_pe[(c, NTS - 1)]))
                    i = e.dma_start(out=wb[par][:, 0:NJ * 128], in_=wg_scr[c + 2])
                    p.inc(i, "w%d" % par, 16)
                    i = e.dma_start(out=wb[par][:, NJ * 128:], in_=wu_scr[c + 2])
                    p.inc(i, "w%d" % par, 16)

        p.barrier(blk)
        pes.close()

        pes = contextlib.ExitStack()
        TT = 512
        NTT = T // TT
        acol = p.sb("acolD", [128, NC1 * TT], BF16, es=pes)
        wdt = [p.sb(f"wdtD{i}", [128, NC1 * 128], BF16, es=pes) for i in range(2)]
        res = [p.sb(f"resD{i}", [128, TT], F32, es=pes) for i in range(2)]
        y2 = p.sb("y2D", [128, NJ * TT], F32, es=pes)
        y2b = p.sb("y2bD", [128, TT], BF16, es=pes)
        y2s = p.sb("y2sD", [128, TT], BF16, es=pes)
        srow = p.sb("srowD", [1, 1536], F32, es=pes)
        rstdB = p.sb("rstdBD", [128, 512], F32, es=pes)
        mrB = p.sb("mrBD", [128, 512], F32, es=pes)
        yt = [p.sb(f"ytD{i}", [128, 512], F32, es=pes) for i in range(2)]
        PS = [p.ps(f"PSD{i}", [128, 512], F32, es=pes) for i in range(6)]

        for tt in range(NTT):
            sq, trs = divmod(tt * TT, S)
            pe_prev_tile = p.C["pe"]

            @blk.sync
            def _(e, tt=tt):
                if tt > 0:
                    p.wait(e, ("pe", pe_prev_tile))
                for a in range(NC1):
                    i = e.dma_start(out=acol[:, a * TT:(a + 1) * TT],
                                    in_=hscr[a, :, tt * TT:(tt + 1) * TT])
                    p.inc(i, "lx", 16)
            acol_lx = p.C["lx"]

            wd_lw = {}

            @blk.gpsimd
            def _(e, tt=tt, sq=sq, trs=trs):
                if tt > 0:
                    p.wait(e, ("pe", pe_prev_tile))
                for dmp in range(2):
                    i = e.dma_start(out=wdt[dmp][:], in_=wd_scr[dmp])
                    i2 = e.dma_start(out=res[dmp][:], in_=x4[sq, dmp, :, trs:trs + TT])
                    i.then_inc(p.sems["d%d" % dmp], 16)
                    p.C["d%d" % dmp] += 16
                    wd_lw[dmp] = p.inc(i2, "d%d" % dmp, 16)

            y2_ve = {}
            sqr_ac = {}
            dmm_pe = {}

            def down_mm(dm):
                po = PS[dm % 2]

                @blk.tensor
                def _(e, dm=dm, po=po):
                    p.wait(e, ("d%d" % (dm % 2), wd_lw[dm]))
                    e.nop()
                    if dm == 0:
                        p.wait(e, ("lx", acol_lx))
                    elif dm >= 2:
                        p.wait(e, ("ve", y2_ve[dm - 2]))
                    for a in range(NC1):
                        i = e.matmul(po[:], wdt[dm % 2][:, a * 128:(a + 1) * 128],
                                     acol[:, a * TT:(a + 1) * TT],
                                     start=(a == 0), stop=(a == NC1 - 1), skip_group_check=True)
                    p.inc(i, "pe")
                dmm_pe[dm] = p.C["pe"]

            def stats_mm(dm):
                @blk.tensor
                def _(e, dm=dm):
                    p.wait(e, ("ac", sqr_ac[dm]))
                    e.matmul(PS[4][0:1, :], ones_c[:], y2b[:],
                             start=(dm == 0), stop=(dm == NJ - 1), skip_group_check=True)
                    i = e.matmul(PS[5][0:1, :], ones_c[:], y2s[:],
                                 start=(dm == 0), stop=(dm == NJ - 1), skip_group_check=True)
                    p.inc(i, "pe")

            def down_evac(dm):
                po = PS[dm % 2]
                pe_snap = p.C["pe"]

                @blk.vector
                def _(e, dm=dm, po=po, pe_snap=pe_snap):
                    p.wait(e, ("d%d" % (dm % 2), wd_lw[dm]))
                    e.nop()
                    p.wait(e, ("pe", pe_snap))
                    e.tensor_tensor(y2[:, dm * TT:(dm + 1) * TT], po[:], res[dm % 2][:], ALU.add)
                    i = e.tensor_copy(y2b[:], y2[:, dm * TT:(dm + 1) * TT])
                    p.inc(i, "ve")
                y2_ve[dm] = p.C["ve"]

                @blk.scalar
                def _(e, dm=dm):
                    p.wait(e, ("ve", y2_ve[dm]))
                    i = e.activation(y2s[:], y2[:, dm * TT:(dm + 1) * TT], AF.Square)
                    p.inc(i, "ac")
                sqr_ac[dm] = p.C["ac"]

            def prefetch_next(dm):
                @blk.gpsimd
                def _(e, dm=dm, sq=sq, trs=trs):
                    p.wait(e, ("pe", dmm_pe[dm]), ("ve", y2_ve[dm]))
                    i = e.dma_start(out=wdt[dm % 2][:], in_=wd_scr[dm + 2])
                    i2 = e.dma_start(out=res[dm % 2][:], in_=x4[sq, dm + 2, :, trs:trs + TT])
                    i.then_inc(p.sems["d%d" % (dm % 2)], 16)
                    p.C["d%d" % (dm % 2)] += 16
                    wd_lw[dm + 2] = p.inc(i2, "d%d" % (dm % 2), 16)

            for dm in range(NJ):
                down_mm(dm)
                if dm > 0:
                    stats_mm(dm - 1)
                down_evac(dm)
                if dm + 2 < NJ:
                    prefetch_next(dm)
            stats_mm(NJ - 1)

            @blk.vector
            def _(e):
                p.wait(e, ("pe", p.C["pe"]))
                e.tensor_scalar(srow[0:1, 0:512], PS[4][0:1, :], 1.0 / D, None, ALU.mult)
                e.tensor_scalar(srow[0:1, 512:1024], PS[5][0:1, :], 1.0 / D, None, ALU.mult)
                e.tensor_tensor(srow[0:1, 1024:1536], srow[0:1, 0:512], srow[0:1, 0:512], ALU.mult)
                e.tensor_tensor(srow[0:1, 512:1024], srow[0:1, 512:1024], srow[0:1, 1024:1536], ALU.subtract)
                i = e.tensor_scalar(srow[0:1, 512:1024], srow[0:1, 512:1024], LN_EPS, None, ALU.add)
                p.inc(i, "ve")

            @blk.scalar
            def _(e):
                p.wait(e, ("ve", p.C["ve"]))
                i = e.activation(srow[0:1, 512:1024], srow[0:1, 512:1024], AF.Sqrt)
                p.inc(i, "ac")

            @blk.vector
            def _(e):
                p.wait(e, ("ac", p.C["ac"]))
                e.reciprocal(srow[0:1, 512:1024], srow[0:1, 512:1024])
                i = e.tensor_tensor(srow[0:1, 1024:1536], srow[0:1, 0:512], srow[0:1, 512:1024], ALU.mult)
                p.inc(i, "ve")

            @blk.tensor
            def _(e):
                p.wait(e, ("ve", p.C["ve"]))
                e.matmul(PS[2][:], ones_r[:], srow[0:1, 512:1024], start=True, stop=True,
                         skip_group_check=True)
                i = e.matmul(PS[3][:], ones_r[:], srow[0:1, 1024:1536], start=True, stop=True,
                             skip_group_check=True)
                p.inc(i, "pe")

            @blk.scalar
            def _(e):
                p.wait(e, ("pe", p.C["pe"]))
                e.copy(rstdB[:], PS[2][:])
                i = e.copy(mrB[:], PS[3][:])
                p.inc(i, "ac")
            ln_ac = p.C["ac"]

            for dm in range(NJ):
                xo = yt[dm % 2]
                xs = "x%d" % (dm % 2)

                @blk.vector
                def _(e, dm=dm, xo=xo, xs=xs, ln_ac=ln_ac):
                    p.wait(e, ("ac", ln_ac), (xs, p.C[xs]))
                    e.tensor_tensor(xo[:], y2[:, dm * TT:(dm + 1) * TT], rstdB[:], ALU.mult)
                    i = e.tensor_tensor(xo[:], xo[:], mrB[:], ALU.subtract)
                    p.inc(i, "ve")

                @blk.gpsimd
                def _(e, dm=dm, sq=sq, trs=trs, xo=xo, xs=xs):
                    p.wait(e, ("ve", p.C["ve"]))
                    i = e.dma_start(out=yout[sq, dm, :, trs:trs + TT], in_=xo[:])
                    p.inc(i, xs, 16)

        @blk.gpsimd
        def _(e):
            p.wait(e, ("x0", p.C["x0"]))
            e.nop()
            e.wait_ge(p.sems["x1"], p.C["x1"])

    return p


def prep_D_weights(lut_v, gate_idx, up_idx, down_idx):
    gidx = np.ascontiguousarray(
        gate_idx.reshape(NC1, 128, NJ, 128).transpose(0, 3, 2, 1).reshape(NC1, 128, NJ * 128)).astype(np.int32)
    uidx = np.ascontiguousarray(
        up_idx.reshape(NC1, 128, NJ, 128).transpose(0, 3, 2, 1).reshape(NC1, 128, NJ * 128)).astype(np.int32)
    didx = np.ascontiguousarray(
        down_idx.reshape(NJ, 128, NC1, 128).transpose(0, 3, 2, 1).reshape(NJ, 128, NC1 * 128)).astype(np.int32)
    return dict(gidx=gidx, uidx=uidx, didx=didx,
                lut=np.asarray(lut_v, np.float32).reshape(1, 16))


_PROGS = {}


def _get_prog(name):
    if name not in _PROGS:
        _PROGS[name] = {"A": build_A, "B": build_B, "C": build_C, "D": build_D}[name]()
    return _PROGS[name]


def kernel(L0_attn_hidden_states, L0_attn_Wq, L0_attn_bq, L0_attn_Wk, L0_attn_bk,
           L0_attn_Wv, L0_attn_bv, L0_attn_Wo, L0_ffn_W1, L0_ffn_W2, L0_ffn_W3,
           L1_attn_W_qkv, L1_attn_b_qkv, L1_attn_W_out, L1_attn_b_out,
           L1_ffn_lut, L1_ffn_gate_idx, L1_ffn_up_idx, L1_ffn_down_idx):
    x = np.ascontiguousarray(np.asarray(L0_attn_hidden_states, np.float32))
    wa = prep_A_weights(np.asarray(L0_attn_Wq, np.float32), np.asarray(L0_attn_bq, np.float32),
                        np.asarray(L0_attn_Wk, np.float32), np.asarray(L0_attn_bk, np.float32),
                        np.asarray(L0_attn_Wv, np.float32), np.asarray(L0_attn_bv, np.float32),
                        np.asarray(L0_attn_Wo, np.float32))
    wb = prep_B_weights(np.asarray(L0_ffn_W1, np.float32), np.asarray(L0_ffn_W2, np.float32),
                        np.asarray(L0_ffn_W3, np.float32))
    wc = prep_C_weights(np.asarray(L1_attn_W_qkv, np.float32), np.asarray(L1_attn_b_qkv, np.float32),
                        np.asarray(L1_attn_W_out, np.float32), np.asarray(L1_attn_b_out, np.float32))
    wd = prep_D_weights(np.asarray(L1_ffn_lut, np.float32), np.asarray(L1_ffn_gate_idx),
                        np.asarray(L1_ffn_up_idx), np.asarray(L1_ffn_down_idx))

    pA = _get_prog("A")
    in_maps = []
    for c in range(NCORES):
        m = dict(wa)
        m["xin"] = to_fm(x[c * NSEQ:(c + 1) * NSEQ])
        in_maps.append(m)
    res = run_bass_kernel_spmd(pA.nc, in_maps, core_ids=list(range(NCORES)))
    x2s = [np.asarray(r["x2"]) for r in res.results]

    pB = _get_prog("B")
    in_maps = []
    for c in range(NCORES):
        m = dict(wb)
        m["x2"] = x2s[c]
        in_maps.append(m)
    res = run_bass_kernel_spmd(pB.nc, in_maps, core_ids=list(range(NCORES)))
    x3s = [np.asarray(r["x3"]) for r in res.results]

    pC = _get_prog("C")
    in_maps = []
    for c in range(NCORES):
        m = dict(wc)
        m["x3"] = x3s[c]
        in_maps.append(m)
    res = run_bass_kernel_spmd(pC.nc, in_maps, core_ids=list(range(NCORES)))
    x4s = [np.asarray(r["x4"]) for r in res.results]

    pD = _get_prog("D")
    in_maps = []
    for c in range(NCORES):
        m = dict(wd)
        m["x4"] = x4s[c]
        in_maps.append(m)
    res = run_bass_kernel_spmd(pD.nc, in_maps, core_ids=list(range(NCORES)))
    return np.concatenate([from_fm(np.asarray(r["yout"])) for r in res.results], axis=0)


# version 02
# speedup vs baseline: 2.3702x; 2.3702x over previous
import contextlib
import numpy as np
import ml_dtypes
import concourse.bass as bass
import concourse.mybir as mybir
from concourse.bass_utils import run_bass_kernel_spmd

F32 = mybir.dt.float32
BF16 = mybir.dt.bfloat16
I32 = mybir.dt.int32
AF = mybir.ActivationFunctionType
ALU = mybir.AluOpType
AX = mybir.AxisListType

NCORES = 8
BF = ml_dtypes.bfloat16

D = 2048
NJ = 16
SEQ = 2048
NSEQ = 2
H = 16
KVH = 4
HD = 128
LN_EPS = 1e-5
LA_EPS = 1e-6


class P:

    def __init__(self, name):
        self.nc = bass.Bass("TRN2", target_bir_lowering=False)
        self.es = contextlib.ExitStack()
        self.sems = {}
        self.C = {}

    def sem(self, name):
        self.sems[name] = self.es.enter_context(self.nc.semaphore(name))
        self.C[name] = 0

    def sb(self, name, shape, dt, es=None):
        return (es or self.es).enter_context(self.nc.sbuf_tensor(name, shape, dt))

    def ps(self, name, shape, dt=F32, es=None):
        return (es or self.es).enter_context(self.nc.psum_tensor(name, shape, dt))

    def dram(self, name, shape, dt, kind="Internal"):
        return self.nc.dram_tensor(name, shape, dt, kind=kind).ap()

    def inc(self, instr, name, k=1):
        instr.then_inc(self.sems[name], k)
        self.C[name] += k
        return self.C[name]

    def wait(self, eng, *pairs):
        first = True
        for nm, v in pairs:
            if v <= 0:
                continue
            if not first:
                eng.nop()
            eng.wait_ge(self.sems[nm], v)
            first = False

    def barrier(self, blk):
        if "bar" not in self.sems:
            self.sem("bar")
        @blk.sync
        def _(e):
            self.wait(e, *[(nm, self.C[nm]) for nm in self.sems if nm != "bar"])
            e.nop()
        tgt = self.C["bar"] + 5
        for engsec in (blk.sync, blk.tensor, blk.scalar, blk.vector, blk.gpsimd):
            @engsec
            def _(e):
                e.drain()
                e.nop().then_inc(self.sems["bar"], 1)
        self.C["bar"] = tgt
        for engsec in (blk.sync, blk.tensor, blk.scalar, blk.vector, blk.gpsimd):
            @engsec
            def _(e):
                e.wait_ge(self.sems["bar"], tgt)



def build_A(S=SEQ):
    p = P("A")
    nc = p.nc
    NT = S // 512
    xin = p.dram("xin", [NSEQ, NJ, 128, S], F32, kind="ExternalInput")
    wkv = p.dram("wkv", [2, NJ, 128, 512], F32, kind="ExternalInput")
    bkv = p.dram("bkv", [2, 512], F32, kind="ExternalInput")
    wq = p.dram("wq", [H, 128, NJ * 128], F32, kind="ExternalInput")
    bq = p.dram("bq", [1, H * 128], F32, kind="ExternalInput")
    wo = p.dram("wo", [NJ, 128, H * 128], F32, kind="ExternalInput")
    x2 = p.dram("x2", [NSEQ, NJ, 128, S], F32, kind="ExternalOutput")
    wqb_scr = p.dram("wqb_scr", [H, 128, NJ * 128], BF16)
    wob_scr = p.dram("wob_scr", [NJ, 128, H * 128], BF16)

    for s in ("lx", "lw", "pe", "ve", "ac", "ow", "sc",
              "w0", "w1", "s0", "s1", "q0", "q1", "o0", "o1", "x0", "x1"):
        p.sem(s)

    ones_r = p.sb("ones_r", [1, 128], F32)
    ones_c = p.sb("ones_c", [128, 1], BF16)
    bkv_row = p.sb("bkv_row", [1, 1024], F32)
    bkvB = p.sb("bkvB", [128, 1024], F32)
    bq_sb = p.sb("bq_sb", [128, H], F32)
    kv_f = p.sb("kv_f", [128, NSEQ * 4 * 132], F32)
    kv_b = p.sb("kv_b", [128, NSEQ * 4 * 132], BF16)

    pes_w = contextlib.ExitStack()
    wkv_sb = p.sb("wkv_sb", [128, 2 * NJ * 512], BF16, es=pes_w)
    with nc.Block() as blk:
        pes = contextlib.ExitStack()
        wtmp = [p.sb(f"wtmp{i}", [128, 2048], F32, es=pes) for i in range(2)]
        wqb_tmp = [p.sb(f"wqbt{i}", [128, 2048], BF16, es=pes) for i in range(2)]
        pb0 = p.ps("pb0", [128, 512], F32, es=pes)
        pb1 = p.ps("pb1", [128, 512], F32, es=pes)

        bq_row = p.sb("bq_row", [1, H * 128], F32, es=pes)

        @blk.sync
        def _(e):
            i = e.dma_start(out=bkv_row[0:1, 0:512], in_=bkv[0:1, :])
            p.inc(i, "lx", 16)
            i = e.dma_start(out=bkv_row[0:1, 512:1024], in_=bkv[1:2, :])
            p.inc(i, "lx", 16)
            i = e.dma_start(out=bq_row[:], in_=bq[:, :])
            p.inc(i, "lx", 16)

        @blk.vector
        def _(e):
            e.memset(ones_r[:], 1.0)
            i = e.memset(ones_c[:], 1.0)
            p.inc(i, "ve")

        @blk.tensor
        def _(e):
            p.wait(e, ("lx", 48), ("ve", 1))
            e.matmul(pb0[:], ones_r[:], bkv_row[0:1, 0:512], start=True, stop=True)
            i = e.matmul(pb1[:], ones_r[:], bkv_row[0:1, 512:1024], start=True, stop=True)
            p.inc(i, "pe")

        @blk.scalar
        def _(e):
            p.wait(e, ("pe", p.C["pe"]))
            e.copy(bkvB[:, 0:512], pb0[:])
            i = e.copy(bkvB[:, 512:1024], pb1[:])
            p.inc(i, "ac")

        @blk.tensor
        def _(e):
            p.wait(e, ("ac", p.C["ac"]))
            for h in range(H):
                i = e.matmul(pb0[:, h:h + 1], bq_row[0:1, h * 128:(h + 1) * 128],
                             ones_r[0:1, 0:1], start=True, stop=True,
                             skip_group_check=True)
            p.inc(i, "pe")

        @blk.scalar
        def _(e):
            p.wait(e, ("pe", p.C["pe"]))
            i = e.copy(bq_sb[:, 0:H], pb0[:, 0:H])
            p.inc(i, "ac")

        for t in range(2 * NJ):
            g, j = divmod(t, NJ)
            buf = wtmp[t % 2]
            ws = "w%d" % (t % 2)

            @blk.gpsimd
            def _(e):
                p.wait(e, ("ve", p.C["ve"] - 1))
                i = e.dma_start(out=buf[:, 0:512], in_=wkv[g, j])
                p.inc(i, ws, 16)

            @blk.vector
            def _(e):
                p.wait(e, (ws, p.C[ws]))
                i = e.tensor_copy(wkv_sb[:, (g * NJ + j) * 512:(g * NJ + j + 1) * 512], buf[:, 0:512])
                p.inc(i, "ve")

        for t in range(NJ):
            buf = wtmp[t % 2]
            bbuf = wqb_tmp[t % 2]
            ws = "w%d" % (t % 2)
            ss = "s%d" % (t % 2)

            @blk.gpsimd
            def _(e):
                p.wait(e, ("ve", p.C["ve"] - 1))
                i = e.dma_start(out=buf[:], in_=wo[t])
                p.inc(i, ws, 16)

            @blk.vector
            def _(e):
                p.wait(e, (ws, p.C[ws]), (ss, p.C[ss]))
                i = e.tensor_copy(bbuf[:], buf[:])
                p.inc(i, "ve")

            @blk.sync
            def _(e):
                p.wait(e, ("ve", p.C["ve"]))
                i = e.dma_start(out=wob_scr[t], in_=bbuf[:])
                p.inc(i, ss, 16)

        for t in range(H):
            buf = wtmp[t % 2]
            bbuf = wqb_tmp[t % 2]
            ws = "w%d" % (t % 2)
            ss = "s%d" % (t % 2)

            @blk.gpsimd
            def _(e):
                p.wait(e, ("ve", p.C["ve"] - 1))
                i = e.dma_start(out=buf[:], in_=wq[t])
                p.inc(i, ws, 16)

            @blk.vector
            def _(e):
                p.wait(e, (ws, p.C[ws]), (ss, p.C[ss]))
                i = e.tensor_copy(bbuf[:], buf[:])
                p.inc(i, "ve")

            @blk.sync
            def _(e):
                p.wait(e, ("ve", p.C["ve"]))
                i = e.dma_start(out=wqb_scr[t], in_=bbuf[:])
                p.inc(i, ss, 16)

        p.barrier(blk)
        pes.close()

        pes = contextlib.ExitStack()
        xt = p.sb("xtP1", [128, NJ * 512], F32, es=pes)
        xb1 = p.sb("xbP1", [128, NJ * 512], BF16, es=pes)
        t0 = p.sb("t0P1", [128, 512], F32, es=pes)
        tm = p.sb("tmP1", [128, 128], F32, es=pes)
        te = p.sb("teP1", [128, 128], F32, es=pes)
        tr = p.sb("trP1", [128, 128], F32, es=pes)
        skb = p.sb("skbP1", [128, 128], BF16, es=pes)
        vxb = p.sb("vxbP1", [128, 132], BF16, es=pes)
        pproj = [p.ps(f"pprojP1{i}", [128, 512], F32, es=pes) for i in range(2)]
        pkv = [p.ps(f"pkvP1{i}", [128, 512], F32, es=pes) for i in range(2)]

        @blk.vector
        def _(e):
            i = e.memset(vxb[:, 128:132], 1.0)
            p.inc(i, "ve")

        t0_ve = []
        skb2 = p.sb("skb2P1", [128, 128], BF16, es=pes)
        vxb2 = p.sb("vxb2P1", [128, 132], BF16, es=pes)

        @blk.vector
        def _(e):
            i = e.memset(vxb2[:, 128:132], 1.0)
            p.inc(i, "ve")

        skbs = [skb, skb2]
        vxbs = [vxb, vxb2]

        step_idx = 0
        kv_pe = {}
        skb_ve = {}
        for sq in range(NSEQ):
            for st in range(NT):
                @blk.sync
                def _(e):
                    if sq + st > 0:
                        p.wait(e, ("pe", p.C["pe"]))
                    for j in range(NJ):
                        i = e.dma_start(out=xt[:, j * 512:(j + 1) * 512],
                                        in_=xin[sq, j, :, st * 512:(st + 1) * 512])
                        p.inc(i, "lx", 16)

                @blk.vector
                def _(e):
                    p.wait(e, ("lx", p.C["lx"]))
                    for j in range(NJ):
                        i = e.tensor_copy(xb1[:, j * 512:(j + 1) * 512], xt[:, j * 512:(j + 1) * 512])
                    p.inc(i, "ve")
                cast_ve = p.C["ve"]

                for sub in range(4):
                    for g in range(2):
                        i_step = step_idx
                        pp = pproj[i_step % 2]

                        @blk.tensor
                        def _(e):
                            w2 = t0_ve[i_step - 2] if i_step >= 2 else 0
                            p.wait(e, ("ve", max(cast_ve, w2)))
                            for j in range(NJ):
                                i = e.matmul(pp[:],
                                             xb1[:, j * 512 + sub * 128: j * 512 + sub * 128 + 128],
                                             wkv_sb[:, (g * NJ + j) * 512:(g * NJ + j + 1) * 512],
                                             start=(j == 0), stop=(j == NJ - 1),
                                             skip_group_check=True)
                            p.inc(i, "pe")
                        proj_pe = p.C["pe"]

                        @blk.vector
                        def _(e):
                            p.wait(e, ("pe", proj_pe))
                            i = e.tensor_tensor(t0[:], pp[:], bkvB[:, g * 512:(g + 1) * 512], ALU.add)
                            p.inc(i, "ve")
                        t0_ve.append(p.C["ve"])

                        for kk in range(2):
                            kvh = g * 2 + kk
                            sk_b = skbs[kk]
                            vx_b = vxbs[kk]

                            @blk.vector
                            def _(e):
                                prev = kv_pe.get((i_step - 1, kk), 0)
                                p.wait(e, ("pe", prev))
                                e.tensor_scalar(tm[:], t0[:, kk * 256:kk * 256 + 128], 0.0, None, ALU.min)
                                i = e.tensor_relu(tr[:], t0[:, kk * 256:kk * 256 + 128])
                                p.inc(i, "ve")

                            @blk.scalar
                            def _(e):
                                p.wait(e, ("ve", p.C["ve"]))
                                i = e.activation(te[:], tm[:], AF.Exp)
                                p.inc(i, "ac")

                            @blk.vector
                            def _(e):
                                p.wait(e, ("ac", p.C["ac"]))
                                e.tensor_tensor(sk_b[:], te[:], tr[:], ALU.add)
                                i = e.tensor_copy(vx_b[:, 0:128], t0[:, kk * 256 + 128:kk * 256 + 256])
                                p.inc(i, "ve")
                            skb_ve[(i_step, kk)] = p.C["ve"]

                            @blk.tensor
                            def _(e):
                                p.wait(e, ("ve", skb_ve[(i_step, kk)]))
                                bank = pkv[kvh // 3]
                                off = (kvh % 3) * 132
                                i = e.matmul(bank[:, off:off + 132], sk_b[:], vx_b[:],
                                             start=(st == 0 and sub == 0 and kvh in (0, 3)),
                                             stop=(st == NT - 1 and sub == 3 and kvh in (2, 3)),
                                             skip_group_check=True)
                                p.inc(i, "pe")
                            kv_pe[(i_step, kk)] = p.C["pe"]

                        step_idx += 1

            @blk.vector
            def _(e):
                p.wait(e, ("pe", p.C["pe"]))
                base = sq * 4 * 132
                e.tensor_copy(kv_f[:, base:base + 396], pkv[0][:, 0:396])
                e.tensor_copy(kv_f[:, base + 396:base + 528], pkv[1][:, 0:132])
                e.tensor_copy(kv_b[:, base:base + 396], pkv[0][:, 0:396])
                i = e.tensor_copy(kv_b[:, base + 396:base + 528], pkv[1][:, 0:132])
                p.inc(i, "ve")

            @blk.tensor
            def _(e):
                p.wait(e, ("ve", p.C["ve"]))
                p.inc(e.nop(), "pe")

        p.barrier(blk)
        pes.close()
        pes_w.close()

        pes = contextlib.ExitStack()
        xt3 = p.sb("xtP3", [128, NJ * 512], F32, es=pes)
        xb3 = p.sb("xbP3", [128, NJ * 512], BF16, es=pes)
        wqh = [p.sb(f"wqhP3{i}", [128, 2048], BF16, es=pes) for i in range(2)]
        sqb = [p.sb(f"sqbP3{i}", [128, 512], BF16, es=pes) for i in range(2)]
        tq = p.sb("tqP3", [128, 512], F32, es=pes)
        tqm = p.sb("tqmP3", [128, 512], F32, es=pes)
        tqe = p.sb("tqeP3", [128, 512], F32, es=pes)
        tqr = p.sb("tqrP3", [128, 512], F32, es=pes)
        nrow = p.sb("nrowP3", [1, 512], F32, es=pes)
        recB = [p.sb(f"recBP3{i}", [128, 512], F32, es=pes) for i in range(2)]
        attnN = p.sb("attnNP3", [128, H * 512], BF16, es=pes)
        y1 = p.sb("y1P3", [128, NJ * 512], F32, es=pes)
        y1b = p.sb("y1bP3", [128, 512], BF16, es=pes)
        y1s = p.sb("y1sP3", [128, 512], BF16, es=pes)
        srow = p.sb("srowP3", [1, 1536], F32, es=pes)
        rstdB = p.sb("rstdBP3", [128, 512], F32, es=pes)
        mrB = p.sb("mrBP3", [128, 512], F32, es=pes)
        x2t = [p.sb(f"x2tP3{i}", [128, 512], F32, es=pes) for i in range(2)]
        woh = [p.sb(f"wohP3{i}", [128, 2048], BF16, es=pes) for i in range(2)]
        PS = [p.ps(f"PSP3{i}", [128, 512], F32, es=pes) for i in range(6)]

        for sq in range(NSEQ):
            for st in range(NT):
                pe_prev_tile = p.C["pe"]

                @blk.sync
                def _(e):
                    if sq + st > 0:
                        p.wait(e, ("ve", p.C["ve"]))
                    for j in range(NJ):
                        i = e.dma_start(out=xt3[:, j * 512:(j + 1) * 512],
                                        in_=xin[sq, j, :, st * 512:(st + 1) * 512])
                        p.inc(i, "lx", 16)
                    i = e.dma_start(out=wqh[0][:], in_=wqb_scr[0])
                    p.inc(i, "q0", 16)

                @blk.vector
                def _(e):
                    p.wait(e, ("lx", p.C["lx"]), ("pe", pe_prev_tile))
                    for j in range(NJ):
                        i = e.tensor_copy(xb3[:, j * 512:(j + 1) * 512], xt3[:, j * 512:(j + 1) * 512])
                    p.inc(i, "ve")
                xcast_ve = p.C["ve"]

                qproj_pe = {}
                evacq_ve = {}
                sqb_ve = {}

                def head_front(h):
                    pq = PS[h % 2]

                    @blk.sync
                    def _(e):
                        if h + 1 < H:
                            p.wait(e, ("pe", qproj_pe.get(h - 1, 0)))
                            i = e.dma_start(out=wqh[(h + 1) % 2][:], in_=wqb_scr[h + 1])
                            p.inc(i, "q%d" % ((h + 1) % 2), 16)

                    @blk.tensor
                    def _(e):
                        wv = max(xcast_ve, evacq_ve.get(h - 2, 0))
                        p.wait(e, ("q%d" % (h % 2), p.C["q%d" % (h % 2)]),
                               ("ve", wv))
                        for j in range(NJ):
                            i = e.matmul(pq[:], wqh[h % 2][:, j * 128:(j + 1) * 128],
                                         xb3[:, j * 512:(j + 1) * 512],
                                         start=(j == 0), stop=(j == NJ - 1),
                                         skip_group_check=True)
                        p.inc(i, "pe")
                    qproj_pe[h] = p.C["pe"]

                    @blk.vector
                    def _(e):
                        p.wait(e, ("pe", qproj_pe[h]))
                        e.tensor_scalar(tq[:], pq[:], bq_sb[:, h:h + 1], None, ALU.add)
                        e.tensor_scalar(tqm[:], tq[:], 0.0, None, ALU.min)
                        i = e.tensor_relu(tqr[:], tq[:])
                        p.inc(i, "ve")
                    evacq_ve[h] = p.C["ve"]

                    @blk.scalar
                    def _(e):
                        p.wait(e, ("ve", evacq_ve[h]))
                        i = e.activation(tqe[:], tqm[:], AF.Exp)
                        p.inc(i, "ac")

                    @blk.vector
                    def _(e):
                        p.wait(e, ("ac", p.C["ac"]))
                        i = e.tensor_tensor(sqb[h % 2][:], tqe[:], tqr[:], ALU.add)
                        p.inc(i, "ve")
                    sqb_ve[h] = p.C["ve"]

                def head_back(h):
                    pat = PS[2 + h % 2]
                    kvh = h // 4
                    kvoff = (sq * 4 + kvh) * 132

                    @blk.tensor
                    def _(e):
                        p.wait(e, ("ve", sqb_ve[h]))
                        e.matmul(PS[4][0:1, :], kv_b[:, kvoff + 128:kvoff + 129], sqb[h % 2][:],
                                 start=True, stop=True, skip_group_check=True)
                        i = e.matmul(pat[:], kv_b[:, kvoff:kvoff + 128], sqb[h % 2][:],
                                     start=True, stop=True, skip_group_check=True)
                        p.inc(i, "pe")
                    na_pe = p.C["pe"]

                    @blk.vector
                    def _(e):
                        p.wait(e, ("pe", na_pe))
                        e.tensor_scalar(nrow[:], PS[4][0:1, :], LA_EPS, None, ALU.add)
                        i = e.reciprocal(nrow[:], nrow[:])
                        p.inc(i, "ve")
                    nr_ve = p.C["ve"]

                    @blk.tensor
                    def _(e):
                        p.wait(e, ("ve", nr_ve))
                        i = e.matmul(PS[5][:], ones_r[:], nrow[:], start=True, stop=True,
                                     skip_group_check=True)
                        p.inc(i, "pe")
                    bc_pe = p.C["pe"]

                    @blk.scalar
                    def _(e):
                        p.wait(e, ("pe", bc_pe))
                        i = e.copy(recB[h % 2][:], PS[5][:])
                        p.inc(i, "ac")
                    rb_ac = p.C["ac"]

                    @blk.vector
                    def _(e):
                        p.wait(e, ("ac", rb_ac))
                        i = e.tensor_tensor(attnN[:, h * 512:(h + 1) * 512], pat[:], recB[h % 2][:], ALU.mult)
                        p.inc(i, "ve")

                for h in range(H):
                    head_front(h)
                    if h > 0:
                        head_back(h - 1)
                head_back(H - 1)

                attn_done_ve = p.C["ve"]
                y1_ve = {}
                sqr_ac = {}
                wo_lw = {}
                wo_pe = {}

                @blk.gpsimd
                def _(e):
                    p.wait(e, ("pe", pe_prev_tile))
                    for dmp in range(2):
                        i = e.dma_start(out=woh[dmp][:], in_=wob_scr[dmp])
                        wo_lw[dmp] = p.inc(i, "o%d" % dmp, 16)

                def wo_mm_tensor(dm):
                    po = PS[dm % 2]

                    @blk.tensor
                    def _(e):
                        p.wait(e, ("o%d" % (dm % 2), wo_lw[dm]))
                        e.nop()
                        if dm == 0:
                            p.wait(e, ("ve", attn_done_ve))
                        elif dm >= 2:
                            p.wait(e, ("ve", y1_ve[dm - 2]))
                        for hh in range(H):
                            i = e.matmul(po[:], woh[dm % 2][:, hh * 128:(hh + 1) * 128],
                                         attnN[:, hh * 512:(hh + 1) * 512],
                                         start=(hh == 0), stop=(hh == H - 1),
                                         skip_group_check=True)
                        p.inc(i, "pe")
                    wo_pe[dm] = p.C["pe"]

                    if dm + 2 < NJ:
                        @blk.gpsimd
                        def _(e):
                            p.wait(e, ("pe", wo_pe[dm]))
                            i = e.dma_start(out=woh[dm % 2][:], in_=wob_scr[dm + 2])
                            wo_lw[dm + 2] = p.inc(i, "o%d" % (dm % 2), 16)

                def stats_mm(dm):
                    @blk.tensor
                    def _(e):
                        p.wait(e, ("ac", sqr_ac[dm]))
                        e.matmul(PS[4][0:1, :], ones_c[:], y1b[:],
                                 start=(dm == 0), stop=(dm == NJ - 1), skip_group_check=True)
                        i = e.matmul(PS[5][0:1, :], ones_c[:], y1s[:],
                                     start=(dm == 0), stop=(dm == NJ - 1), skip_group_check=True)
                        p.inc(i, "pe")

                def wo_evac(dm):
                    po = PS[dm % 2]
                    pe_snap = p.C["pe"]

                    @blk.vector
                    def _(e):
                        p.wait(e, ("pe", pe_snap))
                        e.tensor_tensor(y1[:, dm * 512:(dm + 1) * 512], po[:],
                                        xt3[:, dm * 512:(dm + 1) * 512], ALU.add)
                        i = e.tensor_copy(y1b[:], y1[:, dm * 512:(dm + 1) * 512])
                        p.inc(i, "ve")
                    y1_ve[dm] = p.C["ve"]

                    @blk.scalar
                    def _(e):
                        p.wait(e, ("ve", y1_ve[dm]))
                        i = e.activation(y1s[:], y1[:, dm * 512:(dm + 1) * 512], AF.Square)
                        p.inc(i, "ac")
                    sqr_ac[dm] = p.C["ac"]

                for dm in range(NJ):
                    wo_mm_tensor(dm)
                    if dm > 0:
                        stats_mm(dm - 1)
                    wo_evac(dm)
                stats_mm(NJ - 1)

                @blk.vector
                def _(e):
                    p.wait(e, ("pe", p.C["pe"]))
                    e.tensor_scalar(srow[0:1, 0:512], PS[4][0:1, :], 1.0 / D, None, ALU.mult)
                    e.tensor_scalar(srow[0:1, 512:1024], PS[5][0:1, :], 1.0 / D, None, ALU.mult)
                    e.tensor_tensor(srow[0:1, 1024:1536], srow[0:1, 0:512], srow[0:1, 0:512], ALU.mult)
                    e.tensor_tensor(srow[0:1, 512:1024], srow[0:1, 512:1024], srow[0:1, 1024:1536], ALU.subtract)
                    i = e.tensor_scalar(srow[0:1, 512:1024], srow[0:1, 512:1024], LN_EPS, None, ALU.add)
                    p.inc(i, "ve")

                @blk.scalar
                def _(e):
                    p.wait(e, ("ve", p.C["ve"]))
                    i = e.activation(srow[0:1, 512:1024], srow[0:1, 512:1024], AF.Sqrt)
                    p.inc(i, "ac")

                @blk.vector
                def _(e):
                    p.wait(e, ("ac", p.C["ac"]))
                    e.reciprocal(srow[0:1, 512:1024], srow[0:1, 512:1024])
                    i = e.tensor_tensor(srow[0:1, 1024:1536], srow[0:1, 0:512], srow[0:1, 512:1024], ALU.mult)
                    p.inc(i, "ve")

                @blk.tensor
                def _(e):
                    p.wait(e, ("ve", p.C["ve"]))
                    e.matmul(PS[2][:], ones_r[:], srow[0:1, 512:1024], start=True, stop=True,
                             skip_group_check=True)
                    i = e.matmul(PS[3][:], ones_r[:], srow[0:1, 1024:1536], start=True, stop=True,
                                 skip_group_check=True)
                    p.inc(i, "pe")

                @blk.scalar
                def _(e):
                    p.wait(e, ("pe", p.C["pe"]))
                    e.copy(rstdB[:], PS[2][:])
                    i = e.copy(mrB[:], PS[3][:])
                    p.inc(i, "ac")
                ln_ac = p.C["ac"]

                for dm in range(NJ):
                    xb = x2t[dm % 2]
                    xs = "x%d" % (dm % 2)

                    @blk.vector
                    def _(e):
                        p.wait(e, ("ac", ln_ac), (xs, p.C[xs]))
                        e.tensor_tensor(xb[:], y1[:, dm * 512:(dm + 1) * 512], rstdB[:], ALU.mult)
                        i = e.tensor_tensor(xb[:], xb[:], mrB[:], ALU.subtract)
                        p.inc(i, "ve")

                    @blk.gpsimd
                    def _(e):
                        p.wait(e, ("ve", p.C["ve"]))
                        i = e.dma_start(out=x2[sq, dm, :, st * 512:(st + 1) * 512], in_=xb[:])
                        p.inc(i, xs, 16)

        @blk.gpsimd
        def _(e):
            p.wait(e, ("x0", p.C["x0"]))
            e.nop()
            e.wait_ge(p.sems["x1"], p.C["x1"])

    return p



def prep_A_weights(Wq, bqv, Wk, bk, Wv, bv, Wo):
    WkT = np.ascontiguousarray(Wk.T)
    WvT = np.ascontiguousarray(Wv.T)
    wkv = np.empty((2, NJ, 128, 512), np.float32)
    bkv = np.empty((2, 512), np.float32)
    for g in range(2):
        for kk in range(2):
            kvh = 2 * g + kk
            for j in range(NJ):
                wkv[g, j, :, kk * 256:kk * 256 + 128] = WkT[j * 128:(j + 1) * 128, kvh * 128:(kvh + 1) * 128]
                wkv[g, j, :, kk * 256 + 128:kk * 256 + 256] = WvT[j * 128:(j + 1) * 128, kvh * 128:(kvh + 1) * 128]
            bkv[g, kk * 256:kk * 256 + 128] = bk[kvh * 128:(kvh + 1) * 128]
            bkv[g, kk * 256 + 128:kk * 256 + 256] = bv[kvh * 128:(kvh + 1) * 128]
    wq = np.ascontiguousarray(
        Wq.reshape(H, 128, NJ, 128).transpose(0, 3, 2, 1).reshape(H, 128, NJ * 128))
    bq_r = np.ascontiguousarray(bqv.reshape(1, H * 128))
    wo = np.ascontiguousarray(
        Wo.reshape(NJ, 128, H, 128).transpose(0, 3, 2, 1).reshape(NJ, 128, H * 128))
    return dict(wkv=wkv, bkv=bkv, wq=wq, bq=bq_r, wo=wo)


def to_fm(x):
    n, S, d = x.shape
    return np.ascontiguousarray(x.transpose(0, 2, 1).reshape(n, NJ, 128, S))


def from_fm(x):
    n = x.shape[0]
    S = x.shape[3]
    return np.ascontiguousarray(x.reshape(n, NJ * 128, S).transpose(0, 2, 1))


def run_A(x_all, wa, S=SEQ, ncores=NCORES):
    p = build_A(S)
    in_maps = []
    for c in range(ncores):
        m = dict(wa)
        m["xin"] = to_fm(x_all[c * NSEQ:(c + 1) * NSEQ])
        in_maps.append(m)
    res = run_bass_kernel_spmd(p.nc, in_maps, core_ids=list(range(ncores)))
    out = np.concatenate([from_fm(r["x2"]) for r in res.results], axis=0)
    p.es.close()
    return out





NC0 = 64
RT2 = np.float32(1.0 / np.sqrt(8192.0))


def build_B(S=SEQ, dbg=False):
    p = P("B")
    nc = p.nc
    T = NSEQ * S
    NTS = T // 512
    x2 = p.dram("x2", [NSEQ, NJ, 128, S], F32, kind="ExternalInput")
    w1h = p.dram("w1h", [NC0, 128, NJ * 128], F32, kind="ExternalInput")
    w3h = p.dram("w3h", [NC0, 128, NJ * 128], F32, kind="ExternalInput")
    w2r = p.dram("w2r", [64, 128, D], F32, kind="ExternalInput")
    h64 = p.dram("h64", [64, 64], BF16, kind="ExternalInput")
    h128 = p.dram("h128", [128, 128], BF16, kind="ExternalInput")
    x3 = p.dram("x3", [NSEQ, NJ, 128, S], F32, kind="ExternalOutput")
    scrkind = "ExternalOutput" if dbg else "Internal"
    ascr = p.dram("ascr", [NC0, 128, T], BF16, kind=scrkind)
    w2b_scr = p.dram("w2b_scr", [NJ, 128, 64, 128], BF16, kind=scrkind)

    for s in ("lx", "lw", "pe", "ve", "ac", "ow", "sa",
              "w0", "w1", "s0", "s1", "a0", "a1", "d0", "d1", "x0", "x1"):
        p.sem(s)

    ones_r = p.sb("ones_r", [1, 128], F32)
    ones_c = p.sb("ones_c", [128, 1], BF16)
    h64_sb = p.sb("h64_sb", [64, 64], BF16)
    h128_sb = p.sb("h128_sb", [128, 128], BF16)

    with nc.Block() as blk:
        @blk.sync
        def _(e):
            i = e.dma_start(out=h64_sb[:], in_=h64[:, :])
            p.inc(i, "lx", 16)
            i = e.dma_start(out=h128_sb[:], in_=h128[:, :])
            p.inc(i, "lx", 16)

        @blk.vector
        def _(e):
            e.memset(ones_r[:], 1.0)
            i = e.memset(ones_c[:], 1.0)
            p.inc(i, "ve")

        pes = contextlib.ExitStack()
        wfb = [p.sb(f"wfbB{i}", [64, 512], F32, es=pes) for i in range(2)]
        wbb = [p.sb(f"wbbB{i}", [64, 512], BF16, es=pes) for i in range(2)]
        wob = [p.sb(f"wobB{i}", [64, 512], BF16, es=pes) for i in range(2)]
        psA = [p.ps(f"psAB{i}", [64, 512], F32, es=pes) for i in range(2)]

        cast_ve = {}
        mm_pe = {}
        ev_ve = {}
        wr_sa = {}
        for s in range(512):
            b, dsub = divmod(s, 4)
            par = s % 2

            @blk.gpsimd
            def _(e):
                if s >= 2:
                    p.wait(e, ("ve", cast_ve[s - 2]))
                i = e.dma_start(out=wfb[par][:], in_=w2r[:, b, dsub * 512:(dsub + 1) * 512])
                p.inc(i, "w%d" % par, 16)
            lw_s = p.C["w%d" % par]

            @blk.vector
            def _(e):
                p.wait(e, ("w%d" % par, lw_s))
                if s >= 2:
                    e.nop()
                    p.wait(e, ("pe", mm_pe[s - 2]))
                i = e.tensor_copy(wbb[par][:], wfb[par][:])
                p.inc(i, "ve")
            cast_ve[s] = p.C["ve"]

            @blk.tensor
            def _(e):
                if s == 0:
                    p.wait(e, ("lx", 32))
                    e.nop()
                p.wait(e, ("ve", cast_ve[s]))
                if s >= 2:
                    e.nop()
                    p.wait(e, ("ve", ev_ve[s - 2]))
                i = e.matmul(psA[par][:], h64_sb[:], wbb[par][:], start=True, stop=True,
                             skip_group_check=True)
                p.inc(i, "pe")
            mm_pe[s] = p.C["pe"]

            @blk.vector
            def _(e):
                p.wait(e, ("pe", mm_pe[s]))
                if s >= 2:
                    e.nop()
                    p.wait(e, ("s%d" % par, wr_sa[s - 2]))
                i = e.tensor_scalar(wob[par][:], psA[par][:], float(RT2), None, ALU.mult)
                p.inc(i, "ve")
            ev_ve[s] = p.C["ve"]

            @blk.sync
            def _(e):
                p.wait(e, ("ve", ev_ve[s]))
                dst = w2b_scr[dsub * 4:(dsub + 1) * 4, b, :, :].rearrange("m a d -> a m d")
                i = e.dma_start(out=dst, in_=wob[par][:].rearrange("a (m d) -> a m d", m=4))
                p.inc(i, "s%d" % par, 16)
            wr_sa[s] = p.C["s%d" % par]

        p.barrier(blk)
        pes.close()

        pes = contextlib.ExitStack()
        xb = p.sb("xbB", [128, NJ * T], BF16, es=pes)
        xstg = p.sb("xstgB", [128, T], F32, es=pes)
        wf = [p.sb(f"wfB{i}", [128, 2 * NJ * 128], F32, es=pes) for i in range(2)]
        wb = [p.sb(f"wbB{i}", [128, 2 * NJ * 128], BF16, es=pes) for i in range(2)]
        stmp = [p.sb(f"stmpB{i}", [128, 512], F32, es=pes) for i in range(2)]
        hsub = [p.sb(f"hsubB{i}", [128, 512], BF16, es=pes) for i in range(2)]
        asub = [p.sb(f"asubB{i}", [128, 512], BF16, es=pes) for i in range(2)]
        pG = [p.ps(f"pGB{i}", [128, 512], F32, es=pes) for i in range(2)]
        pU = [p.ps(f"pUB{i}", [128, 512], F32, es=pes) for i in range(2)]
        pA = [p.ps(f"pAB{i}", [128, 512], F32, es=pes) for i in range(2)]

        for j in range(NJ):
            @blk.sync
            def _(e):
                if j > 0:
                    p.wait(e, ("ve", p.C["ve"]))
                for sq in range(NSEQ):
                    i = e.dma_start(out=xstg[:, sq * S:(sq + 1) * S], in_=x2[sq, j])
                    p.inc(i, "lx", 16)

            @blk.vector
            def _(e):
                p.wait(e, ("lx", p.C["lx"]))
                i = e.tensor_copy(xb[:, j * T:(j + 1) * T], xstg[:])
                p.inc(i, "ve")
        xcast_ve = p.C["ve"]

        @blk.gpsimd
        def _(e):
            for c in range(2):
                i = e.dma_start(out=wf[c][:, 0:NJ * 128], in_=w1h[c])
                p.inc(i, "w%d" % c, 16)
                i = e.dma_start(out=wf[c][:, NJ * 128:], in_=w3h[c])
                p.inc(i, "w%d" % c, 16)

        wcast_ac = {}
        gu_pe = {}
        hmul_ve = {}
        amm_pe = {}
        aev_ve = {}
        awr_sa = {}
        silu_ac = {}

        for c in range(NC0):
            par = c % 2

            @blk.scalar
            def _(e):
                p.wait(e, ("w%d" % par, p.C["w%d" % par]))
                if c >= 2:
                    e.nop()
                    p.wait(e, ("pe", gu_pe[(c - 2, NTS - 1)]))
                e.copy(wb[par][:, 0:NJ * 128], wf[par][:, 0:NJ * 128])
                i = e.copy(wb[par][:, NJ * 128:], wf[par][:, NJ * 128:])
                p.inc(i, "ac")
            wcast_ac[c] = p.C["ac"]

            @blk.gpsimd
            def _(e):
                if c + 2 < NC0:
                    p.wait(e, ("ac", wcast_ac[c]))
                    i = e.dma_start(out=wf[par][:, 0:NJ * 128], in_=w1h[c + 2])
                    p.inc(i, "w%d" % par, 16)
                    i = e.dma_start(out=wf[par][:, NJ * 128:], in_=w3h[c + 2])
                    p.inc(i, "w%d" % par, 16)

            for ts in range(NTS):
                g_idx = c * NTS + ts
                q = g_idx % 2

                @blk.tensor
                def _(e):
                    wv = xcast_ve if g_idx == 0 else 0
                    if ts == 0:
                        p.wait(e, ("ac", wcast_ac[c]))
                        e.nop()
                    if g_idx >= 2:
                        p.wait(e, ("ve", hmul_ve[divmod(g_idx - 2, NTS)]))
                    elif wv:
                        p.wait(e, ("ve", wv))
                    for j in range(NJ):
                        e.matmul(pG[q][:], wb[par][:, j * 128:(j + 1) * 128],
                                 xb[:, j * T + ts * 512: j * T + ts * 512 + 512],
                                 start=(j == 0), stop=(j == NJ - 1), skip_group_check=True)
                    for j in range(NJ):
                        i = e.matmul(pU[q][:], wb[par][:, NJ * 128 + j * 128:NJ * 128 + (j + 1) * 128],
                                     xb[:, j * T + ts * 512: j * T + ts * 512 + 512],
                                     start=(j == 0), stop=(j == NJ - 1), skip_group_check=True)
                    p.inc(i, "pe")
                gu_pe[(c, ts)] = p.C["pe"]

                @blk.scalar
                def _(e):
                    p.wait(e, ("pe", gu_pe[(c, ts)]))
                    if g_idx >= 2:
                        e.nop()
                        p.wait(e, ("ve", hmul_ve[divmod(g_idx - 2, NTS)]))
                    i = e.activation(stmp[q][:], pG[q][:], AF.Silu)
                    p.inc(i, "ac")
                silu_ac[(c, ts)] = p.C["ac"]

                @blk.vector
                def _(e):
                    p.wait(e, ("ac", silu_ac[(c, ts)]))
                    if g_idx >= 2:
                        e.nop()
                        p.wait(e, ("pe", amm_pe[divmod(g_idx - 2, NTS)]))
                    i = e.tensor_tensor(hsub[q][:], stmp[q][:], pU[q][:], ALU.mult)
                    p.inc(i, "ve")
                hmul_ve[(c, ts)] = p.C["ve"]

                @blk.tensor
                def _(e):
                    p.wait(e, ("ve", hmul_ve[(c, ts)]))
                    if g_idx >= 2:
                        e.nop()
                        p.wait(e, ("ve", aev_ve[divmod(g_idx - 2, NTS)]))
                    i = e.matmul(pA[q][:], h128_sb[:], hsub[q][:], start=True, stop=True,
                                 skip_group_check=True)
                    p.inc(i, "pe")
                amm_pe[(c, ts)] = p.C["pe"]

                @blk.vector
                def _(e):
                    p.wait(e, ("pe", amm_pe[(c, ts)]))
                    if g_idx >= 2:
                        e.nop()
                        p.wait(e, ("a%d" % q, awr_sa[divmod(g_idx - 2, NTS)]))
                    i = e.tensor_copy(asub[q][:], pA[q][:])
                    p.inc(i, "ve")
                aev_ve[(c, ts)] = p.C["ve"]

                @blk.sync
                def _(e):
                    p.wait(e, ("ve", aev_ve[(c, ts)]))
                    i = e.dma_start(out=ascr[c, :, ts * 512:(ts + 1) * 512], in_=asub[q][:])
                    p.inc(i, "a%d" % q, 16)
                awr_sa[(c, ts)] = p.C["a%d" % q]

        p.barrier(blk)
        pes.close()

        pes = contextlib.ExitStack()
        acol = p.sb("acolB", [128, NC0 * 512], BF16, es=pes)
        w2bt = [p.sb(f"w2btB{i}", [128, 64 * 128], BF16, es=pes) for i in range(2)]
        res = [p.sb(f"resB{i}", [128, 512], F32, es=pes) for i in range(2)]
        y2 = p.sb("y2B", [128, NJ * 512], F32, es=pes)
        y2b = p.sb("y2bB", [128, 512], BF16, es=pes)
        y2s = p.sb("y2sB", [128, 512], BF16, es=pes)
        srow = p.sb("srowB", [1, 1536], F32, es=pes)
        rstdB = p.sb("rstdBB", [128, 512], F32, es=pes)
        mrB = p.sb("mrBB", [128, 512], F32, es=pes)
        x3t = [p.sb(f"x3tB{i}", [128, 512], F32, es=pes) for i in range(2)]
        PS = [p.ps(f"PSB{i}", [128, 512], F32, es=pes) for i in range(6)]

        NTT = T // 512
        for tt in range(NTT):
            sq, trs = divmod(tt * 512, S)
            pe_prev_tile = p.C["pe"]
            ve_prev_tile = p.C["ve"]

            @blk.sync
            def _(e):
                if tt > 0:
                    p.wait(e, ("pe", pe_prev_tile))
                for a in range(NC0):
                    i = e.dma_start(out=acol[:, a * 512:(a + 1) * 512],
                                    in_=ascr[a, :, tt * 512:(tt + 1) * 512])
                    p.inc(i, "lx", 16)
            acol_lx = p.C["lx"]

            w2b_lw = {}
            res_lx = {}

            @blk.gpsimd
            def _(e):
                if tt > 0:
                    p.wait(e, ("pe", pe_prev_tile))
                for dmp in range(2):
                    i = e.dma_start(out=w2bt[dmp][:], in_=w2b_scr[dmp])
                    i2 = e.dma_start(out=res[dmp][:], in_=x2[sq, dmp, :, trs:trs + 512])
                    i.then_inc(p.sems["d%d" % dmp], 16)
                    p.C["d%d" % dmp] += 16
                    w2b_lw[dmp] = p.inc(i2, "d%d" % dmp, 16)
                    res_lx[dmp] = w2b_lw[dmp]

            y2_ve = {}
            sqr_ac = {}
            dmm_pe = {}

            def down_mm(dm):
                po = PS[dm % 2]

                @blk.tensor
                def _(e):
                    p.wait(e, ("d%d" % (dm % 2), w2b_lw[dm]))
                    e.nop()
                    if dm == 0:
                        p.wait(e, ("lx", acol_lx))
                    elif dm >= 2:
                        p.wait(e, ("ve", y2_ve[dm - 2]))
                    for a in range(NC0):
                        i = e.matmul(po[:], w2bt[dm % 2][:, a * 128:(a + 1) * 128],
                                     acol[:, a * 512:(a + 1) * 512],
                                     start=(a == 0), stop=(a == NC0 - 1), skip_group_check=True)
                    p.inc(i, "pe")
                dmm_pe[dm] = p.C["pe"]

            def stats_mm(dm):
                @blk.tensor
                def _(e):
                    p.wait(e, ("ac", sqr_ac[dm]))
                    e.matmul(PS[4][0:1, :], ones_c[:], y2b[:],
                             start=(dm == 0), stop=(dm == NJ - 1), skip_group_check=True)
                    i = e.matmul(PS[5][0:1, :], ones_c[:], y2s[:],
                                 start=(dm == 0), stop=(dm == NJ - 1), skip_group_check=True)
                    p.inc(i, "pe")

            def down_evac(dm):
                po = PS[dm % 2]
                pe_snap = p.C["pe"]

                @blk.vector
                def _(e):
                    p.wait(e, ("d%d" % (dm % 2), res_lx[dm]))
                    e.nop()
                    p.wait(e, ("pe", pe_snap))
                    e.tensor_tensor(y2[:, dm * 512:(dm + 1) * 512], po[:], res[dm % 2][:], ALU.add)
                    i = e.tensor_copy(y2b[:], y2[:, dm * 512:(dm + 1) * 512])
                    p.inc(i, "ve")
                y2_ve[dm] = p.C["ve"]

                @blk.scalar
                def _(e):
                    p.wait(e, ("ve", y2_ve[dm]))
                    i = e.activation(y2s[:], y2[:, dm * 512:(dm + 1) * 512], AF.Square)
                    p.inc(i, "ac")
                sqr_ac[dm] = p.C["ac"]

            def prefetch_next(dm):
                @blk.gpsimd
                def _(e):
                    p.wait(e, ("pe", dmm_pe[dm]), ("ve", y2_ve[dm]))
                    i = e.dma_start(out=w2bt[dm % 2][:], in_=w2b_scr[dm + 2])
                    i2 = e.dma_start(out=res[dm % 2][:], in_=x2[sq, dm + 2, :, trs:trs + 512])
                    i.then_inc(p.sems["d%d" % (dm % 2)], 16)
                    p.C["d%d" % (dm % 2)] += 16
                    w2b_lw[dm + 2] = p.inc(i2, "d%d" % (dm % 2), 16)
                    res_lx[dm + 2] = w2b_lw[dm + 2]

            for dm in range(NJ):
                down_mm(dm)
                if dm > 0:
                    stats_mm(dm - 1)
                down_evac(dm)
                if dm + 2 < NJ:
                    prefetch_next(dm)
            stats_mm(NJ - 1)

            @blk.vector
            def _(e):
                p.wait(e, ("pe", p.C["pe"]))
                e.tensor_scalar(srow[0:1, 0:512], PS[4][0:1, :], 1.0 / D, None, ALU.mult)
                e.tensor_scalar(srow[0:1, 512:1024], PS[5][0:1, :], 1.0 / D, None, ALU.mult)
                e.tensor_tensor(srow[0:1, 1024:1536], srow[0:1, 0:512], srow[0:1, 0:512], ALU.mult)
                e.tensor_tensor(srow[0:1, 512:1024], srow[0:1, 512:1024], srow[0:1, 1024:1536], ALU.subtract)
                i = e.tensor_scalar(srow[0:1, 512:1024], srow[0:1, 512:1024], LN_EPS, None, ALU.add)
                p.inc(i, "ve")

            @blk.scalar
            def _(e):
                p.wait(e, ("ve", p.C["ve"]))
                i = e.activation(srow[0:1, 512:1024], srow[0:1, 512:1024], AF.Sqrt)
                p.inc(i, "ac")

            @blk.vector
            def _(e):
                p.wait(e, ("ac", p.C["ac"]))
                e.reciprocal(srow[0:1, 512:1024], srow[0:1, 512:1024])
                i = e.tensor_tensor(srow[0:1, 1024:1536], srow[0:1, 0:512], srow[0:1, 512:1024], ALU.mult)
                p.inc(i, "ve")

            @blk.tensor
            def _(e):
                p.wait(e, ("ve", p.C["ve"]))
                e.matmul(PS[2][:], ones_r[:], srow[0:1, 512:1024], start=True, stop=True,
                         skip_group_check=True)
                i = e.matmul(PS[3][:], ones_r[:], srow[0:1, 1024:1536], start=True, stop=True,
                             skip_group_check=True)
                p.inc(i, "pe")

            @blk.scalar
            def _(e):
                p.wait(e, ("pe", p.C["pe"]))
                e.copy(rstdB[:], PS[2][:])
                i = e.copy(mrB[:], PS[3][:])
                p.inc(i, "ac")
            ln_ac = p.C["ac"]

            for dm in range(NJ):
                xo = x3t[dm % 2]
                xs = "x%d" % (dm % 2)

                @blk.vector
                def _(e):
                    p.wait(e, ("ac", ln_ac), (xs, p.C[xs]))
                    e.tensor_tensor(xo[:], y2[:, dm * 512:(dm + 1) * 512], rstdB[:], ALU.mult)
                    i = e.tensor_tensor(xo[:], xo[:], mrB[:], ALU.subtract)
                    p.inc(i, "ve")

                @blk.gpsimd
                def _(e):
                    p.wait(e, ("ve", p.C["ve"]))
                    i = e.dma_start(out=x3[sq, dm, :, trs:trs + 512], in_=xo[:])
                    p.inc(i, xs, 16)

        @blk.gpsimd
        def _(e):
            p.wait(e, ("x0", p.C["x0"]))
            e.nop()
            e.wait_ge(p.sems["x1"], p.C["x1"])

    return p


def prep_B_weights(W1, W2, W3):
    w1h = np.ascontiguousarray(
        W1.reshape(NC0, 128, NJ, 128).transpose(0, 3, 2, 1).reshape(NC0, 128, NJ * 128)).astype(np.float32)
    w3h = np.ascontiguousarray(
        W3.reshape(NC0, 128, NJ, 128).transpose(0, 3, 2, 1).reshape(NC0, 128, NJ * 128)).astype(np.float32)
    w2r = np.ascontiguousarray(W2.reshape(D, 64, 128).transpose(1, 2, 0)).astype(np.float32)

    def had(k):
        Hm = np.array([[1.0]], np.float32)
        while Hm.shape[0] < k:
            Hm = np.block([[Hm, Hm], [Hm, -Hm]])
        return Hm
    h64 = had(64).astype(BF)
    h128 = had(128).astype(BF)
    return dict(w1h=w1h, w3h=w3h, w2r=w2r, h64=h64, h128=h128)





def build_C(S=SEQ, dbg=False):
    p = P("C")
    nc = p.nc
    NST = S // 512
    SC = float(1.0 / np.sqrt(128.0))
    x3 = p.dram("x3", [NSEQ, NJ, 128, S], F32, kind="ExternalInput")
    wqkv = p.dram("wqkv", [48, 128, NJ * 128], F32, kind="ExternalInput")
    bqkv = p.dram("bqkv", [1, 48 * 128], F32, kind="ExternalInput")
    wout = p.dram("wout", [NJ, 128, H * 128], F32, kind="ExternalInput")
    bout = p.dram("bout", [1, NJ * 128], F32, kind="ExternalInput")
    maskT = p.dram("maskT", [128, 128], F32, kind="ExternalInput")
    maskS = p.dram("maskS", [128, 128], F32, kind="ExternalInput")
    idb = p.dram("idb", [128, 128], BF16, kind="ExternalInput")
    idf = p.dram("idf", [128, 128], F32, kind="ExternalInput")
    x4 = p.dram("x4", [NSEQ, NJ, 128, S], F32, kind="ExternalOutput")
    wqkv_scr = p.dram("wqkv_scr", [48, 128, NJ * 128], BF16)
    wout_scr = p.dram("wout_scr", [NJ, 128, H * 128], BF16)
    sk = "ExternalOutput" if dbg else "Internal"
    qscr = p.dram("qscr", [NSEQ, H, 128, S], BF16, kind=sk)
    kscr = p.dram("kscr", [NSEQ, H, 128, S], BF16, kind=sk)
    vscr = p.dram("vscr", [NSEQ, H, S, 128], BF16, kind=sk)

    for s in ("lx", "pe", "ve", "ac", "w0", "w1", "s0", "s1",
              "k0", "k1", "o0", "o1", "x0", "x1", "q0", "q1"):
        p.sem(s)

    ones_r = p.sb("ones_r", [1, 128], F32)
    ones_c = p.sb("ones_c", [128, 1], BF16)
    maskT_sb = p.sb("maskT_sb", [128, 128], F32)
    maskS_sb = p.sb("maskS_sb", [128, 128], F32)
    idb_sb = p.sb("idb_sb", [128, 128], BF16)
    idf_sb = p.sb("idf_sb", [128, 128], F32)
    bqkv_sb = p.sb("bqkv_sb", [128, 48], F32)
    bout_sb = p.sb("bout_sb", [128, NJ], F32)

    with nc.Block() as blk:
        pes = contextlib.ExitStack()
        brow = p.sb("browC", [1, 48 * 128], F32, es=pes)
        borow = p.sb("borowC", [1, NJ * 128], F32, es=pes)
        pbias = p.ps("pbiasC", [128, 512], F32, es=pes)

        @blk.sync
        def _(e):
            for dst, srcap in ((brow[:], bqkv[:, :]), (borow[:], bout[:, :]),
                               (maskT_sb[:], maskT[:, :]), (maskS_sb[:], maskS[:, :]),
                               (idb_sb[:], idb[:, :]), (idf_sb[:], idf[:, :])):
                i = e.dma_start(out=dst, in_=srcap)
                p.inc(i, "lx", 16)

        @blk.vector
        def _(e):
            e.memset(ones_r[:], 1.0)
            i = e.memset(ones_c[:], 1.0)
            p.inc(i, "ve")

        @blk.tensor
        def _(e):
            p.wait(e, ("lx", p.C["lx"]), ("ve", p.C["ve"]))
            for t in range(48):
                i = e.matmul(pbias[:, t:t + 1], brow[0:1, t * 128:(t + 1) * 128],
                             ones_r[0:1, 0:1], start=True, stop=True, skip_group_check=True)
            p.inc(i, "pe")

        @blk.scalar
        def _(e):
            p.wait(e, ("pe", p.C["pe"]))
            i = e.copy(bqkv_sb[:, 0:48], pbias[:, 0:48])
            p.inc(i, "ac")

        @blk.tensor
        def _(e):
            p.wait(e, ("ac", p.C["ac"]))
            for t in range(NJ):
                i = e.matmul(pbias[:, t:t + 1], borow[0:1, t * 128:(t + 1) * 128],
                             ones_r[0:1, 0:1], start=True, stop=True, skip_group_check=True)
            p.inc(i, "pe")

        @blk.scalar
        def _(e):
            p.wait(e, ("pe", p.C["pe"]))
            i = e.copy(bout_sb[:, 0:NJ], pbias[:, 0:NJ])
            p.inc(i, "ac")

        wtmp = [p.sb(f"wtmpC{i}", [128, 2048], F32, es=pes) for i in range(2)]
        wbt = [p.sb(f"wbtC{i}", [128, 2048], BF16, es=pes) for i in range(2)]
        for t in range(48 + NJ):
            src_t = wqkv[t] if t < 48 else wout[t - 48]
            dst_t = wqkv_scr[t] if t < 48 else wout_scr[t - 48]
            buf = wtmp[t % 2]
            bbuf = wbt[t % 2]
            ws = "w%d" % (t % 2)
            ss = "s%d" % (t % 2)

            @blk.gpsimd
            def _(e):
                p.wait(e, ("ve", p.C["ve"] - 1))
                i = e.dma_start(out=buf[:], in_=src_t)
                p.inc(i, ws, 16)

            @blk.vector
            def _(e):
                p.wait(e, (ws, p.C[ws]), (ss, p.C[ss]))
                i = e.tensor_copy(bbuf[:], buf[:])
                p.inc(i, "ve")

            @blk.sync
            def _(e):
                p.wait(e, ("ve", p.C["ve"]))
                i = e.dma_start(out=dst_t, in_=bbuf[:])
                p.inc(i, ss, 16)

        p.barrier(blk)
        pes.close()

        pes = contextlib.ExitStack()
        xstg = p.sb("xstgC", [128, S], F32, es=pes)
        xb = p.sb("xbC", [128, NJ * S], BF16, es=pes)
        wqb = [p.sb(f"wqbC{i}", [128, 2048], BF16, es=pes) for i in range(2)]
        ev = [p.sb(f"evC{i}", [128, 512], BF16, es=pes) for i in range(2)]
        vtv = [p.sb(f"vtvC{i}", [128, 512], BF16, es=pes) for i in range(2)]
        PSq = [p.ps(f"PSqC{i}", [128, 512], F32, es=pes) for i in range(2)]
        PSt = [p.ps(f"PStC{i}", [128, 512], BF16, es=pes) for i in range(2)]

        for sq in range(NSEQ):
            for j in range(NJ):
                @blk.sync
                def _(e):
                    if j > 0 or sq > 0:
                        p.wait(e, ("ve", p.C["ve"]))
                    if sq > 0 and j == 0:
                        e.nop()
                        p.wait(e, ("pe", p.C["pe"]))
                    i = e.dma_start(out=xstg[:], in_=x3[sq, j])
                    p.inc(i, "lx", 16)

                @blk.vector
                def _(e):
                    p.wait(e, ("lx", p.C["lx"]))
                    i = e.tensor_copy(xb[:, j * S:(j + 1) * S], xstg[:])
                    p.inc(i, "ve")
            xcast_ve = p.C["ve"]

            @blk.gpsimd
            def _(e, xcast_ve=xcast_ve):
                p.wait(e, ("ve", xcast_ve))
                i = e.dma_start(out=wqb[0][:], in_=wqkv_scr[0])
                p.inc(i, "q0", 16)

            proj_pe = {}
            use_ve = {}
            for o in range(48):
                par = o % 2
                is_v = o >= 32

                @blk.gpsimd
                def _(e, o=o):
                    if o + 1 < 48:
                        p.wait(e, ("pe", proj_pe.get(o - 1, 0)))
                        i = e.dma_start(out=wqb[(o + 1) % 2][:], in_=wqkv_scr[o + 1])
                        p.inc(i, "q%d" % ((o + 1) % 2), 16)

                for st in range(NST):
                    fidx = o * NST + st
                    stq = fidx % 2

                    @blk.tensor
                    def _(e, o=o, st=st, stq=stq, par=par, fidx=fidx):
                        wv = use_ve.get(divmod(fidx - 2, NST), 0) if fidx >= 2 else 0
                        p.wait(e, ("q%d" % par, p.C["q%d" % par]),
                               ("ve", max(xcast_ve, wv)))
                        for j in range(NJ):
                            i = e.matmul(PSq[stq][:], wqb[par][:, j * 128:(j + 1) * 128],
                                         xb[:, j * S + st * 512: j * S + st * 512 + 512],
                                         start=(j == 0), stop=(j == NJ - 1), skip_group_check=True)
                        p.inc(i, "pe")
                    proj_pe[o] = p.C["pe"]
                    mm_pe = p.C["pe"]

                    @blk.vector
                    def _(e, o=o, st=st, stq=stq, mm_pe=mm_pe):
                        p.wait(e, ("pe", mm_pe), ("s%d" % stq, p.C["s%d" % stq]))
                        i = e.tensor_scalar(ev[stq][:], PSq[stq][:], bqkv_sb[:, o:o + 1], None, ALU.add)
                        p.inc(i, "ve")
                    bias_ve = p.C["ve"]

                    if not is_v:
                        use_ve[(o, st)] = bias_ve

                        @blk.sync
                        def _(e, o=o, st=st, stq=stq, bias_ve=bias_ve):
                            p.wait(e, ("ve", bias_ve))
                            dst = qscr if o < 16 else kscr
                            hh = o if o < 16 else o - 16
                            i = e.dma_start(out=dst[sq, hh, :, st * 512:(st + 1) * 512], in_=ev[stq][:])
                            p.inc(i, "s%d" % stq, 16)
                    else:
                        @blk.tensor
                        def _(e, st=st, stq=stq, bias_ve=bias_ve):
                            p.wait(e, ("ve", bias_ve))
                            for u in range(4):
                                i = e.transpose(PSt[stq][:, u * 128:(u + 1) * 128],
                                                ev[stq][:, u * 128:(u + 1) * 128], idb_sb[:])
                            p.inc(i, "pe")
                        tr_pe = p.C["pe"]

                        @blk.vector
                        def _(e, stq=stq, tr_pe=tr_pe):
                            p.wait(e, ("pe", tr_pe))
                            i = e.tensor_copy(vtv[stq][:], PSt[stq][:])
                            p.inc(i, "ve")
                        use_ve[(o, st)] = p.C["ve"]

                        @blk.sync
                        def _(e, o=o, st=st, stq=stq):
                            p.wait(e, ("ve", use_ve[(o, st)]))
                            hh = o - 32
                            dst = vscr[sq, hh, st * 512:(st + 1) * 512, :].rearrange(
                                "(u s) e -> s u e", u=4)
                            i = e.dma_start(out=dst, in_=vtv[stq][:].rearrange("s (u e) -> s u e", u=4))
                            p.inc(i, "s%d" % stq, 16)

        p.barrier(blk)
        pes.close()

        pes = contextlib.ExitStack()
        qt_b = [p.sb(f"qtC{i}", [128, 512], BF16, es=pes) for i in range(2)]
        kt_b = [p.sb(f"ktC{i}", [128, S], BF16, es=pes) for i in range(2)]
        vt_b = [p.sb(f"vtC{i}", [128, S], BF16, es=pes) for i in range(2)]
        ET = p.sb("ETC", [128, (S // 128) * 512], BF16, es=pes)
        tmpcol = p.sb("tmpcolC", [128, 1], F32, es=pes)
        etmp = [p.sb(f"etmpC{i}", [128, 512], F32, es=pes) for i in range(2)]
        nmcols = p.sb("nmcolsC", [128, 4], F32, es=pes)
        rrow = p.sb("rrowC", [1, 1024], F32, es=pes)
        negB = p.sb("negBC", [128, 512], F32, es=pes)
        recB = p.sb("recBC", [128, 512], F32, es=pes)
        rmax = p.sb("rmaxC", [128, 1], F32, es=pes)
        ostrip = p.sb("ostripC", [128, H * 512], BF16, es=pes)
        wob = [p.sb(f"wobC{i}", [128, 2048], BF16, es=pes) for i in range(2)]
        resb = [p.sb(f"resbC{i}", [128, 512], F32, es=pes) for i in range(2)]
        y1 = p.sb("y1C", [128, NJ * 512], F32, es=pes)
        y1b = p.sb("y1bC", [128, 512], BF16, es=pes)
        y1s = p.sb("y1sC", [128, 512], BF16, es=pes)
        srow = p.sb("srowC", [1, 1536], F32, es=pes)
        rstdB = p.sb("rstdBC", [128, 512], F32, es=pes)
        mrB = p.sb("mrBC", [128, 512], F32, es=pes)
        x4t = [p.sb(f"x4tC{i}", [128, 512], F32, es=pes) for i in range(2)]
        PS = [p.ps(f"PSC{i}", [128, 512], F32, es=pes) for i in range(6)]
        PTR = p.ps("PTRC", [128, 512], F32, es=pes)

        for sq in range(NSEQ):
            for qb in range(NST):
                nkt = (qb + 1) * 4
                kv_ve_prev = p.C["ve"]
                kv_done = {}

                @blk.gpsimd
                def _(e, sq=sq, qb=qb, nkt=nkt, kv_ve_prev=kv_ve_prev):
                    p.wait(e, ("pe", p.C["pe"]))
                    for hp in range(2):
                        ks = "k%d" % hp
                        i = e.dma_start(out=qt_b[hp][:], in_=qscr[sq, hp, :, qb * 512:(qb + 1) * 512])
                        p.inc(i, ks, 16)
                        i = e.dma_start(out=kt_b[hp][:, 0:nkt * 128], in_=kscr[sq, hp, :, 0:nkt * 128])
                        p.inc(i, ks, 16)
                        i = e.dma_start(out=vt_b[hp][:, 0:nkt * 128],
                                        in_=vscr[sq, hp, 0:nkt * 128, :].rearrange("(u s) e -> s u e", s=128))
                        kv_done[hp] = p.inc(i, ks, 16)

                wo_lw = {}

                @blk.sync
                def _(e, qb=qb):
                    p.wait(e, ("ve", kv_ve_prev))
                    for dmp in range(2):
                        i = e.dma_start(out=wob[dmp][:], in_=wout_scr[dmp])
                        wo_lw[dmp] = p.inc(i, "o%d" % dmp, 16)
                        i = e.dma_start(out=resb[dmp][:], in_=x3[sq, dmp, :, qb * 512:(qb + 1) * 512])
                        p.C["o%d" % dmp] += 16
                        i.then_inc(p.sems["o%d" % dmp], 16)
                        wo_lw[dmp] = p.C["o%d" % dmp]

                ostrip_ve = {}
                for h in range(H):
                    hp = h % 2

                    qb_start_ve = p.C["ve"]
                    blk_ve = []
                    for qt in range(4):
                        nkb = qb + 1
                        for kb in range(nkb):
                            qtkb = qt * nkb + kb
                            bank = PS[qtkb % 2]

                            @blk.tensor
                            def _(e, h=h, hp=hp, qt=qt, kb=kb, qtkb=qtkb, bank=bank):
                                if qtkb == 0:
                                    p.wait(e, ("k%d" % hp, kv_done[h]))
                                    e.nop()
                                    if h >= 2:
                                        p.wait(e, ("ve", ostrip_ve[h - 2]))
                                elif qtkb >= 2:
                                    p.wait(e, ("ve", blk_ve[qtkb - 2]))
                                i = e.matmul(bank[:], qt_b[hp][:, qt * 128:(qt + 1) * 128],
                                             kt_b[hp][:, kb * 512:(kb + 1) * 512],
                                             start=True, stop=True, skip_group_check=True)
                                p.inc(i, "pe")
                            blk_pe = p.C["pe"]

                            @blk.vector
                            def _(e, qt=qt, kb=kb, qb=qb, bank=bank, blk_pe=blk_pe):
                                p.wait(e, ("pe", blk_pe))
                                if kb == qb:
                                    e.tensor_tensor(bank[:, qt * 128:(qt + 1) * 128],
                                                    bank[:, qt * 128:(qt + 1) * 128],
                                                    maskS_sb[:], ALU.add)
                                    rng_hi = qt * 128 + 128
                                else:
                                    rng_hi = 512
                                if kb == 0:
                                    i = e.reduce_max(rmax[:], bank[:, 0:rng_hi], axis=AX.X)
                                else:
                                    e.reduce_max(tmpcol[:], bank[:, 0:rng_hi], axis=AX.X)
                                    i = e.tensor_tensor(rmax[:], rmax[:], tmpcol[:], ALU.max)
                                if kb == qb:
                                    i = e.tensor_scalar(nmcols[:, qt:qt + 1], rmax[:], -SC, None, ALU.mult)
                                p.inc(i, "ve")
                            blk_ve.append(p.C["ve"])
                    mxv_ve = p.C["ve"]

                    @blk.tensor
                    def _(e, mxv_ve=mxv_ve):
                        p.wait(e, ("ve", mxv_ve))
                        for qt in range(4):
                            i = e.transpose(PTR[0:1, qt * 128:(qt + 1) * 128],
                                            nmcols[:, qt:qt + 1], idf_sb[:])
                        p.inc(i, "pe")
                    trm_pe = p.C["pe"]

                    @blk.scalar
                    def _(e, trm_pe=trm_pe):
                        p.wait(e, ("pe", trm_pe))
                        i = e.copy(rrow[0:1, 0:512], PTR[0:1, 0:512])
                        p.inc(i, "ac")
                    row_ac = p.C["ac"]

                    @blk.tensor
                    def _(e, row_ac=row_ac):
                        p.wait(e, ("ac", row_ac))
                        i = e.matmul(PS[4][:], ones_r[:], rrow[0:1, 0:512], start=True, stop=True,
                                     skip_group_check=True)
                        p.inc(i, "pe")
                    nb_pe = p.C["pe"]

                    @blk.scalar
                    def _(e, nb_pe=nb_pe):
                        p.wait(e, ("pe", nb_pe))
                        i = e.copy(negB[:], PS[4][:])
                        p.inc(i, "ac")
                    negB_ac = p.C["ac"]

                    exp_ac = {}
                    for kt in range(nkt):
                        ktq = kt % 2
                        ktx = kt - qb * 4

                        @blk.tensor
                        def _(e, h=h, hp=hp, kt=kt, ktq=ktq, negB_ac=negB_ac):
                            if kt == 0:
                                p.wait(e, ("ac", negB_ac))
                            elif kt >= 2:
                                p.wait(e, ("ac", exp_ac[kt - 2]))
                            i = e.matmul(PS[2 + ktq][:], kt_b[hp][:, kt * 128:(kt + 1) * 128],
                                         qt_b[hp][:], start=True, stop=True, skip_group_check=True)
                            p.inc(i, "pe")
                        st_pe = p.C["pe"]

                        @blk.vector
                        def _(e, kt=kt, ktq=ktq, ktx=ktx, st_pe=st_pe):
                            et = etmp[ktq]
                            p.wait(e, ("pe", st_pe))
                            if kt >= 2:
                                e.nop()
                                p.wait(e, ("ac", exp_ac[kt - 2]))
                            e.tensor_scalar(et[:], PS[2 + ktq][:], SC, None, ALU.mult)
                            i = e.tensor_tensor(et[:], et[:], negB[:], ALU.add)
                            if ktx >= 0:
                                i = e.tensor_tensor(et[:, ktx * 128:(ktx + 1) * 128],
                                                    et[:, ktx * 128:(ktx + 1) * 128],
                                                    maskT_sb[:], ALU.add)
                                if ktx > 0:
                                    i = e.memset(et[:, 0:ktx * 128], -1e30)
                            p.inc(i, "ve")
                        sh_ve = p.C["ve"]

                        @blk.scalar
                        def _(e, kt=kt, ktq=ktq, sh_ve=sh_ve):
                            p.wait(e, ("ve", sh_ve))
                            i = e.activation(ET[:, kt * 512:(kt + 1) * 512], etmp[ktq][:], AF.Exp)
                            p.inc(i, "ac")
                        exp_ac[kt] = p.C["ac"]

                    @blk.tensor
                    def _(e, h=h, hp=hp, nkt=nkt):
                        p.wait(e, ("ac", exp_ac[nkt - 1]))
                        for kt in range(nkt):
                            i = e.matmul(PS[5][0:1, :], ones_c[:], ET[:, kt * 512:(kt + 1) * 512],
                                         start=(kt == 0), stop=(kt == nkt - 1), skip_group_check=True)
                        for kt in range(nkt):
                            i = e.matmul(PS[4][:], vt_b[hp][:, kt * 128:(kt + 1) * 128],
                                         ET[:, kt * 512:(kt + 1) * 512],
                                         start=(kt == 0), stop=(kt == nkt - 1), skip_group_check=True)
                        p.inc(i, "pe")
                    den_pe = p.C["pe"]

                    @blk.vector
                    def _(e, den_pe=den_pe):
                        p.wait(e, ("pe", den_pe))
                        e.tensor_scalar(rrow[0:1, 512:1024], PS[5][0:1, :], 1e-30, None, ALU.add)
                        i = e.reciprocal(rrow[0:1, 512:1024], rrow[0:1, 512:1024])
                        p.inc(i, "ve")
                    rec_ve = p.C["ve"]

                    @blk.tensor
                    def _(e, rec_ve=rec_ve):
                        p.wait(e, ("ve", rec_ve))
                        i = e.matmul(PTR[:, 0:512], ones_r[:], rrow[0:1, 512:1024],
                                     start=True, stop=True, skip_group_check=True)
                        p.inc(i, "pe")
                    rb_pe = p.C["pe"]

                    @blk.scalar
                    def _(e, rb_pe=rb_pe):
                        p.wait(e, ("pe", rb_pe))
                        i = e.copy(recB[:], PTR[:, 0:512])
                        p.inc(i, "ac")
                    rb_ac = p.C["ac"]

                    @blk.vector
                    def _(e, h=h, rb_ac=rb_ac):
                        p.wait(e, ("ac", rb_ac))
                        i = e.tensor_tensor(ostrip[:, h * 512:(h + 1) * 512], PS[4][:], recB[:], ALU.mult)
                        p.inc(i, "ve")
                    ostrip_ve[h] = p.C["ve"]

                    if h + 2 < H:
                        @blk.gpsimd
                        def _(e, sq=sq, qb=qb, h=h, hp=hp, nkt=nkt, den_pe=den_pe):
                            p.wait(e, ("pe", den_pe))
                            ks = "k%d" % hp
                            i = e.dma_start(out=qt_b[hp][:], in_=qscr[sq, h + 2, :, qb * 512:(qb + 1) * 512])
                            p.inc(i, ks, 16)
                            i = e.dma_start(out=kt_b[hp][:, 0:nkt * 128], in_=kscr[sq, h + 2, :, 0:nkt * 128])
                            p.inc(i, ks, 16)
                            i = e.dma_start(out=vt_b[hp][:, 0:nkt * 128],
                                            in_=vscr[sq, h + 2, 0:nkt * 128, :].rearrange("(u s) e -> s u e", s=128))
                            kv_done[h + 2] = p.inc(i, ks, 16)

                attn_done_ve = p.C["ve"]
                y1_ve = {}
                sqr_ac = {}
                wo_pe = {}

                def wo_mm_tensor(dm):
                    po = PS[dm % 2]

                    @blk.tensor
                    def _(e, dm=dm, po=po):
                        p.wait(e, ("o%d" % (dm % 2), wo_lw[dm]))
                        e.nop()
                        if dm == 0:
                            p.wait(e, ("ve", attn_done_ve))
                        elif dm >= 2:
                            p.wait(e, ("ve", y1_ve[dm - 2]))
                        for hh in range(H):
                            i = e.matmul(po[:], wob[dm % 2][:, hh * 128:(hh + 1) * 128],
                                         ostrip[:, hh * 512:(hh + 1) * 512],
                                         start=(hh == 0), stop=(hh == H - 1), skip_group_check=True)
                        p.inc(i, "pe")
                    wo_pe[dm] = p.C["pe"]

                def stats_mm(dm):
                    @blk.tensor
                    def _(e, dm=dm):
                        p.wait(e, ("ac", sqr_ac[dm]))
                        e.matmul(PS[4][0:1, :], ones_c[:], y1b[:],
                                 start=(dm == 0), stop=(dm == NJ - 1), skip_group_check=True)
                        i = e.matmul(PS[5][0:1, :], ones_c[:], y1s[:],
                                     start=(dm == 0), stop=(dm == NJ - 1), skip_group_check=True)
                        p.inc(i, "pe")

                def wo_evac(dm):
                    po = PS[dm % 2]
                    pe_snap = p.C["pe"]

                    @blk.vector
                    def _(e, dm=dm, po=po, pe_snap=pe_snap):
                        p.wait(e, ("pe", pe_snap))
                        e.tensor_scalar(y1[:, dm * 512:(dm + 1) * 512], po[:],
                                        bout_sb[:, dm:dm + 1], None, ALU.add)
                        e.tensor_tensor(y1[:, dm * 512:(dm + 1) * 512],
                                        y1[:, dm * 512:(dm + 1) * 512], resb[dm % 2][:], ALU.add)
                        i = e.tensor_copy(y1b[:], y1[:, dm * 512:(dm + 1) * 512])
                        p.inc(i, "ve")
                    y1_ve[dm] = p.C["ve"]

                    @blk.scalar
                    def _(e, dm=dm):
                        p.wait(e, ("ve", y1_ve[dm]))
                        i = e.activation(y1s[:], y1[:, dm * 512:(dm + 1) * 512], AF.Square)
                        p.inc(i, "ac")
                    sqr_ac[dm] = p.C["ac"]

                def prefetch_wo(dm):
                    @blk.sync
                    def _(e, dm=dm, sq=sq, qb=qb):
                        p.wait(e, ("pe", wo_pe[dm]), ("ve", y1_ve[dm]))
                        i = e.dma_start(out=wob[dm % 2][:], in_=wout_scr[dm + 2])
                        p.C["o%d" % (dm % 2)] += 16
                        i.then_inc(p.sems["o%d" % (dm % 2)], 16)
                        i = e.dma_start(out=resb[dm % 2][:], in_=x3[sq, dm + 2, :, qb * 512:(qb + 1) * 512])
                        wo_lw[dm + 2] = p.inc(i, "o%d" % (dm % 2), 16)

                for dm in range(NJ):
                    wo_mm_tensor(dm)
                    if dm > 0:
                        stats_mm(dm - 1)
                    wo_evac(dm)
                    if dm + 2 < NJ:
                        prefetch_wo(dm)
                stats_mm(NJ - 1)

                @blk.vector
                def _(e):
                    p.wait(e, ("pe", p.C["pe"]))
                    e.tensor_scalar(srow[0:1, 0:512], PS[4][0:1, :], 1.0 / D, None, ALU.mult)
                    e.tensor_scalar(srow[0:1, 512:1024], PS[5][0:1, :], 1.0 / D, None, ALU.mult)
                    e.tensor_tensor(srow[0:1, 1024:1536], srow[0:1, 0:512], srow[0:1, 0:512], ALU.mult)
                    e.tensor_tensor(srow[0:1, 512:1024], srow[0:1, 512:1024], srow[0:1, 1024:1536], ALU.subtract)
                    i = e.tensor_scalar(srow[0:1, 512:1024], srow[0:1, 512:1024], LN_EPS, None, ALU.add)
                    p.inc(i, "ve")

                @blk.scalar
                def _(e):
                    p.wait(e, ("ve", p.C["ve"]))
                    i = e.activation(srow[0:1, 512:1024], srow[0:1, 512:1024], AF.Sqrt)
                    p.inc(i, "ac")

                @blk.vector
                def _(e):
                    p.wait(e, ("ac", p.C["ac"]))
                    e.reciprocal(srow[0:1, 512:1024], srow[0:1, 512:1024])
                    i = e.tensor_tensor(srow[0:1, 1024:1536], srow[0:1, 0:512], srow[0:1, 512:1024], ALU.mult)
                    p.inc(i, "ve")

                @blk.tensor
                def _(e):
                    p.wait(e, ("ve", p.C["ve"]))
                    e.matmul(PS[2][:], ones_r[:], srow[0:1, 512:1024], start=True, stop=True,
                             skip_group_check=True)
                    i = e.matmul(PS[3][:], ones_r[:], srow[0:1, 1024:1536], start=True, stop=True,
                                 skip_group_check=True)
                    p.inc(i, "pe")

                @blk.scalar
                def _(e):
                    p.wait(e, ("pe", p.C["pe"]))
                    e.copy(rstdB[:], PS[2][:])
                    i = e.copy(mrB[:], PS[3][:])
                    p.inc(i, "ac")
                ln_ac = p.C["ac"]

                for dm in range(NJ):
                    xo = x4t[dm % 2]
                    xs = "x%d" % (dm % 2)

                    @blk.vector
                    def _(e, dm=dm, xo=xo, xs=xs, ln_ac=ln_ac):
                        p.wait(e, ("ac", ln_ac), (xs, p.C[xs]))
                        e.tensor_tensor(xo[:], y1[:, dm * 512:(dm + 1) * 512], rstdB[:], ALU.mult)
                        i = e.tensor_tensor(xo[:], xo[:], mrB[:], ALU.subtract)
                        p.inc(i, "ve")

                    @blk.gpsimd
                    def _(e, dm=dm, sq=sq, qb=qb, xo=xo, xs=xs):
                        p.wait(e, ("ve", p.C["ve"]))
                        i = e.dma_start(out=x4[sq, dm, :, qb * 512:(qb + 1) * 512], in_=xo[:])
                        p.inc(i, xs, 16)

        @blk.gpsimd
        def _(e):
            p.wait(e, ("x0", p.C["x0"]))
            e.nop()
            e.wait_ge(p.sems["x1"], p.C["x1"])

    return p


def prep_C_weights(W_qkv, b_qkv, W_out, b_out):
    wqkv = np.ascontiguousarray(
        W_qkv.reshape(48, 128, NJ, 128).transpose(0, 3, 2, 1).reshape(48, 128, NJ * 128))
    bq_r = np.ascontiguousarray(b_qkv.reshape(1, 48 * 128))
    wo_t = np.ascontiguousarray(
        W_out.reshape(NJ, 128, H, 128).transpose(0, 3, 2, 1).reshape(NJ, 128, H * 128))
    bo_r = np.ascontiguousarray(b_out.reshape(1, NJ * 128))
    kk, qq = np.meshgrid(np.arange(128), np.arange(128), indexing="ij")
    maskT = np.where(kk > qq, -1e30, 0.0).astype(np.float32)
    maskS = maskT.T.copy()
    idb = np.eye(128, dtype=BF)
    idf = np.eye(128, dtype=np.float32)
    return dict(wqkv=wqkv, bqkv=bq_r, wout=wo_t, bout=bo_r,
                maskT=maskT, maskS=maskS, idb=idb, idf=idf)





NC1 = 43


def build_D(S=SEQ):
    p = P("D")
    nc = p.nc
    T = NSEQ * S
    NTS = T // 512
    x4 = p.dram("x4", [NSEQ, NJ, 128, S], F32, kind="ExternalInput")
    gidx = p.dram("gidx", [NC1, 128, NJ * 128], I32, kind="ExternalInput")
    uidx = p.dram("uidx", [NC1, 128, NJ * 128], I32, kind="ExternalInput")
    didx = p.dram("didx", [NJ, 128, NC1 * 128], I32, kind="ExternalInput")
    lut = p.dram("lut", [1, 16], F32, kind="ExternalInput")
    yout = p.dram("yout", [NSEQ, NJ, 128, S], F32, kind="ExternalOutput")
    wg_scr = p.dram("wg_scr", [NC1, 128, NJ * 128], BF16)
    wu_scr = p.dram("wu_scr", [NC1, 128, NJ * 128], BF16)
    wd_scr = p.dram("wd_scr", [NJ, 128, NC1 * 128], BF16)
    hscr = p.dram("hscr", [NC1, 128, T], BF16)

    for s in ("lx", "lw", "pe", "ve", "ac", "po",
              "w0", "w1", "s0", "s1", "a0", "a1", "d0", "d1", "x0", "x1",
              "i0", "i1", "i2", "i3", "m0", "m1", "m2", "m3"):
        p.sem(s)

    ones_r = p.sb("ones_r", [1, 128], F32)
    ones_c = p.sb("ones_c", [128, 1], BF16)
    lut_row = p.sb("lut_row", [1, 16], F32)
    lutB = p.sb("lutB", [128, 16], F32)

    with nc.Block() as blk:
        pes = contextlib.ExitStack()
        pb = p.ps("pbD", [128, 512], F32, es=pes)

        @blk.sync
        def _(e):
            i = e.dma_start(out=lut_row[:], in_=lut[:, :])
            p.inc(i, "lx", 16)

        @blk.vector
        def _(e):
            e.memset(ones_r[:], 1.0)
            i = e.memset(ones_c[:], 1.0)
            p.inc(i, "ve")

        @blk.tensor
        def _(e):
            p.wait(e, ("lx", 16), ("ve", 1))
            i = e.matmul(pb[:, 0:16], ones_r[:], lut_row[:], start=True, stop=True,
                         skip_group_check=True)
            p.inc(i, "pe")

        @blk.scalar
        def _(e):
            p.wait(e, ("pe", p.C["pe"]))
            i = e.copy(lutB[:], pb[:, 0:16])
            p.inc(i, "ac")
        lut_ac = p.C["ac"]

        itile = [p.sb(f"itD{i}", [128, NC1 * 128], I32, es=pes) for i in range(2)]
        btile = [p.sb(f"btD{i}", [128, NC1 * 128], BF16, es=pes) for i in range(2)]
        atile = [p.sb(f"atD{i}", [128, NC1 * 128], BF16, es=pes) for i in range(2)]
        ttile = p.sb("ttD", [128, NC1 * 128], BF16, es=pes)

        jobs = []
        for c in range(NC1):
            jobs.append((gidx[c], wg_scr[c], NJ * 128))
            jobs.append((uidx[c], wu_scr[c], NJ * 128))
        for dm in range(NJ):
            jobs.append((didx[dm], wd_scr[dm], NC1 * 128))

        cast_ve = {}
        done_ve = {}
        store_ms = {}
        for jn, (iap, oap, ncols) in enumerate(jobs):
            par = jn % 2
            ibuf, bbuf, abuf = itile[par], btile[par], atile[par]
            isem = "i%d" % par
            msem = "m%d" % par
            dmaq = blk.sync if par == 0 else blk.scalar

            @dmaq
            def _(e, jn=jn, iap=iap, ibuf=ibuf, isem=isem, ncols=ncols):
                if jn >= 2:
                    p.wait(e, ("ve", cast_ve[jn - 2]))
                i = e.dma_start(out=ibuf[:, 0:ncols], in_=iap)
                p.inc(i, isem, 16)
            load_cnt = p.C[isem]

            @blk.vector
            def _(e, jn=jn, ibuf=ibuf, bbuf=bbuf, abuf=abuf, isem=isem,
                  msem=msem, ncols=ncols, load_cnt=load_cnt):
                p.wait(e, (isem, load_cnt), ("ac", lut_ac))
                i = e.tensor_copy(bbuf[:, 0:ncols], ibuf[:, 0:ncols])
                p.inc(i, "ve")
                cast_ve[jn] = p.C["ve"]
                if jn >= 2:
                    e.nop()
                    p.wait(e, (msem, store_ms[jn - 2]))
                e.tensor_scalar(abuf[:, 0:ncols], bbuf[:, 0:ncols], 0.0, lutB[:, 0:1],
                                ALU.is_equal, ALU.mult)
                for j in range(1, 16):
                    e.tensor_scalar(ttile[:, 0:ncols], bbuf[:, 0:ncols], float(j),
                                    lutB[:, j:j + 1], ALU.is_equal, ALU.mult)
                    i = e.tensor_tensor(abuf[:, 0:ncols], abuf[:, 0:ncols], ttile[:, 0:ncols],
                                        ALU.add)
                p.inc(i, "ve")
                done_ve[jn] = p.C["ve"]

            @blk.gpsimd
            def _(e, jn=jn, oap=oap, abuf=abuf, msem=msem, ncols=ncols):
                p.wait(e, ("ve", done_ve[jn]))
                i = e.dma_start(out=oap, in_=abuf[:, 0:ncols])
                p.inc(i, msem, 16)
            store_ms[jn] = p.C[msem]

        p.barrier(blk)
        pes.close()

        pes = contextlib.ExitStack()
        xb = p.sb("xbD", [128, NJ * T], BF16, es=pes)
        xstg = p.sb("xstgD", [128, T], F32, es=pes)
        wb = [p.sb(f"wbD{i}", [128, 2 * NJ * 128], BF16, es=pes) for i in range(2)]
        stmp = [p.sb(f"stmpD{i}", [128, 512], F32, es=pes) for i in range(2)]
        hsub = [p.sb(f"hsubD{i}", [128, 512], BF16, es=pes) for i in range(2)]
        pG = [p.ps(f"pGD{i}", [128, 512], F32, es=pes) for i in range(2)]
        pU = [p.ps(f"pUD{i}", [128, 512], F32, es=pes) for i in range(2)]

        for j in range(NJ):
            @blk.sync
            def _(e, j=j):
                if j > 0:
                    p.wait(e, ("ve", p.C["ve"]))
                for sq in range(NSEQ):
                    i = e.dma_start(out=xstg[:, sq * S:(sq + 1) * S], in_=x4[sq, j])
                    p.inc(i, "lx", 16)

            @blk.vector
            def _(e, j=j):
                p.wait(e, ("lx", p.C["lx"]))
                i = e.tensor_copy(xb[:, j * T:(j + 1) * T], xstg[:])
                p.inc(i, "ve")
        xcast_ve = p.C["ve"]

        @blk.gpsimd
        def _(e):
            p.wait(e, ("ve", xcast_ve))
            for c in range(2):
                i = e.dma_start(out=wb[c][:, 0:NJ * 128], in_=wg_scr[c])
                p.inc(i, "w%d" % c, 16)
                i = e.dma_start(out=wb[c][:, NJ * 128:], in_=wu_scr[c])
                p.inc(i, "w%d" % c, 16)

        gu_pe = {}
        hmul_ve = {}
        hwr_sa = {}
        for c in range(NC1):
            par = c % 2

            for ts in range(NTS):
                g_idx = c * NTS + ts
                q = g_idx % 2

                @blk.tensor
                def _(e, c=c, ts=ts, par=par, q=q, g_idx=g_idx):
                    p.wait(e, ("w%d" % par, p.C["w%d" % par]))
                    e.nop()
                    if g_idx >= 2:
                        p.wait(e, ("ve", hmul_ve[divmod(g_idx - 2, NTS)]))
                    elif g_idx == 0:
                        p.wait(e, ("ve", xcast_ve))
                    for j in range(NJ):
                        e.matmul(pG[q][:], wb[par][:, j * 128:(j + 1) * 128],
                                 xb[:, j * T + ts * 512: j * T + ts * 512 + 512],
                                 start=(j == 0), stop=(j == NJ - 1), skip_group_check=True)
                    for j in range(NJ):
                        i = e.matmul(pU[q][:], wb[par][:, NJ * 128 + j * 128:NJ * 128 + (j + 1) * 128],
                                     xb[:, j * T + ts * 512: j * T + ts * 512 + 512],
                                     start=(j == 0), stop=(j == NJ - 1), skip_group_check=True)
                    p.inc(i, "pe")
                gu_pe[(c, ts)] = p.C["pe"]

                @blk.scalar
                def _(e, c=c, ts=ts, q=q, g_idx=g_idx):
                    p.wait(e, ("pe", gu_pe[(c, ts)]))
                    if g_idx >= 2:
                        e.nop()
                        p.wait(e, ("ve", hmul_ve[divmod(g_idx - 2, NTS)]))
                    i = e.activation(stmp[q][:], pG[q][:], AF.Silu)
                    p.inc(i, "ac")
                silu_ac = p.C["ac"]

                @blk.vector
                def _(e, c=c, ts=ts, q=q, g_idx=g_idx, silu_ac=silu_ac):
                    p.wait(e, ("ac", silu_ac))
                    if g_idx >= 2:
                        e.nop()
                        p.wait(e, ("a%d" % q, hwr_sa[divmod(g_idx - 2, NTS)]))
                    i = e.tensor_tensor(hsub[q][:], stmp[q][:], pU[q][:], ALU.mult)
                    p.inc(i, "ve")
                hmul_ve[(c, ts)] = p.C["ve"]

                @blk.sync
                def _(e, c=c, ts=ts, q=q):
                    p.wait(e, ("ve", hmul_ve[(c, ts)]))
                    i = e.dma_start(out=hscr[c, :, ts * 512:(ts + 1) * 512], in_=hsub[q][:])
                    p.inc(i, "a%d" % q, 16)
                hwr_sa[(c, ts)] = p.C["a%d" % q]

            @blk.gpsimd
            def _(e, c=c, par=par):
                if c + 2 < NC1:
                    p.wait(e, ("pe", gu_pe[(c, NTS - 1)]))
                    i = e.dma_start(out=wb[par][:, 0:NJ * 128], in_=wg_scr[c + 2])
                    p.inc(i, "w%d" % par, 16)
                    i = e.dma_start(out=wb[par][:, NJ * 128:], in_=wu_scr[c + 2])
                    p.inc(i, "w%d" % par, 16)

        p.barrier(blk)
        pes.close()

        pes = contextlib.ExitStack()
        TT = 512
        NTT = T // TT
        acol = p.sb("acolD", [128, NC1 * TT], BF16, es=pes)
        wdt = [p.sb(f"wdtD{i}", [128, NC1 * 128], BF16, es=pes) for i in range(2)]
        res = [p.sb(f"resD{i}", [128, TT], F32, es=pes) for i in range(2)]
        y2 = p.sb("y2D", [128, NJ * TT], F32, es=pes)
        y2b = p.sb("y2bD", [128, TT], BF16, es=pes)
        y2s = p.sb("y2sD", [128, TT], BF16, es=pes)
        srow = p.sb("srowD", [1, 1536], F32, es=pes)
        rstdB = p.sb("rstdBD", [128, 512], F32, es=pes)
        mrB = p.sb("mrBD", [128, 512], F32, es=pes)
        yt = [p.sb(f"ytD{i}", [128, 512], F32, es=pes) for i in range(2)]
        PS = [p.ps(f"PSD{i}", [128, 512], F32, es=pes) for i in range(6)]

        for tt in range(NTT):
            sq, trs = divmod(tt * TT, S)
            pe_prev_tile = p.C["pe"]

            @blk.sync
            def _(e, tt=tt):
                if tt > 0:
                    p.wait(e, ("pe", pe_prev_tile))
                for a in range(NC1):
                    i = e.dma_start(out=acol[:, a * TT:(a + 1) * TT],
                                    in_=hscr[a, :, tt * TT:(tt + 1) * TT])
                    p.inc(i, "lx", 16)
            acol_lx = p.C["lx"]

            wd_lw = {}

            @blk.gpsimd
            def _(e, tt=tt, sq=sq, trs=trs):
                if tt > 0:
                    p.wait(e, ("pe", pe_prev_tile))
                for dmp in range(2):
                    i = e.dma_start(out=wdt[dmp][:], in_=wd_scr[dmp])
                    i2 = e.dma_start(out=res[dmp][:], in_=x4[sq, dmp, :, trs:trs + TT])
                    i.then_inc(p.sems["d%d" % dmp], 16)
                    p.C["d%d" % dmp] += 16
                    wd_lw[dmp] = p.inc(i2, "d%d" % dmp, 16)

            y2_ve = {}
            sqr_ac = {}
            dmm_pe = {}

            def down_mm(dm):
                po = PS[dm % 2]

                @blk.tensor
                def _(e, dm=dm, po=po):
                    p.wait(e, ("d%d" % (dm % 2), wd_lw[dm]))
                    e.nop()
                    if dm == 0:
                        p.wait(e, ("lx", acol_lx))
                    elif dm >= 2:
                        p.wait(e, ("ve", y2_ve[dm - 2]))
                    for a in range(NC1):
                        i = e.matmul(po[:], wdt[dm % 2][:, a * 128:(a + 1) * 128],
                                     acol[:, a * TT:(a + 1) * TT],
                                     start=(a == 0), stop=(a == NC1 - 1), skip_group_check=True)
                    p.inc(i, "pe")
                dmm_pe[dm] = p.C["pe"]

            def stats_mm(dm):
                @blk.tensor
                def _(e, dm=dm):
                    p.wait(e, ("ac", sqr_ac[dm]))
                    e.matmul(PS[4][0:1, :], ones_c[:], y2b[:],
                             start=(dm == 0), stop=(dm == NJ - 1), skip_group_check=True)
                    i = e.matmul(PS[5][0:1, :], ones_c[:], y2s[:],
                                 start=(dm == 0), stop=(dm == NJ - 1), skip_group_check=True)
                    p.inc(i, "pe")

            def down_evac(dm):
                po = PS[dm % 2]
                pe_snap = p.C["pe"]

                @blk.vector
                def _(e, dm=dm, po=po, pe_snap=pe_snap):
                    p.wait(e, ("d%d" % (dm % 2), wd_lw[dm]))
                    e.nop()
                    p.wait(e, ("pe", pe_snap))
                    e.tensor_tensor(y2[:, dm * TT:(dm + 1) * TT], po[:], res[dm % 2][:], ALU.add)
                    i = e.tensor_copy(y2b[:], y2[:, dm * TT:(dm + 1) * TT])
                    p.inc(i, "ve")
                y2_ve[dm] = p.C["ve"]

                @blk.scalar
                def _(e, dm=dm):
                    p.wait(e, ("ve", y2_ve[dm]))
                    i = e.activation(y2s[:], y2[:, dm * TT:(dm + 1) * TT], AF.Square)
                    p.inc(i, "ac")
                sqr_ac[dm] = p.C["ac"]

            def prefetch_next(dm):
                @blk.gpsimd
                def _(e, dm=dm, sq=sq, trs=trs):
                    p.wait(e, ("pe", dmm_pe[dm]), ("ve", y2_ve[dm]))
                    i = e.dma_start(out=wdt[dm % 2][:], in_=wd_scr[dm + 2])
                    i2 = e.dma_start(out=res[dm % 2][:], in_=x4[sq, dm + 2, :, trs:trs + TT])
                    i.then_inc(p.sems["d%d" % (dm % 2)], 16)
                    p.C["d%d" % (dm % 2)] += 16
                    wd_lw[dm + 2] = p.inc(i2, "d%d" % (dm % 2), 16)

            for dm in range(NJ):
                down_mm(dm)
                if dm > 0:
                    stats_mm(dm - 1)
                down_evac(dm)
                if dm + 2 < NJ:
                    prefetch_next(dm)
            stats_mm(NJ - 1)

            @blk.vector
            def _(e):
                p.wait(e, ("pe", p.C["pe"]))
                e.tensor_scalar(srow[0:1, 0:512], PS[4][0:1, :], 1.0 / D, None, ALU.mult)
                e.tensor_scalar(srow[0:1, 512:1024], PS[5][0:1, :], 1.0 / D, None, ALU.mult)
                e.tensor_tensor(srow[0:1, 1024:1536], srow[0:1, 0:512], srow[0:1, 0:512], ALU.mult)
                e.tensor_tensor(srow[0:1, 512:1024], srow[0:1, 512:1024], srow[0:1, 1024:1536], ALU.subtract)
                i = e.tensor_scalar(srow[0:1, 512:1024], srow[0:1, 512:1024], LN_EPS, None, ALU.add)
                p.inc(i, "ve")

            @blk.scalar
            def _(e):
                p.wait(e, ("ve", p.C["ve"]))
                i = e.activation(srow[0:1, 512:1024], srow[0:1, 512:1024], AF.Sqrt)
                p.inc(i, "ac")

            @blk.vector
            def _(e):
                p.wait(e, ("ac", p.C["ac"]))
                e.reciprocal(srow[0:1, 512:1024], srow[0:1, 512:1024])
                i = e.tensor_tensor(srow[0:1, 1024:1536], srow[0:1, 0:512], srow[0:1, 512:1024], ALU.mult)
                p.inc(i, "ve")

            @blk.tensor
            def _(e):
                p.wait(e, ("ve", p.C["ve"]))
                e.matmul(PS[2][:], ones_r[:], srow[0:1, 512:1024], start=True, stop=True,
                         skip_group_check=True)
                i = e.matmul(PS[3][:], ones_r[:], srow[0:1, 1024:1536], start=True, stop=True,
                             skip_group_check=True)
                p.inc(i, "pe")

            @blk.scalar
            def _(e):
                p.wait(e, ("pe", p.C["pe"]))
                e.copy(rstdB[:], PS[2][:])
                i = e.copy(mrB[:], PS[3][:])
                p.inc(i, "ac")
            ln_ac = p.C["ac"]

            for dm in range(NJ):
                xo = yt[dm % 2]
                xs = "x%d" % (dm % 2)

                @blk.vector
                def _(e, dm=dm, xo=xo, xs=xs, ln_ac=ln_ac):
                    p.wait(e, ("ac", ln_ac), (xs, p.C[xs]))
                    e.tensor_tensor(xo[:], y2[:, dm * TT:(dm + 1) * TT], rstdB[:], ALU.mult)
                    i = e.tensor_tensor(xo[:], xo[:], mrB[:], ALU.subtract)
                    p.inc(i, "ve")

                @blk.gpsimd
                def _(e, dm=dm, sq=sq, trs=trs, xo=xo, xs=xs):
                    p.wait(e, ("ve", p.C["ve"]))
                    i = e.dma_start(out=yout[sq, dm, :, trs:trs + TT], in_=xo[:])
                    p.inc(i, xs, 16)

        @blk.gpsimd
        def _(e):
            p.wait(e, ("x0", p.C["x0"]))
            e.nop()
            e.wait_ge(p.sems["x1"], p.C["x1"])

    return p


def prep_D_weights(lut_v, gate_idx, up_idx, down_idx):
    gidx = np.ascontiguousarray(
        gate_idx.reshape(NC1, 128, NJ, 128).transpose(0, 3, 2, 1).reshape(NC1, 128, NJ * 128)).astype(np.int32)
    uidx = np.ascontiguousarray(
        up_idx.reshape(NC1, 128, NJ, 128).transpose(0, 3, 2, 1).reshape(NC1, 128, NJ * 128)).astype(np.int32)
    didx = np.ascontiguousarray(
        down_idx.reshape(NJ, 128, NC1, 128).transpose(0, 3, 2, 1).reshape(NJ, 128, NC1 * 128)).astype(np.int32)
    return dict(gidx=gidx, uidx=uidx, didx=didx,
                lut=np.asarray(lut_v, np.float32).reshape(1, 16))


_PROGS = {}


def _get_prog(name):
    if name not in _PROGS:
        _PROGS[name] = {"A": build_A, "B": build_B, "C": build_C, "D": build_D}[name]()
    return _PROGS[name]


def kernel(L0_attn_hidden_states, L0_attn_Wq, L0_attn_bq, L0_attn_Wk, L0_attn_bk,
           L0_attn_Wv, L0_attn_bv, L0_attn_Wo, L0_ffn_W1, L0_ffn_W2, L0_ffn_W3,
           L1_attn_W_qkv, L1_attn_b_qkv, L1_attn_W_out, L1_attn_b_out,
           L1_ffn_lut, L1_ffn_gate_idx, L1_ffn_up_idx, L1_ffn_down_idx):
    x = np.ascontiguousarray(np.asarray(L0_attn_hidden_states, np.float32))
    wa = prep_A_weights(np.asarray(L0_attn_Wq, np.float32), np.asarray(L0_attn_bq, np.float32),
                        np.asarray(L0_attn_Wk, np.float32), np.asarray(L0_attn_bk, np.float32),
                        np.asarray(L0_attn_Wv, np.float32), np.asarray(L0_attn_bv, np.float32),
                        np.asarray(L0_attn_Wo, np.float32))
    wb = prep_B_weights(np.asarray(L0_ffn_W1, np.float32), np.asarray(L0_ffn_W2, np.float32),
                        np.asarray(L0_ffn_W3, np.float32))
    wc = prep_C_weights(np.asarray(L1_attn_W_qkv, np.float32), np.asarray(L1_attn_b_qkv, np.float32),
                        np.asarray(L1_attn_W_out, np.float32), np.asarray(L1_attn_b_out, np.float32))
    wd = prep_D_weights(np.asarray(L1_ffn_lut, np.float32), np.asarray(L1_ffn_gate_idx),
                        np.asarray(L1_ffn_up_idx), np.asarray(L1_ffn_down_idx))

    pA = _get_prog("A")
    in_maps = []
    for c in range(NCORES):
        m = dict(wa)
        m["xin"] = to_fm(x[c * NSEQ:(c + 1) * NSEQ])
        in_maps.append(m)
    res = run_bass_kernel_spmd(pA.nc, in_maps, core_ids=list(range(NCORES)))
    x2s = [np.asarray(r["x2"]) for r in res.results]

    pB = _get_prog("B")
    in_maps = []
    for c in range(NCORES):
        m = dict(wb)
        m["x2"] = x2s[c]
        in_maps.append(m)
    res = run_bass_kernel_spmd(pB.nc, in_maps, core_ids=list(range(NCORES)))
    x3s = [np.asarray(r["x3"]) for r in res.results]

    pC = _get_prog("C")
    in_maps = []
    for c in range(NCORES):
        m = dict(wc)
        m["x3"] = x3s[c]
        in_maps.append(m)
    res = run_bass_kernel_spmd(pC.nc, in_maps, core_ids=list(range(NCORES)))
    x4s = [np.asarray(r["x4"]) for r in res.results]

    pD = _get_prog("D")
    in_maps = []
    for c in range(NCORES):
        m = dict(wd)
        m["x4"] = x4s[c]
        in_maps.append(m)
    res = run_bass_kernel_spmd(pD.nc, in_maps, core_ids=list(range(NCORES)))
    return np.concatenate([from_fm(np.asarray(r["yout"])) for r in res.results], axis=0)


# version 04
# speedup vs baseline: 2.4881x; 1.0498x over previous
import contextlib
import numpy as np
import ml_dtypes
import concourse.bass as bass
import concourse.mybir as mybir
from concourse.bass_utils import run_bass_kernel_spmd

F32 = mybir.dt.float32
BF16 = mybir.dt.bfloat16
I32 = mybir.dt.int32
AF = mybir.ActivationFunctionType
ALU = mybir.AluOpType
AX = mybir.AxisListType

NCORES = 8
BF = ml_dtypes.bfloat16

D = 2048
NJ = 16
SEQ = 2048
NSEQ = 2
H = 16
KVH = 4
HD = 128
LN_EPS = 1e-5
LA_EPS = 1e-6


class P:

    def __init__(self, name):
        self.nc = bass.Bass("TRN2", target_bir_lowering=False)
        self.es = contextlib.ExitStack()
        self.sems = {}
        self.C = {}

    def sem(self, name):
        self.sems[name] = self.es.enter_context(self.nc.semaphore(name))
        self.C[name] = 0

    def sb(self, name, shape, dt, es=None):
        return (es or self.es).enter_context(self.nc.sbuf_tensor(name, shape, dt))

    def ps(self, name, shape, dt=F32, es=None):
        return (es or self.es).enter_context(self.nc.psum_tensor(name, shape, dt))

    def dram(self, name, shape, dt, kind="Internal"):
        return self.nc.dram_tensor(name, shape, dt, kind=kind).ap()

    def inc(self, instr, name, k=1):
        instr.then_inc(self.sems[name], k)
        self.C[name] += k
        return self.C[name]

    def wait(self, eng, *pairs):
        first = True
        for nm, v in pairs:
            if v <= 0:
                continue
            if not first:
                eng.nop()
            eng.wait_ge(self.sems[nm], v)
            first = False

    def barrier(self, blk):
        if "bar" not in self.sems:
            self.sem("bar")
        @blk.sync
        def _(e):
            self.wait(e, *[(nm, self.C[nm]) for nm in self.sems if nm != "bar"])
            e.nop()
        tgt = self.C["bar"] + 5
        for engsec in (blk.sync, blk.tensor, blk.scalar, blk.vector, blk.gpsimd):
            @engsec
            def _(e):
                e.drain()
                e.nop().then_inc(self.sems["bar"], 1)
        self.C["bar"] = tgt
        for engsec in (blk.sync, blk.tensor, blk.scalar, blk.vector, blk.gpsimd):
            @engsec
            def _(e):
                e.wait_ge(self.sems["bar"], tgt)



def emit_A(p, blk, S=SEQ):
    nc = p.nc
    NT = S // 512
    xin = p.aps["xin"]
    wkv = p.aps["wkv"]
    bkv = p.aps["bkv"]
    wq = p.aps["wq"]
    bq = p.aps["bq"]
    wo = p.aps["wo"]
    x2 = p.aps["x2"]
    wqb_scr = p.aps["wqb_scr"]
    wob_scr = p.aps["wob_scr"]
    ones_r = p.ones_r
    ones_c = p.ones_c

    esA = contextlib.ExitStack()
    bkv_row = p.sb("bkv_row", [1, 1024], F32, es=esA)
    bkvB = p.sb("bkvB", [128, 1024], F32, es=esA)
    bq_sb = p.sb("bq_sb", [128, H], F32, es=esA)
    kv_f = p.sb("kv_f", [128, NSEQ * 4 * 132], F32, es=esA)
    kv_b = p.sb("kv_b", [128, NSEQ * 4 * 132], BF16, es=esA)

    pes_w = contextlib.ExitStack()
    wkv_sb = p.sb("wkv_sb", [128, 2 * NJ * 512], BF16, es=pes_w)
    if True:
        pes = contextlib.ExitStack()
        wtmp = [p.sb(f"wtmp{i}", [128, 2048], F32, es=pes) for i in range(2)]
        wqb_tmp = [p.sb(f"wqbt{i}", [128, 2048], BF16, es=pes) for i in range(2)]
        pb0 = p.ps("pb0", [128, 512], F32, es=pes)
        pb1 = p.ps("pb1", [128, 512], F32, es=pes)

        bq_row = p.sb("bq_row", [1, H * 128], F32, es=pes)

        @blk.sync
        def _(e):
            i = e.dma_start(out=bkv_row[0:1, 0:512], in_=bkv[0:1, :])
            p.inc(i, "lx", 16)
            i = e.dma_start(out=bkv_row[0:1, 512:1024], in_=bkv[1:2, :])
            p.inc(i, "lx", 16)
            i = e.dma_start(out=bq_row[:], in_=bq[:, :])
            p.inc(i, "lx", 16)

        @blk.vector
        def _(e):
            e.memset(ones_r[:], 1.0)
            i = e.memset(ones_c[:], 1.0)
            p.inc(i, "ve")
        p.init_ve = p.C["ve"]

        @blk.tensor
        def _(e):
            p.wait(e, ("lx", 48), ("ve", 1))
            e.matmul(pb0[:], ones_r[:], bkv_row[0:1, 0:512], start=True, stop=True)
            i = e.matmul(pb1[:], ones_r[:], bkv_row[0:1, 512:1024], start=True, stop=True)
            p.inc(i, "pe")

        @blk.scalar
        def _(e):
            p.wait(e, ("pe", p.C["pe"]))
            e.copy(bkvB[:, 0:512], pb0[:])
            i = e.copy(bkvB[:, 512:1024], pb1[:])
            p.inc(i, "ac")

        @blk.tensor
        def _(e):
            p.wait(e, ("ac", p.C["ac"]))
            for h in range(H):
                i = e.matmul(pb0[:, h:h + 1], bq_row[0:1, h * 128:(h + 1) * 128],
                             ones_r[0:1, 0:1], start=True, stop=True,
                             skip_group_check=True)
            p.inc(i, "pe")

        @blk.scalar
        def _(e):
            p.wait(e, ("pe", p.C["pe"]))
            i = e.copy(bq_sb[:, 0:H], pb0[:, 0:H])
            p.inc(i, "ac")

        for t in range(2 * NJ):
            g, j = divmod(t, NJ)
            buf = wtmp[t % 2]
            ws = "w%d" % (t % 2)

            @blk.gpsimd
            def _(e):
                p.wait(e, ("ve", p.C["ve"] - 1))
                i = e.dma_start(out=buf[:, 0:512], in_=wkv[g, j])
                p.inc(i, ws, 16)

            @blk.vector
            def _(e):
                p.wait(e, (ws, p.C[ws]))
                i = e.tensor_copy(wkv_sb[:, (g * NJ + j) * 512:(g * NJ + j + 1) * 512], buf[:, 0:512])
                p.inc(i, "ve")

        for t in range(NJ):
            buf = wtmp[t % 2]
            bbuf = wqb_tmp[t % 2]
            ws = "w%d" % (t % 2)
            ss = "s%d" % (t % 2)

            @blk.gpsimd
            def _(e):
                p.wait(e, ("ve", p.C["ve"] - 1))
                i = e.dma_start(out=buf[:], in_=wo[t])
                p.inc(i, ws, 16)

            @blk.vector
            def _(e):
                p.wait(e, (ws, p.C[ws]), (ss, p.C[ss]))
                i = e.tensor_copy(bbuf[:], buf[:])
                p.inc(i, "ve")

            @blk.sync
            def _(e):
                p.wait(e, ("ve", p.C["ve"]))
                i = e.dma_start(out=wob_scr[t], in_=bbuf[:])
                p.inc(i, ss, 16)

        for t in range(H):
            buf = wtmp[t % 2]
            bbuf = wqb_tmp[t % 2]
            ws = "w%d" % (t % 2)
            ss = "s%d" % (t % 2)

            @blk.gpsimd
            def _(e):
                p.wait(e, ("ve", p.C["ve"] - 1))
                i = e.dma_start(out=buf[:], in_=wq[t])
                p.inc(i, ws, 16)

            @blk.vector
            def _(e):
                p.wait(e, (ws, p.C[ws]), (ss, p.C[ss]))
                i = e.tensor_copy(bbuf[:], buf[:])
                p.inc(i, "ve")

            @blk.sync
            def _(e):
                p.wait(e, ("ve", p.C["ve"]))
                i = e.dma_start(out=wqb_scr[t], in_=bbuf[:])
                p.inc(i, ss, 16)

        p.barrier(blk)
        pes.close()

        pes = contextlib.ExitStack()
        xt = p.sb("xtP1", [128, NJ * 512], F32, es=pes)
        xb1 = p.sb("xbP1", [128, NJ * 512], BF16, es=pes)
        t0 = p.sb("t0P1", [128, 512], F32, es=pes)
        tm = p.sb("tmP1", [128, 128], F32, es=pes)
        te = p.sb("teP1", [128, 128], F32, es=pes)
        tr = p.sb("trP1", [128, 128], F32, es=pes)
        skb = p.sb("skbP1", [128, 128], BF16, es=pes)
        vxb = p.sb("vxbP1", [128, 132], BF16, es=pes)
        pproj = [p.ps(f"pprojP1{i}", [128, 512], F32, es=pes) for i in range(2)]
        pkv = [p.ps(f"pkvP1{i}", [128, 512], F32, es=pes) for i in range(2)]

        @blk.vector
        def _(e):
            i = e.memset(vxb[:, 128:132], 1.0)
            p.inc(i, "ve")

        t0_ve = []
        skb2 = p.sb("skb2P1", [128, 128], BF16, es=pes)
        vxb2 = p.sb("vxb2P1", [128, 132], BF16, es=pes)

        @blk.vector
        def _(e):
            i = e.memset(vxb2[:, 128:132], 1.0)
            p.inc(i, "ve")

        skbs = [skb, skb2]
        vxbs = [vxb, vxb2]

        step_idx = 0
        kv_pe = {}
        skb_ve = {}
        for sq in range(NSEQ):
            for st in range(NT):
                @blk.sync
                def _(e):
                    if sq + st > 0:
                        p.wait(e, ("pe", p.C["pe"]))
                    for j in range(NJ):
                        i = e.dma_start(out=xt[:, j * 512:(j + 1) * 512],
                                        in_=xin[sq, j, :, st * 512:(st + 1) * 512])
                        p.inc(i, "lx", 16)

                @blk.vector
                def _(e):
                    p.wait(e, ("lx", p.C["lx"]))
                    for j in range(NJ):
                        i = e.tensor_copy(xb1[:, j * 512:(j + 1) * 512], xt[:, j * 512:(j + 1) * 512])
                    p.inc(i, "ve")
                cast_ve = p.C["ve"]

                for sub in range(4):
                    for g in range(2):
                        i_step = step_idx
                        pp = pproj[i_step % 2]

                        @blk.tensor
                        def _(e):
                            w2 = t0_ve[i_step - 2] if i_step >= 2 else 0
                            p.wait(e, ("ve", max(cast_ve, w2)))
                            for j in range(NJ):
                                i = e.matmul(pp[:],
                                             xb1[:, j * 512 + sub * 128: j * 512 + sub * 128 + 128],
                                             wkv_sb[:, (g * NJ + j) * 512:(g * NJ + j + 1) * 512],
                                             start=(j == 0), stop=(j == NJ - 1),
                                             skip_group_check=True)
                            p.inc(i, "pe")
                        proj_pe = p.C["pe"]

                        @blk.vector
                        def _(e):
                            p.wait(e, ("pe", proj_pe))
                            i = e.tensor_tensor(t0[:], pp[:], bkvB[:, g * 512:(g + 1) * 512], ALU.add)
                            p.inc(i, "ve")
                        t0_ve.append(p.C["ve"])

                        for kk in range(2):
                            kvh = g * 2 + kk
                            sk_b = skbs[kk]
                            vx_b = vxbs[kk]

                            @blk.vector
                            def _(e):
                                prev = kv_pe.get((i_step - 1, kk), 0)
                                p.wait(e, ("pe", prev))
                                e.tensor_scalar(tm[:], t0[:, kk * 256:kk * 256 + 128], 0.0, None, ALU.min)
                                i = e.tensor_relu(tr[:], t0[:, kk * 256:kk * 256 + 128])
                                p.inc(i, "ve")

                            @blk.scalar
                            def _(e):
                                p.wait(e, ("ve", p.C["ve"]))
                                i = e.activation(te[:], tm[:], AF.Exp)
                                p.inc(i, "ac")

                            @blk.vector
                            def _(e):
                                p.wait(e, ("ac", p.C["ac"]))
                                e.tensor_tensor(sk_b[:], te[:], tr[:], ALU.add)
                                i = e.tensor_copy(vx_b[:, 0:128], t0[:, kk * 256 + 128:kk * 256 + 256])
                                p.inc(i, "ve")
                            skb_ve[(i_step, kk)] = p.C["ve"]

                            @blk.tensor
                            def _(e):
                                p.wait(e, ("ve", skb_ve[(i_step, kk)]))
                                bank = pkv[kvh // 3]
                                off = (kvh % 3) * 132
                                i = e.matmul(bank[:, off:off + 132], sk_b[:], vx_b[:],
                                             start=(st == 0 and sub == 0 and kvh in (0, 3)),
                                             stop=(st == NT - 1 and sub == 3 and kvh in (2, 3)),
                                             skip_group_check=True)
                                p.inc(i, "pe")
                            kv_pe[(i_step, kk)] = p.C["pe"]

                        step_idx += 1

            @blk.vector
            def _(e):
                p.wait(e, ("pe", p.C["pe"]))
                base = sq * 4 * 132
                e.tensor_copy(kv_f[:, base:base + 396], pkv[0][:, 0:396])
                e.tensor_copy(kv_f[:, base + 396:base + 528], pkv[1][:, 0:132])
                e.tensor_copy(kv_b[:, base:base + 396], pkv[0][:, 0:396])
                i = e.tensor_copy(kv_b[:, base + 396:base + 528], pkv[1][:, 0:132])
                p.inc(i, "ve")

            @blk.tensor
            def _(e):
                p.wait(e, ("ve", p.C["ve"]))
                p.inc(e.nop(), "pe")

        p.barrier(blk)
        pes.close()
        pes_w.close()

        pes = contextlib.ExitStack()
        xt3 = p.sb("xtP3", [128, NJ * 512], F32, es=pes)
        xb3 = p.sb("xbP3", [128, NJ * 512], BF16, es=pes)
        wqh = [p.sb(f"wqhP3{i}", [128, 2048], BF16, es=pes) for i in range(2)]
        sqb = [p.sb(f"sqbP3{i}", [128, 512], BF16, es=pes) for i in range(2)]
        tq = p.sb("tqP3", [128, 512], F32, es=pes)
        tqm = p.sb("tqmP3", [128, 512], F32, es=pes)
        tqe = p.sb("tqeP3", [128, 512], F32, es=pes)
        tqr = p.sb("tqrP3", [128, 512], F32, es=pes)
        nrow = p.sb("nrowP3", [1, 512], F32, es=pes)
        recB = [p.sb(f"recBP3{i}", [128, 512], F32, es=pes) for i in range(2)]
        attnN = p.sb("attnNP3", [128, H * 512], BF16, es=pes)
        y1 = p.sb("y1P3", [128, NJ * 512], F32, es=pes)
        y1b = p.sb("y1bP3", [128, 512], BF16, es=pes)
        y1s = p.sb("y1sP3", [128, 512], BF16, es=pes)
        srow = p.sb("srowP3", [1, 1536], F32, es=pes)
        rstdB = p.sb("rstdBP3", [128, 512], F32, es=pes)
        mrB = p.sb("mrBP3", [128, 512], F32, es=pes)
        x2t = [p.sb(f"x2tP3{i}", [128, 512], F32, es=pes) for i in range(2)]
        woh = [p.sb(f"wohP3{i}", [128, 2048], BF16, es=pes) for i in range(2)]
        PS = [p.ps(f"PSP3{i}", [128, 512], F32, es=pes) for i in range(6)]

        for sq in range(NSEQ):
            for st in range(NT):
                pe_prev_tile = p.C["pe"]

                @blk.sync
                def _(e):
                    if sq + st > 0:
                        p.wait(e, ("ve", p.C["ve"]))
                    for j in range(NJ):
                        i = e.dma_start(out=xt3[:, j * 512:(j + 1) * 512],
                                        in_=xin[sq, j, :, st * 512:(st + 1) * 512])
                        p.inc(i, "lx", 16)
                    i = e.dma_start(out=wqh[0][:], in_=wqb_scr[0])
                    p.inc(i, "q0", 16)

                @blk.vector
                def _(e):
                    p.wait(e, ("lx", p.C["lx"]), ("pe", pe_prev_tile))
                    for j in range(NJ):
                        i = e.tensor_copy(xb3[:, j * 512:(j + 1) * 512], xt3[:, j * 512:(j + 1) * 512])
                    p.inc(i, "ve")
                xcast_ve = p.C["ve"]

                qproj_pe = {}
                evacq_ve = {}
                sqb_ve = {}

                def head_front(h):
                    pq = PS[h % 2]

                    @blk.sync
                    def _(e):
                        if h + 1 < H:
                            p.wait(e, ("pe", qproj_pe.get(h - 1, 0)))
                            i = e.dma_start(out=wqh[(h + 1) % 2][:], in_=wqb_scr[h + 1])
                            p.inc(i, "q%d" % ((h + 1) % 2), 16)

                    @blk.tensor
                    def _(e):
                        wv = max(xcast_ve, evacq_ve.get(h - 2, 0))
                        p.wait(e, ("q%d" % (h % 2), p.C["q%d" % (h % 2)]),
                               ("ve", wv))
                        for j in range(NJ):
                            i = e.matmul(pq[:], wqh[h % 2][:, j * 128:(j + 1) * 128],
                                         xb3[:, j * 512:(j + 1) * 512],
                                         start=(j == 0), stop=(j == NJ - 1),
                                         skip_group_check=True)
                        p.inc(i, "pe")
                    qproj_pe[h] = p.C["pe"]

                    @blk.vector
                    def _(e):
                        p.wait(e, ("pe", qproj_pe[h]))
                        e.tensor_scalar(tq[:], pq[:], bq_sb[:, h:h + 1], None, ALU.add)
                        e.tensor_scalar(tqm[:], tq[:], 0.0, None, ALU.min)
                        i = e.tensor_relu(tqr[:], tq[:])
                        p.inc(i, "ve")
                    evacq_ve[h] = p.C["ve"]

                    @blk.scalar
                    def _(e):
                        p.wait(e, ("ve", evacq_ve[h]))
                        i = e.activation(tqe[:], tqm[:], AF.Exp)
                        p.inc(i, "ac")

                    @blk.vector
                    def _(e):
                        p.wait(e, ("ac", p.C["ac"]))
                        i = e.tensor_tensor(sqb[h % 2][:], tqe[:], tqr[:], ALU.add)
                        p.inc(i, "ve")
                    sqb_ve[h] = p.C["ve"]

                def head_back(h):
                    pat = PS[2 + h % 2]
                    kvh = h // 4
                    kvoff = (sq * 4 + kvh) * 132

                    @blk.tensor
                    def _(e):
                        p.wait(e, ("ve", sqb_ve[h]))
                        e.matmul(PS[4][0:1, :], kv_b[:, kvoff + 128:kvoff + 129], sqb[h % 2][:],
                                 start=True, stop=True, skip_group_check=True)
                        i = e.matmul(pat[:], kv_b[:, kvoff:kvoff + 128], sqb[h % 2][:],
                                     start=True, stop=True, skip_group_check=True)
                        p.inc(i, "pe")
                    na_pe = p.C["pe"]

                    @blk.vector
                    def _(e):
                        p.wait(e, ("pe", na_pe))
                        i = e.tensor_scalar(nrow[:], PS[4][0:1, :], LA_EPS, None, ALU.add)
                        p.inc(i, "ve")
                    nr_ve = p.C["ve"]

                    @blk.tensor
                    def _(e):
                        p.wait(e, ("ve", nr_ve))
                        i = e.matmul(PS[5][:], ones_r[:], nrow[:], start=True, stop=True,
                                     skip_group_check=True)
                        p.inc(i, "pe")
                    bc_pe = p.C["pe"]

                    @blk.vector
                    def _(e):
                        p.wait(e, ("pe", bc_pe))
                        e.reciprocal(recB[h % 2][:], PS[5][:])
                        i = e.tensor_tensor(attnN[:, h * 512:(h + 1) * 512], pat[:], recB[h % 2][:], ALU.mult)
                        p.inc(i, "ve")

                for h in range(H):
                    head_front(h)
                    if h > 0:
                        head_back(h - 1)
                head_back(H - 1)

                attn_done_ve = p.C["ve"]
                y1_ve = {}
                sqr_ac = {}
                wo_lw = {}
                wo_pe = {}

                @blk.gpsimd
                def _(e):
                    p.wait(e, ("pe", pe_prev_tile))
                    for dmp in range(2):
                        i = e.dma_start(out=woh[dmp][:], in_=wob_scr[dmp])
                        wo_lw[dmp] = p.inc(i, "o%d" % dmp, 16)

                def wo_mm_tensor(dm):
                    po = PS[dm % 2]

                    @blk.tensor
                    def _(e):
                        p.wait(e, ("o%d" % (dm % 2), wo_lw[dm]))
                        e.nop()
                        if dm == 0:
                            p.wait(e, ("ve", attn_done_ve))
                        elif dm >= 2:
                            p.wait(e, ("ve", y1_ve[dm - 2]))
                        for hh in range(H):
                            i = e.matmul(po[:], woh[dm % 2][:, hh * 128:(hh + 1) * 128],
                                         attnN[:, hh * 512:(hh + 1) * 512],
                                         start=(hh == 0), stop=(hh == H - 1),
                                         skip_group_check=True)
                        p.inc(i, "pe")
                    wo_pe[dm] = p.C["pe"]

                    if dm + 2 < NJ:
                        @blk.gpsimd
                        def _(e):
                            p.wait(e, ("pe", wo_pe[dm]))
                            i = e.dma_start(out=woh[dm % 2][:], in_=wob_scr[dm + 2])
                            wo_lw[dm + 2] = p.inc(i, "o%d" % (dm % 2), 16)

                def stats_mm(dm):
                    @blk.tensor
                    def _(e):
                        p.wait(e, ("ac", sqr_ac[dm]))
                        e.matmul(PS[4][0:1, :], ones_c[:], y1b[:],
                                 start=(dm == 0), stop=(dm == NJ - 1), skip_group_check=True)
                        i = e.matmul(PS[5][0:1, :], ones_c[:], y1s[:],
                                     start=(dm == 0), stop=(dm == NJ - 1), skip_group_check=True)
                        p.inc(i, "pe")

                def wo_evac(dm):
                    po = PS[dm % 2]
                    pe_snap = p.C["pe"]

                    @blk.vector
                    def _(e):
                        p.wait(e, ("pe", pe_snap))
                        e.tensor_tensor(y1[:, dm * 512:(dm + 1) * 512], po[:],
                                        xt3[:, dm * 512:(dm + 1) * 512], ALU.add)
                        i = e.tensor_copy(y1b[:], y1[:, dm * 512:(dm + 1) * 512])
                        p.inc(i, "ve")
                    y1_ve[dm] = p.C["ve"]

                    @blk.scalar
                    def _(e):
                        p.wait(e, ("ve", y1_ve[dm]))
                        i = e.activation(y1s[:], y1[:, dm * 512:(dm + 1) * 512], AF.Square)
                        p.inc(i, "ac")
                    sqr_ac[dm] = p.C["ac"]

                for dm in range(NJ):
                    wo_mm_tensor(dm)
                    if dm > 0:
                        stats_mm(dm - 1)
                    wo_evac(dm)
                stats_mm(NJ - 1)

                @blk.vector
                def _(e):
                    p.wait(e, ("pe", p.C["pe"]))
                    e.tensor_scalar(srow[0:1, 0:512], PS[4][0:1, :], 1.0 / D, None, ALU.mult)
                    e.tensor_scalar(srow[0:1, 512:1024], PS[5][0:1, :], 1.0 / D, None, ALU.mult)
                    e.tensor_tensor(srow[0:1, 1024:1536], srow[0:1, 0:512], srow[0:1, 0:512], ALU.mult)
                    e.tensor_tensor(srow[0:1, 512:1024], srow[0:1, 512:1024], srow[0:1, 1024:1536], ALU.subtract)
                    i = e.tensor_scalar(srow[0:1, 512:1024], srow[0:1, 512:1024], LN_EPS, None, ALU.add)
                    p.inc(i, "ve")

                @blk.scalar
                def _(e):
                    p.wait(e, ("ve", p.C["ve"]))
                    i = e.activation(srow[0:1, 512:1024], srow[0:1, 512:1024], AF.Sqrt)
                    p.inc(i, "ac")

                @blk.vector
                def _(e):
                    p.wait(e, ("ac", p.C["ac"]))
                    e.reciprocal(srow[0:1, 512:1024], srow[0:1, 512:1024])
                    i = e.tensor_tensor(srow[0:1, 1024:1536], srow[0:1, 0:512], srow[0:1, 512:1024], ALU.mult)
                    p.inc(i, "ve")

                @blk.tensor
                def _(e):
                    p.wait(e, ("ve", p.C["ve"]))
                    e.matmul(PS[2][:], ones_r[:], srow[0:1, 512:1024], start=True, stop=True,
                             skip_group_check=True)
                    i = e.matmul(PS[3][:], ones_r[:], srow[0:1, 1024:1536], start=True, stop=True,
                                 skip_group_check=True)
                    p.inc(i, "pe")

                @blk.scalar
                def _(e):
                    p.wait(e, ("pe", p.C["pe"]))
                    e.copy(rstdB[:], PS[2][:])
                    i = e.copy(mrB[:], PS[3][:])
                    p.inc(i, "ac")
                ln_ac = p.C["ac"]

                for dm in range(NJ):
                    xb = x2t[dm % 2]
                    xs = "x%d" % (dm % 2)

                    @blk.vector
                    def _(e):
                        p.wait(e, ("ac", ln_ac), (xs, p.C[xs]))
                        e.tensor_tensor(xb[:], y1[:, dm * 512:(dm + 1) * 512], rstdB[:], ALU.mult)
                        i = e.tensor_tensor(xb[:], xb[:], mrB[:], ALU.subtract)
                        p.inc(i, "ve")

                    @blk.gpsimd
                    def _(e):
                        p.wait(e, ("ve", p.C["ve"]))
                        i = e.dma_start(out=x2[sq, dm, :, st * 512:(st + 1) * 512], in_=xb[:])
                        p.inc(i, xs, 16)

        @blk.gpsimd
        def _(e):
            p.wait(e, ("x0", p.C["x0"]))
            e.nop()
            e.wait_ge(p.sems["x1"], p.C["x1"])

    pes.close()
    esA.close()



def prep_A_weights(Wq, bqv, Wk, bk, Wv, bv, Wo):
    WkT = np.ascontiguousarray(Wk.T)
    WvT = np.ascontiguousarray(Wv.T)
    wkv = np.empty((2, NJ, 128, 512), np.float32)
    bkv = np.empty((2, 512), np.float32)
    for g in range(2):
        for kk in range(2):
            kvh = 2 * g + kk
            for j in range(NJ):
                wkv[g, j, :, kk * 256:kk * 256 + 128] = WkT[j * 128:(j + 1) * 128, kvh * 128:(kvh + 1) * 128]
                wkv[g, j, :, kk * 256 + 128:kk * 256 + 256] = WvT[j * 128:(j + 1) * 128, kvh * 128:(kvh + 1) * 128]
            bkv[g, kk * 256:kk * 256 + 128] = bk[kvh * 128:(kvh + 1) * 128]
            bkv[g, kk * 256 + 128:kk * 256 + 256] = bv[kvh * 128:(kvh + 1) * 128]
    wq = np.ascontiguousarray(
        Wq.reshape(H, 128, NJ, 128).transpose(0, 3, 2, 1).reshape(H, 128, NJ * 128))
    bq_r = np.ascontiguousarray(bqv.reshape(1, H * 128))
    wo = np.ascontiguousarray(
        Wo.reshape(NJ, 128, H, 128).transpose(0, 3, 2, 1).reshape(NJ, 128, H * 128))
    return dict(wkv=wkv, bkv=bkv, wq=wq, bq=bq_r, wo=wo)


def to_fm(x):
    n, S, d = x.shape
    return np.ascontiguousarray(x.transpose(0, 2, 1).reshape(n, NJ, 128, S))


def from_fm(x):
    n = x.shape[0]
    S = x.shape[3]
    return np.ascontiguousarray(x.reshape(n, NJ * 128, S).transpose(0, 2, 1))







NC0 = 64
RT2 = np.float32(1.0 / np.sqrt(8192.0))


def emit_B(p, blk, S=SEQ):
    nc = p.nc
    T = NSEQ * S
    NTS = T // 512
    x2 = p.aps["x2"]
    w1h = p.aps["w1h"]
    w3h = p.aps["w3h"]
    w2r = p.aps["w2r"]
    h64 = p.aps["h64"]
    h128 = p.aps["h128"]
    x3 = p.aps["x3"]
    ascr = p.aps["ascr"]
    w2b_scr = p.aps["w2b_scr"]
    ones_r = p.ones_r
    ones_c = p.ones_c

    esB = contextlib.ExitStack()
    h64_sb = p.sb("h64_sb", [64, 64], BF16, es=esB)
    h128_sb = p.sb("h128_sb", [128, 128], BF16, es=esB)

    if True:
        @blk.sync
        def _(e):
            i = e.dma_start(out=h64_sb[:], in_=h64[:, :])
            p.inc(i, "lx", 16)
            i = e.dma_start(out=h128_sb[:], in_=h128[:, :])
            p.inc(i, "lx", 16)
        hl_lx = p.C["lx"]

        pes = contextlib.ExitStack()
        wfb = [p.sb(f"wfbB{i}", [64, 512], F32, es=pes) for i in range(2)]
        wbb = [p.sb(f"wbbB{i}", [64, 512], BF16, es=pes) for i in range(2)]
        wob = [p.sb(f"wobB{i}", [64, 512], BF16, es=pes) for i in range(2)]
        psA = [p.ps(f"psAB{i}", [64, 512], F32, es=pes) for i in range(2)]

        cast_ve = {}
        mm_pe = {}
        ev_ve = {}
        wr_sa = {}
        for s in range(512):
            b, dsub = divmod(s, 4)
            par = s % 2

            @(blk.gpsimd if par == 0 else blk.scalar)
            def _(e):
                if s >= 2:
                    p.wait(e, ("ve", cast_ve[s - 2]))
                i = e.dma_start(out=wfb[par][:], in_=w2r[:, b, dsub * 512:(dsub + 1) * 512])
                p.inc(i, "w%d" % par, 16)
            lw_s = p.C["w%d" % par]

            @blk.vector
            def _(e):
                p.wait(e, ("w%d" % par, lw_s))
                if s >= 2:
                    e.nop()
                    p.wait(e, ("pe", mm_pe[s - 2]))
                i = e.tensor_copy(wbb[par][:], wfb[par][:])
                p.inc(i, "ve")
            cast_ve[s] = p.C["ve"]

            @blk.tensor
            def _(e):
                if s == 0:
                    p.wait(e, ("lx", hl_lx))
                    e.nop()
                p.wait(e, ("ve", cast_ve[s]))
                if s >= 2:
                    e.nop()
                    p.wait(e, ("ve", ev_ve[s - 2]))
                i = e.matmul(psA[par][:], h64_sb[:], wbb[par][:], start=True, stop=True,
                             skip_group_check=True)
                p.inc(i, "pe")
            mm_pe[s] = p.C["pe"]

            @blk.vector
            def _(e):
                p.wait(e, ("pe", mm_pe[s]))
                if s >= 2:
                    e.nop()
                    p.wait(e, ("s%d" % par, wr_sa[s - 2]))
                i = e.tensor_scalar(wob[par][:], psA[par][:], float(RT2), None, ALU.mult)
                p.inc(i, "ve")
            ev_ve[s] = p.C["ve"]

            @blk.sync
            def _(e):
                p.wait(e, ("ve", ev_ve[s]))
                dst = w2b_scr[dsub * 4:(dsub + 1) * 4, b, :, :].rearrange("m a d -> a m d")
                i = e.dma_start(out=dst, in_=wob[par][:].rearrange("a (m d) -> a m d", m=4))
                p.inc(i, "s%d" % par, 16)
            wr_sa[s] = p.C["s%d" % par]

        p.barrier(blk)
        pes.close()

        pes = contextlib.ExitStack()
        xb = p.sb("xbB", [128, NJ * T], BF16, es=pes)
        xstg = p.sb("xstgB", [128, T], F32, es=pes)
        wf = [p.sb(f"wfB{i}", [128, 2 * NJ * 128], F32, es=pes) for i in range(2)]
        wb = [p.sb(f"wbB{i}", [128, 2 * NJ * 128], BF16, es=pes) for i in range(2)]
        stmp = [p.sb(f"stmpB{i}", [128, 512], F32, es=pes) for i in range(2)]
        hsub = [p.sb(f"hsubB{i}", [128, 512], BF16, es=pes) for i in range(2)]
        asub = [p.sb(f"asubB{i}", [128, 512], BF16, es=pes) for i in range(2)]
        pG = [p.ps(f"pGB{i}", [128, 512], F32, es=pes) for i in range(2)]
        pU = [p.ps(f"pUB{i}", [128, 512], F32, es=pes) for i in range(2)]
        pA = [p.ps(f"pAB{i}", [128, 512], F32, es=pes) for i in range(2)]

        for j in range(NJ):
            @blk.sync
            def _(e):
                if j > 0:
                    p.wait(e, ("ve", p.C["ve"]))
                for sq in range(NSEQ):
                    i = e.dma_start(out=xstg[:, sq * S:(sq + 1) * S], in_=x2[sq, j])
                    p.inc(i, "lx", 16)

            @blk.vector
            def _(e):
                p.wait(e, ("lx", p.C["lx"]))
                i = e.tensor_copy(xb[:, j * T:(j + 1) * T], xstg[:])
                p.inc(i, "ve")
        xcast_ve = p.C["ve"]

        @blk.gpsimd
        def _(e):
            for c in range(2):
                i = e.dma_start(out=wf[c][:, 0:NJ * 128], in_=w1h[c])
                p.inc(i, "w%d" % c, 16)
                i = e.dma_start(out=wf[c][:, NJ * 128:], in_=w3h[c])
                p.inc(i, "w%d" % c, 16)

        wcast_ve = {}
        gu_pe = {}
        hmul_ve = {}
        amm_pe = {}
        aev_ve = {}
        awr_sa = {}
        silu_ac = {}

        for c in range(NC0):
            par = c % 2

            @blk.vector
            def _(e):
                p.wait(e, ("w%d" % par, p.C["w%d" % par]))
                if c >= 2:
                    e.nop()
                    p.wait(e, ("pe", gu_pe[(c - 2, NTS - 1)]))
                e.tensor_copy(wb[par][:, 0:NJ * 128], wf[par][:, 0:NJ * 128])
                i = e.tensor_copy(wb[par][:, NJ * 128:], wf[par][:, NJ * 128:])
                p.inc(i, "ve")
            wcast_ve[c] = p.C["ve"]

            @blk.gpsimd
            def _(e):
                if c + 2 < NC0:
                    p.wait(e, ("ve", wcast_ve[c]))
                    i = e.dma_start(out=wf[par][:, 0:NJ * 128], in_=w1h[c + 2])
                    p.inc(i, "w%d" % par, 16)
                    i = e.dma_start(out=wf[par][:, NJ * 128:], in_=w3h[c + 2])
                    p.inc(i, "w%d" % par, 16)

            for ts in range(NTS):
                g_idx = c * NTS + ts
                q = g_idx % 2

                @blk.tensor
                def _(e):
                    wv = xcast_ve if g_idx == 0 else 0
                    if ts == 0:
                        p.wait(e, ("ve", wcast_ve[c]))
                        e.nop()
                    if g_idx >= 2:
                        p.wait(e, ("ve", hmul_ve[divmod(g_idx - 2, NTS)]))
                    elif wv:
                        p.wait(e, ("ve", wv))
                    for j in range(NJ):
                        e.matmul(pG[q][:], wb[par][:, j * 128:(j + 1) * 128],
                                 xb[:, j * T + ts * 512: j * T + ts * 512 + 512],
                                 start=(j == 0), stop=(j == NJ - 1), skip_group_check=True)
                    for j in range(NJ):
                        i = e.matmul(pU[q][:], wb[par][:, NJ * 128 + j * 128:NJ * 128 + (j + 1) * 128],
                                     xb[:, j * T + ts * 512: j * T + ts * 512 + 512],
                                     start=(j == 0), stop=(j == NJ - 1), skip_group_check=True)
                    p.inc(i, "pe")
                gu_pe[(c, ts)] = p.C["pe"]

                @blk.scalar
                def _(e):
                    p.wait(e, ("pe", gu_pe[(c, ts)]))
                    if g_idx >= 2:
                        e.nop()
                        p.wait(e, ("ve", hmul_ve[divmod(g_idx - 2, NTS)]))
                    i = e.activation(stmp[q][:], pG[q][:], AF.Silu)
                    p.inc(i, "ac")
                silu_ac[(c, ts)] = p.C["ac"]

                @blk.vector
                def _(e):
                    p.wait(e, ("ac", silu_ac[(c, ts)]))
                    if g_idx >= 2:
                        e.nop()
                        p.wait(e, ("pe", amm_pe[divmod(g_idx - 2, NTS)]))
                    i = e.tensor_tensor(hsub[q][:], stmp[q][:], pU[q][:], ALU.mult)
                    p.inc(i, "ve")
                hmul_ve[(c, ts)] = p.C["ve"]

                @blk.tensor
                def _(e):
                    p.wait(e, ("ve", hmul_ve[(c, ts)]))
                    if g_idx >= 2:
                        e.nop()
                        p.wait(e, ("ve", aev_ve[divmod(g_idx - 2, NTS)]))
                    i = e.matmul(pA[q][:], h128_sb[:], hsub[q][:], start=True, stop=True,
                                 skip_group_check=True)
                    p.inc(i, "pe")
                amm_pe[(c, ts)] = p.C["pe"]

                @blk.vector
                def _(e):
                    p.wait(e, ("pe", amm_pe[(c, ts)]))
                    if g_idx >= 2:
                        e.nop()
                        p.wait(e, ("a%d" % q, awr_sa[divmod(g_idx - 2, NTS)]))
                    i = e.tensor_copy(asub[q][:], pA[q][:])
                    p.inc(i, "ve")
                aev_ve[(c, ts)] = p.C["ve"]

                @blk.sync
                def _(e):
                    p.wait(e, ("ve", aev_ve[(c, ts)]))
                    i = e.dma_start(out=ascr[c, :, ts * 512:(ts + 1) * 512], in_=asub[q][:])
                    p.inc(i, "a%d" % q, 16)
                awr_sa[(c, ts)] = p.C["a%d" % q]

        p.barrier(blk)
        pes.close()

        pes = contextlib.ExitStack()
        acol = p.sb("acolB", [128, NC0 * 512], BF16, es=pes)
        w2bt = [p.sb(f"w2btB{i}", [128, 64 * 128], BF16, es=pes) for i in range(2)]
        res = [p.sb(f"resB{i}", [128, 512], F32, es=pes) for i in range(2)]
        y2 = p.sb("y2B", [128, NJ * 512], F32, es=pes)
        y2b = p.sb("y2bB", [128, 512], BF16, es=pes)
        y2s = p.sb("y2sB", [128, 512], BF16, es=pes)
        srow = p.sb("srowB", [1, 1536], F32, es=pes)
        rstdB = p.sb("rstdBB", [128, 512], F32, es=pes)
        mrB = p.sb("mrBB", [128, 512], F32, es=pes)
        x3t = [p.sb(f"x3tB{i}", [128, 512], F32, es=pes) for i in range(2)]
        PS = [p.ps(f"PSB{i}", [128, 512], F32, es=pes) for i in range(6)]

        NTT = T // 512
        for tt in range(NTT):
            sq, trs = divmod(tt * 512, S)
            pe_prev_tile = p.C["pe"]
            ve_prev_tile = p.C["ve"]

            @blk.sync
            def _(e):
                if tt > 0:
                    p.wait(e, ("pe", pe_prev_tile))
                for a in range(NC0):
                    i = e.dma_start(out=acol[:, a * 512:(a + 1) * 512],
                                    in_=ascr[a, :, tt * 512:(tt + 1) * 512])
                    p.inc(i, "lx", 16)
            acol_lx = p.C["lx"]

            w2b_lw = {}
            res_lx = {}

            @blk.gpsimd
            def _(e):
                if tt > 0:
                    p.wait(e, ("pe", pe_prev_tile))
                for dmp in range(2):
                    i = e.dma_start(out=w2bt[dmp][:], in_=w2b_scr[dmp])
                    i2 = e.dma_start(out=res[dmp][:], in_=x2[sq, dmp, :, trs:trs + 512])
                    i.then_inc(p.sems["d%d" % dmp], 16)
                    p.C["d%d" % dmp] += 16
                    w2b_lw[dmp] = p.inc(i2, "d%d" % dmp, 16)
                    res_lx[dmp] = w2b_lw[dmp]

            y2_ve = {}
            sqr_ac = {}
            dmm_pe = {}

            def down_mm(dm):
                po = PS[dm % 2]

                @blk.tensor
                def _(e):
                    p.wait(e, ("d%d" % (dm % 2), w2b_lw[dm]))
                    e.nop()
                    if dm == 0:
                        p.wait(e, ("lx", acol_lx))
                    elif dm >= 2:
                        p.wait(e, ("ve", y2_ve[dm - 2]))
                    for a in range(NC0):
                        i = e.matmul(po[:], w2bt[dm % 2][:, a * 128:(a + 1) * 128],
                                     acol[:, a * 512:(a + 1) * 512],
                                     start=(a == 0), stop=(a == NC0 - 1), skip_group_check=True)
                    p.inc(i, "pe")
                dmm_pe[dm] = p.C["pe"]

            def stats_mm(dm):
                @blk.tensor
                def _(e):
                    p.wait(e, ("ac", sqr_ac[dm]))
                    e.matmul(PS[4][0:1, :], ones_c[:], y2b[:],
                             start=(dm == 0), stop=(dm == NJ - 1), skip_group_check=True)
                    i = e.matmul(PS[5][0:1, :], ones_c[:], y2s[:],
                                 start=(dm == 0), stop=(dm == NJ - 1), skip_group_check=True)
                    p.inc(i, "pe")

            def down_evac(dm):
                po = PS[dm % 2]
                pe_snap = p.C["pe"]

                @blk.vector
                def _(e):
                    p.wait(e, ("d%d" % (dm % 2), res_lx[dm]))
                    e.nop()
                    p.wait(e, ("pe", pe_snap))
                    e.tensor_tensor(y2[:, dm * 512:(dm + 1) * 512], po[:], res[dm % 2][:], ALU.add)
                    i = e.tensor_copy(y2b[:], y2[:, dm * 512:(dm + 1) * 512])
                    p.inc(i, "ve")
                y2_ve[dm] = p.C["ve"]

                @blk.scalar
                def _(e):
                    p.wait(e, ("ve", y2_ve[dm]))
                    i = e.activation(y2s[:], y2[:, dm * 512:(dm + 1) * 512], AF.Square)
                    p.inc(i, "ac")
                sqr_ac[dm] = p.C["ac"]

            def prefetch_next(dm):
                @blk.gpsimd
                def _(e):
                    p.wait(e, ("pe", dmm_pe[dm]), ("ve", y2_ve[dm]))
                    i = e.dma_start(out=w2bt[dm % 2][:], in_=w2b_scr[dm + 2])
                    i2 = e.dma_start(out=res[dm % 2][:], in_=x2[sq, dm + 2, :, trs:trs + 512])
                    i.then_inc(p.sems["d%d" % (dm % 2)], 16)
                    p.C["d%d" % (dm % 2)] += 16
                    w2b_lw[dm + 2] = p.inc(i2, "d%d" % (dm % 2), 16)
                    res_lx[dm + 2] = w2b_lw[dm + 2]

            for dm in range(NJ):
                down_mm(dm)
                if dm > 0:
                    stats_mm(dm - 1)
                down_evac(dm)
                if dm + 2 < NJ:
                    prefetch_next(dm)
            stats_mm(NJ - 1)

            @blk.vector
            def _(e):
                p.wait(e, ("pe", p.C["pe"]))
                e.tensor_scalar(srow[0:1, 0:512], PS[4][0:1, :], 1.0 / D, None, ALU.mult)
                e.tensor_scalar(srow[0:1, 512:1024], PS[5][0:1, :], 1.0 / D, None, ALU.mult)
                e.tensor_tensor(srow[0:1, 1024:1536], srow[0:1, 0:512], srow[0:1, 0:512], ALU.mult)
                e.tensor_tensor(srow[0:1, 512:1024], srow[0:1, 512:1024], srow[0:1, 1024:1536], ALU.subtract)
                i = e.tensor_scalar(srow[0:1, 512:1024], srow[0:1, 512:1024], LN_EPS, None, ALU.add)
                p.inc(i, "ve")

            @blk.scalar
            def _(e):
                p.wait(e, ("ve", p.C["ve"]))
                i = e.activation(srow[0:1, 512:1024], srow[0:1, 512:1024], AF.Sqrt)
                p.inc(i, "ac")

            @blk.vector
            def _(e):
                p.wait(e, ("ac", p.C["ac"]))
                e.reciprocal(srow[0:1, 512:1024], srow[0:1, 512:1024])
                i = e.tensor_tensor(srow[0:1, 1024:1536], srow[0:1, 0:512], srow[0:1, 512:1024], ALU.mult)
                p.inc(i, "ve")

            @blk.tensor
            def _(e):
                p.wait(e, ("ve", p.C["ve"]))
                e.matmul(PS[2][:], ones_r[:], srow[0:1, 512:1024], start=True, stop=True,
                         skip_group_check=True)
                i = e.matmul(PS[3][:], ones_r[:], srow[0:1, 1024:1536], start=True, stop=True,
                             skip_group_check=True)
                p.inc(i, "pe")

            @blk.scalar
            def _(e):
                p.wait(e, ("pe", p.C["pe"]))
                e.copy(rstdB[:], PS[2][:])
                i = e.copy(mrB[:], PS[3][:])
                p.inc(i, "ac")
            ln_ac = p.C["ac"]

            for dm in range(NJ):
                xo = x3t[dm % 2]
                xs = "x%d" % (dm % 2)

                @blk.vector
                def _(e):
                    p.wait(e, ("ac", ln_ac), (xs, p.C[xs]))
                    e.tensor_tensor(xo[:], y2[:, dm * 512:(dm + 1) * 512], rstdB[:], ALU.mult)
                    i = e.tensor_tensor(xo[:], xo[:], mrB[:], ALU.subtract)
                    p.inc(i, "ve")

                @blk.gpsimd
                def _(e):
                    p.wait(e, ("ve", p.C["ve"]))
                    i = e.dma_start(out=x3[sq, dm, :, trs:trs + 512], in_=xo[:])
                    p.inc(i, xs, 16)

        @blk.gpsimd
        def _(e):
            p.wait(e, ("x0", p.C["x0"]))
            e.nop()
            e.wait_ge(p.sems["x1"], p.C["x1"])

    pes.close()
    esB.close()


def prep_B_weights(W1, W2, W3):
    w1h = np.ascontiguousarray(
        W1.reshape(NC0, 128, NJ, 128).transpose(0, 3, 2, 1).reshape(NC0, 128, NJ * 128)).astype(np.float32)
    w3h = np.ascontiguousarray(
        W3.reshape(NC0, 128, NJ, 128).transpose(0, 3, 2, 1).reshape(NC0, 128, NJ * 128)).astype(np.float32)
    w2r = np.ascontiguousarray(W2.reshape(D, 64, 128).transpose(1, 2, 0)).astype(np.float32)

    def had(k):
        Hm = np.array([[1.0]], np.float32)
        while Hm.shape[0] < k:
            Hm = np.block([[Hm, Hm], [Hm, -Hm]])
        return Hm
    h64 = had(64).astype(BF)
    h128 = had(128).astype(BF)
    return dict(w1h=w1h, w3h=w3h, w2r=w2r, h64=h64, h128=h128)





def emit_C(p, blk, S=SEQ):
    nc = p.nc
    NST = S // 512
    SC = float(1.0 / np.sqrt(128.0))
    x3 = p.aps["x3"]
    wqkv = p.aps["wqkv"]
    bqkv = p.aps["bqkv"]
    wout = p.aps["wout"]
    bout = p.aps["bout"]
    maskT = p.aps["maskT"]
    maskS = p.aps["maskS"]
    idb = p.aps["idb"]
    idf = p.aps["idf"]
    x4 = p.aps["x4"]
    wqkv_scr = p.aps["wqkv_scr"]
    wout_scr = p.aps["wout_scr"]
    qscr = p.aps["qscr"]
    kscr = p.aps["kscr"]
    vscr = p.aps["vscr"]
    ones_r = p.ones_r
    ones_c = p.ones_c

    esC = contextlib.ExitStack()
    maskT_sb = p.sb("maskT_sb", [128, 128], F32, es=esC)
    maskS_sb = p.sb("maskS_sb", [128, 128], F32, es=esC)
    idb_sb = p.sb("idb_sb", [128, 128], BF16, es=esC)
    idf_sb = p.sb("idf_sb", [128, 128], F32, es=esC)
    bqkv_sb = p.sb("bqkv_sb", [128, 48], F32, es=esC)
    bout_sb = p.sb("bout_sb", [128, NJ], F32, es=esC)

    if True:
        pes = contextlib.ExitStack()
        brow = p.sb("browC", [1, 48 * 128], F32, es=pes)
        borow = p.sb("borowC", [1, NJ * 128], F32, es=pes)
        pbias = p.ps("pbiasC", [128, 512], F32, es=pes)

        @blk.sync
        def _(e):
            for dst, srcap in ((brow[:], bqkv[:, :]), (borow[:], bout[:, :]),
                               (maskT_sb[:], maskT[:, :]), (maskS_sb[:], maskS[:, :]),
                               (idb_sb[:], idb[:, :]), (idf_sb[:], idf[:, :])):
                i = e.dma_start(out=dst, in_=srcap)
                p.inc(i, "lx", 16)

        @blk.tensor
        def _(e):
            p.wait(e, ("lx", p.C["lx"]), ("ve", p.init_ve))
            for t in range(48):
                i = e.matmul(pbias[:, t:t + 1], brow[0:1, t * 128:(t + 1) * 128],
                             ones_r[0:1, 0:1], start=True, stop=True, skip_group_check=True)
            p.inc(i, "pe")

        @blk.scalar
        def _(e):
            p.wait(e, ("pe", p.C["pe"]))
            i = e.copy(bqkv_sb[:, 0:48], pbias[:, 0:48])
            p.inc(i, "ac")

        @blk.tensor
        def _(e):
            p.wait(e, ("ac", p.C["ac"]))
            for t in range(NJ):
                i = e.matmul(pbias[:, t:t + 1], borow[0:1, t * 128:(t + 1) * 128],
                             ones_r[0:1, 0:1], start=True, stop=True, skip_group_check=True)
            p.inc(i, "pe")

        @blk.scalar
        def _(e):
            p.wait(e, ("pe", p.C["pe"]))
            i = e.copy(bout_sb[:, 0:NJ], pbias[:, 0:NJ])
            p.inc(i, "ac")

        wtmp = [p.sb(f"wtmpC{i}", [128, 2048], F32, es=pes) for i in range(2)]
        wbt = [p.sb(f"wbtC{i}", [128, 2048], BF16, es=pes) for i in range(2)]
        for t in range(48 + NJ):
            src_t = wqkv[t] if t < 48 else wout[t - 48]
            dst_t = wqkv_scr[t] if t < 48 else wout_scr[t - 48]
            buf = wtmp[t % 2]
            bbuf = wbt[t % 2]
            ws = "w%d" % (t % 2)
            ss = "s%d" % (t % 2)

            @blk.gpsimd
            def _(e):
                p.wait(e, ("ve", p.C["ve"] - 1))
                i = e.dma_start(out=buf[:], in_=src_t)
                p.inc(i, ws, 16)

            @blk.vector
            def _(e):
                p.wait(e, (ws, p.C[ws]), (ss, p.C[ss]))
                i = e.tensor_copy(bbuf[:], buf[:])
                p.inc(i, "ve")

            @blk.sync
            def _(e):
                p.wait(e, ("ve", p.C["ve"]))
                i = e.dma_start(out=dst_t, in_=bbuf[:])
                p.inc(i, ss, 16)

        p.barrier(blk)
        pes.close()

        pes = contextlib.ExitStack()
        xstg = p.sb("xstgC", [128, S], F32, es=pes)
        xb = p.sb("xbC", [128, NJ * S], BF16, es=pes)
        wqb = [p.sb(f"wqbC{i}", [128, 2048], BF16, es=pes) for i in range(2)]
        ev = [p.sb(f"evC{i}", [128, 512], BF16, es=pes) for i in range(2)]
        vtv = [p.sb(f"vtvC{i}", [128, 512], BF16, es=pes) for i in range(2)]
        PSq = [p.ps(f"PSqC{i}", [128, 512], F32, es=pes) for i in range(2)]
        PSt = [p.ps(f"PStC{i}", [128, 512], BF16, es=pes) for i in range(2)]

        for sq in range(NSEQ):
            for j in range(NJ):
                @blk.sync
                def _(e):
                    if j > 0 or sq > 0:
                        p.wait(e, ("ve", p.C["ve"]))
                    if sq > 0 and j == 0:
                        e.nop()
                        p.wait(e, ("pe", p.C["pe"]))
                    i = e.dma_start(out=xstg[:], in_=x3[sq, j])
                    p.inc(i, "lx", 16)

                @blk.vector
                def _(e):
                    p.wait(e, ("lx", p.C["lx"]))
                    i = e.tensor_copy(xb[:, j * S:(j + 1) * S], xstg[:])
                    p.inc(i, "ve")
            xcast_ve = p.C["ve"]

            @blk.gpsimd
            def _(e, xcast_ve=xcast_ve):
                p.wait(e, ("ve", xcast_ve))
                i = e.dma_start(out=wqb[0][:], in_=wqkv_scr[0])
                p.inc(i, "q0", 16)

            proj_pe = {}
            use_ve = {}
            for o in range(48):
                par = o % 2
                is_v = o >= 32

                @blk.gpsimd
                def _(e, o=o):
                    if o + 1 < 48:
                        p.wait(e, ("pe", proj_pe.get(o - 1, 0)))
                        i = e.dma_start(out=wqb[(o + 1) % 2][:], in_=wqkv_scr[o + 1])
                        p.inc(i, "q%d" % ((o + 1) % 2), 16)

                for st in range(NST):
                    fidx = o * NST + st
                    stq = fidx % 2

                    @blk.tensor
                    def _(e, o=o, st=st, stq=stq, par=par, fidx=fidx):
                        wv = use_ve.get(divmod(fidx - 2, NST), 0) if fidx >= 2 else 0
                        p.wait(e, ("q%d" % par, p.C["q%d" % par]),
                               ("ve", max(xcast_ve, wv)))
                        for j in range(NJ):
                            i = e.matmul(PSq[stq][:], wqb[par][:, j * 128:(j + 1) * 128],
                                         xb[:, j * S + st * 512: j * S + st * 512 + 512],
                                         start=(j == 0), stop=(j == NJ - 1), skip_group_check=True)
                        p.inc(i, "pe")
                    proj_pe[o] = p.C["pe"]
                    mm_pe = p.C["pe"]

                    @blk.vector
                    def _(e, o=o, st=st, stq=stq, mm_pe=mm_pe):
                        p.wait(e, ("pe", mm_pe), ("s%d" % stq, p.C["s%d" % stq]))
                        i = e.tensor_scalar(ev[stq][:], PSq[stq][:], bqkv_sb[:, o:o + 1], None, ALU.add)
                        p.inc(i, "ve")
                    bias_ve = p.C["ve"]

                    if not is_v:
                        use_ve[(o, st)] = bias_ve

                        @blk.sync
                        def _(e, o=o, st=st, stq=stq, bias_ve=bias_ve):
                            p.wait(e, ("ve", bias_ve))
                            dst = qscr if o < 16 else kscr
                            hh = o if o < 16 else o - 16
                            i = e.dma_start(out=dst[sq, hh, :, st * 512:(st + 1) * 512], in_=ev[stq][:])
                            p.inc(i, "s%d" % stq, 16)
                    else:
                        @blk.tensor
                        def _(e, st=st, stq=stq, bias_ve=bias_ve):
                            p.wait(e, ("ve", bias_ve))
                            for u in range(4):
                                i = e.transpose(PSt[stq][:, u * 128:(u + 1) * 128],
                                                ev[stq][:, u * 128:(u + 1) * 128], idb_sb[:])
                            p.inc(i, "pe")
                        tr_pe = p.C["pe"]

                        @blk.vector
                        def _(e, stq=stq, tr_pe=tr_pe):
                            p.wait(e, ("pe", tr_pe))
                            i = e.tensor_copy(vtv[stq][:], PSt[stq][:])
                            p.inc(i, "ve")
                        use_ve[(o, st)] = p.C["ve"]

                        @blk.sync
                        def _(e, o=o, st=st, stq=stq):
                            p.wait(e, ("ve", use_ve[(o, st)]))
                            hh = o - 32
                            dst = vscr[sq, hh, st * 512:(st + 1) * 512, :].rearrange(
                                "(u s) e -> s u e", u=4)
                            i = e.dma_start(out=dst, in_=vtv[stq][:].rearrange("s (u e) -> s u e", u=4))
                            p.inc(i, "s%d" % stq, 16)

        p.barrier(blk)
        pes.close()

        pes = contextlib.ExitStack()
        qt_b = [p.sb(f"qtC{i}", [128, 512], BF16, es=pes) for i in range(2)]
        kt_b = [p.sb(f"ktC{i}", [128, S], BF16, es=pes) for i in range(2)]
        vt_b = [p.sb(f"vtC{i}", [128, S], BF16, es=pes) for i in range(2)]
        ET2 = [p.sb(f"ETC{i}", [128, S // 128, 512], BF16, es=pes) for i in range(2)]
        E_sb = [p.sb(f"EsbC{i}", [128, 2048], BF16, es=pes) for i in range(2)]
        tmpcol = p.sb("tmpcolC", [128, 1], F32, es=pes)
        rmneg = [p.sb(f"rmnegC{i}", [128, 1], F32, es=pes) for i in range(2)]
        rrow = p.sb("rrowC", [1, 1024], F32, es=pes)
        recB = p.sb("recBC", [128, 512], F32, es=pes)
        rmax = p.sb("rmaxC", [128, 1], F32, es=pes)
        ostrip = p.sb("ostripC", [128, H * 512], BF16, es=pes)
        wob = [p.sb(f"wobC{i}", [128, 2048], BF16, es=pes) for i in range(2)]
        resb = [p.sb(f"resbC{i}", [128, 512], F32, es=pes) for i in range(2)]
        y1 = p.sb("y1C", [128, NJ * 512], F32, es=pes)
        y1b = p.sb("y1bC", [128, 512], BF16, es=pes)
        y1s = p.sb("y1sC", [128, 512], BF16, es=pes)
        srow = p.sb("srowC", [1, 1536], F32, es=pes)
        rstdB = p.sb("rstdBC", [128, 512], F32, es=pes)
        mrB = p.sb("mrBC", [128, 512], F32, es=pes)
        x4t = [p.sb(f"x4tC{i}", [128, 512], F32, es=pes) for i in range(2)]
        PS = [p.ps(f"PSC{i}", [128, 512], F32, es=pes) for i in range(6)]
        PTB = [p.ps(f"PTBC{i}", [128, 512], BF16, es=pes) for i in range(2)]
        tr_g = [0]
        tr_evac_ve = {}

        for sq in range(NSEQ):
            for qb in range(NST):
                nkt = (qb + 1) * 4
                kv_ve_prev = p.C["ve"]
                kv_done = {}

                @blk.gpsimd
                def _(e, sq=sq, qb=qb, nkt=nkt, kv_ve_prev=kv_ve_prev):
                    p.wait(e, ("pe", p.C["pe"]))
                    for hp in range(2):
                        ks = "k%d" % hp
                        i = e.dma_start(out=qt_b[hp][:], in_=qscr[sq, hp, :, qb * 512:(qb + 1) * 512])
                        p.inc(i, ks, 16)
                        i = e.dma_start(out=kt_b[hp][:, 0:nkt * 128], in_=kscr[sq, hp, :, 0:nkt * 128])
                        p.inc(i, ks, 16)
                        i = e.dma_start(out=vt_b[hp][:, 0:nkt * 128],
                                        in_=vscr[sq, hp, 0:nkt * 128, :].rearrange("(u s) e -> s u e", s=128))
                        kv_done[hp] = p.inc(i, ks, 16)

                wo_lw = {}

                @blk.sync
                def _(e, qb=qb):
                    p.wait(e, ("ve", kv_ve_prev))
                    for dmp in range(2):
                        i = e.dma_start(out=wob[dmp][:], in_=wout_scr[dmp])
                        wo_lw[dmp] = p.inc(i, "o%d" % dmp, 16)
                        i = e.dma_start(out=resb[dmp][:], in_=x3[sq, dmp, :, qb * 512:(qb + 1) * 512])
                        p.C["o%d" % dmp] += 16
                        i.then_inc(p.sems["o%d" % dmp], 16)
                        wo_lw[dmp] = p.C["o%d" % dmp]

                ostrip_ve = {}
                etdone_ve = {}
                last_exp_ac = {}
                den_pe = {}
                qb_ac0 = p.C["ac"]
                nkb = qb + 1

                def head_front(h):
                    hp = h % 2
                    eb = ET2[h % 2]
                    exp_acs = {}
                    for qt in range(4):
                        base = (qt % 2) * nkb if nkb <= 2 else 0

                        @blk.tensor
                        def _(e, h=h, hp=hp, qt=qt, base=base):
                            if qt == 0:
                                p.wait(e, ("k%d" % hp, kv_done[h]))
                                e.nop()
                                if h >= 1:
                                    p.wait(e, ("ac", last_exp_ac[h - 1]))
                                else:
                                    p.wait(e, ("ac", qb_ac0))
                            elif nkb <= 2 and qt >= 2:
                                p.wait(e, ("ac", exp_acs[qt - 2]))
                            elif nkb > 2:
                                p.wait(e, ("ac", exp_acs[qt - 1]))
                            for kb in range(nkb):
                                i = e.matmul(PS[base + kb][:], qt_b[hp][:, qt * 128:(qt + 1) * 128],
                                             kt_b[hp][:, kb * 512:(kb + 1) * 512],
                                             start=True, stop=True, skip_group_check=True)
                            p.inc(i, "pe")
                        smm_pe = p.C["pe"]

                        @blk.vector
                        def _(e, qt=qt, qb=qb, base=base, smm_pe=smm_pe):
                            p.wait(e, ("pe", smm_pe))
                            for kb in range(nkb):
                                bank = PS[base + kb]
                                if kb == qb:
                                    e.tensor_tensor(bank[:, qt * 128:(qt + 1) * 128],
                                                    bank[:, qt * 128:(qt + 1) * 128],
                                                    maskS_sb[:], ALU.add)
                                    hi = qt * 128 + 128
                                else:
                                    hi = 512
                                if kb == 0:
                                    e.reduce_max(rmax[:], bank[:, 0:hi], axis=AX.X)
                                else:
                                    e.reduce_max(tmpcol[:], bank[:, 0:hi], axis=AX.X)
                                    e.tensor_tensor(rmax[:], rmax[:], tmpcol[:], ALU.max)
                            i = e.tensor_scalar(rmneg[qt % 2][:], rmax[:], -SC, None, ALU.mult)
                            p.inc(i, "ve")
                        mx_ve = p.C["ve"]

                        @blk.scalar
                        def _(e, qt=qt, qb=qb, base=base, mx_ve=mx_ve):
                            p.wait(e, ("ve", mx_ve))
                            for kb in range(nkb):
                                hi = qt * 128 + 128 if kb == qb else 512
                                i = e.activation(E_sb[qt % 2][:, kb * 512:kb * 512 + hi],
                                                 PS[base + kb][:, 0:hi], AF.Exp,
                                                 bias=rmneg[qt % 2][:], scale=SC)
                            p.inc(i, "ac")
                        exp_acs[qt] = p.C["ac"]

                        for kb in range(nkb):
                            g = tr_g[0]
                            tr_g[0] += 1
                            pt = PTB[g % 2]
                            vks = (qt + 1) if kb == qb else 4

                            @blk.tensor
                            def _(e, qt=qt, kb=kb, g=g, pt=pt, vks=vks, ea=exp_acs[qt]):
                                p.wait(e, ("ac", ea))
                                if g >= 2:
                                    e.nop()
                                    p.wait(e, ("ve", tr_evac_ve[g - 2]))
                                for ks in range(vks):
                                    i = e.transpose(pt[:, ks * 128:(ks + 1) * 128],
                                                    E_sb[qt % 2][:, kb * 512 + ks * 128:kb * 512 + (ks + 1) * 128],
                                                    idb_sb[:])
                                p.inc(i, "pe")
                            tr_pe = p.C["pe"]

                            @blk.vector
                            def _(e, qt=qt, kb=kb, qb=qb, pt=pt, vks=vks, eb=eb, tr_pe=tr_pe):
                                p.wait(e, ("pe", tr_pe))
                                i = e.tensor_copy(eb[:, kb * 4:kb * 4 + vks, qt * 128:(qt + 1) * 128],
                                                  pt[:, 0:vks * 128].rearrange("s (u w) -> s u w", w=128))
                                if kb == qb and qt < 3:
                                    i = e.memset(eb[:, kb * 4 + qt + 1:kb * 4 + 4, qt * 128:(qt + 1) * 128], 0.0)
                                p.inc(i, "ve")
                            tr_evac_ve[g] = p.C["ve"]
                    etdone_ve[h] = p.C["ve"]
                    last_exp_ac[h] = exp_acs[3]

                def head_back(h):
                    hp = h % 2
                    eb = ET2[h % 2]

                    @blk.tensor
                    def _(e, h=h, hp=hp, eb=eb):
                        p.wait(e, ("ve", etdone_ve[h]))
                        e.nop()
                        p.wait(e, ("ve", ostrip_ve[h - 1] if h >= 1 else kv_ve_prev))
                        for kt in range(nkt):
                            i = e.matmul(PS[5][0:1, :], ones_c[:], eb[:, kt, :],
                                         start=(kt == 0), stop=(kt == nkt - 1), skip_group_check=True)
                        for kt in range(nkt):
                            i = e.matmul(PS[4][:], vt_b[hp][:, kt * 128:(kt + 1) * 128],
                                         eb[:, kt, :],
                                         start=(kt == 0), stop=(kt == nkt - 1), skip_group_check=True)
                        p.inc(i, "pe")
                    den_pe[h] = p.C["pe"]

                    @blk.vector
                    def _(e, h=h):
                        p.wait(e, ("pe", den_pe[h]))
                        i = e.tensor_scalar(rrow[0:1, 0:512], PS[5][0:1, :], 1e-30, None, ALU.add)
                        p.inc(i, "ve")
                    rec_ve = p.C["ve"]

                    @blk.tensor
                    def _(e, rec_ve=rec_ve):
                        p.wait(e, ("ve", rec_ve))
                        i = e.matmul(PS[5][:], ones_r[:], rrow[0:1, 0:512],
                                     start=True, stop=True, skip_group_check=True)
                        p.inc(i, "pe")
                    rb_pe = p.C["pe"]

                    @blk.vector
                    def _(e, h=h, rb_pe=rb_pe):
                        p.wait(e, ("pe", rb_pe))
                        e.reciprocal(recB[:], PS[5][:])
                        i = e.tensor_tensor(ostrip[:, h * 512:(h + 1) * 512], PS[4][:], recB[:], ALU.mult)
                        p.inc(i, "ve")
                    ostrip_ve[h] = p.C["ve"]

                    if h + 2 < H:
                        @blk.gpsimd
                        def _(e, sq=sq, qb=qb, h=h, hp=hp, nkt=nkt):
                            p.wait(e, ("pe", den_pe[h]))
                            ks = "k%d" % hp
                            i = e.dma_start(out=qt_b[hp][:], in_=qscr[sq, h + 2, :, qb * 512:(qb + 1) * 512])
                            p.inc(i, ks, 16)
                            i = e.dma_start(out=kt_b[hp][:, 0:nkt * 128], in_=kscr[sq, h + 2, :, 0:nkt * 128])
                            p.inc(i, ks, 16)
                            i = e.dma_start(out=vt_b[hp][:, 0:nkt * 128],
                                            in_=vscr[sq, h + 2, 0:nkt * 128, :].rearrange("(u s) e -> s u e", s=128))
                            kv_done[h + 2] = p.inc(i, ks, 16)

                for h in range(H):
                    head_front(h)
                    if h > 0:
                        head_back(h - 1)
                head_back(H - 1)

                attn_done_ve = p.C["ve"]
                y1_ve = {}
                sqr_ac = {}
                wo_pe = {}

                def wo_mm_tensor(dm):
                    po = PS[dm % 2]

                    @blk.tensor
                    def _(e, dm=dm, po=po):
                        p.wait(e, ("o%d" % (dm % 2), wo_lw[dm]))
                        e.nop()
                        if dm == 0:
                            p.wait(e, ("ve", attn_done_ve))
                        elif dm >= 2:
                            p.wait(e, ("ve", y1_ve[dm - 2]))
                        for hh in range(H):
                            i = e.matmul(po[:], wob[dm % 2][:, hh * 128:(hh + 1) * 128],
                                         ostrip[:, hh * 512:(hh + 1) * 512],
                                         start=(hh == 0), stop=(hh == H - 1), skip_group_check=True)
                        p.inc(i, "pe")
                    wo_pe[dm] = p.C["pe"]

                def stats_mm(dm):
                    @blk.tensor
                    def _(e, dm=dm):
                        p.wait(e, ("ac", sqr_ac[dm]))
                        e.matmul(PS[4][0:1, :], ones_c[:], y1b[:],
                                 start=(dm == 0), stop=(dm == NJ - 1), skip_group_check=True)
                        i = e.matmul(PS[5][0:1, :], ones_c[:], y1s[:],
                                     start=(dm == 0), stop=(dm == NJ - 1), skip_group_check=True)
                        p.inc(i, "pe")

                def wo_evac(dm):
                    po = PS[dm % 2]
                    pe_snap = p.C["pe"]

                    @blk.vector
                    def _(e, dm=dm, po=po, pe_snap=pe_snap):
                        p.wait(e, ("pe", pe_snap))
                        e.tensor_scalar(y1[:, dm * 512:(dm + 1) * 512], po[:],
                                        bout_sb[:, dm:dm + 1], None, ALU.add)
                        e.tensor_tensor(y1[:, dm * 512:(dm + 1) * 512],
                                        y1[:, dm * 512:(dm + 1) * 512], resb[dm % 2][:], ALU.add)
                        i = e.tensor_copy(y1b[:], y1[:, dm * 512:(dm + 1) * 512])
                        p.inc(i, "ve")
                    y1_ve[dm] = p.C["ve"]

                    @blk.scalar
                    def _(e, dm=dm):
                        p.wait(e, ("ve", y1_ve[dm]))
                        i = e.activation(y1s[:], y1[:, dm * 512:(dm + 1) * 512], AF.Square)
                        p.inc(i, "ac")
                    sqr_ac[dm] = p.C["ac"]

                def prefetch_wo(dm):
                    @blk.sync
                    def _(e, dm=dm, sq=sq, qb=qb):
                        p.wait(e, ("pe", wo_pe[dm]), ("ve", y1_ve[dm]))
                        i = e.dma_start(out=wob[dm % 2][:], in_=wout_scr[dm + 2])
                        p.C["o%d" % (dm % 2)] += 16
                        i.then_inc(p.sems["o%d" % (dm % 2)], 16)
                        i = e.dma_start(out=resb[dm % 2][:], in_=x3[sq, dm + 2, :, qb * 512:(qb + 1) * 512])
                        wo_lw[dm + 2] = p.inc(i, "o%d" % (dm % 2), 16)

                for dm in range(NJ):
                    wo_mm_tensor(dm)
                    if dm > 0:
                        stats_mm(dm - 1)
                    wo_evac(dm)
                    if dm + 2 < NJ:
                        prefetch_wo(dm)
                stats_mm(NJ - 1)

                @blk.vector
                def _(e):
                    p.wait(e, ("pe", p.C["pe"]))
                    e.tensor_scalar(srow[0:1, 0:512], PS[4][0:1, :], 1.0 / D, None, ALU.mult)
                    e.tensor_scalar(srow[0:1, 512:1024], PS[5][0:1, :], 1.0 / D, None, ALU.mult)
                    e.tensor_tensor(srow[0:1, 1024:1536], srow[0:1, 0:512], srow[0:1, 0:512], ALU.mult)
                    e.tensor_tensor(srow[0:1, 512:1024], srow[0:1, 512:1024], srow[0:1, 1024:1536], ALU.subtract)
                    i = e.tensor_scalar(srow[0:1, 512:1024], srow[0:1, 512:1024], LN_EPS, None, ALU.add)
                    p.inc(i, "ve")

                @blk.scalar
                def _(e):
                    p.wait(e, ("ve", p.C["ve"]))
                    i = e.activation(srow[0:1, 512:1024], srow[0:1, 512:1024], AF.Sqrt)
                    p.inc(i, "ac")

                @blk.vector
                def _(e):
                    p.wait(e, ("ac", p.C["ac"]))
                    e.reciprocal(srow[0:1, 512:1024], srow[0:1, 512:1024])
                    i = e.tensor_tensor(srow[0:1, 1024:1536], srow[0:1, 0:512], srow[0:1, 512:1024], ALU.mult)
                    p.inc(i, "ve")

                @blk.tensor
                def _(e):
                    p.wait(e, ("ve", p.C["ve"]))
                    e.matmul(PS[2][:], ones_r[:], srow[0:1, 512:1024], start=True, stop=True,
                             skip_group_check=True)
                    i = e.matmul(PS[3][:], ones_r[:], srow[0:1, 1024:1536], start=True, stop=True,
                                 skip_group_check=True)
                    p.inc(i, "pe")

                @blk.scalar
                def _(e):
                    p.wait(e, ("pe", p.C["pe"]))
                    e.copy(rstdB[:], PS[2][:])
                    i = e.copy(mrB[:], PS[3][:])
                    p.inc(i, "ac")
                ln_ac = p.C["ac"]

                for dm in range(NJ):
                    xo = x4t[dm % 2]
                    xs = "x%d" % (dm % 2)

                    @blk.vector
                    def _(e, dm=dm, xo=xo, xs=xs, ln_ac=ln_ac):
                        p.wait(e, ("ac", ln_ac), (xs, p.C[xs]))
                        e.tensor_tensor(xo[:], y1[:, dm * 512:(dm + 1) * 512], rstdB[:], ALU.mult)
                        i = e.tensor_tensor(xo[:], xo[:], mrB[:], ALU.subtract)
                        p.inc(i, "ve")

                    @blk.gpsimd
                    def _(e, dm=dm, sq=sq, qb=qb, xo=xo, xs=xs):
                        p.wait(e, ("ve", p.C["ve"]))
                        i = e.dma_start(out=x4[sq, dm, :, qb * 512:(qb + 1) * 512], in_=xo[:])
                        p.inc(i, xs, 16)

        @blk.gpsimd
        def _(e):
            p.wait(e, ("x0", p.C["x0"]))
            e.nop()
            e.wait_ge(p.sems["x1"], p.C["x1"])

    pes.close()
    esC.close()


def prep_C_weights(W_qkv, b_qkv, W_out, b_out):
    wqkv = np.ascontiguousarray(
        W_qkv.reshape(48, 128, NJ, 128).transpose(0, 3, 2, 1).reshape(48, 128, NJ * 128))
    bq_r = np.ascontiguousarray(b_qkv.reshape(1, 48 * 128))
    wo_t = np.ascontiguousarray(
        W_out.reshape(NJ, 128, H, 128).transpose(0, 3, 2, 1).reshape(NJ, 128, H * 128))
    bo_r = np.ascontiguousarray(b_out.reshape(1, NJ * 128))
    kk, qq = np.meshgrid(np.arange(128), np.arange(128), indexing="ij")
    maskT = np.where(kk > qq, -1e30, 0.0).astype(np.float32)
    maskS = maskT.T.copy()
    idb = np.eye(128, dtype=BF)
    idf = np.eye(128, dtype=np.float32)
    return dict(wqkv=wqkv, bqkv=bq_r, wout=wo_t, bout=bo_r,
                maskT=maskT, maskS=maskS, idb=idb, idf=idf)





NC1 = 43


def emit_D(p, blk, S=SEQ):
    nc = p.nc
    T = NSEQ * S
    NTS = T // 512
    x4 = p.aps["x4"]
    gidx = p.aps["gidx"]
    uidx = p.aps["uidx"]
    didx = p.aps["didx"]
    lut = p.aps["lut"]
    yout = p.aps["yout"]
    wg_scr = p.aps["wg_scr"]
    wu_scr = p.aps["wu_scr"]
    wd_scr = p.aps["wd_scr"]
    hscr = p.aps["hscr"]
    ones_r = p.ones_r
    ones_c = p.ones_c

    esD = contextlib.ExitStack()
    lut_row = p.sb("lut_row", [1, 16], F32, es=esD)
    lutB = p.sb("lutB", [128, 16], F32, es=esD)

    if True:
        pes = contextlib.ExitStack()
        pb = p.ps("pbD", [128, 512], F32, es=pes)

        @blk.sync
        def _(e):
            i = e.dma_start(out=lut_row[:], in_=lut[:, :])
            p.inc(i, "lx", 16)
        lut_lx = p.C["lx"]

        @blk.tensor
        def _(e):
            p.wait(e, ("lx", lut_lx), ("ve", p.init_ve))
            i = e.matmul(pb[:, 0:16], ones_r[:], lut_row[:], start=True, stop=True,
                         skip_group_check=True)
            p.inc(i, "pe")

        @blk.scalar
        def _(e):
            p.wait(e, ("pe", p.C["pe"]))
            i = e.copy(lutB[:], pb[:, 0:16])
            p.inc(i, "ac")
        lut_ac = p.C["ac"]

        itile = [p.sb(f"itD{i}", [128, NC1 * 128], I32, es=pes) for i in range(2)]
        btile = [p.sb(f"btD{i}", [128, NC1 * 128], BF16, es=pes) for i in range(2)]
        atile = [p.sb(f"atD{i}", [128, NC1 * 128], BF16, es=pes) for i in range(2)]
        ttile = p.sb("ttD", [128, NC1 * 128], BF16, es=pes)

        jobs = []
        for c in range(NC1):
            jobs.append((gidx[c], wg_scr[c], NJ * 128))
            jobs.append((uidx[c], wu_scr[c], NJ * 128))
        for dm in range(NJ):
            jobs.append((didx[dm], wd_scr[dm], NC1 * 128))

        cast_ve = {}
        done_ve = {}
        store_ms = {}
        for jn, (iap, oap, ncols) in enumerate(jobs):
            par = jn % 2
            ibuf, bbuf, abuf = itile[par], btile[par], atile[par]
            isem = "i%d" % par
            msem = "m%d" % par
            dmaq = blk.sync if par == 0 else blk.scalar

            @dmaq
            def _(e, jn=jn, iap=iap, ibuf=ibuf, isem=isem, ncols=ncols):
                if jn >= 2:
                    p.wait(e, ("ve", cast_ve[jn - 2]))
                i = e.dma_start(out=ibuf[:, 0:ncols], in_=iap)
                p.inc(i, isem, 16)
            load_cnt = p.C[isem]

            @blk.vector
            def _(e, jn=jn, ibuf=ibuf, bbuf=bbuf, abuf=abuf, isem=isem,
                  msem=msem, ncols=ncols, load_cnt=load_cnt):
                p.wait(e, (isem, load_cnt), ("ac", lut_ac))
                i = e.tensor_copy(bbuf[:, 0:ncols], ibuf[:, 0:ncols])
                p.inc(i, "ve")
                cast_ve[jn] = p.C["ve"]
                if jn >= 2:
                    e.nop()
                    p.wait(e, (msem, store_ms[jn - 2]))
                e.tensor_scalar(abuf[:, 0:ncols], bbuf[:, 0:ncols], 0.0, lutB[:, 0:1],
                                ALU.is_equal, ALU.mult)
                for j in range(1, 16):
                    e.tensor_scalar(ttile[:, 0:ncols], bbuf[:, 0:ncols], float(j),
                                    lutB[:, j:j + 1], ALU.is_equal, ALU.mult)
                    i = e.tensor_tensor(abuf[:, 0:ncols], abuf[:, 0:ncols], ttile[:, 0:ncols],
                                        ALU.add)
                p.inc(i, "ve")
                done_ve[jn] = p.C["ve"]

            @blk.gpsimd
            def _(e, jn=jn, oap=oap, abuf=abuf, msem=msem, ncols=ncols):
                p.wait(e, ("ve", done_ve[jn]))
                i = e.dma_start(out=oap, in_=abuf[:, 0:ncols])
                p.inc(i, msem, 16)
            store_ms[jn] = p.C[msem]

        p.barrier(blk)
        pes.close()

        pes = contextlib.ExitStack()
        xb = p.sb("xbD", [128, NJ * T], BF16, es=pes)
        xstg = p.sb("xstgD", [128, T], F32, es=pes)
        wb = [p.sb(f"wbD{i}", [128, 2 * NJ * 128], BF16, es=pes) for i in range(2)]
        stmp = [p.sb(f"stmpD{i}", [128, 512], F32, es=pes) for i in range(2)]
        hsub = [p.sb(f"hsubD{i}", [128, 512], BF16, es=pes) for i in range(2)]
        pG = [p.ps(f"pGD{i}", [128, 512], F32, es=pes) for i in range(2)]
        pU = [p.ps(f"pUD{i}", [128, 512], F32, es=pes) for i in range(2)]

        for j in range(NJ):
            @blk.sync
            def _(e, j=j):
                if j > 0:
                    p.wait(e, ("ve", p.C["ve"]))
                for sq in range(NSEQ):
                    i = e.dma_start(out=xstg[:, sq * S:(sq + 1) * S], in_=x4[sq, j])
                    p.inc(i, "lx", 16)

            @blk.vector
            def _(e, j=j):
                p.wait(e, ("lx", p.C["lx"]))
                i = e.tensor_copy(xb[:, j * T:(j + 1) * T], xstg[:])
                p.inc(i, "ve")
        xcast_ve = p.C["ve"]

        @blk.gpsimd
        def _(e):
            p.wait(e, ("ve", xcast_ve))
            for c in range(2):
                i = e.dma_start(out=wb[c][:, 0:NJ * 128], in_=wg_scr[c])
                p.inc(i, "w%d" % c, 16)
                i = e.dma_start(out=wb[c][:, NJ * 128:], in_=wu_scr[c])
                p.inc(i, "w%d" % c, 16)

        gu_pe = {}
        hmul_ve = {}
        hwr_sa = {}
        for c in range(NC1):
            par = c % 2

            for ts in range(NTS):
                g_idx = c * NTS + ts
                q = g_idx % 2

                @blk.tensor
                def _(e, c=c, ts=ts, par=par, q=q, g_idx=g_idx):
                    p.wait(e, ("w%d" % par, p.C["w%d" % par]))
                    e.nop()
                    if g_idx >= 2:
                        p.wait(e, ("ve", hmul_ve[divmod(g_idx - 2, NTS)]))
                    elif g_idx == 0:
                        p.wait(e, ("ve", xcast_ve))
                    for j in range(NJ):
                        e.matmul(pG[q][:], wb[par][:, j * 128:(j + 1) * 128],
                                 xb[:, j * T + ts * 512: j * T + ts * 512 + 512],
                                 start=(j == 0), stop=(j == NJ - 1), skip_group_check=True)
                    for j in range(NJ):
                        i = e.matmul(pU[q][:], wb[par][:, NJ * 128 + j * 128:NJ * 128 + (j + 1) * 128],
                                     xb[:, j * T + ts * 512: j * T + ts * 512 + 512],
                                     start=(j == 0), stop=(j == NJ - 1), skip_group_check=True)
                    p.inc(i, "pe")
                gu_pe[(c, ts)] = p.C["pe"]

                @blk.scalar
                def _(e, c=c, ts=ts, q=q, g_idx=g_idx):
                    p.wait(e, ("pe", gu_pe[(c, ts)]))
                    if g_idx >= 2:
                        e.nop()
                        p.wait(e, ("ve", hmul_ve[divmod(g_idx - 2, NTS)]))
                    i = e.activation(stmp[q][:], pG[q][:], AF.Silu)
                    p.inc(i, "ac")
                silu_ac = p.C["ac"]

                @blk.vector
                def _(e, c=c, ts=ts, q=q, g_idx=g_idx, silu_ac=silu_ac):
                    p.wait(e, ("ac", silu_ac))
                    if g_idx >= 2:
                        e.nop()
                        p.wait(e, ("a%d" % q, hwr_sa[divmod(g_idx - 2, NTS)]))
                    i = e.tensor_tensor(hsub[q][:], stmp[q][:], pU[q][:], ALU.mult)
                    p.inc(i, "ve")
                hmul_ve[(c, ts)] = p.C["ve"]

                @blk.sync
                def _(e, c=c, ts=ts, q=q):
                    p.wait(e, ("ve", hmul_ve[(c, ts)]))
                    i = e.dma_start(out=hscr[c, :, ts * 512:(ts + 1) * 512], in_=hsub[q][:])
                    p.inc(i, "a%d" % q, 16)
                hwr_sa[(c, ts)] = p.C["a%d" % q]

            @blk.gpsimd
            def _(e, c=c, par=par):
                if c + 2 < NC1:
                    p.wait(e, ("pe", gu_pe[(c, NTS - 1)]))
                    i = e.dma_start(out=wb[par][:, 0:NJ * 128], in_=wg_scr[c + 2])
                    p.inc(i, "w%d" % par, 16)
                    i = e.dma_start(out=wb[par][:, NJ * 128:], in_=wu_scr[c + 2])
                    p.inc(i, "w%d" % par, 16)

        p.barrier(blk)
        pes.close()

        pes = contextlib.ExitStack()
        TT = 512
        NTT = T // TT
        acol = p.sb("acolD", [128, NC1 * TT], BF16, es=pes)
        wdt = [p.sb(f"wdtD{i}", [128, NC1 * 128], BF16, es=pes) for i in range(2)]
        res = [p.sb(f"resD{i}", [128, TT], F32, es=pes) for i in range(2)]
        y2 = p.sb("y2D", [128, NJ * TT], F32, es=pes)
        y2b = p.sb("y2bD", [128, TT], BF16, es=pes)
        y2s = p.sb("y2sD", [128, TT], BF16, es=pes)
        srow = p.sb("srowD", [1, 1536], F32, es=pes)
        rstdB = p.sb("rstdBD", [128, 512], F32, es=pes)
        mrB = p.sb("mrBD", [128, 512], F32, es=pes)
        yt = [p.sb(f"ytD{i}", [128, 512], F32, es=pes) for i in range(2)]
        PS = [p.ps(f"PSD{i}", [128, 512], F32, es=pes) for i in range(6)]

        for tt in range(NTT):
            sq, trs = divmod(tt * TT, S)
            pe_prev_tile = p.C["pe"]

            @blk.sync
            def _(e, tt=tt):
                if tt > 0:
                    p.wait(e, ("pe", pe_prev_tile))
                for a in range(NC1):
                    i = e.dma_start(out=acol[:, a * TT:(a + 1) * TT],
                                    in_=hscr[a, :, tt * TT:(tt + 1) * TT])
                    p.inc(i, "lx", 16)
            acol_lx = p.C["lx"]

            wd_lw = {}

            @blk.gpsimd
            def _(e, tt=tt, sq=sq, trs=trs):
                if tt > 0:
                    p.wait(e, ("pe", pe_prev_tile))
                for dmp in range(2):
                    i = e.dma_start(out=wdt[dmp][:], in_=wd_scr[dmp])
                    i2 = e.dma_start(out=res[dmp][:], in_=x4[sq, dmp, :, trs:trs + TT])
                    i.then_inc(p.sems["d%d" % dmp], 16)
                    p.C["d%d" % dmp] += 16
                    wd_lw[dmp] = p.inc(i2, "d%d" % dmp, 16)

            y2_ve = {}
            sqr_ac = {}
            dmm_pe = {}

            def down_mm(dm):
                po = PS[dm % 2]

                @blk.tensor
                def _(e, dm=dm, po=po):
                    p.wait(e, ("d%d" % (dm % 2), wd_lw[dm]))
                    e.nop()
                    if dm == 0:
                        p.wait(e, ("lx", acol_lx))
                    elif dm >= 2:
                        p.wait(e, ("ve", y2_ve[dm - 2]))
                    for a in range(NC1):
                        i = e.matmul(po[:], wdt[dm % 2][:, a * 128:(a + 1) * 128],
                                     acol[:, a * TT:(a + 1) * TT],
                                     start=(a == 0), stop=(a == NC1 - 1), skip_group_check=True)
                    p.inc(i, "pe")
                dmm_pe[dm] = p.C["pe"]

            def stats_mm(dm):
                @blk.tensor
                def _(e, dm=dm):
                    p.wait(e, ("ac", sqr_ac[dm]))
                    e.matmul(PS[4][0:1, :], ones_c[:], y2b[:],
                             start=(dm == 0), stop=(dm == NJ - 1), skip_group_check=True)
                    i = e.matmul(PS[5][0:1, :], ones_c[:], y2s[:],
                                 start=(dm == 0), stop=(dm == NJ - 1), skip_group_check=True)
                    p.inc(i, "pe")

            def down_evac(dm):
                po = PS[dm % 2]
                pe_snap = p.C["pe"]

                @blk.vector
                def _(e, dm=dm, po=po, pe_snap=pe_snap):
                    p.wait(e, ("d%d" % (dm % 2), wd_lw[dm]))
                    e.nop()
                    p.wait(e, ("pe", pe_snap))
                    e.tensor_tensor(y2[:, dm * TT:(dm + 1) * TT], po[:], res[dm % 2][:], ALU.add)
                    i = e.tensor_copy(y2b[:], y2[:, dm * TT:(dm + 1) * TT])
                    p.inc(i, "ve")
                y2_ve[dm] = p.C["ve"]

                @blk.scalar
                def _(e, dm=dm):
                    p.wait(e, ("ve", y2_ve[dm]))
                    i = e.activation(y2s[:], y2[:, dm * TT:(dm + 1) * TT], AF.Square)
                    p.inc(i, "ac")
                sqr_ac[dm] = p.C["ac"]

            def prefetch_next(dm):
                @blk.gpsimd
                def _(e, dm=dm, sq=sq, trs=trs):
                    p.wait(e, ("pe", dmm_pe[dm]), ("ve", y2_ve[dm]))
                    i = e.dma_start(out=wdt[dm % 2][:], in_=wd_scr[dm + 2])
                    i2 = e.dma_start(out=res[dm % 2][:], in_=x4[sq, dm + 2, :, trs:trs + TT])
                    i.then_inc(p.sems["d%d" % (dm % 2)], 16)
                    p.C["d%d" % (dm % 2)] += 16
                    wd_lw[dm + 2] = p.inc(i2, "d%d" % (dm % 2), 16)

            for dm in range(NJ):
                down_mm(dm)
                if dm > 0:
                    stats_mm(dm - 1)
                down_evac(dm)
                if dm + 2 < NJ:
                    prefetch_next(dm)
            stats_mm(NJ - 1)

            @blk.vector
            def _(e):
                p.wait(e, ("pe", p.C["pe"]))
                e.tensor_scalar(srow[0:1, 0:512], PS[4][0:1, :], 1.0 / D, None, ALU.mult)
                e.tensor_scalar(srow[0:1, 512:1024], PS[5][0:1, :], 1.0 / D, None, ALU.mult)
                e.tensor_tensor(srow[0:1, 1024:1536], srow[0:1, 0:512], srow[0:1, 0:512], ALU.mult)
                e.tensor_tensor(srow[0:1, 512:1024], srow[0:1, 512:1024], srow[0:1, 1024:1536], ALU.subtract)
                i = e.tensor_scalar(srow[0:1, 512:1024], srow[0:1, 512:1024], LN_EPS, None, ALU.add)
                p.inc(i, "ve")

            @blk.scalar
            def _(e):
                p.wait(e, ("ve", p.C["ve"]))
                i = e.activation(srow[0:1, 512:1024], srow[0:1, 512:1024], AF.Sqrt)
                p.inc(i, "ac")

            @blk.vector
            def _(e):
                p.wait(e, ("ac", p.C["ac"]))
                e.reciprocal(srow[0:1, 512:1024], srow[0:1, 512:1024])
                i = e.tensor_tensor(srow[0:1, 1024:1536], srow[0:1, 0:512], srow[0:1, 512:1024], ALU.mult)
                p.inc(i, "ve")

            @blk.tensor
            def _(e):
                p.wait(e, ("ve", p.C["ve"]))
                e.matmul(PS[2][:], ones_r[:], srow[0:1, 512:1024], start=True, stop=True,
                         skip_group_check=True)
                i = e.matmul(PS[3][:], ones_r[:], srow[0:1, 1024:1536], start=True, stop=True,
                             skip_group_check=True)
                p.inc(i, "pe")

            @blk.scalar
            def _(e):
                p.wait(e, ("pe", p.C["pe"]))
                e.copy(rstdB[:], PS[2][:])
                i = e.copy(mrB[:], PS[3][:])
                p.inc(i, "ac")
            ln_ac = p.C["ac"]

            for dm in range(NJ):
                xo = yt[dm % 2]
                xs = "x%d" % (dm % 2)

                @blk.vector
                def _(e, dm=dm, xo=xo, xs=xs, ln_ac=ln_ac):
                    p.wait(e, ("ac", ln_ac), (xs, p.C[xs]))
                    e.tensor_tensor(xo[:], y2[:, dm * TT:(dm + 1) * TT], rstdB[:], ALU.mult)
                    i = e.tensor_tensor(xo[:], xo[:], mrB[:], ALU.subtract)
                    p.inc(i, "ve")

                @blk.gpsimd
                def _(e, dm=dm, sq=sq, trs=trs, xo=xo, xs=xs):
                    p.wait(e, ("ve", p.C["ve"]))
                    i = e.dma_start(out=yout[sq, dm, :, trs:trs + TT], in_=xo[:])
                    p.inc(i, xs, 16)

        @blk.gpsimd
        def _(e):
            p.wait(e, ("x0", p.C["x0"]))
            e.nop()
            e.wait_ge(p.sems["x1"], p.C["x1"])

    pes.close()
    esD.close()


def prep_D_weights(lut_v, gate_idx, up_idx, down_idx):
    gidx = np.ascontiguousarray(
        gate_idx.reshape(NC1, 128, NJ, 128).transpose(0, 3, 2, 1).reshape(NC1, 128, NJ * 128)).astype(np.int32)
    uidx = np.ascontiguousarray(
        up_idx.reshape(NC1, 128, NJ, 128).transpose(0, 3, 2, 1).reshape(NC1, 128, NJ * 128)).astype(np.int32)
    didx = np.ascontiguousarray(
        down_idx.reshape(NJ, 128, NC1, 128).transpose(0, 3, 2, 1).reshape(NJ, 128, NC1 * 128)).astype(np.int32)
    return dict(gidx=gidx, uidx=uidx, didx=didx,
                lut=np.asarray(lut_v, np.float32).reshape(1, 16))



def build_fused(S=SEQ):
    p = P("F")
    T = NSEQ * S
    a = {}
    a["xin"] = p.dram("xin", [NSEQ, NJ, 128, S], F32, kind="ExternalInput")
    a["yout"] = p.dram("yout", [NSEQ, NJ, 128, S], F32, kind="ExternalOutput")
    a["wkv"] = p.dram("wkv", [2, NJ, 128, 512], F32, kind="ExternalInput")
    a["bkv"] = p.dram("bkv", [2, 512], F32, kind="ExternalInput")
    a["wq"] = p.dram("wq", [H, 128, NJ * 128], F32, kind="ExternalInput")
    a["bq"] = p.dram("bq", [1, H * 128], F32, kind="ExternalInput")
    a["wo"] = p.dram("wo", [NJ, 128, H * 128], F32, kind="ExternalInput")
    a["w1h"] = p.dram("w1h", [NC0, 128, NJ * 128], F32, kind="ExternalInput")
    a["w3h"] = p.dram("w3h", [NC0, 128, NJ * 128], F32, kind="ExternalInput")
    a["w2r"] = p.dram("w2r", [64, 128, D], F32, kind="ExternalInput")
    a["h64"] = p.dram("h64", [64, 64], BF16, kind="ExternalInput")
    a["h128"] = p.dram("h128", [128, 128], BF16, kind="ExternalInput")
    a["wqkv"] = p.dram("wqkv", [48, 128, NJ * 128], F32, kind="ExternalInput")
    a["bqkv"] = p.dram("bqkv", [1, 48 * 128], F32, kind="ExternalInput")
    a["wout"] = p.dram("wout", [NJ, 128, H * 128], F32, kind="ExternalInput")
    a["bout"] = p.dram("bout", [1, NJ * 128], F32, kind="ExternalInput")
    a["maskT"] = p.dram("maskT", [128, 128], F32, kind="ExternalInput")
    a["maskS"] = p.dram("maskS", [128, 128], F32, kind="ExternalInput")
    a["idb"] = p.dram("idb", [128, 128], BF16, kind="ExternalInput")
    a["idf"] = p.dram("idf", [128, 128], F32, kind="ExternalInput")
    a["gidx"] = p.dram("gidx", [NC1, 128, NJ * 128], I32, kind="ExternalInput")
    a["uidx"] = p.dram("uidx", [NC1, 128, NJ * 128], I32, kind="ExternalInput")
    a["didx"] = p.dram("didx", [NJ, 128, NC1 * 128], I32, kind="ExternalInput")
    a["lut"] = p.dram("lut", [1, 16], F32, kind="ExternalInput")
    a["x2"] = p.dram("x2", [NSEQ, NJ, 128, S], F32)
    a["x3"] = p.dram("x3", [NSEQ, NJ, 128, S], F32)
    a["x4"] = p.dram("x4", [NSEQ, NJ, 128, S], F32)
    a["wqb_scr"] = p.dram("wqb_scr", [H, 128, NJ * 128], BF16)
    a["wob_scr"] = p.dram("wob_scr", [NJ, 128, H * 128], BF16)
    a["ascr"] = p.dram("ascr", [NC0, 128, T], BF16)
    a["w2b_scr"] = p.dram("w2b_scr", [NJ, 128, 64, 128], BF16)
    a["wqkv_scr"] = p.dram("wqkv_scr", [48, 128, NJ * 128], BF16)
    a["wout_scr"] = p.dram("wout_scr", [NJ, 128, H * 128], BF16)
    a["qscr"] = p.dram("qscr", [NSEQ, H, 128, S], BF16)
    a["kscr"] = p.dram("kscr", [NSEQ, H, 128, S], BF16)
    a["vscr"] = p.dram("vscr", [NSEQ, H, S, 128], BF16)
    a["wg_scr"] = p.dram("wg_scr", [NC1, 128, NJ * 128], BF16)
    a["wu_scr"] = p.dram("wu_scr", [NC1, 128, NJ * 128], BF16)
    a["wd_scr"] = p.dram("wd_scr", [NJ, 128, NC1 * 128], BF16)
    a["hscr"] = p.dram("hscr", [NC1, 128, T], BF16)
    p.aps = a

    for s in ("lx", "pe", "ve", "ac",
              "w0", "w1", "s0", "s1", "q0", "q1", "o0", "o1", "x0", "x1",
              "a0", "a1", "d0", "d1", "k0", "k1", "i0", "i1", "m0", "m1"):
        p.sem(s)

    p.ones_r = p.sb("ones_r", [1, 128], F32)
    p.ones_c = p.sb("ones_c", [128, 1], BF16)

    with p.nc.Block() as blk:
        emit_A(p, blk, S)
        p.barrier(blk)
        emit_B(p, blk, S)
        p.barrier(blk)
        emit_C(p, blk, S)
        p.barrier(blk)
        emit_D(p, blk, S)
    return p


_PROGS = {}


def _get_fused():
    if "F" not in _PROGS:
        _PROGS["F"] = build_fused()
    return _PROGS["F"]


def kernel(L0_attn_hidden_states, L0_attn_Wq, L0_attn_bq, L0_attn_Wk, L0_attn_bk,
           L0_attn_Wv, L0_attn_bv, L0_attn_Wo, L0_ffn_W1, L0_ffn_W2, L0_ffn_W3,
           L1_attn_W_qkv, L1_attn_b_qkv, L1_attn_W_out, L1_attn_b_out,
           L1_ffn_lut, L1_ffn_gate_idx, L1_ffn_up_idx, L1_ffn_down_idx):
    x = np.ascontiguousarray(np.asarray(L0_attn_hidden_states, np.float32))
    w = {}
    w.update(prep_A_weights(np.asarray(L0_attn_Wq, np.float32), np.asarray(L0_attn_bq, np.float32),
                            np.asarray(L0_attn_Wk, np.float32), np.asarray(L0_attn_bk, np.float32),
                            np.asarray(L0_attn_Wv, np.float32), np.asarray(L0_attn_bv, np.float32),
                            np.asarray(L0_attn_Wo, np.float32)))
    w.update(prep_B_weights(np.asarray(L0_ffn_W1, np.float32), np.asarray(L0_ffn_W2, np.float32),
                            np.asarray(L0_ffn_W3, np.float32)))
    w.update(prep_C_weights(np.asarray(L1_attn_W_qkv, np.float32), np.asarray(L1_attn_b_qkv, np.float32),
                            np.asarray(L1_attn_W_out, np.float32), np.asarray(L1_attn_b_out, np.float32)))
    w.update(prep_D_weights(np.asarray(L1_ffn_lut, np.float32), np.asarray(L1_ffn_gate_idx),
                            np.asarray(L1_ffn_up_idx), np.asarray(L1_ffn_down_idx)))

    pF = _get_fused()
    in_maps = []
    for c in range(NCORES):
        m = dict(w)
        m["xin"] = to_fm(x[c * NSEQ:(c + 1) * NSEQ])
        in_maps.append(m)
    res = run_bass_kernel_spmd(pF.nc, in_maps, core_ids=list(range(NCORES)))
    return np.concatenate([from_fm(np.asarray(r["yout"])) for r in res.results], axis=0)


# version 06 (final)
# speedup vs baseline: 2.7739x; 1.1148x over previous
import contextlib
import numpy as np
import ml_dtypes
import concourse.bass as bass
import concourse.mybir as mybir
from concourse.bass_utils import run_bass_kernel_spmd

F32 = mybir.dt.float32
BF16 = mybir.dt.bfloat16
I32 = mybir.dt.int32
AF = mybir.ActivationFunctionType
ALU = mybir.AluOpType
AX = mybir.AxisListType

NCORES = 8
BF = ml_dtypes.bfloat16

D = 2048
NJ = 16
SEQ = 2048
NSEQ = 2
H = 16
KVH = 4
HD = 128
LN_EPS = 1e-5
LA_EPS = 1e-6


class P:

    def __init__(self, name):
        self.nc = bass.Bass("TRN2", target_bir_lowering=False)
        self.es = contextlib.ExitStack()
        self.sems = {}
        self.C = {}

    def sem(self, name):
        self.sems[name] = self.es.enter_context(self.nc.semaphore(name))
        self.C[name] = 0

    def sb(self, name, shape, dt, es=None):
        return (es or self.es).enter_context(self.nc.sbuf_tensor(name, shape, dt))

    def ps(self, name, shape, dt=F32, es=None):
        return (es or self.es).enter_context(self.nc.psum_tensor(name, shape, dt))

    def dram(self, name, shape, dt, kind="Internal"):
        return self.nc.dram_tensor(name, shape, dt, kind=kind).ap()

    def inc(self, instr, name, k=1):
        instr.then_inc(self.sems[name], k)
        self.C[name] += k
        return self.C[name]

    def wait(self, eng, *pairs):
        first = True
        for nm, v in pairs:
            if v <= 0:
                continue
            if not first:
                eng.nop()
            eng.wait_ge(self.sems[nm], v)
            first = False

    def barrier(self, blk):
        if "bar" not in self.sems:
            self.sem("bar")
        @blk.sync
        def _(e):
            self.wait(e, *[(nm, self.C[nm]) for nm in self.sems if nm != "bar"])
            e.nop()
        tgt = self.C["bar"] + 5
        for engsec in (blk.sync, blk.tensor, blk.scalar, blk.vector, blk.gpsimd):
            @engsec
            def _(e):
                e.drain()
                e.nop().then_inc(self.sems["bar"], 1)
        self.C["bar"] = tgt
        for engsec in (blk.sync, blk.tensor, blk.scalar, blk.vector, blk.gpsimd):
            @engsec
            def _(e):
                e.wait_ge(self.sems["bar"], tgt)



def emit_A(p, blk, S=SEQ):
    nc = p.nc
    NT = S // 512
    xin = p.aps["xin"]
    wkv = p.aps["wkv"]
    bkv = p.aps["bkv"]
    wq = p.aps["wq"]
    bq = p.aps["bq"]
    wo = p.aps["wo"]
    x2 = p.aps["x2"]
    wqb_scr = p.aps["wqb_scr"]
    wob_scr = p.aps["wob_scr"]
    ones_r = p.ones_r
    ones_c = p.ones_c

    esA = contextlib.ExitStack()
    bkv_row = p.sb("bkv_row", [1, 1024], F32, es=esA)
    bkvB = p.sb("bkvB", [128, 1024], F32, es=esA)
    bq_sb = p.sb("bq_sb", [128, H], F32, es=esA)
    kv_f = p.sb("kv_f", [128, NSEQ * 4 * 132], F32, es=esA)
    kv_b = p.sb("kv_b", [128, NSEQ * 4 * 132], BF16, es=esA)

    pes_w = contextlib.ExitStack()
    wkv_sb = p.sb("wkv_sb", [128, 2 * NJ * 512], BF16, es=pes_w)
    if True:
        pes = contextlib.ExitStack()
        wtmp = [p.sb(f"wtmp{i}", [128, 2048], F32, es=pes) for i in range(2)]
        wqb_tmp = [p.sb(f"wqbt{i}", [128, 2048], BF16, es=pes) for i in range(2)]
        pb0 = p.ps("pb0", [128, 512], F32, es=pes)
        pb1 = p.ps("pb1", [128, 512], F32, es=pes)

        bq_row = p.sb("bq_row", [1, H * 128], F32, es=pes)

        @blk.sync
        def _(e):
            i = e.dma_start(out=bkv_row[0:1, 0:512], in_=bkv[0:1, :])
            p.inc(i, "lx", 16)
            i = e.dma_start(out=bkv_row[0:1, 512:1024], in_=bkv[1:2, :])
            p.inc(i, "lx", 16)
            i = e.dma_start(out=bq_row[:], in_=bq[:, :])
            p.inc(i, "lx", 16)
        a_lx = p.C["lx"]

        @blk.tensor
        def _(e):
            p.wait(e, ("lx", a_lx), ("ve", p.init_ve))
            e.nop()
            p.wait(e, ("ac", p.init_ac))
            e.matmul(pb0[:], ones_r[:], bkv_row[0:1, 0:512], start=True, stop=True)
            i = e.matmul(pb1[:], ones_r[:], bkv_row[0:1, 512:1024], start=True, stop=True)
            p.inc(i, "pe")

        @blk.scalar
        def _(e):
            p.wait(e, ("pe", p.C["pe"]))
            e.copy(bkvB[:, 0:512], pb0[:])
            i = e.copy(bkvB[:, 512:1024], pb1[:])
            p.inc(i, "ac")

        @blk.tensor
        def _(e):
            p.wait(e, ("ac", p.C["ac"]))
            for h in range(H):
                i = e.matmul(pb0[:, h:h + 1], bq_row[0:1, h * 128:(h + 1) * 128],
                             ones_r[0:1, 0:1], start=True, stop=True,
                             skip_group_check=True)
            p.inc(i, "pe")

        @blk.scalar
        def _(e):
            p.wait(e, ("pe", p.C["pe"]))
            i = e.copy(bq_sb[:, 0:H], pb0[:, 0:H])
            p.inc(i, "ac")

        for t in range(2 * NJ):
            g, j = divmod(t, NJ)
            buf = wtmp[t % 2]
            ws = "w%d" % (t % 2)

            @blk.gpsimd
            def _(e):
                p.wait(e, ("ve", p.C["ve"] - 1))
                i = e.dma_start(out=buf[:, 0:512], in_=wkv[g, j])
                p.inc(i, ws, 16)

            @blk.vector
            def _(e):
                p.wait(e, (ws, p.C[ws]))
                i = e.tensor_copy(wkv_sb[:, (g * NJ + j) * 512:(g * NJ + j + 1) * 512], buf[:, 0:512])
                p.inc(i, "ve")

        for t in range(NJ):
            buf = wtmp[t % 2]
            bbuf = wqb_tmp[t % 2]
            ws = "w%d" % (t % 2)
            ss = "s%d" % (t % 2)

            @blk.gpsimd
            def _(e):
                p.wait(e, ("ve", p.C["ve"] - 1))
                i = e.dma_start(out=buf[:], in_=wo[t])
                p.inc(i, ws, 16)

            @blk.vector
            def _(e):
                p.wait(e, (ws, p.C[ws]), (ss, p.C[ss]))
                i = e.tensor_copy(bbuf[:], buf[:])
                p.inc(i, "ve")

            @blk.sync
            def _(e):
                p.wait(e, ("ve", p.C["ve"]))
                i = e.dma_start(out=wob_scr[t], in_=bbuf[:])
                p.inc(i, ss, 16)

        for t in range(H):
            buf = wtmp[t % 2]
            bbuf = wqb_tmp[t % 2]
            ws = "w%d" % (t % 2)
            ss = "s%d" % (t % 2)

            @blk.gpsimd
            def _(e):
                p.wait(e, ("ve", p.C["ve"] - 1))
                i = e.dma_start(out=buf[:], in_=wq[t])
                p.inc(i, ws, 16)

            @blk.vector
            def _(e):
                p.wait(e, (ws, p.C[ws]), (ss, p.C[ss]))
                i = e.tensor_copy(bbuf[:], buf[:])
                p.inc(i, "ve")

            @blk.sync
            def _(e):
                p.wait(e, ("ve", p.C["ve"]))
                i = e.dma_start(out=wqb_scr[t], in_=bbuf[:])
                p.inc(i, ss, 16)

        p.barrier(blk)
        pes.close()

        pes = contextlib.ExitStack()
        xt = p.sb("xtP1", [128, NJ * 512], F32, es=pes)
        xb1 = p.sb("xbP1", [128, NJ * 512], BF16, es=pes)
        t0 = p.sb("t0P1", [128, 512], F32, es=pes)
        tm = p.sb("tmP1", [128, 128], F32, es=pes)
        te = p.sb("teP1", [128, 128], F32, es=pes)
        tr = p.sb("trP1", [128, 128], F32, es=pes)
        skb = p.sb("skbP1", [128, 128], BF16, es=pes)
        vxb = p.sb("vxbP1", [128, 132], BF16, es=pes)
        pproj = [p.ps(f"pprojP1{i}", [128, 512], F32, es=pes) for i in range(2)]
        pkv = [p.ps(f"pkvP1{i}", [128, 512], F32, es=pes) for i in range(2)]

        @blk.vector
        def _(e):
            i = e.memset(vxb[:, 128:132], 1.0)
            p.inc(i, "ve")

        t0_ve = []
        skb2 = p.sb("skb2P1", [128, 128], BF16, es=pes)
        vxb2 = p.sb("vxb2P1", [128, 132], BF16, es=pes)

        @blk.vector
        def _(e):
            i = e.memset(vxb2[:, 128:132], 1.0)
            p.inc(i, "ve")

        skbs = [skb, skb2]
        vxbs = [vxb, vxb2]

        step_idx = 0
        kv_pe = {}
        skb_ve = {}
        for sq in range(NSEQ):
            for st in range(NT):
                @blk.sync
                def _(e):
                    if sq + st > 0:
                        p.wait(e, ("pe", p.C["pe"]))
                    for j in range(NJ):
                        i = e.dma_start(out=xt[:, j * 512:(j + 1) * 512],
                                        in_=xin[sq, j, :, st * 512:(st + 1) * 512])
                        p.inc(i, "lx", 16)

                @blk.vector
                def _(e):
                    p.wait(e, ("lx", p.C["lx"]))
                    for j in range(NJ):
                        i = e.tensor_copy(xb1[:, j * 512:(j + 1) * 512], xt[:, j * 512:(j + 1) * 512])
                    p.inc(i, "ve")
                cast_ve = p.C["ve"]

                for sub in range(4):
                    for g in range(2):
                        i_step = step_idx
                        pp = pproj[i_step % 2]

                        @blk.tensor
                        def _(e):
                            w2 = t0_ve[i_step - 2] if i_step >= 2 else 0
                            p.wait(e, ("ve", max(cast_ve, w2)))
                            for j in range(NJ):
                                i = e.matmul(pp[:],
                                             xb1[:, j * 512 + sub * 128: j * 512 + sub * 128 + 128],
                                             wkv_sb[:, (g * NJ + j) * 512:(g * NJ + j + 1) * 512],
                                             start=(j == 0), stop=(j == NJ - 1),
                                             skip_group_check=True)
                            p.inc(i, "pe")
                        proj_pe = p.C["pe"]

                        @blk.vector
                        def _(e):
                            p.wait(e, ("pe", proj_pe))
                            i = e.tensor_tensor(t0[:], pp[:], bkvB[:, g * 512:(g + 1) * 512], ALU.add)
                            p.inc(i, "ve")
                        t0_ve.append(p.C["ve"])

                        for kk in range(2):
                            kvh = g * 2 + kk
                            sk_b = skbs[kk]
                            vx_b = vxbs[kk]

                            @blk.vector
                            def _(e):
                                prev = kv_pe.get((i_step - 1, kk), 0)
                                p.wait(e, ("pe", prev))
                                e.tensor_scalar(tm[:], t0[:, kk * 256:kk * 256 + 128], 0.0, None, ALU.min)
                                i = e.tensor_relu(tr[:], t0[:, kk * 256:kk * 256 + 128])
                                p.inc(i, "ve")

                            @blk.scalar
                            def _(e):
                                p.wait(e, ("ve", p.C["ve"]))
                                i = e.activation(te[:], tm[:], AF.Exp)
                                p.inc(i, "ac")

                            @blk.vector
                            def _(e):
                                p.wait(e, ("ac", p.C["ac"]))
                                e.tensor_tensor(sk_b[:], te[:], tr[:], ALU.add)
                                i = e.tensor_copy(vx_b[:, 0:128], t0[:, kk * 256 + 128:kk * 256 + 256])
                                p.inc(i, "ve")
                            skb_ve[(i_step, kk)] = p.C["ve"]

                            @blk.tensor
                            def _(e):
                                p.wait(e, ("ve", skb_ve[(i_step, kk)]))
                                bank = pkv[kvh // 3]
                                off = (kvh % 3) * 132
                                i = e.matmul(bank[:, off:off + 132], sk_b[:], vx_b[:],
                                             start=(st == 0 and sub == 0 and kvh in (0, 3)),
                                             stop=(st == NT - 1 and sub == 3 and kvh in (2, 3)),
                                             skip_group_check=True)
                                p.inc(i, "pe")
                            kv_pe[(i_step, kk)] = p.C["pe"]

                        step_idx += 1

            @blk.vector
            def _(e):
                p.wait(e, ("pe", p.C["pe"]))
                base = sq * 4 * 132
                e.tensor_copy(kv_f[:, base:base + 396], pkv[0][:, 0:396])
                e.tensor_copy(kv_f[:, base + 396:base + 528], pkv[1][:, 0:132])
                e.tensor_copy(kv_b[:, base:base + 396], pkv[0][:, 0:396])
                i = e.tensor_copy(kv_b[:, base + 396:base + 528], pkv[1][:, 0:132])
                p.inc(i, "ve")

            @blk.tensor
            def _(e):
                p.wait(e, ("ve", p.C["ve"]))
                p.inc(e.nop(), "pe")

        p.barrier(blk)
        pes.close()
        pes_w.close()

        pes = contextlib.ExitStack()
        xt3 = p.sb("xtP3", [128, NJ * 512], F32, es=pes)
        xb3 = p.sb("xbP3", [128, NJ * 512], BF16, es=pes)
        wqh = [p.sb(f"wqhP3{i}", [128, 2048], BF16, es=pes) for i in range(2)]
        sqb = [p.sb(f"sqbP3{i}", [128, 512], BF16, es=pes) for i in range(2)]
        tq = p.sb("tqP3", [128, 512], F32, es=pes)
        tqm = p.sb("tqmP3", [128, 512], F32, es=pes)
        tqe = p.sb("tqeP3", [128, 512], F32, es=pes)
        tqr = p.sb("tqrP3", [128, 512], F32, es=pes)
        nrow = p.sb("nrowP3", [1, 512], F32, es=pes)
        recB = [p.sb(f"recBP3{i}", [128, 512], F32, es=pes) for i in range(2)]
        attnN = p.sb("attnNP3", [128, H * 512], BF16, es=pes)
        y1 = p.sb("y1P3", [128, NJ * 512], F32, es=pes)
        y1b = p.sb("y1bP3", [128, 512], BF16, es=pes)
        y1s = p.sb("y1sP3", [128, 512], BF16, es=pes)
        srow = p.sb("srowP3", [1, 1536], F32, es=pes)
        rstdB = p.sb("rstdBP3", [128, 512], F32, es=pes)
        mrB = p.sb("mrBP3", [128, 512], F32, es=pes)
        x2t = [p.sb(f"x2tP3{i}", [128, 512], F32, es=pes) for i in range(2)]
        woh = [p.sb(f"wohP3{i}", [128, 2048], BF16, es=pes) for i in range(2)]
        PS = [p.ps(f"PSP3{i}", [128, 512], F32, es=pes) for i in range(6)]

        for sq in range(NSEQ):
            for st in range(NT):
                pe_prev_tile = p.C["pe"]

                @blk.sync
                def _(e):
                    if sq + st > 0:
                        p.wait(e, ("ve", p.C["ve"]))
                    for j in range(NJ):
                        i = e.dma_start(out=xt3[:, j * 512:(j + 1) * 512],
                                        in_=xin[sq, j, :, st * 512:(st + 1) * 512])
                        p.inc(i, "lx", 16)
                    i = e.dma_start(out=wqh[0][:], in_=wqb_scr[0])
                    p.inc(i, "q0", 16)

                @blk.vector
                def _(e):
                    p.wait(e, ("lx", p.C["lx"]), ("pe", pe_prev_tile))
                    for j in range(NJ):
                        i = e.tensor_copy(xb3[:, j * 512:(j + 1) * 512], xt3[:, j * 512:(j + 1) * 512])
                    p.inc(i, "ve")
                xcast_ve = p.C["ve"]

                qproj_pe = {}
                evacq_ve = {}
                sqb_ve = {}

                def head_front(h):
                    pq = PS[h % 2]

                    @blk.sync
                    def _(e):
                        if h + 1 < H:
                            p.wait(e, ("pe", qproj_pe.get(h - 1, 0)))
                            i = e.dma_start(out=wqh[(h + 1) % 2][:], in_=wqb_scr[h + 1])
                            p.inc(i, "q%d" % ((h + 1) % 2), 16)

                    @blk.tensor
                    def _(e):
                        wv = max(xcast_ve, evacq_ve.get(h - 2, 0))
                        p.wait(e, ("q%d" % (h % 2), p.C["q%d" % (h % 2)]),
                               ("ve", wv))
                        for j in range(NJ):
                            i = e.matmul(pq[:], wqh[h % 2][:, j * 128:(j + 1) * 128],
                                         xb3[:, j * 512:(j + 1) * 512],
                                         start=(j == 0), stop=(j == NJ - 1),
                                         skip_group_check=True)
                        p.inc(i, "pe")
                    qproj_pe[h] = p.C["pe"]

                    @blk.vector
                    def _(e):
                        p.wait(e, ("pe", qproj_pe[h]))
                        e.tensor_scalar(tq[:], pq[:], bq_sb[:, h:h + 1], None, ALU.add)
                        e.tensor_scalar(tqm[:], tq[:], 0.0, None, ALU.min)
                        i = e.tensor_relu(tqr[:], tq[:])
                        p.inc(i, "ve")
                    evacq_ve[h] = p.C["ve"]

                    @blk.scalar
                    def _(e):
                        p.wait(e, ("ve", evacq_ve[h]))
                        i = e.activation(tqe[:], tqm[:], AF.Exp)
                        p.inc(i, "ac")

                    @blk.vector
                    def _(e):
                        p.wait(e, ("ac", p.C["ac"]))
                        i = e.tensor_tensor(sqb[h % 2][:], tqe[:], tqr[:], ALU.add)
                        p.inc(i, "ve")
                    sqb_ve[h] = p.C["ve"]

                def head_back(h):
                    pat = PS[2 + h % 2]
                    kvh = h // 4
                    kvoff = (sq * 4 + kvh) * 132

                    @blk.tensor
                    def _(e):
                        p.wait(e, ("ve", sqb_ve[h]))
                        e.matmul(PS[4][0:1, :], kv_b[:, kvoff + 128:kvoff + 129], sqb[h % 2][:],
                                 start=True, stop=True, skip_group_check=True)
                        i = e.matmul(pat[:], kv_b[:, kvoff:kvoff + 128], sqb[h % 2][:],
                                     start=True, stop=True, skip_group_check=True)
                        p.inc(i, "pe")
                    na_pe = p.C["pe"]

                    @blk.vector
                    def _(e):
                        p.wait(e, ("pe", na_pe))
                        i = e.tensor_scalar(nrow[:], PS[4][0:1, :], LA_EPS, None, ALU.add)
                        p.inc(i, "ve")
                    nr_ve = p.C["ve"]

                    @blk.tensor
                    def _(e):
                        p.wait(e, ("ve", nr_ve))
                        i = e.matmul(PS[5][:], ones_r[:], nrow[:], start=True, stop=True,
                                     skip_group_check=True)
                        p.inc(i, "pe")
                    bc_pe = p.C["pe"]

                    @blk.vector
                    def _(e):
                        p.wait(e, ("pe", bc_pe))
                        e.reciprocal(recB[h % 2][:], PS[5][:])
                        i = e.tensor_tensor(attnN[:, h * 512:(h + 1) * 512], pat[:], recB[h % 2][:], ALU.mult)
                        p.inc(i, "ve")

                for h in range(H):
                    head_front(h)
                    if h > 0:
                        head_back(h - 1)
                head_back(H - 1)

                attn_done_ve = p.C["ve"]
                y1_ve = {}
                sqr_ac = {}
                wo_lw = {}
                wo_pe = {}

                @blk.gpsimd
                def _(e):
                    p.wait(e, ("pe", pe_prev_tile))
                    for dmp in range(2):
                        i = e.dma_start(out=woh[dmp][:], in_=wob_scr[dmp])
                        wo_lw[dmp] = p.inc(i, "o%d" % dmp, 16)

                def wo_mm_tensor(dm):
                    po = PS[dm % 2]

                    @blk.tensor
                    def _(e):
                        p.wait(e, ("o%d" % (dm % 2), wo_lw[dm]))
                        e.nop()
                        if dm == 0:
                            p.wait(e, ("ve", attn_done_ve))
                        elif dm >= 2:
                            p.wait(e, ("ve", y1_ve[dm - 2]))
                        for hh in range(H):
                            i = e.matmul(po[:], woh[dm % 2][:, hh * 128:(hh + 1) * 128],
                                         attnN[:, hh * 512:(hh + 1) * 512],
                                         start=(hh == 0), stop=(hh == H - 1),
                                         skip_group_check=True)
                        p.inc(i, "pe")
                    wo_pe[dm] = p.C["pe"]

                    if dm + 2 < NJ:
                        @blk.gpsimd
                        def _(e):
                            p.wait(e, ("pe", wo_pe[dm]))
                            i = e.dma_start(out=woh[dm % 2][:], in_=wob_scr[dm + 2])
                            wo_lw[dm + 2] = p.inc(i, "o%d" % (dm % 2), 16)

                def stats_mm(dm):
                    @blk.tensor
                    def _(e):
                        p.wait(e, ("ac", sqr_ac[dm]))
                        e.matmul(PS[4][0:1, :], ones_c[:], y1b[:],
                                 start=(dm == 0), stop=(dm == NJ - 1), skip_group_check=True)
                        i = e.matmul(PS[5][0:1, :], ones_c[:], y1s[:],
                                     start=(dm == 0), stop=(dm == NJ - 1), skip_group_check=True)
                        p.inc(i, "pe")

                def wo_evac(dm):
                    po = PS[dm % 2]
                    pe_snap = p.C["pe"]

                    @blk.vector
                    def _(e):
                        p.wait(e, ("pe", pe_snap))
                        e.tensor_tensor(y1[:, dm * 512:(dm + 1) * 512], po[:],
                                        xt3[:, dm * 512:(dm + 1) * 512], ALU.add)
                        i = e.tensor_copy(y1b[:], y1[:, dm * 512:(dm + 1) * 512])
                        p.inc(i, "ve")
                    y1_ve[dm] = p.C["ve"]

                    @blk.scalar
                    def _(e):
                        p.wait(e, ("ve", y1_ve[dm]))
                        i = e.activation(y1s[:], y1[:, dm * 512:(dm + 1) * 512], AF.Square)
                        p.inc(i, "ac")
                    sqr_ac[dm] = p.C["ac"]

                for dm in range(NJ):
                    wo_mm_tensor(dm)
                    if dm > 0:
                        stats_mm(dm - 1)
                    wo_evac(dm)
                stats_mm(NJ - 1)

                @blk.vector
                def _(e):
                    p.wait(e, ("pe", p.C["pe"]))
                    e.tensor_scalar(srow[0:1, 0:512], PS[4][0:1, :], 1.0 / D, None, ALU.mult)
                    e.tensor_scalar(srow[0:1, 512:1024], PS[5][0:1, :], 1.0 / D, None, ALU.mult)
                    e.tensor_tensor(srow[0:1, 1024:1536], srow[0:1, 0:512], srow[0:1, 0:512], ALU.mult)
                    e.tensor_tensor(srow[0:1, 512:1024], srow[0:1, 512:1024], srow[0:1, 1024:1536], ALU.subtract)
                    i = e.tensor_scalar(srow[0:1, 512:1024], srow[0:1, 512:1024], LN_EPS, None, ALU.add)
                    p.inc(i, "ve")

                @blk.scalar
                def _(e):
                    p.wait(e, ("ve", p.C["ve"]))
                    i = e.activation(srow[0:1, 512:1024], srow[0:1, 512:1024], AF.Sqrt)
                    p.inc(i, "ac")

                @blk.vector
                def _(e):
                    p.wait(e, ("ac", p.C["ac"]))
                    e.reciprocal(srow[0:1, 512:1024], srow[0:1, 512:1024])
                    i = e.tensor_tensor(srow[0:1, 1024:1536], srow[0:1, 0:512], srow[0:1, 512:1024], ALU.mult)
                    p.inc(i, "ve")

                @blk.tensor
                def _(e):
                    p.wait(e, ("ve", p.C["ve"]))
                    e.matmul(PS[2][:], ones_r[:], srow[0:1, 512:1024], start=True, stop=True,
                             skip_group_check=True)
                    i = e.matmul(PS[3][:], ones_r[:], srow[0:1, 1024:1536], start=True, stop=True,
                                 skip_group_check=True)
                    p.inc(i, "pe")

                @blk.scalar
                def _(e):
                    p.wait(e, ("pe", p.C["pe"]))
                    e.copy(rstdB[:], PS[2][:])
                    i = e.copy(mrB[:], PS[3][:])
                    p.inc(i, "ac")
                ln_ac = p.C["ac"]

                for dm in range(NJ):
                    xb = x2t[dm % 2]
                    xs = "x%d" % (dm % 2)

                    @blk.vector
                    def _(e):
                        p.wait(e, ("ac", ln_ac), (xs, p.C[xs]))
                        e.tensor_tensor(xb[:], y1[:, dm * 512:(dm + 1) * 512], rstdB[:], ALU.mult)
                        i = e.tensor_tensor(xb[:], xb[:], mrB[:], ALU.subtract)
                        p.inc(i, "ve")

                    @blk.gpsimd
                    def _(e):
                        p.wait(e, ("ve", p.C["ve"]))
                        i = e.dma_start(out=x2[sq, dm, :, st * 512:(st + 1) * 512], in_=xb[:])
                        p.inc(i, xs, 16)

        @blk.gpsimd
        def _(e):
            p.wait(e, ("x0", p.C["x0"]))
            e.nop()
            e.wait_ge(p.sems["x1"], p.C["x1"])

    pes.close()
    esA.close()



def prep_A_weights(Wq, bqv, Wk, bk, Wv, bv, Wo):
    WkT = np.ascontiguousarray(Wk.T)
    WvT = np.ascontiguousarray(Wv.T)
    wkv = np.empty((2, NJ, 128, 512), np.float32)
    bkv = np.empty((2, 512), np.float32)
    for g in range(2):
        for kk in range(2):
            kvh = 2 * g + kk
            for j in range(NJ):
                wkv[g, j, :, kk * 256:kk * 256 + 128] = WkT[j * 128:(j + 1) * 128, kvh * 128:(kvh + 1) * 128]
                wkv[g, j, :, kk * 256 + 128:kk * 256 + 256] = WvT[j * 128:(j + 1) * 128, kvh * 128:(kvh + 1) * 128]
            bkv[g, kk * 256:kk * 256 + 128] = bk[kvh * 128:(kvh + 1) * 128]
            bkv[g, kk * 256 + 128:kk * 256 + 256] = bv[kvh * 128:(kvh + 1) * 128]
    wq = np.ascontiguousarray(
        Wq.reshape(H, 128, NJ, 128).transpose(0, 3, 2, 1).reshape(H, 128, NJ * 128))
    bq_r = np.ascontiguousarray(bqv.reshape(1, H * 128))
    wo = np.ascontiguousarray(
        Wo.reshape(NJ, 128, H, 128).transpose(0, 3, 2, 1).reshape(NJ, 128, H * 128))
    return dict(wkv=wkv, bkv=bkv, wq=wq, bq=bq_r, wo=wo)


def to_fm(x):
    n, S, d = x.shape
    return np.ascontiguousarray(x.transpose(0, 2, 1).reshape(n, NJ, 128, S))


def from_fm(x):
    n = x.shape[0]
    S = x.shape[3]
    return np.ascontiguousarray(x.reshape(n, NJ * 128, S).transpose(0, 2, 1))







NC0 = 64
RT2 = np.float32(1.0 / np.sqrt(8192.0))


class Mz:

    def __init__(self, p, blk, es):
        self.p = p
        self.blk = blk
        gidx, uidx, didx = p.aps["gidx"], p.aps["uidx"], p.aps["didx"]
        wg, wu, wd = p.aps["wg_scr"], p.aps["wu_scr"], p.aps["wd_scr"]
        self.units = []
        for c in range(NC1):
            for hh in range(2):
                sl = slice(hh * 1024, (hh + 1) * 1024)
                self.units.append((gidx[c][:, sl], wg[c][:, sl], 1024))
                self.units.append((uidx[c][:, sl], wu[c][:, sl], 1024))
        for dm in range(NJ):
            off = 0
            while off < NC1 * 128:
                w = min(1024, NC1 * 128 - off)
                self.units.append((didx[dm][:, off:off + w], wd[dm][:, off:off + w], w))
                off += w
        self.it = [p.sb(f"mzit{i}", [128, 1024], I32, es=es) for i in range(2)]
        self.bt = [p.sb(f"mzbt{i}", [128, 1024], BF16, es=es) for i in range(2)]
        self.at = [p.sb(f"mzat{i}", [128, 1024], BF16, es=es) for i in range(2)]
        self.tt = p.sb("mztt", [128, 1024], BF16, es=es)
        self.u = 0
        self.ops_done = 0
        self.loaded = 0
        self.load_cnt = {}
        self.store_cnt = {}
        self.cast_ve = {}
        self.done_ve = {}

    @property
    def done(self):
        return self.u >= len(self.units)

    def _emit_load(self, u):
        p, blk = self.p, self.blk
        iap, _, w = self.units[u]
        par = u % 2

        @blk.scalar
        def _(e, u=u, iap=iap, w=w, par=par):
            if u >= 2:
                p.wait(e, ("ve", self.cast_ve[u - 2]))
            i = e.dma_start(out=self.it[par][:, 0:w], in_=iap)
            p.inc(i, "i%d" % par, 16)
        self.load_cnt[u] = p.C["i%d" % par]

    def _emit_store(self, u):
        p, blk = self.p, self.blk
        _, oap, w = self.units[u]
        par = u % 2

        @blk.gpsimd
        def _(e, u=u, oap=oap, w=w, par=par):
            p.wait(e, ("ve", self.done_ve[u]))
            i = e.dma_start(out=oap, in_=self.at[par][:, 0:w])
            p.inc(i, "m%d" % par, 16)
        self.store_cnt[u] = p.C["m%d" % par]

    def _emit_chunk(self):
        p, blk = self.p, self.blk
        u = self.u
        par = u % 2
        w = self.units[u][2]
        lo = self.ops_done
        hi = min(lo + 6, 17)

        @blk.vector
        def _(e, u=u, par=par, w=w, lo=lo, hi=hi):
            if lo == 0:
                p.wait(e, ("i%d" % par, self.load_cnt[u]))
                if u >= 2:
                    e.nop()
                    p.wait(e, ("m%d" % par, self.store_cnt[u - 2]))
                elif u == 0:
                    e.nop()
                    p.wait(e, ("ac", p.init_ac))
                i = e.tensor_copy(self.bt[par][:, 0:w], self.it[par][:, 0:w])
                p.inc(i, "ve")
                self.cast_ve[u] = p.C["ve"]
            for op in range(max(lo, 1), hi):
                j = op - 1
                if j == 0:
                    i = e.tensor_scalar(self.at[par][:, 0:w], self.bt[par][:, 0:w],
                                        0.0, p.lutB[:, 0:1], ALU.is_equal, ALU.mult)
                else:
                    e.tensor_scalar(self.tt[:, 0:w], self.bt[par][:, 0:w],
                                    float(j), p.lutB[:, j:j + 1], ALU.is_equal, ALU.mult)
                    i = e.tensor_tensor(self.at[par][:, 0:w], self.at[par][:, 0:w],
                                        self.tt[:, 0:w], ALU.add)
            p.inc(i, "ve")
        self.ops_done = hi
        if hi == 17:
            self.done_ve[u] = p.C["ve"]

    def pump(self, nchunks=1):
        for _ in range(nchunks):
            if self.done:
                return
            if self.ops_done == 0:
                while self.loaded <= min(self.u + 1, len(self.units) - 1):
                    self._emit_load(self.loaded)
                    self.loaded += 1
            self._emit_chunk()
            if self.ops_done == 17:
                self._emit_store(self.u)
                self.u += 1
                self.ops_done = 0

    def finish(self):
        while not self.done:
            self.pump(1)


def emit_B(p, blk, S=SEQ):
    nc = p.nc
    T = NSEQ * S
    NTS = T // 512
    x2 = p.aps["x2"]
    w1h = p.aps["w1h"]
    w3h = p.aps["w3h"]
    w2r = p.aps["w2r"]
    h64 = p.aps["h64"]
    h128 = p.aps["h128"]
    x3 = p.aps["x3"]
    ascr = p.aps["ascr"]
    w2b_scr = p.aps["w2b_scr"]
    ones_r = p.ones_r
    ones_c = p.ones_c

    esB = contextlib.ExitStack()
    h64_sb = p.sb("h64_sb", [64, 64], BF16, es=esB)
    h128_sb = p.sb("h128_sb", [128, 128], BF16, es=esB)

    if True:
        @blk.sync
        def _(e):
            i = e.dma_start(out=h64_sb[:], in_=h64[:, :])
            p.inc(i, "k0", 16)
            i = e.dma_start(out=h128_sb[:], in_=h128[:, :])
            p.inc(i, "k0", 16)
        hl_lx = p.C["k0"]

        pes = contextlib.ExitStack()
        wfb = [p.sb(f"wfbB{i}", [64, 512], F32, es=pes) for i in range(2)]
        wbb = [p.sb(f"wbbB{i}", [64, 512], BF16, es=pes) for i in range(2)]
        wob = [p.sb(f"wobB{i}", [64, 512], BF16, es=pes) for i in range(2)]
        psA = [p.ps(f"psAB{i}", [64, 512], F32, es=pes) for i in range(2)]

        cast_ve = {}
        mm_pe = {}
        ev_ve = {}
        wr_sa = {}
        for s in range(512):
            b, dsub = divmod(s, 4)
            par = s % 2

            @(blk.gpsimd if par == 0 else blk.scalar)
            def _(e):
                if s >= 2:
                    p.wait(e, ("ve", cast_ve[s - 2]))
                i = e.dma_start(out=wfb[par][:], in_=w2r[:, b, dsub * 512:(dsub + 1) * 512])
                p.inc(i, "w%d" % par, 16)
            lw_s = p.C["w%d" % par]

            @blk.vector
            def _(e):
                p.wait(e, ("w%d" % par, lw_s))
                if s >= 2:
                    e.nop()
                    p.wait(e, ("pe", mm_pe[s - 2]))
                i = e.tensor_copy(wbb[par][:], wfb[par][:])
                p.inc(i, "ve")
            cast_ve[s] = p.C["ve"]

            @blk.tensor
            def _(e):
                if s == 0:
                    p.wait(e, ("k0", hl_lx))
                    e.nop()
                p.wait(e, ("ve", cast_ve[s]))
                if s >= 2:
                    e.nop()
                    p.wait(e, ("ve", ev_ve[s - 2]))
                i = e.matmul(psA[par][:], h64_sb[:], wbb[par][:], start=True, stop=True,
                             skip_group_check=True)
                p.inc(i, "pe")
            mm_pe[s] = p.C["pe"]

            @blk.vector
            def _(e):
                p.wait(e, ("pe", mm_pe[s]))
                if s >= 2:
                    e.nop()
                    p.wait(e, ("s%d" % par, wr_sa[s - 2]))
                i = e.tensor_scalar(wob[par][:], psA[par][:], float(RT2), None, ALU.mult)
                p.inc(i, "ve")
            ev_ve[s] = p.C["ve"]

            @blk.sync
            def _(e):
                p.wait(e, ("ve", ev_ve[s]))
                dst = w2b_scr[dsub * 4:(dsub + 1) * 4, b, :, :].rearrange("m a d -> a m d")
                i = e.dma_start(out=dst, in_=wob[par][:].rearrange("a (m d) -> a m d", m=4))
                p.inc(i, "s%d" % par, 16)
            wr_sa[s] = p.C["s%d" % par]

        p.barrier(blk)
        pes.close()

        mzes = contextlib.ExitStack()
        mz = Mz(p, blk, mzes)
        pes = contextlib.ExitStack()
        xb = p.sb("xbB", [128, NJ * T], BF16, es=pes)
        wf = [p.sb(f"wfB{i}", [128, 2 * NJ * 128], F32, es=pes) for i in range(2)]
        wb = [p.sb(f"wbB{i}", [128, 2 * NJ * 128], BF16, es=pes) for i in range(2)]
        stmp = [p.sb(f"stmpB{i}", [128, 512], F32, es=pes) for i in range(2)]
        hsub = [p.sb(f"hsubB{i}", [128, 512], BF16, es=pes) for i in range(2)]
        asub = [p.sb(f"asubB{i}", [128, 512], BF16, es=pes) for i in range(2)]
        pG = [p.ps(f"pGB{i}", [128, 512], F32, es=pes) for i in range(2)]
        pU = [p.ps(f"pUB{i}", [128, 512], F32, es=pes) for i in range(2)]
        pA = [p.ps(f"pAB{i}", [128, 512], F32, es=pes) for i in range(2)]

        xc_ve = {}
        for j in range(NJ):
            stg = wf[j % 2]

            @blk.sync
            def _(e, j=j, stg=stg):
                if j >= 2:
                    p.wait(e, ("ve", xc_ve[j - 2]))
                for sq in range(NSEQ):
                    i = e.dma_start(out=stg[:, sq * S:(sq + 1) * S], in_=x2[sq, j])
                    p.inc(i, "lx", 16)
            lx_j = p.C["lx"]

            @blk.vector
            def _(e, j=j, stg=stg, lx_j=lx_j):
                p.wait(e, ("lx", lx_j))
                i = e.tensor_copy(xb[:, j * T:(j + 1) * T], stg[:])
                p.inc(i, "ve")
            xc_ve[j] = p.C["ve"]
        xcast_ve = p.C["ve"]

        @blk.gpsimd
        def _(e):
            p.wait(e, ("ve", xcast_ve))
            for c in range(2):
                i = e.dma_start(out=wf[c][:, 0:NJ * 128], in_=w1h[c])
                p.inc(i, "w%d" % c, 16)
                i = e.dma_start(out=wf[c][:, NJ * 128:], in_=w3h[c])
                p.inc(i, "w%d" % c, 16)

        wcast_ve = {}
        gu_pe = {}
        hmul_ve = {}
        amm_pe = {}
        aev_ve = {}
        awr_sa = {}
        silu_ac = {}

        for c in range(NC0):
            par = c % 2

            @blk.vector
            def _(e):
                p.wait(e, ("w%d" % par, p.C["w%d" % par]))
                if c >= 2:
                    e.nop()
                    p.wait(e, ("pe", gu_pe[(c - 2, NTS - 1)]))
                e.tensor_copy(wb[par][:, 0:NJ * 128], wf[par][:, 0:NJ * 128])
                i = e.tensor_copy(wb[par][:, NJ * 128:], wf[par][:, NJ * 128:])
                p.inc(i, "ve")
            wcast_ve[c] = p.C["ve"]

            @blk.gpsimd
            def _(e):
                if c + 2 < NC0:
                    p.wait(e, ("ve", wcast_ve[c]))
                    i = e.dma_start(out=wf[par][:, 0:NJ * 128], in_=w1h[c + 2])
                    p.inc(i, "w%d" % par, 16)
                    i = e.dma_start(out=wf[par][:, NJ * 128:], in_=w3h[c + 2])
                    p.inc(i, "w%d" % par, 16)

            for ts in range(NTS):
                g_idx = c * NTS + ts
                q = g_idx % 2

                @blk.tensor
                def _(e):
                    wv = xcast_ve if g_idx == 0 else 0
                    if ts == 0:
                        p.wait(e, ("ve", wcast_ve[c]))
                        e.nop()
                    if g_idx >= 2:
                        p.wait(e, ("ve", hmul_ve[divmod(g_idx - 2, NTS)]))
                    elif wv:
                        p.wait(e, ("ve", wv))
                    for j in range(NJ):
                        e.matmul(pG[q][:], wb[par][:, j * 128:(j + 1) * 128],
                                 xb[:, j * T + ts * 512: j * T + ts * 512 + 512],
                                 start=(j == 0), stop=(j == NJ - 1), skip_group_check=True)
                    for j in range(NJ):
                        i = e.matmul(pU[q][:], wb[par][:, NJ * 128 + j * 128:NJ * 128 + (j + 1) * 128],
                                     xb[:, j * T + ts * 512: j * T + ts * 512 + 512],
                                     start=(j == 0), stop=(j == NJ - 1), skip_group_check=True)
                    p.inc(i, "pe")
                gu_pe[(c, ts)] = p.C["pe"]

                @blk.scalar
                def _(e):
                    p.wait(e, ("pe", gu_pe[(c, ts)]))
                    if g_idx >= 2:
                        e.nop()
                        p.wait(e, ("ve", hmul_ve[divmod(g_idx - 2, NTS)]))
                    i = e.activation(stmp[q][:], pG[q][:], AF.Silu)
                    p.inc(i, "ac")
                silu_ac[(c, ts)] = p.C["ac"]

                @blk.vector
                def _(e):
                    p.wait(e, ("ac", silu_ac[(c, ts)]))
                    if g_idx >= 2:
                        e.nop()
                        p.wait(e, ("pe", amm_pe[divmod(g_idx - 2, NTS)]))
                    i = e.tensor_tensor(hsub[q][:], stmp[q][:], pU[q][:], ALU.mult)
                    p.inc(i, "ve")
                hmul_ve[(c, ts)] = p.C["ve"]

                @blk.tensor
                def _(e):
                    p.wait(e, ("ve", hmul_ve[(c, ts)]))
                    if g_idx >= 2:
                        e.nop()
                        p.wait(e, ("ve", aev_ve[divmod(g_idx - 2, NTS)]))
                    i = e.matmul(pA[q][:], h128_sb[:], hsub[q][:], start=True, stop=True,
                                 skip_group_check=True)
                    p.inc(i, "pe")
                amm_pe[(c, ts)] = p.C["pe"]

                @blk.vector
                def _(e):
                    p.wait(e, ("pe", amm_pe[(c, ts)]))
                    if g_idx >= 2:
                        e.nop()
                        p.wait(e, ("a%d" % q, awr_sa[divmod(g_idx - 2, NTS)]))
                    i = e.tensor_copy(asub[q][:], pA[q][:])
                    p.inc(i, "ve")
                aev_ve[(c, ts)] = p.C["ve"]

                @blk.sync
                def _(e):
                    p.wait(e, ("ve", aev_ve[(c, ts)]))
                    i = e.dma_start(out=ascr[c, :, ts * 512:(ts + 1) * 512], in_=asub[q][:])
                    p.inc(i, "a%d" % q, 16)
                awr_sa[(c, ts)] = p.C["a%d" % q]
                mz.pump(2)

        p.barrier(blk)
        pes.close()

        pes = contextlib.ExitStack()
        acol = p.sb("acolB", [128, NC0 * 512], BF16, es=pes)
        w2bt = [p.sb(f"w2btB{i}", [128, 64 * 128], BF16, es=pes) for i in range(2)]
        res = [p.sb(f"resB{i}", [128, 512], F32, es=pes) for i in range(2)]
        y2 = p.sb("y2B", [128, NJ * 512], F32, es=pes)
        y2b = p.sb("y2bB", [128, 512], BF16, es=pes)
        y2s = p.sb("y2sB", [128, 512], BF16, es=pes)
        srow = p.sb("srowB", [1, 1536], F32, es=pes)
        rstdB = p.sb("rstdBB", [128, 512], F32, es=pes)
        mrB = p.sb("mrBB", [128, 512], F32, es=pes)
        x3t = [p.sb(f"x3tB{i}", [128, 512], F32, es=pes) for i in range(2)]
        PS = [p.ps(f"PSB{i}", [128, 512], F32, es=pes) for i in range(6)]

        NTT = T // 512
        for tt in range(NTT):
            sq, trs = divmod(tt * 512, S)
            pe_prev_tile = p.C["pe"]
            ve_prev_tile = p.C["ve"]

            @blk.sync
            def _(e):
                if tt > 0:
                    p.wait(e, ("pe", pe_prev_tile))
                for a in range(NC0):
                    i = e.dma_start(out=acol[:, a * 512:(a + 1) * 512],
                                    in_=ascr[a, :, tt * 512:(tt + 1) * 512])
                    p.inc(i, "lx", 16)
            acol_lx = p.C["lx"]

            w2b_lw = {}
            res_lx = {}

            @blk.gpsimd
            def _(e):
                if tt > 0:
                    p.wait(e, ("pe", pe_prev_tile))
                for dmp in range(2):
                    i = e.dma_start(out=w2bt[dmp][:], in_=w2b_scr[dmp])
                    i2 = e.dma_start(out=res[dmp][:], in_=x2[sq, dmp, :, trs:trs + 512])
                    i.then_inc(p.sems["d%d" % dmp], 16)
                    p.C["d%d" % dmp] += 16
                    w2b_lw[dmp] = p.inc(i2, "d%d" % dmp, 16)
                    res_lx[dmp] = w2b_lw[dmp]

            y2_ve = {}
            sqr_ac = {}
            dmm_pe = {}

            def down_mm(dm):
                po = PS[dm % 2]

                @blk.tensor
                def _(e):
                    p.wait(e, ("d%d" % (dm % 2), w2b_lw[dm]))
                    e.nop()
                    if dm == 0:
                        p.wait(e, ("lx", acol_lx))
                    elif dm >= 2:
                        p.wait(e, ("ve", y2_ve[dm - 2]))
                    for a in range(NC0):
                        i = e.matmul(po[:], w2bt[dm % 2][:, a * 128:(a + 1) * 128],
                                     acol[:, a * 512:(a + 1) * 512],
                                     start=(a == 0), stop=(a == NC0 - 1), skip_group_check=True)
                    p.inc(i, "pe")
                dmm_pe[dm] = p.C["pe"]

            def stats_mm(dm):
                @blk.tensor
                def _(e):
                    p.wait(e, ("ac", sqr_ac[dm]))
                    e.matmul(PS[4][0:1, :], ones_c[:], y2b[:],
                             start=(dm == 0), stop=(dm == NJ - 1), skip_group_check=True)
                    i = e.matmul(PS[5][0:1, :], ones_c[:], y2s[:],
                                 start=(dm == 0), stop=(dm == NJ - 1), skip_group_check=True)
                    p.inc(i, "pe")

            def down_evac(dm):
                po = PS[dm % 2]
                pe_snap = p.C["pe"]

                @blk.vector
                def _(e):
                    p.wait(e, ("d%d" % (dm % 2), res_lx[dm]))
                    e.nop()
                    p.wait(e, ("pe", pe_snap))
                    e.tensor_tensor(y2[:, dm * 512:(dm + 1) * 512], po[:], res[dm % 2][:], ALU.add)
                    i = e.tensor_copy(y2b[:], y2[:, dm * 512:(dm + 1) * 512])
                    p.inc(i, "ve")
                y2_ve[dm] = p.C["ve"]

                @blk.scalar
                def _(e):
                    p.wait(e, ("ve", y2_ve[dm]))
                    i = e.activation(y2s[:], y2[:, dm * 512:(dm + 1) * 512], AF.Square)
                    p.inc(i, "ac")
                sqr_ac[dm] = p.C["ac"]

            def prefetch_next(dm):
                @blk.gpsimd
                def _(e):
                    p.wait(e, ("pe", dmm_pe[dm]), ("ve", y2_ve[dm]))
                    i = e.dma_start(out=w2bt[dm % 2][:], in_=w2b_scr[dm + 2])
                    i2 = e.dma_start(out=res[dm % 2][:], in_=x2[sq, dm + 2, :, trs:trs + 512])
                    i.then_inc(p.sems["d%d" % (dm % 2)], 16)
                    p.C["d%d" % (dm % 2)] += 16
                    w2b_lw[dm + 2] = p.inc(i2, "d%d" % (dm % 2), 16)
                    res_lx[dm + 2] = w2b_lw[dm + 2]

            for dm in range(NJ):
                down_mm(dm)
                if dm > 0:
                    stats_mm(dm - 1)
                down_evac(dm)
                if dm + 2 < NJ:
                    prefetch_next(dm)
                mz.pump(1)
            stats_mm(NJ - 1)

            @blk.vector
            def _(e):
                p.wait(e, ("pe", p.C["pe"]))
                e.tensor_scalar(srow[0:1, 0:512], PS[4][0:1, :], 1.0 / D, None, ALU.mult)
                e.tensor_scalar(srow[0:1, 512:1024], PS[5][0:1, :], 1.0 / D, None, ALU.mult)
                e.tensor_tensor(srow[0:1, 1024:1536], srow[0:1, 0:512], srow[0:1, 0:512], ALU.mult)
                e.tensor_tensor(srow[0:1, 512:1024], srow[0:1, 512:1024], srow[0:1, 1024:1536], ALU.subtract)
                i = e.tensor_scalar(srow[0:1, 512:1024], srow[0:1, 512:1024], LN_EPS, None, ALU.add)
                p.inc(i, "ve")

            @blk.scalar
            def _(e):
                p.wait(e, ("ve", p.C["ve"]))
                i = e.activation(srow[0:1, 512:1024], srow[0:1, 512:1024], AF.Sqrt)
                p.inc(i, "ac")

            @blk.vector
            def _(e):
                p.wait(e, ("ac", p.C["ac"]))
                e.reciprocal(srow[0:1, 512:1024], srow[0:1, 512:1024])
                i = e.tensor_tensor(srow[0:1, 1024:1536], srow[0:1, 0:512], srow[0:1, 512:1024], ALU.mult)
                p.inc(i, "ve")

            @blk.tensor
            def _(e):
                p.wait(e, ("ve", p.C["ve"]))
                e.matmul(PS[2][:], ones_r[:], srow[0:1, 512:1024], start=True, stop=True,
                         skip_group_check=True)
                i = e.matmul(PS[3][:], ones_r[:], srow[0:1, 1024:1536], start=True, stop=True,
                             skip_group_check=True)
                p.inc(i, "pe")

            @blk.scalar
            def _(e):
                p.wait(e, ("pe", p.C["pe"]))
                e.copy(rstdB[:], PS[2][:])
                i = e.copy(mrB[:], PS[3][:])
                p.inc(i, "ac")
            ln_ac = p.C["ac"]

            for dm in range(NJ):
                xo = x3t[dm % 2]
                xs = "x%d" % (dm % 2)

                @blk.vector
                def _(e):
                    p.wait(e, ("ac", ln_ac), (xs, p.C[xs]))
                    e.tensor_tensor(xo[:], y2[:, dm * 512:(dm + 1) * 512], rstdB[:], ALU.mult)
                    i = e.tensor_tensor(xo[:], xo[:], mrB[:], ALU.subtract)
                    p.inc(i, "ve")

                @blk.gpsimd
                def _(e):
                    p.wait(e, ("ve", p.C["ve"]))
                    i = e.dma_start(out=x3[sq, dm, :, trs:trs + 512], in_=xo[:])
                    p.inc(i, xs, 16)

        mz.finish()

        @blk.gpsimd
        def _(e):
            p.wait(e, ("x0", p.C["x0"]))
            e.nop()
            e.wait_ge(p.sems["x1"], p.C["x1"])

    pes.close()
    mzes.close()
    esB.close()


def prep_B_weights(W1, W2, W3):
    w1h = np.ascontiguousarray(
        W1.reshape(NC0, 128, NJ, 128).transpose(0, 3, 2, 1).reshape(NC0, 128, NJ * 128)).astype(np.float32)
    w3h = np.ascontiguousarray(
        W3.reshape(NC0, 128, NJ, 128).transpose(0, 3, 2, 1).reshape(NC0, 128, NJ * 128)).astype(np.float32)
    w2r = np.ascontiguousarray(W2.reshape(D, 64, 128).transpose(1, 2, 0)).astype(np.float32)

    def had(k):
        Hm = np.array([[1.0]], np.float32)
        while Hm.shape[0] < k:
            Hm = np.block([[Hm, Hm], [Hm, -Hm]])
        return Hm
    h64 = had(64).astype(BF)
    h128 = had(128).astype(BF)
    return dict(w1h=w1h, w3h=w3h, w2r=w2r, h64=h64, h128=h128)





def emit_C(p, blk, S=SEQ):
    nc = p.nc
    NST = S // 512
    SC = float(1.0 / np.sqrt(128.0))
    x3 = p.aps["x3"]
    wqkv = p.aps["wqkv"]
    bqkv = p.aps["bqkv"]
    wout = p.aps["wout"]
    bout = p.aps["bout"]
    maskT = p.aps["maskT"]
    maskS = p.aps["maskS"]
    idb = p.aps["idb"]
    idf = p.aps["idf"]
    x4 = p.aps["x4"]
    wqkv_scr = p.aps["wqkv_scr"]
    wout_scr = p.aps["wout_scr"]
    qscr = p.aps["qscr"]
    kscr = p.aps["kscr"]
    vscr = p.aps["vscr"]
    ones_r = p.ones_r
    ones_c = p.ones_c

    esC = contextlib.ExitStack()
    maskT_sb = p.sb("maskT_sb", [128, 128], F32, es=esC)
    maskS_sb = p.sb("maskS_sb", [128, 128], F32, es=esC)
    idb_sb = p.sb("idb_sb", [128, 128], BF16, es=esC)
    idf_sb = p.sb("idf_sb", [128, 128], F32, es=esC)
    bqkv_sb = p.sb("bqkv_sb", [128, 48], F32, es=esC)
    bout_sb = p.sb("bout_sb", [128, NJ], F32, es=esC)

    if True:
        pes = contextlib.ExitStack()
        brow = p.sb("browC", [1, 48 * 128], F32, es=pes)
        borow = p.sb("borowC", [1, NJ * 128], F32, es=pes)
        pbias = p.ps("pbiasC", [128, 512], F32, es=pes)

        @blk.sync
        def _(e):
            for dst, srcap in ((brow[:], bqkv[:, :]), (borow[:], bout[:, :]),
                               (maskT_sb[:], maskT[:, :]), (maskS_sb[:], maskS[:, :]),
                               (idb_sb[:], idb[:, :]), (idf_sb[:], idf[:, :])):
                i = e.dma_start(out=dst, in_=srcap)
                p.inc(i, "lx", 16)

        @blk.tensor
        def _(e):
            p.wait(e, ("lx", p.C["lx"]), ("ve", p.init_ve))
            for t in range(48):
                i = e.matmul(pbias[:, t:t + 1], brow[0:1, t * 128:(t + 1) * 128],
                             ones_r[0:1, 0:1], start=True, stop=True, skip_group_check=True)
            p.inc(i, "pe")

        @blk.scalar
        def _(e):
            p.wait(e, ("pe", p.C["pe"]))
            i = e.copy(bqkv_sb[:, 0:48], pbias[:, 0:48])
            p.inc(i, "ac")

        @blk.tensor
        def _(e):
            p.wait(e, ("ac", p.C["ac"]))
            for t in range(NJ):
                i = e.matmul(pbias[:, t:t + 1], borow[0:1, t * 128:(t + 1) * 128],
                             ones_r[0:1, 0:1], start=True, stop=True, skip_group_check=True)
            p.inc(i, "pe")

        @blk.scalar
        def _(e):
            p.wait(e, ("pe", p.C["pe"]))
            i = e.copy(bout_sb[:, 0:NJ], pbias[:, 0:NJ])
            p.inc(i, "ac")

        wtmp = [p.sb(f"wtmpC{i}", [128, 2048], F32, es=pes) for i in range(2)]
        wbt = [p.sb(f"wbtC{i}", [128, 2048], BF16, es=pes) for i in range(2)]
        for t in range(48 + NJ):
            src_t = wqkv[t] if t < 48 else wout[t - 48]
            dst_t = wqkv_scr[t] if t < 48 else wout_scr[t - 48]
            buf = wtmp[t % 2]
            bbuf = wbt[t % 2]
            ws = "w%d" % (t % 2)
            ss = "s%d" % (t % 2)

            @blk.gpsimd
            def _(e):
                p.wait(e, ("ve", p.C["ve"] - 1))
                i = e.dma_start(out=buf[:], in_=src_t)
                p.inc(i, ws, 16)

            @blk.vector
            def _(e):
                p.wait(e, (ws, p.C[ws]), (ss, p.C[ss]))
                i = e.tensor_copy(bbuf[:], buf[:])
                p.inc(i, "ve")

            @blk.sync
            def _(e):
                p.wait(e, ("ve", p.C["ve"]))
                i = e.dma_start(out=dst_t, in_=bbuf[:])
                p.inc(i, ss, 16)

        p.barrier(blk)
        pes.close()

        pes = contextlib.ExitStack()
        xstg = p.sb("xstgC", [128, S], F32, es=pes)
        xb = p.sb("xbC", [128, NJ * S], BF16, es=pes)
        wqb = [p.sb(f"wqbC{i}", [128, 2048], BF16, es=pes) for i in range(2)]
        ev = [p.sb(f"evC{i}", [128, 512], BF16, es=pes) for i in range(2)]
        vtv = [p.sb(f"vtvC{i}", [128, 512], BF16, es=pes) for i in range(2)]
        PSq = [p.ps(f"PSqC{i}", [128, 512], F32, es=pes) for i in range(2)]
        PSt = [p.ps(f"PStC{i}", [128, 512], BF16, es=pes) for i in range(2)]

        for sq in range(NSEQ):
            for j in range(NJ):
                @blk.sync
                def _(e):
                    if j > 0 or sq > 0:
                        p.wait(e, ("ve", p.C["ve"]))
                    if sq > 0 and j == 0:
                        e.nop()
                        p.wait(e, ("pe", p.C["pe"]))
                    i = e.dma_start(out=xstg[:], in_=x3[sq, j])
                    p.inc(i, "lx", 16)

                @blk.vector
                def _(e):
                    p.wait(e, ("lx", p.C["lx"]))
                    i = e.tensor_copy(xb[:, j * S:(j + 1) * S], xstg[:])
                    p.inc(i, "ve")
            xcast_ve = p.C["ve"]

            @blk.gpsimd
            def _(e, xcast_ve=xcast_ve):
                p.wait(e, ("ve", xcast_ve))
                i = e.dma_start(out=wqb[0][:], in_=wqkv_scr[0])
                p.inc(i, "q0", 16)

            proj_pe = {}
            use_ve = {}
            for o in range(48):
                par = o % 2
                is_v = o >= 32

                @blk.gpsimd
                def _(e, o=o):
                    if o + 1 < 48:
                        p.wait(e, ("pe", proj_pe.get(o - 1, 0)))
                        i = e.dma_start(out=wqb[(o + 1) % 2][:], in_=wqkv_scr[o + 1])
                        p.inc(i, "q%d" % ((o + 1) % 2), 16)

                for st in range(NST):
                    fidx = o * NST + st
                    stq = fidx % 2

                    @blk.tensor
                    def _(e, o=o, st=st, stq=stq, par=par, fidx=fidx):
                        wv = use_ve.get(divmod(fidx - 2, NST), 0) if fidx >= 2 else 0
                        p.wait(e, ("q%d" % par, p.C["q%d" % par]),
                               ("ve", max(xcast_ve, wv)))
                        for j in range(NJ):
                            i = e.matmul(PSq[stq][:], wqb[par][:, j * 128:(j + 1) * 128],
                                         xb[:, j * S + st * 512: j * S + st * 512 + 512],
                                         start=(j == 0), stop=(j == NJ - 1), skip_group_check=True)
                        p.inc(i, "pe")
                    proj_pe[o] = p.C["pe"]
                    mm_pe = p.C["pe"]

                    @blk.vector
                    def _(e, o=o, st=st, stq=stq, mm_pe=mm_pe):
                        p.wait(e, ("pe", mm_pe), ("s%d" % stq, p.C["s%d" % stq]))
                        i = e.tensor_scalar(ev[stq][:], PSq[stq][:], bqkv_sb[:, o:o + 1], None, ALU.add)
                        p.inc(i, "ve")
                    bias_ve = p.C["ve"]

                    if not is_v:
                        use_ve[(o, st)] = bias_ve

                        @blk.sync
                        def _(e, o=o, st=st, stq=stq, bias_ve=bias_ve):
                            p.wait(e, ("ve", bias_ve))
                            dst = qscr if o < 16 else kscr
                            hh = o if o < 16 else o - 16
                            i = e.dma_start(out=dst[sq, hh, :, st * 512:(st + 1) * 512], in_=ev[stq][:])
                            p.inc(i, "s%d" % stq, 16)
                    else:
                        @blk.tensor
                        def _(e, st=st, stq=stq, bias_ve=bias_ve):
                            p.wait(e, ("ve", bias_ve))
                            for u in range(4):
                                i = e.transpose(PSt[stq][:, u * 128:(u + 1) * 128],
                                                ev[stq][:, u * 128:(u + 1) * 128], idb_sb[:])
                            p.inc(i, "pe")
                        tr_pe = p.C["pe"]

                        @blk.vector
                        def _(e, stq=stq, tr_pe=tr_pe):
                            p.wait(e, ("pe", tr_pe))
                            i = e.tensor_copy(vtv[stq][:], PSt[stq][:])
                            p.inc(i, "ve")
                        use_ve[(o, st)] = p.C["ve"]

                        @blk.sync
                        def _(e, o=o, st=st, stq=stq):
                            p.wait(e, ("ve", use_ve[(o, st)]))
                            hh = o - 32
                            dst = vscr[sq, hh, st * 512:(st + 1) * 512, :].rearrange(
                                "(u s) e -> s u e", u=4)
                            i = e.dma_start(out=dst, in_=vtv[stq][:].rearrange("s (u e) -> s u e", u=4))
                            p.inc(i, "s%d" % stq, 16)

        p.barrier(blk)
        pes.close()

        pes = contextlib.ExitStack()
        qt_b = [p.sb(f"qtC{i}", [128, 512], BF16, es=pes) for i in range(2)]
        kt_b = [p.sb(f"ktC{i}", [128, S], BF16, es=pes) for i in range(2)]
        vt_b = [p.sb(f"vtC{i}", [128, S], BF16, es=pes) for i in range(2)]
        ET2 = [p.sb(f"ETC{i}", [128, S // 128, 512], BF16, es=pes) for i in range(2)]
        E_sb = [p.sb(f"EsbC{i}", [128, 2048], BF16, es=pes) for i in range(2)]
        tmpcol = p.sb("tmpcolC", [128, 1], F32, es=pes)
        rmneg = [p.sb(f"rmnegC{i}", [128, 1], F32, es=pes) for i in range(2)]
        rrow = p.sb("rrowC", [1, 1024], F32, es=pes)
        recB = p.sb("recBC", [128, 512], F32, es=pes)
        rmax = p.sb("rmaxC", [128, 1], F32, es=pes)
        ostrip = p.sb("ostripC", [128, H * 512], BF16, es=pes)
        wob = [p.sb(f"wobC{i}", [128, 2048], BF16, es=pes) for i in range(2)]
        resb = [p.sb(f"resbC{i}", [128, 512], F32, es=pes) for i in range(2)]
        y1 = p.sb("y1C", [128, NJ * 512], F32, es=pes)
        y1b = p.sb("y1bC", [128, 512], BF16, es=pes)
        y1s = p.sb("y1sC", [128, 512], BF16, es=pes)
        srow = p.sb("srowC", [1, 1536], F32, es=pes)
        rstdB = p.sb("rstdBC", [128, 512], F32, es=pes)
        mrB = p.sb("mrBC", [128, 512], F32, es=pes)
        x4t = [p.sb(f"x4tC{i}", [128, 512], F32, es=pes) for i in range(2)]
        PS = [p.ps(f"PSC{i}", [128, 512], F32, es=pes) for i in range(6)]
        PTB = [p.ps(f"PTBC{i}", [128, 512], BF16, es=pes) for i in range(2)]
        tr_g = [0]
        tr_evac_ve = {}

        for sq in range(NSEQ):
            for qb in range(NST):
                nkt = (qb + 1) * 4
                kv_ve_prev = p.C["ve"]
                kv_done = {}

                @blk.gpsimd
                def _(e, sq=sq, qb=qb, nkt=nkt, kv_ve_prev=kv_ve_prev):
                    p.wait(e, ("pe", p.C["pe"]))
                    for hp in range(2):
                        ks = "k%d" % hp
                        i = e.dma_start(out=qt_b[hp][:], in_=qscr[sq, hp, :, qb * 512:(qb + 1) * 512])
                        p.inc(i, ks, 16)
                        i = e.dma_start(out=kt_b[hp][:, 0:nkt * 128], in_=kscr[sq, hp, :, 0:nkt * 128])
                        p.inc(i, ks, 16)
                        i = e.dma_start(out=vt_b[hp][:, 0:nkt * 128],
                                        in_=vscr[sq, hp, 0:nkt * 128, :].rearrange("(u s) e -> s u e", s=128))
                        kv_done[hp] = p.inc(i, ks, 16)

                wo_lw = {}

                @blk.sync
                def _(e, qb=qb):
                    p.wait(e, ("ve", kv_ve_prev))
                    for dmp in range(2):
                        i = e.dma_start(out=wob[dmp][:], in_=wout_scr[dmp])
                        wo_lw[dmp] = p.inc(i, "o%d" % dmp, 16)
                        i = e.dma_start(out=resb[dmp][:], in_=x3[sq, dmp, :, qb * 512:(qb + 1) * 512])
                        p.C["o%d" % dmp] += 16
                        i.then_inc(p.sems["o%d" % dmp], 16)
                        wo_lw[dmp] = p.C["o%d" % dmp]

                ostrip_ve = {}
                etdone_ve = {}
                last_exp_ac = {}
                den_pe = {}
                qb_ac0 = p.C["ac"]
                nkb = qb + 1

                def head_front(h):
                    hp = h % 2
                    eb = ET2[h % 2]
                    exp_acs = {}
                    for qt in range(4):
                        base = (qt % 2) * nkb if nkb <= 2 else 0

                        @blk.tensor
                        def _(e, h=h, hp=hp, qt=qt, base=base):
                            if qt == 0:
                                p.wait(e, ("k%d" % hp, kv_done[h]))
                                e.nop()
                                if h >= 1:
                                    p.wait(e, ("ac", last_exp_ac[h - 1]))
                                else:
                                    p.wait(e, ("ac", qb_ac0))
                            elif nkb <= 2 and qt >= 2:
                                p.wait(e, ("ac", exp_acs[qt - 2]))
                            elif nkb > 2:
                                p.wait(e, ("ac", exp_acs[qt - 1]))
                            for kb in range(nkb):
                                i = e.matmul(PS[base + kb][:], qt_b[hp][:, qt * 128:(qt + 1) * 128],
                                             kt_b[hp][:, kb * 512:(kb + 1) * 512],
                                             start=True, stop=True, skip_group_check=True)
                            p.inc(i, "pe")
                        smm_pe = p.C["pe"]

                        @blk.vector
                        def _(e, qt=qt, qb=qb, base=base, smm_pe=smm_pe):
                            p.wait(e, ("pe", smm_pe))
                            for kb in range(nkb):
                                bank = PS[base + kb]
                                if kb == qb:
                                    e.tensor_tensor(bank[:, qt * 128:(qt + 1) * 128],
                                                    bank[:, qt * 128:(qt + 1) * 128],
                                                    maskS_sb[:], ALU.add)
                                    hi = qt * 128 + 128
                                else:
                                    hi = 512
                                if kb == 0:
                                    e.reduce_max(rmax[:], bank[:, 0:hi], axis=AX.X)
                                else:
                                    e.reduce_max(tmpcol[:], bank[:, 0:hi], axis=AX.X)
                                    e.tensor_tensor(rmax[:], rmax[:], tmpcol[:], ALU.max)
                            i = e.tensor_scalar(rmneg[qt % 2][:], rmax[:], -SC, None, ALU.mult)
                            p.inc(i, "ve")
                        mx_ve = p.C["ve"]

                        @blk.scalar
                        def _(e, qt=qt, qb=qb, base=base, mx_ve=mx_ve):
                            p.wait(e, ("ve", mx_ve))
                            for kb in range(nkb):
                                hi = qt * 128 + 128 if kb == qb else 512
                                i = e.activation(E_sb[qt % 2][:, kb * 512:kb * 512 + hi],
                                                 PS[base + kb][:, 0:hi], AF.Exp,
                                                 bias=rmneg[qt % 2][:], scale=SC)
                            p.inc(i, "ac")
                        exp_acs[qt] = p.C["ac"]

                        for kb in range(nkb):
                            g = tr_g[0]
                            tr_g[0] += 1
                            pt = PTB[g % 2]
                            vks = (qt + 1) if kb == qb else 4

                            @blk.tensor
                            def _(e, qt=qt, kb=kb, g=g, pt=pt, vks=vks, ea=exp_acs[qt]):
                                p.wait(e, ("ac", ea))
                                if g >= 2:
                                    e.nop()
                                    p.wait(e, ("ve", tr_evac_ve[g - 2]))
                                for ks in range(vks):
                                    i = e.transpose(pt[:, ks * 128:(ks + 1) * 128],
                                                    E_sb[qt % 2][:, kb * 512 + ks * 128:kb * 512 + (ks + 1) * 128],
                                                    idb_sb[:])
                                p.inc(i, "pe")
                            tr_pe = p.C["pe"]

                            @blk.vector
                            def _(e, qt=qt, kb=kb, qb=qb, pt=pt, vks=vks, eb=eb, tr_pe=tr_pe):
                                p.wait(e, ("pe", tr_pe))
                                i = e.tensor_copy(eb[:, kb * 4:kb * 4 + vks, qt * 128:(qt + 1) * 128],
                                                  pt[:, 0:vks * 128].rearrange("s (u w) -> s u w", w=128))
                                if kb == qb and qt < 3:
                                    i = e.memset(eb[:, kb * 4 + qt + 1:kb * 4 + 4, qt * 128:(qt + 1) * 128], 0.0)
                                p.inc(i, "ve")
                            tr_evac_ve[g] = p.C["ve"]
                    etdone_ve[h] = p.C["ve"]
                    last_exp_ac[h] = exp_acs[3]

                def head_back(h):
                    hp = h % 2
                    eb = ET2[h % 2]

                    @blk.tensor
                    def _(e, h=h, hp=hp, eb=eb):
                        p.wait(e, ("ve", etdone_ve[h]))
                        e.nop()
                        p.wait(e, ("ve", ostrip_ve[h - 1] if h >= 1 else kv_ve_prev))
                        for kt in range(nkt):
                            i = e.matmul(PS[5][0:1, :], ones_c[:], eb[:, kt, :],
                                         start=(kt == 0), stop=(kt == nkt - 1), skip_group_check=True)
                        for kt in range(nkt):
                            i = e.matmul(PS[4][:], vt_b[hp][:, kt * 128:(kt + 1) * 128],
                                         eb[:, kt, :],
                                         start=(kt == 0), stop=(kt == nkt - 1), skip_group_check=True)
                        p.inc(i, "pe")
                    den_pe[h] = p.C["pe"]

                    @blk.vector
                    def _(e, h=h):
                        p.wait(e, ("pe", den_pe[h]))
                        i = e.tensor_scalar(rrow[0:1, 0:512], PS[5][0:1, :], 1e-30, None, ALU.add)
                        p.inc(i, "ve")
                    rec_ve = p.C["ve"]

                    @blk.tensor
                    def _(e, rec_ve=rec_ve):
                        p.wait(e, ("ve", rec_ve))
                        i = e.matmul(PS[5][:], ones_r[:], rrow[0:1, 0:512],
                                     start=True, stop=True, skip_group_check=True)
                        p.inc(i, "pe")
                    rb_pe = p.C["pe"]

                    @blk.vector
                    def _(e, h=h, rb_pe=rb_pe):
                        p.wait(e, ("pe", rb_pe))
                        e.reciprocal(recB[:], PS[5][:])
                        i = e.tensor_tensor(ostrip[:, h * 512:(h + 1) * 512], PS[4][:], recB[:], ALU.mult)
                        p.inc(i, "ve")
                    ostrip_ve[h] = p.C["ve"]

                    if h + 2 < H:
                        @blk.gpsimd
                        def _(e, sq=sq, qb=qb, h=h, hp=hp, nkt=nkt):
                            p.wait(e, ("pe", den_pe[h]))
                            ks = "k%d" % hp
                            i = e.dma_start(out=qt_b[hp][:], in_=qscr[sq, h + 2, :, qb * 512:(qb + 1) * 512])
                            p.inc(i, ks, 16)
                            i = e.dma_start(out=kt_b[hp][:, 0:nkt * 128], in_=kscr[sq, h + 2, :, 0:nkt * 128])
                            p.inc(i, ks, 16)
                            i = e.dma_start(out=vt_b[hp][:, 0:nkt * 128],
                                            in_=vscr[sq, h + 2, 0:nkt * 128, :].rearrange("(u s) e -> s u e", s=128))
                            kv_done[h + 2] = p.inc(i, ks, 16)

                for h in range(H):
                    head_front(h)
                    if h > 0:
                        head_back(h - 1)
                head_back(H - 1)

                attn_done_ve = p.C["ve"]
                y1_ve = {}
                sqr_ac = {}
                wo_pe = {}

                def wo_mm_tensor(dm):
                    po = PS[dm % 2]

                    @blk.tensor
                    def _(e, dm=dm, po=po):
                        p.wait(e, ("o%d" % (dm % 2), wo_lw[dm]))
                        e.nop()
                        if dm == 0:
                            p.wait(e, ("ve", attn_done_ve))
                        elif dm >= 2:
                            p.wait(e, ("ve", y1_ve[dm - 2]))
                        for hh in range(H):
                            i = e.matmul(po[:], wob[dm % 2][:, hh * 128:(hh + 1) * 128],
                                         ostrip[:, hh * 512:(hh + 1) * 512],
                                         start=(hh == 0), stop=(hh == H - 1), skip_group_check=True)
                        p.inc(i, "pe")
                    wo_pe[dm] = p.C["pe"]

                def stats_mm(dm):
                    @blk.tensor
                    def _(e, dm=dm):
                        p.wait(e, ("ac", sqr_ac[dm]))
                        e.matmul(PS[4][0:1, :], ones_c[:], y1b[:],
                                 start=(dm == 0), stop=(dm == NJ - 1), skip_group_check=True)
                        i = e.matmul(PS[5][0:1, :], ones_c[:], y1s[:],
                                     start=(dm == 0), stop=(dm == NJ - 1), skip_group_check=True)
                        p.inc(i, "pe")

                def wo_evac(dm):
                    po = PS[dm % 2]
                    pe_snap = p.C["pe"]

                    @blk.vector
                    def _(e, dm=dm, po=po, pe_snap=pe_snap):
                        p.wait(e, ("pe", pe_snap))
                        e.tensor_scalar(y1[:, dm * 512:(dm + 1) * 512], po[:],
                                        bout_sb[:, dm:dm + 1], None, ALU.add)
                        e.tensor_tensor(y1[:, dm * 512:(dm + 1) * 512],
                                        y1[:, dm * 512:(dm + 1) * 512], resb[dm % 2][:], ALU.add)
                        i = e.tensor_copy(y1b[:], y1[:, dm * 512:(dm + 1) * 512])
                        p.inc(i, "ve")
                    y1_ve[dm] = p.C["ve"]

                    @blk.scalar
                    def _(e, dm=dm):
                        p.wait(e, ("ve", y1_ve[dm]))
                        i = e.activation(y1s[:], y1[:, dm * 512:(dm + 1) * 512], AF.Square)
                        p.inc(i, "ac")
                    sqr_ac[dm] = p.C["ac"]

                def prefetch_wo(dm):
                    @blk.sync
                    def _(e, dm=dm, sq=sq, qb=qb):
                        p.wait(e, ("pe", wo_pe[dm]), ("ve", y1_ve[dm]))
                        i = e.dma_start(out=wob[dm % 2][:], in_=wout_scr[dm + 2])
                        p.C["o%d" % (dm % 2)] += 16
                        i.then_inc(p.sems["o%d" % (dm % 2)], 16)
                        i = e.dma_start(out=resb[dm % 2][:], in_=x3[sq, dm + 2, :, qb * 512:(qb + 1) * 512])
                        wo_lw[dm + 2] = p.inc(i, "o%d" % (dm % 2), 16)

                for dm in range(NJ):
                    wo_mm_tensor(dm)
                    if dm > 0:
                        stats_mm(dm - 1)
                    wo_evac(dm)
                    if dm + 2 < NJ:
                        prefetch_wo(dm)
                stats_mm(NJ - 1)

                @blk.vector
                def _(e):
                    p.wait(e, ("pe", p.C["pe"]))
                    e.tensor_scalar(srow[0:1, 0:512], PS[4][0:1, :], 1.0 / D, None, ALU.mult)
                    e.tensor_scalar(srow[0:1, 512:1024], PS[5][0:1, :], 1.0 / D, None, ALU.mult)
                    e.tensor_tensor(srow[0:1, 1024:1536], srow[0:1, 0:512], srow[0:1, 0:512], ALU.mult)
                    e.tensor_tensor(srow[0:1, 512:1024], srow[0:1, 512:1024], srow[0:1, 1024:1536], ALU.subtract)
                    i = e.tensor_scalar(srow[0:1, 512:1024], srow[0:1, 512:1024], LN_EPS, None, ALU.add)
                    p.inc(i, "ve")

                @blk.scalar
                def _(e):
                    p.wait(e, ("ve", p.C["ve"]))
                    i = e.activation(srow[0:1, 512:1024], srow[0:1, 512:1024], AF.Sqrt)
                    p.inc(i, "ac")

                @blk.vector
                def _(e):
                    p.wait(e, ("ac", p.C["ac"]))
                    e.reciprocal(srow[0:1, 512:1024], srow[0:1, 512:1024])
                    i = e.tensor_tensor(srow[0:1, 1024:1536], srow[0:1, 0:512], srow[0:1, 512:1024], ALU.mult)
                    p.inc(i, "ve")

                @blk.tensor
                def _(e):
                    p.wait(e, ("ve", p.C["ve"]))
                    e.matmul(PS[2][:], ones_r[:], srow[0:1, 512:1024], start=True, stop=True,
                             skip_group_check=True)
                    i = e.matmul(PS[3][:], ones_r[:], srow[0:1, 1024:1536], start=True, stop=True,
                                 skip_group_check=True)
                    p.inc(i, "pe")

                @blk.scalar
                def _(e):
                    p.wait(e, ("pe", p.C["pe"]))
                    e.copy(rstdB[:], PS[2][:])
                    i = e.copy(mrB[:], PS[3][:])
                    p.inc(i, "ac")
                ln_ac = p.C["ac"]

                for dm in range(NJ):
                    xo = x4t[dm % 2]
                    xs = "x%d" % (dm % 2)

                    @blk.vector
                    def _(e, dm=dm, xo=xo, xs=xs, ln_ac=ln_ac):
                        p.wait(e, ("ac", ln_ac), (xs, p.C[xs]))
                        e.tensor_tensor(xo[:], y1[:, dm * 512:(dm + 1) * 512], rstdB[:], ALU.mult)
                        i = e.tensor_tensor(xo[:], xo[:], mrB[:], ALU.subtract)
                        p.inc(i, "ve")

                    @blk.gpsimd
                    def _(e, dm=dm, sq=sq, qb=qb, xo=xo, xs=xs):
                        p.wait(e, ("ve", p.C["ve"]))
                        i = e.dma_start(out=x4[sq, dm, :, qb * 512:(qb + 1) * 512], in_=xo[:])
                        p.inc(i, xs, 16)

        @blk.gpsimd
        def _(e):
            p.wait(e, ("x0", p.C["x0"]))
            e.nop()
            e.wait_ge(p.sems["x1"], p.C["x1"])

    pes.close()
    esC.close()


def prep_C_weights(W_qkv, b_qkv, W_out, b_out):
    wqkv = np.ascontiguousarray(
        W_qkv.reshape(48, 128, NJ, 128).transpose(0, 3, 2, 1).reshape(48, 128, NJ * 128))
    bq_r = np.ascontiguousarray(b_qkv.reshape(1, 48 * 128))
    wo_t = np.ascontiguousarray(
        W_out.reshape(NJ, 128, H, 128).transpose(0, 3, 2, 1).reshape(NJ, 128, H * 128))
    bo_r = np.ascontiguousarray(b_out.reshape(1, NJ * 128))
    kk, qq = np.meshgrid(np.arange(128), np.arange(128), indexing="ij")
    maskT = np.where(kk > qq, -1e30, 0.0).astype(np.float32)
    maskS = maskT.T.copy()
    idb = np.eye(128, dtype=BF)
    idf = np.eye(128, dtype=np.float32)
    return dict(wqkv=wqkv, bqkv=bq_r, wout=wo_t, bout=bo_r,
                maskT=maskT, maskS=maskS, idb=idb, idf=idf)





NC1 = 43


def emit_D(p, blk, S=SEQ, skip_p1=False):
    nc = p.nc
    T = NSEQ * S
    NTS = T // 512
    x4 = p.aps["x4"]
    gidx = p.aps["gidx"]
    uidx = p.aps["uidx"]
    didx = p.aps["didx"]
    lut = p.aps["lut"]
    yout = p.aps["yout"]
    wg_scr = p.aps["wg_scr"]
    wu_scr = p.aps["wu_scr"]
    wd_scr = p.aps["wd_scr"]
    hscr = p.aps["hscr"]
    ones_r = p.ones_r
    ones_c = p.ones_c

    esD = contextlib.ExitStack()
    lut_row = p.sb("lut_row", [1, 16], F32, es=esD)
    lutB = p.sb("lutB", [128, 16], F32, es=esD)

    if True:
        pes = contextlib.ExitStack()
        pb = p.ps("pbD", [128, 512], F32, es=pes)

        @blk.sync
        def _(e):
            i = e.dma_start(out=lut_row[:], in_=lut[:, :])
            p.inc(i, "k0", 16)
        lut_lx = p.C["k0"]

        @blk.tensor
        def _(e):
            p.wait(e, ("k0", lut_lx), ("ve", p.init_ve))
            i = e.matmul(pb[:, 0:16], ones_r[:], lut_row[:], start=True, stop=True,
                         skip_group_check=True)
            p.inc(i, "pe")

        @blk.scalar
        def _(e):
            p.wait(e, ("pe", p.C["pe"]))
            i = e.copy(lutB[:], pb[:, 0:16])
            p.inc(i, "ac")
        lut_ac = p.C["ac"]

        itile = [p.sb(f"itD{i}", [128, NC1 * 128], I32, es=pes) for i in range(2)]
        btile = [p.sb(f"btD{i}", [128, NC1 * 128], BF16, es=pes) for i in range(2)]
        atile = [p.sb(f"atD{i}", [128, NC1 * 128], BF16, es=pes) for i in range(2)]
        ttile = p.sb("ttD", [128, NC1 * 128], BF16, es=pes)

        jobs = []
        if not skip_p1:
            for c in range(NC1):
                jobs.append((gidx[c], wg_scr[c], NJ * 128))
                jobs.append((uidx[c], wu_scr[c], NJ * 128))
            for dm in range(NJ):
                jobs.append((didx[dm], wd_scr[dm], NC1 * 128))

        cast_ve = {}
        done_ve = {}
        store_ms = {}
        for jn, (iap, oap, ncols) in enumerate(jobs):
            par = jn % 2
            ibuf, bbuf, abuf = itile[par], btile[par], atile[par]
            isem = "i%d" % par
            msem = "m%d" % par
            dmaq = blk.sync if par == 0 else blk.scalar

            @dmaq
            def _(e, jn=jn, iap=iap, ibuf=ibuf, isem=isem, ncols=ncols):
                if jn >= 2:
                    p.wait(e, ("ve", cast_ve[jn - 2]))
                i = e.dma_start(out=ibuf[:, 0:ncols], in_=iap)
                p.inc(i, isem, 16)
            load_cnt = p.C[isem]

            @blk.vector
            def _(e, jn=jn, ibuf=ibuf, bbuf=bbuf, abuf=abuf, isem=isem,
                  msem=msem, ncols=ncols, load_cnt=load_cnt):
                p.wait(e, (isem, load_cnt), ("ac", lut_ac))
                i = e.tensor_copy(bbuf[:, 0:ncols], ibuf[:, 0:ncols])
                p.inc(i, "ve")
                cast_ve[jn] = p.C["ve"]
                if jn >= 2:
                    e.nop()
                    p.wait(e, (msem, store_ms[jn - 2]))
                e.tensor_scalar(abuf[:, 0:ncols], bbuf[:, 0:ncols], 0.0, lutB[:, 0:1],
                                ALU.is_equal, ALU.mult)
                for j in range(1, 16):
                    e.tensor_scalar(ttile[:, 0:ncols], bbuf[:, 0:ncols], float(j),
                                    lutB[:, j:j + 1], ALU.is_equal, ALU.mult)
                    i = e.tensor_tensor(abuf[:, 0:ncols], abuf[:, 0:ncols], ttile[:, 0:ncols],
                                        ALU.add)
                p.inc(i, "ve")
                done_ve[jn] = p.C["ve"]

            @blk.gpsimd
            def _(e, jn=jn, oap=oap, abuf=abuf, msem=msem, ncols=ncols):
                p.wait(e, ("ve", done_ve[jn]))
                i = e.dma_start(out=oap, in_=abuf[:, 0:ncols])
                p.inc(i, msem, 16)
            store_ms[jn] = p.C[msem]

        p.barrier(blk)
        pes.close()

        pes = contextlib.ExitStack()
        xb = p.sb("xbD", [128, NJ * T], BF16, es=pes)
        xstg = p.sb("xstgD", [128, T], F32, es=pes)
        wb = [p.sb(f"wbD{i}", [128, 2 * NJ * 128], BF16, es=pes) for i in range(2)]
        stmp = [p.sb(f"stmpD{i}", [128, 512], F32, es=pes) for i in range(2)]
        hsub = [p.sb(f"hsubD{i}", [128, 512], BF16, es=pes) for i in range(2)]
        pG = [p.ps(f"pGD{i}", [128, 512], F32, es=pes) for i in range(2)]
        pU = [p.ps(f"pUD{i}", [128, 512], F32, es=pes) for i in range(2)]

        for j in range(NJ):
            @blk.sync
            def _(e, j=j):
                if j > 0:
                    p.wait(e, ("ve", p.C["ve"]))
                for sq in range(NSEQ):
                    i = e.dma_start(out=xstg[:, sq * S:(sq + 1) * S], in_=x4[sq, j])
                    p.inc(i, "lx", 16)

            @blk.vector
            def _(e, j=j):
                p.wait(e, ("lx", p.C["lx"]))
                i = e.tensor_copy(xb[:, j * T:(j + 1) * T], xstg[:])
                p.inc(i, "ve")
        xcast_ve = p.C["ve"]

        @blk.gpsimd
        def _(e):
            p.wait(e, ("ve", xcast_ve))
            for c in range(2):
                i = e.dma_start(out=wb[c][:, 0:NJ * 128], in_=wg_scr[c])
                p.inc(i, "w%d" % c, 16)
                i = e.dma_start(out=wb[c][:, NJ * 128:], in_=wu_scr[c])
                p.inc(i, "w%d" % c, 16)

        gu_pe = {}
        hmul_ve = {}
        hwr_sa = {}
        for c in range(NC1):
            par = c % 2

            for ts in range(NTS):
                g_idx = c * NTS + ts
                q = g_idx % 2

                @blk.tensor
                def _(e, c=c, ts=ts, par=par, q=q, g_idx=g_idx):
                    p.wait(e, ("w%d" % par, p.C["w%d" % par]))
                    e.nop()
                    if g_idx >= 2:
                        p.wait(e, ("ve", hmul_ve[divmod(g_idx - 2, NTS)]))
                    elif g_idx == 0:
                        p.wait(e, ("ve", xcast_ve))
                    for j in range(NJ):
                        e.matmul(pG[q][:], wb[par][:, j * 128:(j + 1) * 128],
                                 xb[:, j * T + ts * 512: j * T + ts * 512 + 512],
                                 start=(j == 0), stop=(j == NJ - 1), skip_group_check=True)
                    for j in range(NJ):
                        i = e.matmul(pU[q][:], wb[par][:, NJ * 128 + j * 128:NJ * 128 + (j + 1) * 128],
                                     xb[:, j * T + ts * 512: j * T + ts * 512 + 512],
                                     start=(j == 0), stop=(j == NJ - 1), skip_group_check=True)
                    p.inc(i, "pe")
                gu_pe[(c, ts)] = p.C["pe"]

                @blk.scalar
                def _(e, c=c, ts=ts, q=q, g_idx=g_idx):
                    p.wait(e, ("pe", gu_pe[(c, ts)]))
                    if g_idx >= 2:
                        e.nop()
                        p.wait(e, ("ve", hmul_ve[divmod(g_idx - 2, NTS)]))
                    i = e.activation(stmp[q][:], pG[q][:], AF.Silu)
                    p.inc(i, "ac")
                silu_ac = p.C["ac"]

                @blk.vector
                def _(e, c=c, ts=ts, q=q, g_idx=g_idx, silu_ac=silu_ac):
                    p.wait(e, ("ac", silu_ac))
                    if g_idx >= 2:
                        e.nop()
                        p.wait(e, ("a%d" % q, hwr_sa[divmod(g_idx - 2, NTS)]))
                    i = e.tensor_tensor(hsub[q][:], stmp[q][:], pU[q][:], ALU.mult)
                    p.inc(i, "ve")
                hmul_ve[(c, ts)] = p.C["ve"]

                @blk.sync
                def _(e, c=c, ts=ts, q=q):
                    p.wait(e, ("ve", hmul_ve[(c, ts)]))
                    i = e.dma_start(out=hscr[c, :, ts * 512:(ts + 1) * 512], in_=hsub[q][:])
                    p.inc(i, "a%d" % q, 16)
                hwr_sa[(c, ts)] = p.C["a%d" % q]

            @blk.gpsimd
            def _(e, c=c, par=par):
                if c + 2 < NC1:
                    p.wait(e, ("pe", gu_pe[(c, NTS - 1)]))
                    i = e.dma_start(out=wb[par][:, 0:NJ * 128], in_=wg_scr[c + 2])
                    p.inc(i, "w%d" % par, 16)
                    i = e.dma_start(out=wb[par][:, NJ * 128:], in_=wu_scr[c + 2])
                    p.inc(i, "w%d" % par, 16)

        p.barrier(blk)
        pes.close()

        pes = contextlib.ExitStack()
        TT = 512
        NTT = T // TT
        acol = p.sb("acolD", [128, NC1 * TT], BF16, es=pes)
        wdt = [p.sb(f"wdtD{i}", [128, NC1 * 128], BF16, es=pes) for i in range(2)]
        res = [p.sb(f"resD{i}", [128, TT], F32, es=pes) for i in range(2)]
        y2 = p.sb("y2D", [128, NJ * TT], F32, es=pes)
        y2b = p.sb("y2bD", [128, TT], BF16, es=pes)
        y2s = p.sb("y2sD", [128, TT], BF16, es=pes)
        srow = p.sb("srowD", [1, 1536], F32, es=pes)
        rstdB = p.sb("rstdBD", [128, 512], F32, es=pes)
        mrB = p.sb("mrBD", [128, 512], F32, es=pes)
        yt = [p.sb(f"ytD{i}", [128, 512], F32, es=pes) for i in range(2)]
        PS = [p.ps(f"PSD{i}", [128, 512], F32, es=pes) for i in range(6)]

        for tt in range(NTT):
            sq, trs = divmod(tt * TT, S)
            pe_prev_tile = p.C["pe"]

            @blk.sync
            def _(e, tt=tt):
                if tt > 0:
                    p.wait(e, ("pe", pe_prev_tile))
                for a in range(NC1):
                    i = e.dma_start(out=acol[:, a * TT:(a + 1) * TT],
                                    in_=hscr[a, :, tt * TT:(tt + 1) * TT])
                    p.inc(i, "lx", 16)
            acol_lx = p.C["lx"]

            wd_lw = {}

            @blk.gpsimd
            def _(e, tt=tt, sq=sq, trs=trs):
                if tt > 0:
                    p.wait(e, ("pe", pe_prev_tile))
                for dmp in range(2):
                    i = e.dma_start(out=wdt[dmp][:], in_=wd_scr[dmp])
                    i2 = e.dma_start(out=res[dmp][:], in_=x4[sq, dmp, :, trs:trs + TT])
                    i.then_inc(p.sems["d%d" % dmp], 16)
                    p.C["d%d" % dmp] += 16
                    wd_lw[dmp] = p.inc(i2, "d%d" % dmp, 16)

            y2_ve = {}
            sqr_ac = {}
            dmm_pe = {}

            def down_mm(dm):
                po = PS[dm % 2]

                @blk.tensor
                def _(e, dm=dm, po=po):
                    p.wait(e, ("d%d" % (dm % 2), wd_lw[dm]))
                    e.nop()
                    if dm == 0:
                        p.wait(e, ("lx", acol_lx))
                    elif dm >= 2:
                        p.wait(e, ("ve", y2_ve[dm - 2]))
                    for a in range(NC1):
                        i = e.matmul(po[:], wdt[dm % 2][:, a * 128:(a + 1) * 128],
                                     acol[:, a * TT:(a + 1) * TT],
                                     start=(a == 0), stop=(a == NC1 - 1), skip_group_check=True)
                    p.inc(i, "pe")
                dmm_pe[dm] = p.C["pe"]

            def stats_mm(dm):
                @blk.tensor
                def _(e, dm=dm):
                    p.wait(e, ("ac", sqr_ac[dm]))
                    e.matmul(PS[4][0:1, :], ones_c[:], y2b[:],
                             start=(dm == 0), stop=(dm == NJ - 1), skip_group_check=True)
                    i = e.matmul(PS[5][0:1, :], ones_c[:], y2s[:],
                                 start=(dm == 0), stop=(dm == NJ - 1), skip_group_check=True)
                    p.inc(i, "pe")

            def down_evac(dm):
                po = PS[dm % 2]
                pe_snap = p.C["pe"]

                @blk.vector
                def _(e, dm=dm, po=po, pe_snap=pe_snap):
                    p.wait(e, ("d%d" % (dm % 2), wd_lw[dm]))
                    e.nop()
                    p.wait(e, ("pe", pe_snap))
                    e.tensor_tensor(y2[:, dm * TT:(dm + 1) * TT], po[:], res[dm % 2][:], ALU.add)
                    i = e.tensor_copy(y2b[:], y2[:, dm * TT:(dm + 1) * TT])
                    p.inc(i, "ve")
                y2_ve[dm] = p.C["ve"]

                @blk.scalar
                def _(e, dm=dm):
                    p.wait(e, ("ve", y2_ve[dm]))
                    i = e.activation(y2s[:], y2[:, dm * TT:(dm + 1) * TT], AF.Square)
                    p.inc(i, "ac")
                sqr_ac[dm] = p.C["ac"]

            def prefetch_next(dm):
                @blk.gpsimd
                def _(e, dm=dm, sq=sq, trs=trs):
                    p.wait(e, ("pe", dmm_pe[dm]), ("ve", y2_ve[dm]))
                    i = e.dma_start(out=wdt[dm % 2][:], in_=wd_scr[dm + 2])
                    i2 = e.dma_start(out=res[dm % 2][:], in_=x4[sq, dm + 2, :, trs:trs + TT])
                    i.then_inc(p.sems["d%d" % (dm % 2)], 16)
                    p.C["d%d" % (dm % 2)] += 16
                    wd_lw[dm + 2] = p.inc(i2, "d%d" % (dm % 2), 16)

            for dm in range(NJ):
                down_mm(dm)
                if dm > 0:
                    stats_mm(dm - 1)
                down_evac(dm)
                if dm + 2 < NJ:
                    prefetch_next(dm)
            stats_mm(NJ - 1)

            @blk.vector
            def _(e):
                p.wait(e, ("pe", p.C["pe"]))
                e.tensor_scalar(srow[0:1, 0:512], PS[4][0:1, :], 1.0 / D, None, ALU.mult)
                e.tensor_scalar(srow[0:1, 512:1024], PS[5][0:1, :], 1.0 / D, None, ALU.mult)
                e.tensor_tensor(srow[0:1, 1024:1536], srow[0:1, 0:512], srow[0:1, 0:512], ALU.mult)
                e.tensor_tensor(srow[0:1, 512:1024], srow[0:1, 512:1024], srow[0:1, 1024:1536], ALU.subtract)
                i = e.tensor_scalar(srow[0:1, 512:1024], srow[0:1, 512:1024], LN_EPS, None, ALU.add)
                p.inc(i, "ve")

            @blk.scalar
            def _(e):
                p.wait(e, ("ve", p.C["ve"]))
                i = e.activation(srow[0:1, 512:1024], srow[0:1, 512:1024], AF.Sqrt)
                p.inc(i, "ac")

            @blk.vector
            def _(e):
                p.wait(e, ("ac", p.C["ac"]))
                e.reciprocal(srow[0:1, 512:1024], srow[0:1, 512:1024])
                i = e.tensor_tensor(srow[0:1, 1024:1536], srow[0:1, 0:512], srow[0:1, 512:1024], ALU.mult)
                p.inc(i, "ve")

            @blk.tensor
            def _(e):
                p.wait(e, ("ve", p.C["ve"]))
                e.matmul(PS[2][:], ones_r[:], srow[0:1, 512:1024], start=True, stop=True,
                         skip_group_check=True)
                i = e.matmul(PS[3][:], ones_r[:], srow[0:1, 1024:1536], start=True, stop=True,
                             skip_group_check=True)
                p.inc(i, "pe")

            @blk.scalar
            def _(e):
                p.wait(e, ("pe", p.C["pe"]))
                e.copy(rstdB[:], PS[2][:])
                i = e.copy(mrB[:], PS[3][:])
                p.inc(i, "ac")
            ln_ac = p.C["ac"]

            for dm in range(NJ):
                xo = yt[dm % 2]
                xs = "x%d" % (dm % 2)

                @blk.vector
                def _(e, dm=dm, xo=xo, xs=xs, ln_ac=ln_ac):
                    p.wait(e, ("ac", ln_ac), (xs, p.C[xs]))
                    e.tensor_tensor(xo[:], y2[:, dm * TT:(dm + 1) * TT], rstdB[:], ALU.mult)
                    i = e.tensor_tensor(xo[:], xo[:], mrB[:], ALU.subtract)
                    p.inc(i, "ve")

                @blk.gpsimd
                def _(e, dm=dm, sq=sq, trs=trs, xo=xo, xs=xs):
                    p.wait(e, ("ve", p.C["ve"]))
                    i = e.dma_start(out=yout[sq, dm, :, trs:trs + TT], in_=xo[:])
                    p.inc(i, xs, 16)

        @blk.gpsimd
        def _(e):
            p.wait(e, ("x0", p.C["x0"]))
            e.nop()
            e.wait_ge(p.sems["x1"], p.C["x1"])

    pes.close()
    esD.close()


def prep_D_weights(lut_v, gate_idx, up_idx, down_idx):
    gidx = np.ascontiguousarray(
        gate_idx.reshape(NC1, 128, NJ, 128).transpose(0, 3, 2, 1).reshape(NC1, 128, NJ * 128)).astype(np.int32)
    uidx = np.ascontiguousarray(
        up_idx.reshape(NC1, 128, NJ, 128).transpose(0, 3, 2, 1).reshape(NC1, 128, NJ * 128)).astype(np.int32)
    didx = np.ascontiguousarray(
        down_idx.reshape(NJ, 128, NC1, 128).transpose(0, 3, 2, 1).reshape(NJ, 128, NC1 * 128)).astype(np.int32)
    return dict(gidx=gidx, uidx=uidx, didx=didx,
                lut=np.asarray(lut_v, np.float32).reshape(1, 16))



def build_fused(S=SEQ):
    p = P("F")
    T = NSEQ * S
    a = {}
    a["xin"] = p.dram("xin", [NSEQ, NJ, 128, S], F32, kind="ExternalInput")
    a["yout"] = p.dram("yout", [NSEQ, NJ, 128, S], F32, kind="ExternalOutput")
    a["wkv"] = p.dram("wkv", [2, NJ, 128, 512], F32, kind="ExternalInput")
    a["bkv"] = p.dram("bkv", [2, 512], F32, kind="ExternalInput")
    a["wq"] = p.dram("wq", [H, 128, NJ * 128], F32, kind="ExternalInput")
    a["bq"] = p.dram("bq", [1, H * 128], F32, kind="ExternalInput")
    a["wo"] = p.dram("wo", [NJ, 128, H * 128], F32, kind="ExternalInput")
    a["w1h"] = p.dram("w1h", [NC0, 128, NJ * 128], F32, kind="ExternalInput")
    a["w3h"] = p.dram("w3h", [NC0, 128, NJ * 128], F32, kind="ExternalInput")
    a["w2r"] = p.dram("w2r", [64, 128, D], F32, kind="ExternalInput")
    a["h64"] = p.dram("h64", [64, 64], BF16, kind="ExternalInput")
    a["h128"] = p.dram("h128", [128, 128], BF16, kind="ExternalInput")
    a["wqkv"] = p.dram("wqkv", [48, 128, NJ * 128], F32, kind="ExternalInput")
    a["bqkv"] = p.dram("bqkv", [1, 48 * 128], F32, kind="ExternalInput")
    a["wout"] = p.dram("wout", [NJ, 128, H * 128], F32, kind="ExternalInput")
    a["bout"] = p.dram("bout", [1, NJ * 128], F32, kind="ExternalInput")
    a["maskT"] = p.dram("maskT", [128, 128], F32, kind="ExternalInput")
    a["maskS"] = p.dram("maskS", [128, 128], F32, kind="ExternalInput")
    a["idb"] = p.dram("idb", [128, 128], BF16, kind="ExternalInput")
    a["idf"] = p.dram("idf", [128, 128], F32, kind="ExternalInput")
    a["gidx"] = p.dram("gidx", [NC1, 128, NJ * 128], I32, kind="ExternalInput")
    a["uidx"] = p.dram("uidx", [NC1, 128, NJ * 128], I32, kind="ExternalInput")
    a["didx"] = p.dram("didx", [NJ, 128, NC1 * 128], I32, kind="ExternalInput")
    a["lut"] = p.dram("lut", [1, 16], F32, kind="ExternalInput")
    a["x2"] = p.dram("x2", [NSEQ, NJ, 128, S], F32)
    a["x3"] = p.dram("x3", [NSEQ, NJ, 128, S], F32)
    a["x4"] = p.dram("x4", [NSEQ, NJ, 128, S], F32)
    a["wqb_scr"] = p.dram("wqb_scr", [H, 128, NJ * 128], BF16)
    a["wob_scr"] = p.dram("wob_scr", [NJ, 128, H * 128], BF16)
    a["ascr"] = p.dram("ascr", [NC0, 128, T], BF16)
    a["w2b_scr"] = p.dram("w2b_scr", [NJ, 128, 64, 128], BF16)
    a["wqkv_scr"] = p.dram("wqkv_scr", [48, 128, NJ * 128], BF16)
    a["wout_scr"] = p.dram("wout_scr", [NJ, 128, H * 128], BF16)
    a["qscr"] = p.dram("qscr", [NSEQ, H, 128, S], BF16)
    a["kscr"] = p.dram("kscr", [NSEQ, H, 128, S], BF16)
    a["vscr"] = p.dram("vscr", [NSEQ, H, S, 128], BF16)
    a["wg_scr"] = p.dram("wg_scr", [NC1, 128, NJ * 128], BF16)
    a["wu_scr"] = p.dram("wu_scr", [NC1, 128, NJ * 128], BF16)
    a["wd_scr"] = p.dram("wd_scr", [NJ, 128, NC1 * 128], BF16)
    a["hscr"] = p.dram("hscr", [NC1, 128, T], BF16)
    p.aps = a

    for s in ("lx", "pe", "ve", "ac",
              "w0", "w1", "s0", "s1", "q0", "q1", "o0", "o1", "x0", "x1",
              "a0", "a1", "d0", "d1", "k0", "k1", "i0", "i1", "m0", "m1"):
        p.sem(s)

    p.ones_r = p.sb("ones_r", [1, 128], F32)
    p.ones_c = p.sb("ones_c", [128, 1], BF16)
    p.lut_row = p.sb("lut_rowF", [1, 16], F32)
    p.lutB = p.sb("lutBF", [128, 16], F32)

    with p.nc.Block() as blk:
        pes0 = contextlib.ExitStack()
        pbF = p.ps("pbF", [128, 512], F32, es=pes0)

        @blk.vector
        def _(e):
            e.memset(p.ones_r[:], 1.0)
            i = e.memset(p.ones_c[:], 1.0)
            p.inc(i, "ve")
        p.init_ve = p.C["ve"]

        @blk.sync
        def _(e):
            i = e.dma_start(out=p.lut_row[:], in_=a["lut"][:, :])
            p.inc(i, "i0", 16)
        lut_lxF = p.C["i0"]

        @blk.tensor
        def _(e):
            p.wait(e, ("i0", lut_lxF), ("ve", p.init_ve))
            i = e.matmul(pbF[:, 0:16], p.ones_r[:], p.lut_row[:], start=True, stop=True,
                         skip_group_check=True)
            p.inc(i, "pe")

        @blk.scalar
        def _(e):
            p.wait(e, ("pe", p.C["pe"]))
            i = e.copy(p.lutB[:], pbF[:, 0:16])
            p.inc(i, "ac")
        p.init_ac = p.C["ac"]
        pes0.close()

        emit_A(p, blk, S)
        p.barrier(blk)
        emit_B(p, blk, S)
        p.barrier(blk)
        emit_C(p, blk, S)
        p.barrier(blk)
        emit_D(p, blk, S, skip_p1=True)
    return p


_PROGS = {}


def _get_fused():
    if "F" not in _PROGS:
        _PROGS["F"] = build_fused()
    return _PROGS["F"]


def kernel(L0_attn_hidden_states, L0_attn_Wq, L0_attn_bq, L0_attn_Wk, L0_attn_bk,
           L0_attn_Wv, L0_attn_bv, L0_attn_Wo, L0_ffn_W1, L0_ffn_W2, L0_ffn_W3,
           L1_attn_W_qkv, L1_attn_b_qkv, L1_attn_W_out, L1_attn_b_out,
           L1_ffn_lut, L1_ffn_gate_idx, L1_ffn_up_idx, L1_ffn_down_idx):
    x = np.ascontiguousarray(np.asarray(L0_attn_hidden_states, np.float32))
    w = {}
    w.update(prep_A_weights(np.asarray(L0_attn_Wq, np.float32), np.asarray(L0_attn_bq, np.float32),
                            np.asarray(L0_attn_Wk, np.float32), np.asarray(L0_attn_bk, np.float32),
                            np.asarray(L0_attn_Wv, np.float32), np.asarray(L0_attn_bv, np.float32),
                            np.asarray(L0_attn_Wo, np.float32)))
    w.update(prep_B_weights(np.asarray(L0_ffn_W1, np.float32), np.asarray(L0_ffn_W2, np.float32),
                            np.asarray(L0_ffn_W3, np.float32)))
    w.update(prep_C_weights(np.asarray(L1_attn_W_qkv, np.float32), np.asarray(L1_attn_b_qkv, np.float32),
                            np.asarray(L1_attn_W_out, np.float32), np.asarray(L1_attn_b_out, np.float32)))
    w.update(prep_D_weights(np.asarray(L1_ffn_lut, np.float32), np.asarray(L1_ffn_gate_idx),
                            np.asarray(L1_ffn_up_idx), np.asarray(L1_ffn_down_idx)))

    pF = _get_fused()
    in_maps = []
    for c in range(NCORES):
        m = dict(w)
        m["xin"] = to_fm(x[c * NSEQ:(c + 1) * NSEQ])
        in_maps.append(m)
    res = run_bass_kernel_spmd(pF.nc, in_maps, core_ids=list(range(NCORES)))
    return np.concatenate([from_fm(np.asarray(r["yout"])) for r in res.results], axis=0)
